# version 10
# baseline (speedup 1.0000x reference)
import math
from contextlib import ExitStack

import numpy as np
import ml_dtypes

import concourse.bass as bass
import concourse.mybir as mybir
from concourse.bass_utils import run_bass_kernel_spmd

F32 = mybir.dt.float32
BF16 = mybir.dt.bfloat16
AF = mybir.ActivationFunctionType
ALU = mybir.AluOpType
AX = mybir.AxisListType.X

BF = ml_dtypes.bfloat16
NCORES = 8
PAIRS = [[0, 1], [2, 3], [4, 5], [6, 7]]
B, T, D = 4, 4096, 1024
TLOC = T // 2
NH, DH, DK = 16, 64, 128
DI = 4096
DN = 2048
DS = 16
DTR = 64
E, HMOE = 8, 2048
PI = float(np.pi)
EPS = 1e-5
NCH = D // 128
NDC = DN // 128
MT512 = [(o, 512) for o in range(0, TLOC, 512)]



class L:

    def __init__(self, name):
        self.name = name
        self.nc = bass.Bass("TRN2", target_bir_lowering=False)
        self.stack = ExitStack()
        self.ph = None
        self.c = {}
        self.sems = {}

    def din(self, name, shape, dt=F32):
        return self.nc.dram_tensor(name, list(shape), dt, kind="ExternalInput").ap()

    def dout(self, name, shape, dt=F32):
        return self.nc.dram_tensor(name, list(shape), dt, kind="ExternalOutput").ap()

    def dscratch(self, name, shape, dt=F32):
        return self.nc.dram_tensor(name, list(shape), dt, kind="Internal").ap()

    def sb(self, name, shape, dt=F32):
        st = self.ph if self.ph is not None else self.stack
        return st.enter_context(self.nc.sbuf_tensor(name, list(shape), dt))

    def ps(self, name, shape, dt=F32):
        st = self.ph if self.ph is not None else self.stack
        return st.enter_context(self.nc.psum_tensor(name, list(shape), dt))

    def sem(self, *names):
        for n in names:
            self.sems[n] = self.stack.enter_context(self.nc.semaphore(n))
            self.c[n] = 0

    def block(self):
        self.dummy = self.stack.enter_context(self.nc.sbuf_tensor("l_dummy", [1, 16], F32))
        self.blk = self.stack.enter_context(self.nc.Block())

    def mark(self, eng, sname):
        if hasattr(eng, "tensor_copy"):
            i = eng.tensor_copy(self.dummy[:1, 0:8], self.dummy[:1, 8:16])
        else:
            i = eng.copy(self.dummy[:1, 0:8], self.dummy[:1, 8:16])
        self.inc(i, sname)

    def inc(self, inst, sname, n=1):
        inst.then_inc(self.sems[sname], n)
        self.c[sname] += n
        return inst

    def wait(self, eng, sname, cnt=None):
        eng.wait_ge(self.sems[sname], self.c[sname] if cnt is None else cnt)

    def on(self, engine_name, fn):
        getattr(self.blk, engine_name)(fn)

    def begin_phase(self, name):
        assert self.ph is None
        self.ph = ExitStack()
        self._scope = self.nc.named_scope(name)
        self._scope.__enter__()
        waits = [(s, self.c[s]) for s in self.sems if self.c[s] > 0]
        for eng in ("sync", "scalar", "vector", "tensor", "gpsimd"):
            def bw(e, waits=tuple(waits)):
                for s, cnt in waits:
                    e.wait_ge(self.sems[s], cnt)
            self.on(eng, bw)

    def end_phase(self):
        for eng in ("scalar", "vector", "gpsimd"):
            def mk(e):
                self.mark(e, "ph")
            self.on(eng, mk)
        self.ph.close()
        self.ph = None
        self._scope.__exit__(None, None, None)

    def cc(self, kind, op, in_ap, out_ap, sname):
        def g_fn(g):
            g.wait_ge(self.sems["st"], self.c["st"])
            i = g.collective_compute(kind, op, replica_groups=PAIRS,
                                     ins=[in_ap.opt()], outs=[out_ap.opt()])
            self.inc(i, sname)
            g.wait_ge(self.sems[sname], self.c[sname])
        self.on("gpsimd", g_fn)

    def close(self):
        self.stack.close()


def _bfh(a):
    return np.ascontiguousarray(np.asarray(a).astype(BF))


def _f32h(a):
    return np.ascontiguousarray(np.asarray(a, dtype=np.float32))


def _tile_widths(n, w):
    out = []
    while n > 0:
        out.append(min(w, n))
        n -= out[-1]
    return out


def host_prep_feat_cols(vec):
    v = np.asarray(vec, np.float32)
    return np.ascontiguousarray(v.reshape(-1, 128).T)



def emit_ln(lk, kit, get_x, put_out, w, sq_full, out_war_sems=(), nchunks=NCH):
    nc = lk.nc

    def s_fn(s):
        lk.wait(s, "gp", kit.gp_done)
        lk.wait(s, "ld")
        lk.wait(s, "ev")
        for c in range(nchunks):
            i = s.activation(sq_full[:, c * w:(c + 1) * w], get_x(c), AF.Square)
        lk.inc(i, "sc")

    lk.on("scalar", s_fn)

    def t_fn(t):
        lk.wait(t, "gp", kit.gp_done)
        lk.wait(t, "ev")
        lk.wait(t, "ld")
        ones_c = kit.ones_col_bf if get_x(0).dtype == BF16 else kit.ones_col
        for c in range(nchunks):
            nc.tensor.matmul(kit.p_sum[:1, 0:w], ones_c[:], get_x(c),
                             start=(c == 0), stop=(c == nchunks - 1))
        lk.wait(t, "sc")
        for c in range(nchunks):
            i = nc.tensor.matmul(kit.p_sumsq[:1, 0:w], kit.ones_col[:],
                                 sq_full[:, c * w:(c + 1) * w],
                                 start=(c == 0), stop=(c == nchunks - 1))
        lk.inc(i, "mm")

    lk.on("tensor", t_fn)

    def v_fn(v):
        lk.wait(v, "mm")
        mean = kit.row[:1, 0:w]
        var = kit.row[:1, kit.maxw:kit.maxw + w]
        v.tensor_scalar(mean, kit.p_sum[:1, 0:w], 1.0 / D, 0.0, ALU.mult, ALU.add)
        v.tensor_tensor(var, mean, mean, ALU.mult)
        i = v.scalar_tensor_tensor(var, kit.p_sumsq[:1, 0:w], 1.0 / D, var,
                                   ALU.mult, ALU.subtract)
        lk.inc(i, "ev")

    lk.on("vector", v_fn)

    def s2_fn(s):
        lk.wait(s, "gp", kit.gp_done)
        lk.wait(s, "ev")
        var = kit.row[:1, kit.maxw:kit.maxw + w]
        s.activation(var, var, AF.Ln, bias=kit.eps_col[:1])
        i = s.activation(var, var, AF.Exp, scale=kit.half_col[:1])
        lk.inc(i, "sc")

    lk.on("scalar", s2_fn)

    def t2_fn(t):
        lk.wait(t, "sc")
        nc.tensor.matmul(kit.p_rep[:, 0:w], kit.ones_row[:], kit.row[:1, 0:w],
                         start=True, stop=True)
        i = nc.tensor.matmul(kit.p_rep[:, kit.maxw:kit.maxw + w], kit.ones_row[:],
                             kit.row[:1, kit.maxw:kit.maxw + w], start=True, stop=True)
        lk.inc(i, "mm")

    lk.on("tensor", t2_fn)

    def v2_fn(v):
        lk.wait(v, "mm")
        for s_ in out_war_sems:
            lk.wait(v, s_)
        for c in range(nchunks):
            put_out(c, v)
        lk.mark(v, "ev")

    lk.on("vector", v2_fn)


class LNScratch:
    def __init__(self, lk, maxw=512, name="ln"):
        self.lk = lk
        self.maxw = maxw
        self.row = lk.sb(f"{name}_row", [1, 2 * maxw], F32)
        self.p_sum = lk.ps(f"{name}_psum", [1, maxw], F32)
        self.p_sumsq = lk.ps(f"{name}_psumsq", [1, maxw], F32)
        self.p_rep = lk.ps(f"{name}_prep", [128, 2 * maxw], F32)
        self.ones_col = lk.sb(f"{name}_ones_c", [128, 1], F32)
        self.ones_col_bf = lk.sb(f"{name}_ones_cb", [128, 1], BF16)
        self.ones_row = lk.sb(f"{name}_ones_r", [1, 128], F32)
        self.eps_col = lk.sb(f"{name}_eps", [1, 1], F32)
        self.half_col = lk.sb(f"{name}_half", [1, 1], F32)

    def init_consts(self):
        lk = self.lk

        def g_fn(g):
            g.memset(self.ones_col[:], 1.0)
            g.memset(self.ones_col_bf[:], 1.0)
            g.memset(self.ones_row[:], 1.0)
            g.memset(self.eps_col[:], EPS)
            i = g.memset(self.half_col[:], -0.5)
            lk.inc(i, "gp")

        lk.on("gpsimd", g_fn)
        self.gp_done = lk.c["gp"]



class PsumPool:

    def __init__(self, lk, nbanks=2, tag="pb"):
        self.lk = lk
        self.n = nbanks
        self.banks = [lk.ps(f"{tag}{i}", [128, 512], F32) for i in range(nbanks)]
        self.free_sems = [f"{tag}f{i}" for i in range(nbanks)]
        self.uses = [0] * nbanks
        self.rr = 0

    def acquire(self, teng):
        i = self.rr
        self.rr = (self.rr + 1) % self.n
        if self.uses[i] > 0:
            teng.wait_ge(self.lk.sems[self.free_sems[i]], self.uses[i])
        self.uses[i] += 1
        return i, self.banks[i]

    def freed(self, inst, i):
        inst.then_inc(self.lk.sems[self.free_sems[i]], 1)
        self.lk.c[self.free_sems[i]] += 1


def mm_block(lk, pool, get_lhsT, get_rhs, nK, n_list, m_list, evac,
             evac_engine="vector", pre_wait=None):
    nc = lk.nc
    first = True
    for n in n_list:
        for (mo, w) in m_list:
            holder = {}

            def t_fn(t, n=n, mo=mo, w=w, first=first):
                if pre_wait is not None and first:
                    pre_wait(t)
                ib, bank = pool.acquire(t)
                holder["b"] = (ib, bank)
                for kc in range(nK):
                    i = nc.tensor.matmul(bank[:, 0:w], get_lhsT(kc, n),
                                         get_rhs(kc, mo, w),
                                         start=(kc == 0), stop=(kc == nK - 1))
                lk.inc(i, "mm")

            lk.on("tensor", t_fn)
            ib, bank = holder["b"]

            def e_fn(e, n=n, mo=mo, w=w, ib=ib, bank=bank):
                lk.wait(e, "mm")
                last = evac(e, bank, n, mo, w)
                pool.freed(last, ib)

            lk.on(evac_engine, e_fn)
            first = False


def emit_barrier(lk, engine_name, sname="ev"):
    def f(e):
        lk.mark(e, sname)

    lk.on(engine_name, f)


def load_weight_bf16(lk, g, dst_sb, hbm_ap, nK, ncols, sem="wt"):
    for kc in range(nK):
        i = g.dma_start(out=dst_sb[:, kc * ncols:(kc + 1) * ncols],
                        in_=hbm_ap[kc * 128:(kc + 1) * 128, :])
        lk.inc(i, sem, 16)



def emit_xn(lk, io):
    NT = TLOC + 1
    x_in, g_in, b_in = io["xT"], io["ln1g"], io["ln1b"]
    xn_out, x2_out = io["xn_s"], io["x2_s"]
    lk.begin_phase("xn")
    widths = [257] + _tile_widths(TLOC - 256, 512)
    xt = lk.sb("xn_xt", [128, NCH * 512], F32)
    xo = lk.sb("xn_xo", [128, NCH * 512], BF16)
    x2o = lk.sb("xn_x2o", [128, NCH * 512], BF16)
    sq_full = lk.sb("xn_sq", [128, NCH * 512], F32)
    gcol = lk.sb("xn_gcol", [128, NCH], F32)
    bcol = lk.sb("xn_bcol", [128, NCH], F32)
    kit = LNScratch(lk, 512, name="xn_ln")
    kit.init_consts()

    def ld0(s):
        i = s.dma_start(out=gcol[:], in_=g_in)
        lk.inc(i, "ld", 16)
        i = s.dma_start(out=bcol[:], in_=b_in)
        lk.inc(i, "ld", 16)

    lk.on("sync", ld0)

    t0 = 0
    for w in widths:
        o = t0

        def ld_fn(s, o=o, w=w):
            lk.wait(s, "ev")
            for c in range(NCH):
                i = s.dma_start(out=xt[:, c * w:(c + 1) * w],
                                in_=x_in[c * 128:(c + 1) * 128, o:o + w])
                lk.inc(i, "ld", 16)

        lk.on("sync", ld_fn)

        def get_x(c, w=w):
            return xt[:, c * w:(c + 1) * w]

        def put_out(c, v, w=w):
            tmp = sq_full[:, c * w:(c + 1) * w]
            v.tensor_tensor(tmp, get_x(c), kit.p_rep[:, 0:w], ALU.subtract)
            v.tensor_tensor(xo[:, c * w:(c + 1) * w], tmp,
                            kit.p_rep[:, 512:512 + w], ALU.mult)

        emit_ln(lk, kit, get_x, put_out, w, sq_full, out_war_sems=("st",))

        def get_x2(c, w=w):
            return xo[:, c * w:(c + 1) * w]

        def put_out2(c, v, w=w):
            tmp = sq_full[:, c * w:(c + 1) * w]
            d = x2o[:, c * w:(c + 1) * w]
            v.tensor_tensor(tmp, get_x2(c), kit.p_rep[:, 0:w], ALU.subtract)
            v.tensor_tensor(tmp, tmp, kit.p_rep[:, 512:512 + w], ALU.mult)
            v.tensor_scalar(d, tmp, gcol[:, c:c + 1], bcol[:, c:c + 1],
                            ALU.mult, ALU.add)

        emit_ln(lk, kit, get_x2, put_out2, w, sq_full, out_war_sems=("st",))

        def st_fn(s, o=o, w=w):
            lk.wait(s, "ev")
            for c in range(NCH):
                i = s.dma_start(out=xn_out[c * 128:(c + 1) * 128, o:o + w],
                                in_=xo[:, c * w:(c + 1) * w])
                lk.inc(i, "st", 16)
            x2src0 = 1 if o == 0 else 0
            for c in range(NCH):
                i = s.dma_start(
                    out=x2_out[c * 128:(c + 1) * 128, max(o - 1, 0):o + w - 1],
                    in_=x2o[:, c * w + x2src0:(c + 1) * w])
                lk.inc(i, "st", 16)

        lk.on("sync", st_fn)
        t0 += w

    def fin(s):
        lk.wait(s, "st")

    lk.on("sync", fin)
    lk.end_phase()



def emit_perf_kv(lk, io, pool):
    x2_in = io["x2_s"]
    wq_in, wk_in, wv_in = io["p_wq"], io["p_wk"], io["p_wv"]
    bq_in, bk_in, bv_in, proj_in = io["p_bq"], io["p_bk"], io["p_bv"], io["proj"]
    qf_out, kvp_out = io["qf_s"], io["kvp_i"]
    lk.begin_phase("perfkv")
    wb = lk.c["wt"]
    x2 = lk.sb("pk_x2", [128, NCH * TLOC], BF16)
    wq = lk.sb("pk_wq", [128, NCH * D], BF16)
    wk = lk.sb("pk_wk", [128, NCH * D], BF16)
    wv = lk.sb("pk_wv", [128, NCH * D], BF16)
    qf = lk.sb("pk_qf", [128, NCH * TLOC], BF16)
    kf = lk.sb("pk_kf", [128, NCH * TLOC], BF16)
    v_full = lk.sb("pk_v", [128, 16 * D], BF16)
    proj = lk.sb("pk_proj", [128, 2 * DK], BF16)
    bq = lk.sb("pk_bq", [128, NCH], F32)
    bk = lk.sb("pk_bk", [128, NCH], F32)
    bvr = lk.sb("pk_bvr", [1, D], F32)
    bv_rep = lk.sb("pk_bvrep", [128, D], BF16)
    onesr = lk.sb("pk_onesr", [1, 128], F32)
    wsc = lk.sb("pk_wsc", [128, 128], F32)
    csc = lk.sb("pk_csc", [128, 128], F32)
    aw = lk.sb("pk_aw", [128, 128], F32)
    kp = lk.sb("pk_kp", [128, 2 * DK], BF16)
    kvsb = lk.sb("pk_kvsb", [128, 4 * 512], F32)
    cst = lk.sb("pk_cst", [128, 2], F32)
    p_pk = lk.ps("pk_ppk", [128, DK], F32)
    kv4 = [lk.ps(f"pk_pkv{i}", [128, 512], F32) for i in range(4)]
    bgp = lk.c["gp"]

    def g0(g):
        g.memset(cst[:, 0:1], PI / 2)
        g.memset(cst[:, 1:2], -1.0)
        i = g.memset(onesr[:], 1.0)
        lk.inc(i, "gp")
        load_weight_bf16(lk, g, wq, wq_in, NCH, D)
        load_weight_bf16(lk, g, wk, wk_in, NCH, D)
        load_weight_bf16(lk, g, wv, wv_in, NCH, D)
        g.memset(proj[:], 0.0)
        i = g.dma_start(out=proj[0:64, 0:DK], in_=proj_in)
        lk.inc(i, "wt", 16)
        i = g.dma_start(out=proj[64:128, DK:2 * DK], in_=proj_in)
        lk.inc(i, "wt", 16)

    lk.on("gpsimd", g0)

    def ld0(s):
        for c in range(NCH):
            i = s.dma_start(out=x2[:, c * TLOC:(c + 1) * TLOC],
                            in_=x2_in[c * 128:(c + 1) * 128, :])
            lk.inc(i, "ld", 16)
        for t, srcp in [(bq, bq_in), (bk, bk_in), (bvr, bv_in)]:
            i = s.dma_start(out=t[:], in_=srcp)
            lk.inc(i, "ld", 16)

    lk.on("sync", ld0)

    def t_bv(t):
        lk.wait(t, "ld")
        lk.wait(t, "gp", bgp + 1)
        ib0, b0 = pool.acquire(t)
        i = lk.nc.tensor.matmul(b0[:, 0:512], onesr[:], bvr[:1, 0:512],
                                start=True, stop=True)
        lk.inc(i, "mm")
        ib1, b1 = pool.acquire(t)
        i = lk.nc.tensor.matmul(b1[:, 0:512], onesr[:], bvr[:1, 512:1024],
                                start=True, stop=True)
        lk.inc(i, "mm")
        t_bv.banks = [(ib0, b0), (ib1, b1)]

    lk.on("tensor", t_bv)

    def v_bv(v):
        lk.wait(v, "mm")
        (ib0, b0), (ib1, b1) = t_bv.banks
        i = v.tensor_copy(bv_rep[:, 0:512], b0[:, 0:512])
        pool.freed(i, ib0)
        i = v.tensor_copy(bv_rep[:, 512:1024], b1[:, 0:512])
        pool.freed(i, ib1)
        lk.mark(v, "ev")

    lk.on("vector", v_bv)

    for (wt_sb, b_sb, dst, wt_cnt) in [(wq, bq, qf, 16 * 8), (wk, bk, kf, 16 * 16)]:
        def evac_qk(e, bank, n, mo, w, b_sb=b_sb, dst=dst):
            return e.tensor_scalar(dst[:, n * TLOC + mo:n * TLOC + mo + w],
                                   bank[:, 0:w], b_sb[:, n:n + 1], 0.0,
                                   ALU.add, ALU.add)

        mm_block(lk, pool,
                 get_lhsT=lambda kc, n, wt_sb=wt_sb: wt_sb[:, kc * D + n * 128:kc * D + (n + 1) * 128],
                 get_rhs=lambda kc, mo, w: x2[:, kc * TLOC + mo:kc * TLOC + mo + w],
                 nK=NCH, n_list=range(NCH), m_list=MT512,
                 evac=evac_qk, evac_engine="vector",
                 pre_wait=lambda t, wc=wt_cnt: (lk.wait(t, "wt", wb + wc), lk.wait(t, "ld")))
    emit_barrier(lk, "vector", "ev")

    def evac_v(e, bank, n, mo, w):
        return e.tensor_tensor(v_full[:, n * D + mo:n * D + mo + w], bank[:, 0:w],
                               bv_rep[:, mo:mo + w], ALU.add)

    mm_block(lk, pool,
             get_lhsT=lambda kc, n: x2[:, kc * TLOC + n * 128:kc * TLOC + n * 128 + 128],
             get_rhs=lambda kc, mo, w: wv[:, kc * D + mo:kc * D + mo + w],
             nK=NCH, n_list=range(16), m_list=[(0, 512), (512, 512)],
             evac=evac_v, evac_engine="vector",
             pre_wait=lambda t: (lk.wait(t, "wt", wb + 16 * 24), lk.wait(t, "ev")))
    emit_barrier(lk, "vector", "ev")

    for h in range(NH):
        ch, po = h // 2, (h % 2) * 64
        q4, ph_, mh = h // 4, (h % 4) // 2, h % 2
        for lc in range(16):
            lo = lc * 128

            def t_pk(t, ch=ch, po=po, lo=lo):
                lk.wait(t, "ev")
                i = lk.nc.tensor.matmul(p_pk[:, 0:DK],
                                        kf[:, ch * TLOC + lo:ch * TLOC + lo + 128],
                                        proj[:, (po // 64) * DK:(po // 64) * DK + DK],
                                        start=True, stop=True)
                lk.inc(i, "mm")

            lk.on("tensor", t_pk)

            def v_wrap(v):
                lk.wait(v, "mm")
                lk.wait(v, "sc")
                v.tensor_scalar(wsc[:], p_pk[:, 0:DK], 0.125, 0.0, ALU.mult, ALU.add)
                v.tensor_scalar(csc[:], wsc[:], PI, -2 * PI, ALU.is_gt, ALU.mult)
                v.tensor_tensor(wsc[:], wsc[:], csc[:], ALU.add)
                v.tensor_scalar(csc[:], wsc[:], -PI, 2 * PI, ALU.is_lt, ALU.mult)
                i = v.tensor_tensor(wsc[:], wsc[:], csc[:], ALU.add)
                lk.inc(i, "ev")

            lk.on("vector", v_wrap)

            def s_sin(s):
                lk.wait(s, "ev")
                s.activation(aw[:], wsc[:], AF.Abs)
                s.activation(kp[:, DK:2 * DK], wsc[:], AF.Sin)
                i = s.activation(kp[:, 0:DK], aw[:], AF.Sin,
                                 bias=cst[:, 0:1], scale=cst[:, 1:2])
                lk.inc(i, "sc")

            lk.on("scalar", s_sin)

            def t_kv(t, h=h, lc=lc, q4=q4, ph_=ph_, mh=mh):
                lk.wait(t, "sc")
                i = lk.nc.tensor.matmul(
                    kv4[q4][ph_ * 64:ph_ * 64 + 64, mh * 256:mh * 256 + 256],
                    v_full[:, lc * D + h * 64:lc * D + (h + 1) * 64], kp[:, 0:256],
                    start=(lc == 0), stop=(lc == 15))
                if lc == 15:
                    lk.inc(i, "mm")

            lk.on("tensor", t_kv)

    def v_kv(v):
        lk.wait(v, "mm")
        for q4 in range(4):
            i = v.tensor_copy(kvsb[:, q4 * 512:(q4 + 1) * 512], kv4[q4][:, 0:512])
        lk.mark(v, "ev")

    lk.on("vector", v_kv)

    def st0(s):
        lk.wait(s, "ev")
        for c in range(NCH):
            i = s.dma_start(out=qf_out[c * 128:(c + 1) * 128, :],
                            in_=qf[:, c * TLOC:(c + 1) * TLOC])
            lk.inc(i, "st", 16)
        for h in range(NH):
            q4, ph_, mh = h // 4, (h % 4) // 2, h % 2
            i = s.dma_start(out=kvp_out[h * DH:(h + 1) * DH, :],
                            in_=kvsb[ph_ * 64:ph_ * 64 + 64,
                                     q4 * 512 + mh * 256:q4 * 512 + mh * 256 + 256])
            lk.inc(i, "st", 16)
        s.wait_ge(lk.sems["st"], lk.c["st"])

    lk.on("sync", st0)
    lk.end_phase()



def emit_attn(lk, io, pool):
    qf_in, kvp_in, xn_in = io["qf_s"], io["kvp_o"], io["xn_s"]
    wo_in, bo_in, proj_in, id_in = io["a_wo"], io["a_bo"], io["proj"], io["ident"]
    xs_out = io["xs_s"]
    lk.begin_phase("attn")
    wb = lk.c["wt"]
    qf = lk.sb("at_qf", [128, NCH * TLOC], BF16)
    xn = lk.sb("at_xn", [128, NCH * TLOC], BF16)
    kvp_sb = lk.sb("at_kvp", [128, 8 * 256], F32)
    kvt = lk.sb("at_kvt", [128, 2 * D], BF16)
    wo = lk.sb("at_wo", [128, NCH * D], BF16)
    bo = lk.sb("at_bo", [128, NCH], F32)
    proj = lk.sb("at_proj", [128, 2 * DK], BF16)
    ident = lk.sb("at_id", [128, 128], F32)
    wsc = lk.sb("at_wsc", [128, 512], F32)
    csc = lk.sb("at_csc", [128, 512], F32)
    aw = lk.sb("at_aw", [128, 512], F32)
    qp = lk.sb("at_qp", [128, 2 * 512], BF16)
    attn_sb = lk.sb("at_attn", [128, NCH * 512], BF16)
    xs_sb = lk.sb("at_xs", [128, NCH * TLOC], BF16)
    cst = lk.sb("at_cst", [128, 2], F32)
    p_pq = lk.ps("at_ppq", [128, 512], F32)
    p_at = lk.ps("at_pat", [64, 512], F32)
    p_tr = lk.ps("at_ptr", [128, 128], F32)

    def g0(g):
        g.memset(cst[:, 0:1], PI / 2)
        g.memset(cst[:, 1:2], -1.0)
        g.memset(proj[:], 0.0)
        i = g.dma_start(out=proj[0:64, 0:DK], in_=proj_in)
        lk.inc(i, "wt", 16)
        i = g.dma_start(out=proj[64:128, DK:2 * DK], in_=proj_in)
        lk.inc(i, "wt", 16)
        load_weight_bf16(lk, g, wo, wo_in, NCH, D)

    lk.on("gpsimd", g0)

    def ld0(s):
        for c in range(NCH):
            i = s.dma_start(out=qf[:, c * TLOC:(c + 1) * TLOC],
                            in_=qf_in[c * 128:(c + 1) * 128, :])
            lk.inc(i, "ld", 16)
        for c in range(NCH):
            i = s.dma_start(out=xn[:, c * TLOC:(c + 1) * TLOC],
                            in_=xn_in[c * 128:(c + 1) * 128, 1:TLOC + 1])
            lk.inc(i, "ld", 16)
        for c in range(NCH):
            i = s.dma_start(out=kvp_sb[:, c * 256:(c + 1) * 256],
                            in_=kvp_in[c * 128:(c + 1) * 128, :])
            lk.inc(i, "ld", 16)
        i = s.dma_start(out=bo[:], in_=bo_in)
        lk.inc(i, "ld", 16)
        i = s.dma_start(out=ident[:], in_=id_in)
        lk.inc(i, "ld", 16)

    lk.on("sync", ld0)

    for c in range(NCH):
        for mc in range(2):
            def t_tr(t, c=c, mc=mc):
                if c == 0 and mc == 0:
                    lk.wait(t, "ld")
                lk.wait(t, "ev")
                i = lk.nc.tensor.transpose(p_tr[:, 0:128],
                                           kvp_sb[:, c * 256 + mc * 128:c * 256 + (mc + 1) * 128],
                                           ident[:])
                lk.inc(i, "mm")

            lk.on("tensor", t_tr)

            def v_tr(v, c=c, mc=mc):
                lk.wait(v, "mm")
                i = v.tensor_scalar(kvt[:, mc * D + c * 128:mc * D + (c + 1) * 128],
                                    p_tr[:, 0:128], 1.0 / 128.0, 0.0, ALU.mult, ALU.add)
                lk.inc(i, "ev")

            lk.on("vector", v_tr)
    emit_barrier(lk, "vector", "ev")

    for (mo, w) in MT512:
        for h in range(NH):
            ch, po = h // 2, (h % 2) * 64

            def t_pq(t, ch=ch, po=po, mo=mo, w=w):
                lk.wait(t, "ev")
                lk.wait(t, "wt", wb + 16 * 2)
                i = lk.nc.tensor.matmul(p_pq[:, 0:w],
                                        proj[:, (po // 64) * DK:(po // 64) * DK + DK],
                                        qf[:, ch * TLOC + mo:ch * TLOC + mo + w],
                                        start=True, stop=True)
                lk.inc(i, "mm")

            lk.on("tensor", t_pq)

            def v_wrap(v, w=w):
                lk.wait(v, "mm")
                lk.wait(v, "sc")
                v.tensor_scalar(wsc[:, 0:w], p_pq[:, 0:w], 0.125, 0.0, ALU.mult, ALU.add)
                v.tensor_scalar(csc[:, 0:w], wsc[:, 0:w], PI, -2 * PI, ALU.is_gt, ALU.mult)
                v.tensor_tensor(wsc[:, 0:w], wsc[:, 0:w], csc[:, 0:w], ALU.add)
                v.tensor_scalar(csc[:, 0:w], wsc[:, 0:w], -PI, 2 * PI, ALU.is_lt, ALU.mult)
                i = v.tensor_tensor(wsc[:, 0:w], wsc[:, 0:w], csc[:, 0:w], ALU.add)
                lk.inc(i, "ev")

            lk.on("vector", v_wrap)

            def s_sin(s, w=w):
                lk.wait(s, "ev")
                s.activation(aw[:, 0:w], wsc[:, 0:w], AF.Abs)
                s.activation(qp[:, 512:512 + w], wsc[:, 0:w], AF.Sin)
                i = s.activation(qp[:, 0:w], aw[:, 0:w], AF.Sin,
                                 bias=cst[:, 0:1], scale=cst[:, 1:2])
                lk.inc(i, "sc")

            lk.on("scalar", s_sin)

            def t_at(t, h=h, w=w):
                lk.wait(t, "sc")
                lk.nc.tensor.matmul(p_at[:, 0:w], kvt[:, 0 * D + h * 64:0 * D + h * 64 + 64],
                                    qp[:, 0:w], start=True, stop=False)
                i = lk.nc.tensor.matmul(p_at[:, 0:w], kvt[:, 1 * D + h * 64:1 * D + h * 64 + 64],
                                        qp[:, 512:512 + w], start=False, stop=True)
                lk.inc(i, "mm")

            lk.on("tensor", t_at)

            def v_at(v, h=h, ch=ch, po=po, w=w):
                lk.wait(v, "mm")
                i = v.tensor_copy(attn_sb[po:po + 64, ch * 512:ch * 512 + w], p_at[:, 0:w])
                lk.inc(i, "ev")

            lk.on("vector", v_at)

        def evac_o(e, bank, n, mo2, w2, mo=mo):
            return e.scalar_tensor_tensor(xs_sb[:, n * TLOC + mo:n * TLOC + mo + w2],
                                          bank[:, 0:w2], bo[:, n:n + 1],
                                          xn[:, n * TLOC + mo:n * TLOC + mo + w2],
                                          ALU.add, ALU.add)

        mm_block(lk, pool,
                 get_lhsT=lambda kc, n: wo[:, kc * D + n * 128:kc * D + (n + 1) * 128],
                 get_rhs=lambda kc, mo2, w2: attn_sb[:, kc * 512:kc * 512 + w2],
                 nK=NCH, n_list=range(NCH), m_list=[(0, w)],
                 evac=evac_o, evac_engine="vector",
                 pre_wait=lambda t, c=lk.c["ev"]: (lk.wait(t, "wt"), lk.wait(t, "ev", c)))
        emit_barrier(lk, "vector", "ev")

    def st0(s):
        lk.wait(s, "ev")
        for c in range(NCH):
            i = s.dma_start(out=xs_out[c * 128:(c + 1) * 128, :],
                            in_=xs_sb[:, c * TLOC:(c + 1) * TLOC])
            lk.inc(i, "st", 16)
        s.wait_ge(lk.sems["st"], lk.c["st"])

    lk.on("sync", st0)
    lk.end_phase()



def emit_ln2(lk, io):
    xs_in, g_in, b_in, out = io["xs_s"], io["ln2g"], io["ln2b"], io["x2b_s"]
    lk.begin_phase("ln2")
    xt = lk.sb("l2_xt", [128, NCH * 512], BF16)
    xo = lk.sb("l2_xo", [128, NCH * 512], BF16)
    sq_full = lk.sb("l2_sq", [128, NCH * 512], F32)
    tsc = lk.sb("l2_tsc", [128, 512], F32)
    gcol = lk.sb("l2_gcol", [128, NCH], F32)
    bcol = lk.sb("l2_bcol", [128, NCH], F32)
    kit = LNScratch(lk, 512, name="l2_ln")
    kit.init_consts()

    def ld0(s):
        i = s.dma_start(out=gcol[:], in_=g_in)
        lk.inc(i, "ld", 16)
        i = s.dma_start(out=bcol[:], in_=b_in)
        lk.inc(i, "ld", 16)

    lk.on("sync", ld0)

    for (mo, w) in MT512:
        def ld_fn(s, mo=mo, w=w):
            lk.wait(s, "ev")
            for c in range(NCH):
                i = s.dma_start(out=xt[:, c * w:(c + 1) * w],
                                in_=xs_in[c * 128:(c + 1) * 128, mo:mo + w])
                lk.inc(i, "ld", 16)

        lk.on("sync", ld_fn)

        def get_x(c, w=w):
            return xt[:, c * w:(c + 1) * w]

        def put_out(c, v, w=w):
            v.tensor_tensor(tsc[:, 0:w], get_x(c), kit.p_rep[:, 0:w], ALU.subtract)
            v.tensor_tensor(tsc[:, 0:w], tsc[:, 0:w], kit.p_rep[:, 512:512 + w], ALU.mult)
            v.tensor_scalar(xo[:, c * w:(c + 1) * w], tsc[:, 0:w],
                            gcol[:, c:c + 1], bcol[:, c:c + 1], ALU.mult, ALU.add)

        emit_ln(lk, kit, get_x, put_out, w, sq_full, out_war_sems=("st",))

        def st_fn(s, mo=mo, w=w):
            lk.wait(s, "ev")
            for c in range(NCH):
                i = s.dma_start(out=out[c * 128:(c + 1) * 128, mo:mo + w],
                                in_=xo[:, c * w:(c + 1) * w])
                lk.inc(i, "st", 16)

        lk.on("sync", st_fn)

    def fin(s):
        lk.wait(s, "st")

    lk.on("sync", fin)
    lk.end_phase()



def emit_ffn(lk, io, pool):
    NI = DI // 128
    x2b_in, xs_in = io["x2b_s"], io["xs_s"]
    w1_in, b1_in, w2_in, b2_in = io["f_w1"], io["f_b1"], io["f_w2"], io["f_b2"]
    s_out = io["s0_s"]
    lk.begin_phase("ffn")
    wb = lk.c["wt"]
    w1 = lk.sb("ff_w1", [128, NCH * DI], BF16)
    w2 = lk.sb("ff_w2", [128, NI * D], BF16)
    b1c = lk.sb("ff_b1c", [128, NI], F32)
    b2c = lk.sb("ff_b2c", [128, NCH], F32)
    xb = lk.sb("ff_xb", [128, NCH * 512], BF16)
    h = lk.sb("ff_h", [128, NI * 512], BF16)
    xs = lk.sb("ff_xs", [128, NCH * 512], BF16)
    s_t = lk.sb("ff_st", [128, NCH * 512], BF16)

    def g0(g):
        load_weight_bf16(lk, g, w1, w1_in, NCH, DI)
        load_weight_bf16(lk, g, w2, w2_in, NI, D)

    lk.on("gpsimd", g0)

    def ld0(s):
        i = s.dma_start(out=b1c[:], in_=b1_in)
        lk.inc(i, "ld", 16)
        i = s.dma_start(out=b2c[:], in_=b2_in)
        lk.inc(i, "ld", 16)

    lk.on("sync", ld0)

    for (mo, w) in MT512:
        def ld_t(s, mo=mo, w=w, evn=lk.c["ev"]):
            lk.wait(s, "ev", evn)
            for c in range(NCH):
                i = s.dma_start(out=xb[:, c * w:(c + 1) * w],
                                in_=x2b_in[c * 128:(c + 1) * 128, mo:mo + w])
                lk.inc(i, "ld", 16)
                i = s.dma_start(out=xs[:, c * w:(c + 1) * w],
                                in_=xs_in[c * 128:(c + 1) * 128, mo:mo + w])
                lk.inc(i, "ld", 16)

        lk.on("sync", ld_t)

        def evac_h(e, bank, n, mo2, w2_):
            return e.tensor_scalar(h[:, n * 512:n * 512 + w2_], bank[:, 0:w2_],
                                   b1c[:, n:n + 1], 0.0, ALU.add, ALU.max)

        mm_block(lk, pool,
                 get_lhsT=lambda kc, n: w1[:, kc * DI + n * 128:kc * DI + (n + 1) * 128],
                 get_rhs=lambda kc, mo2, w2_, w=w: xb[:, kc * w:kc * w + w2_],
                 nK=NCH, n_list=range(NI), m_list=[(0, w)],
                 evac=evac_h, evac_engine="vector",
                 pre_wait=lambda t, c=lk.c["mm"]: (lk.wait(t, "wt", wb + 16 * NCH),
                                                   lk.wait(t, "ld"),
                                                   lk.wait(t, "mm", c)))
        emit_barrier(lk, "vector", "ev")

        def evac_s(e, bank, n, mo2, w2_, w=w):
            return e.scalar_tensor_tensor(s_t[:, n * 512:n * 512 + w2_],
                                          bank[:, 0:w2_], b2c[:, n:n + 1],
                                          xs[:, n * w:n * w + w2_],
                                          ALU.add, ALU.add)

        mm_block(lk, pool,
                 get_lhsT=lambda kc, n: w2[:, kc * D + n * 128:kc * D + (n + 1) * 128],
                 get_rhs=lambda kc, mo2, w2_: h[:, kc * 512:kc * 512 + w2_],
                 nK=NI, n_list=range(NCH), m_list=[(0, w)],
                 evac=evac_s, evac_engine="vector",
                 pre_wait=lambda t, c=lk.c["ev"], stc=lk.c["st"]: (
                     lk.wait(t, "wt"), lk.wait(t, "ev", c), lk.wait(t, "st", stc)))
        emit_barrier(lk, "vector", "ev")

        def st_t(s, mo=mo, w=w):
            lk.wait(s, "ev")
            for c in range(NCH):
                i = s.dma_start(out=s_out[c * 128:(c + 1) * 128, mo:mo + w],
                                in_=s_t[:, c * 512:c * 512 + w])
                lk.inc(i, "st", 16)

        lk.on("sync", st_t)

    def fin(s):
        lk.wait(s, "st")

    lk.on("sync", fin)
    lk.end_phase()



def emit_rwkv(lk, io, pool):
    xn_in = io["xn_s"]
    wrt, wkt, wvt = io["r_wrt"], io["r_wkt"], io["r_wvt"]
    mur_in, muk_in = io["r_mur"], io["r_muk"]
    g_out = io["g0_s"]
    lk.begin_phase("rwkv")
    wb = lk.c["wt"]
    NT = TLOC + 2
    xn = lk.sb("rw_xn", [128, NCH * NT], BF16)
    wr = lk.sb("rw_wr", [128, NCH * D], BF16)
    wk = lk.sb("rw_wk", [128, NCH * D], BF16)
    wv = lk.sb("rw_wv", [128, NCH * D], BF16)
    mur = lk.sb("rw_mur", [128, NCH], F32)
    muk = lk.sb("rw_muk", [128, NCH], F32)
    dsc = lk.sb("rw_dsc", [128, NCH * 512], F32)
    xr = lk.sb("rw_xr", [128, NCH * 512], BF16)
    xk = lk.sb("rw_xk", [128, NCH * 512], BF16)
    rbf = lk.sb("rw_rbf", [128, NCH * 512], BF16)
    kabf = lk.sb("rw_kabf", [128, NCH * 512], BF16)
    ksc = lk.sb("rw_ksc", [128, 512], BF16)
    rwsc = lk.sb("rw_rwsc", [128, NCH * 512], F32)
    gbf = lk.sb("rw_gbf", [128, NCH * 512], BF16)

    def ld_fn(s):
        for c in range(NCH):
            i = s.dma_start(out=xn[:, c * NT:c * NT + (TLOC + 1)],
                            in_=xn_in[c * 128:(c + 1) * 128, :])
            lk.inc(i, "ld", 16)
        i = s.dma_start(out=mur[:], in_=mur_in)
        lk.inc(i, "ld", 16)
        i = s.dma_start(out=muk[:], in_=muk_in)
        lk.inc(i, "ld", 16)

    lk.on("sync", ld_fn)

    def wt_fn(g):
        load_weight_bf16(lk, g, wr, wrt, NCH, D)
        load_weight_bf16(lk, g, wk, wkt, NCH, D)
        load_weight_bf16(lk, g, wv, wvt, NCH, D)

    lk.on("gpsimd", wt_fn)

    for (mo, w) in MT512:
        mm_before_mix = lk.c["mm"]

        def v_mix(v, mo=mo, w=w, mmw=mm_before_mix):
            lk.wait(v, "ld")
            lk.wait(v, "mm", mmw)
            for c in range(NCH):
                cur = xn[:, c * NT + mo + 1: c * NT + mo + 1 + w]
                prv = xn[:, c * NT + mo: c * NT + mo + w]
                ds = dsc[:, c * 512:c * 512 + w]
                v.tensor_tensor(ds, cur, prv, ALU.subtract)
                v.scalar_tensor_tensor(xr[:, c * w:(c + 1) * w], ds,
                                       mur[:, c:c + 1], prv, ALU.mult, ALU.add)
                i = v.scalar_tensor_tensor(xk[:, c * w:(c + 1) * w], ds,
                                           muk[:, c:c + 1], prv, ALU.mult, ALU.add)
            lk.inc(i, "ev")

        lk.on("vector", v_mix)
        mix_ev = lk.c["ev"]

        def s_war(s, c=lk.c["ev"]):
            lk.wait(s, "ev", c)

        lk.on("scalar", s_war)

        def evac_r(e, bank, n, mo2, w2):
            return e.activation(rbf[:, n * 512:n * 512 + w2], bank[:, 0:w2], AF.Sigmoid)

        mm_block(lk, pool,
                 get_lhsT=lambda kc, n: wr[:, kc * D + n * 128: kc * D + (n + 1) * 128],
                 get_rhs=lambda kc, mo2, w2, w=w: xr[:, kc * w: kc * w + w2],
                 nK=NCH, n_list=range(NCH), m_list=[(0, w)],
                 evac=evac_r, evac_engine="scalar",
                 pre_wait=lambda t, me=mix_ev: (lk.wait(t, "wt", wb + 16 * 8),
                                                lk.wait(t, "ev", me)))

        def evac_k(e, bank, n, mo2, w2):
            e.tensor_scalar(ksc[:, 0:w2], bank[:, 0:w2], 0.0, 0.0, ALU.max, ALU.add)
            return e.tensor_tensor(kabf[:, n * 512:n * 512 + w2], ksc[:, 0:w2],
                                   bank[:, 0:w2], ALU.mult)

        mm_block(lk, pool,
                 get_lhsT=lambda kc, n: wk[:, kc * D + n * 128: kc * D + (n + 1) * 128],
                 get_rhs=lambda kc, mo2, w2, w=w: xk[:, kc * w: kc * w + w2],
                 nK=NCH, n_list=range(NCH), m_list=[(0, w)],
                 evac=evac_k, evac_engine="vector",
                 pre_wait=lambda t: lk.wait(t, "wt", wb + 16 * 16))
        emit_barrier(lk, "vector", "ev")
        kact_ev = lk.c["ev"]

        def evac_vo(e, bank, n, mo2, w2):
            return e.tensor_tensor(rwsc[:, n * 512:n * 512 + w2],
                                   rbf[:, n * 512:n * 512 + w2],
                                   bank[:, 0:w2], ALU.mult)

        mm_block(lk, pool,
                 get_lhsT=lambda kc, n: wv[:, kc * D + n * 128: kc * D + (n + 1) * 128],
                 get_rhs=lambda kc, mo2, w2, w=w: kabf[:, kc * w: kc * w + w2],
                 nK=NCH, n_list=range(NCH), m_list=[(0, w)],
                 evac=evac_vo, evac_engine="vector",
                 pre_wait=lambda t, ke=kact_ev: (lk.wait(t, "wt", wb + 16 * 24),
                                                 lk.wait(t, "ev", ke)))
        emit_barrier(lk, "vector", "ev")
        vo_ev = lk.c["ev"]

        def s_silu(s, mo=mo, w=w, ve=vo_ev):
            lk.wait(s, "ev", ve)
            lk.wait(s, "st")
            for c in range(NCH):
                i = s.activation(gbf[:, c * 512:c * 512 + w],
                                 rwsc[:, c * 512:c * 512 + w], AF.Silu)
            lk.inc(i, "sc")

        lk.on("scalar", s_silu)

        def st_fn(s, mo=mo, w=w, sc=lk.c["sc"]):
            lk.wait(s, "sc", sc)
            for c in range(NCH):
                i = s.dma_start(out=g_out[c * 128:(c + 1) * 128, mo:mo + w],
                                in_=gbf[:, c * 512:c * 512 + w])
                lk.inc(i, "st", 16)

        lk.on("sync", st_fn)

    def fin(s):
        lk.wait(s, "st")

    lk.on("sync", fin)
    lk.end_phase()



def emit_combine(lk, io, x_src, s_src, g_src, out_ap, out_off=0, halo_dst=None,
                 tag="cb1"):
    lk.begin_phase("comb" + tag)
    xt = lk.sb(f"{tag}_xt", [128, NCH * 512], F32)
    st_ = lk.sb(f"{tag}_st", [128, NCH * 512], BF16)
    gt = lk.sb(f"{tag}_gt", [128, NCH * 512], BF16)
    tf = lk.sb(f"{tag}_tf", [128, 512], F32)
    x1 = lk.sb(f"{tag}_x1", [128, NCH * 512], F32)

    for (mo, w) in MT512:
        def ld_fn(s, mo=mo, w=w):
            lk.wait(s, "ev")
            for c in range(NCH):
                i = s.dma_start(out=xt[:, c * w:(c + 1) * w],
                                in_=x_src(c, mo, w))
                lk.inc(i, "ld", 16)
                i = s.dma_start(out=st_[:, c * w:(c + 1) * w],
                                in_=s_src[c * 128:(c + 1) * 128, mo:mo + w])
                lk.inc(i, "ld", 16)
                i = s.dma_start(out=gt[:, c * w:(c + 1) * w],
                                in_=g_src[c * 128:(c + 1) * 128, mo:mo + w])
                lk.inc(i, "ld", 16)

        lk.on("sync", ld_fn)

        def v_fn(v, mo=mo, w=w, stc=lk.c["st"]):
            lk.wait(v, "ld")
            lk.wait(v, "st", stc)
            for c in range(NCH):
                v.tensor_tensor(tf[:, 0:w], st_[:, c * w:(c + 1) * w],
                                gt[:, c * w:(c + 1) * w], ALU.mult)
                v.tensor_tensor(x1[:, c * w:(c + 1) * w], tf[:, 0:w],
                                xt[:, c * w:(c + 1) * w], ALU.add)
            lk.mark(v, "ev")

        lk.on("vector", v_fn)

        def st_fn(s, mo=mo, w=w):
            lk.wait(s, "ev")
            for c in range(NCH):
                i = s.dma_start(out=out_ap[c * 128:(c + 1) * 128,
                                           out_off + mo:out_off + mo + w],
                                in_=x1[:, c * w:(c + 1) * w])
                lk.inc(i, "st", 16)
            if halo_dst is not None and mo + w == TLOC:
                for c in range(NCH):
                    i = s.dma_start(out=halo_dst[c * 128:(c + 1) * 128, 0:4],
                                    in_=x1[:, c * w + w - 4:(c + 1) * w])
                    lk.inc(i, "st", 16)

        lk.on("sync", st_fn)

    def fin(s):
        lk.wait(s, "st")

    lk.on("sync", fin)
    lk.end_phase()



def emit_halo_fix(lk, io):
    halo_o, m1_in, x1_s = io["halo_o"], io["m1c"], io["x1_s"]
    lk.begin_phase("halo")
    hsb = lk.sb("hx_hsb", [128, NCH * 4], F32)
    m1 = lk.sb("hx_m1", [128, 1], F32)

    def ld0(s):
        for c in range(NCH):
            i = s.dma_start(out=hsb[:, c * 4:(c + 1) * 4],
                            in_=halo_o[c * 128:(c + 1) * 128, 0:4])
            lk.inc(i, "ld", 16)
        i = s.dma_start(out=m1[:], in_=m1_in)
        lk.inc(i, "ld", 16)

    lk.on("sync", ld0)

    def v0(v):
        lk.wait(v, "ld")
        v.tensor_scalar(hsb[:], hsb[:], m1[:, 0:1], 0.0, ALU.mult, ALU.add)
        lk.mark(v, "ev")

    lk.on("vector", v0)

    def st0(s):
        lk.wait(s, "ev")
        for c in range(NCH):
            i = s.dma_start(out=x1_s[c * 128:(c + 1) * 128, 0:3],
                            in_=hsb[:, c * 4 + 1:(c + 1) * 4])
            lk.inc(i, "st", 16)
        s.wait_ge(lk.sems["st"], lk.c["st"])

    lk.on("sync", st0)
    lk.end_phase()



def emit_l1front(lk, io, pool):
    EXT = TLOC + 3
    x1_in = io["x1_s"]
    win_in, cw_in, cb_in = io["l_winT"], io["l_convw"], io["l_convb"]
    xp_in, wdt_in, bdt_in = io["l_xprojT"], io["l_wdtT"], io["l_bdt"]
    xn1_out, u_out, dt_out, z_out = io["xn1_s"], io["u_s"], io["dt_s"], io["z_s"]
    B_out, C_out = io["Bs_s"], io["Cs_s"]
    lk.begin_phase("l1f")
    bnt = lk.c["nt"]
    bnf = lk.c["nf"]
    NTE = EXT + 1
    xt = lk.sb("lf_xt", [128, NCH * 512], F32)
    xn1 = lk.sb("lf_xn1", [128, NCH * NTE], BF16)
    sq_full = lk.sb("lf_sq", [128, NCH * 512], F32)
    nring = lk.sb("lf_nring", [128, 2 * NCH * 128], BF16)
    xi = lk.sb("lf_xi", [128, NDC * NTE], BF16)
    u_t = lk.sb("lf_ut", [128, NDC * 512], BF16)
    z_t = lk.sb("lf_zt", [128, NDC * 512], BF16)
    dt_t = lk.sb("lf_dtt", [128, NDC * 512], BF16)
    cw = lk.sb("lf_cw", [128, NDC * 4], F32)
    cb = lk.sb("lf_cb", [128, NDC], F32)
    bdt = lk.sb("lf_bdt", [128, NDC], F32)
    xp = lk.sb("lf_xp", [128, NDC * 96], BF16)
    wdt = lk.sb("lf_wdt", [DTR, DN], BF16)
    dtr_bf = lk.sb("lf_dtrbf", [DTR, 512], BF16)
    bc_sb = lk.sb("lf_bc", [DS, 2 * 512], F32)
    asc = lk.sb("lf_asc", [128, 512], F32)
    csc = lk.sb("lf_csc", [128, 512], F32)
    esc = lk.sb("lf_esc", [128, 512], F32)
    cst = lk.sb("lf_cstf", [128, 2], F32)
    kit = LNScratch(lk, 512, name="lf_ln")
    p_dtr = lk.ps("lf_pdtr", [DTR, 512], F32)
    p_bc = lk.ps("lf_pbc", [128, 512], F32)
    kit.init_consts()

    def g0(g):
        g.memset(cst[:, 0:1], -1.0)
        g.memset(cst[:, 1:2], 1.0)
        load_weight_bf16(lk, g, xp, xp_in, NDC, 96)
        i = g.dma_start(out=wdt[:], in_=wdt_in)
        lk.inc(i, "wt", 16)

    lk.on("gpsimd", g0)

    def ld0(s):
        for t, srcp in [(cw, cw_in), (cb, cb_in), (bdt, bdt_in)]:
            i = s.dma_start(out=t[:], in_=srcp)
            lk.inc(i, "ld", 16)

    lk.on("sync", ld0)

    widths = [259] + _tile_widths(EXT - 259, 512)
    t0 = 0
    for w in widths:
        o = t0

        def ld_fn(s, o=o, w=w):
            lk.wait(s, "ev")
            for c in range(NCH):
                i = s.dma_start(out=xt[:, c * w:(c + 1) * w],
                                in_=x1_in[c * 128:(c + 1) * 128, o:o + w])
                lk.inc(i, "ld", 16)

        lk.on("sync", ld_fn)

        def get_x(c, w=w):
            return xt[:, c * w:(c + 1) * w]

        def put_out(c, v, o=o, w=w):
            tmp = sq_full[:, c * w:(c + 1) * w]
            v.tensor_tensor(tmp, get_x(c), kit.p_rep[:, 0:w], ALU.subtract)
            v.tensor_tensor(xn1[:, c * NTE + o:c * NTE + o + w], tmp,
                            kit.p_rep[:, 512:512 + w], ALU.mult)

        emit_ln(lk, kit, get_x, put_out, w, sq_full, out_war_sems=("st",))

        def st_fn(s, o=o, w=w):
            if o + w > 3:
                lk.wait(s, "ev")
                so = max(o, 3)
                for c in range(NCH):
                    i = s.dma_start(out=xn1_out[c * 128:(c + 1) * 128, so - 3:o + w - 3],
                                    in_=xn1[:, c * NTE + so:c * NTE + o + w])
                    lk.inc(i, "st", 16)

        lk.on("sync", st_fn)
        t0 += w
    xn1_ev = lk.c["ev"]

    n_loads = [0]

    def stream_n(n):
        def g_n(g, n=n):
            uses = n_loads[0]
            if uses >= 2:
                g.wait_ge(lk.sems["nf"], bnf + uses - 1)
            for kc in range(NCH):
                i = g.dma_start(out=nring[:, (n % 2) * (NCH * 128) + kc * 128:
                                          (n % 2) * (NCH * 128) + (kc + 1) * 128],
                                in_=win_in[kc * 128:(kc + 1) * 128, n * 128:(n + 1) * 128])
                lk.inc(i, "nt", 16)
            n_loads[0] += 1

        lk.on("gpsimd", g_n)

    XI_M = [(0, 259), (259, 512), (771, 512), (1283, 512), (1795, 256)]
    Z_M = [(3, 512), (515, 512), (1027, 512), (1539, 512)]

    for n in range(32):
        stream_n(n)
        is_xi = n < 16
        mlist = XI_M if is_xi else Z_M

        def t_xz(t, n=n, mlist=mlist, nl=n_loads[0], ev_need=xn1_ev):
            lk.wait(t, "ev", ev_need)
            t.wait_ge(lk.sems["nt"], bnt + 16 * 8 * nl)
            for gi_, (mo, w) in enumerate(mlist):
                ib, bank = pool.acquire(t)
                for kc in range(NCH):
                    i = lk.nc.tensor.matmul(
                        bank[:, 0:w],
                        nring[:, (n % 2) * (NCH * 128) + kc * 128:(n % 2) * (NCH * 128) + (kc + 1) * 128],
                        xn1[:, kc * NTE + mo:kc * NTE + mo + w],
                        start=(kc == 0), stop=(kc == NCH - 1))
                lk.inc(i, "mm")
                t_xz.banks.append((ib, bank, mo, w, lk.c["mm"]))

        t_xz.banks = []
        lk.on("tensor", t_xz)

        def v_xz(v, n=n, is_xi=is_xi, banks=t_xz.banks, stc=lk.c["st"]):
            if not is_xi:
                lk.wait(v, "st", stc)
            for (ib, bank, mo, w, mmc) in banks:
                lk.wait(v, "mm", mmc)
                if is_xi:
                    i = v.tensor_copy(xi[:, n * NTE + mo:n * NTE + mo + w], bank[:, 0:w])
                else:
                    i = v.tensor_copy(z_t[:, mo - 3:mo - 3 + w], bank[:, 0:w])
                pool.freed(i, ib)
            i = v.tensor_copy(lk.dummy[:1, 0:8], lk.dummy[:1, 8:16])
            i.then_inc(lk.sems["nf"], 1)
            lk.c["nf"] += 1
            lk.mark(v, "ev")

        lk.on("vector", v_xz)

        if not is_xi:
            def st_z(s, n=n):
                lk.wait(s, "ev")
                i = s.dma_start(out=z_out[(n - 16) * 128:(n - 15) * 128, :],
                                in_=z_t[:, 0:TLOC])
                lk.inc(i, "st", 16)

            lk.on("sync", st_z)

    xi_ev = lk.c["ev"]

    for mt, (mo, w) in enumerate(MT512):
        for ci in range(NDC):
            def v_conv(v, ci=ci, mo=mo, w=w, xe=xi_ev, stc=lk.c["st"], scc=lk.c["sc"]):
                lk.wait(v, "ev", xe)
                lk.wait(v, "st", stc)
                lk.wait(v, "sc", scc)
                base = ci * NTE + mo
                v.tensor_scalar(csc[:, 0:w], xi[:, base:base + w],
                                cw[:, ci * 4:ci * 4 + 1], 0.0, ALU.mult, ALU.add)
                v.scalar_tensor_tensor(esc[:, 0:w], xi[:, base + 1:base + 1 + w],
                                       cw[:, ci * 4 + 1:ci * 4 + 2], csc[:, 0:w],
                                       ALU.mult, ALU.add)
                v.scalar_tensor_tensor(csc[:, 0:w], xi[:, base + 2:base + 2 + w],
                                       cw[:, ci * 4 + 2:ci * 4 + 3], esc[:, 0:w],
                                       ALU.mult, ALU.add)
                i = v.scalar_tensor_tensor(esc[:, 0:w], xi[:, base + 3:base + 3 + w],
                                           cw[:, ci * 4 + 3:ci * 4 + 4], csc[:, 0:w],
                                           ALU.mult, ALU.add)
                lk.inc(i, "ev")

            lk.on("vector", v_conv)

            def s_u(s, ci=ci, w=w):
                lk.wait(s, "ev")
                i = s.activation(u_t[:, ci * 512:ci * 512 + w], esc[:, 0:w],
                                 AF.Silu, bias=cb[:, ci:ci + 1])
                lk.inc(i, "sc")

            lk.on("scalar", s_u)

        u_sc = lk.c["sc"]

        def t_xp(t, mo=mo, w=w, us=u_sc):
            lk.wait(t, "sc", us)
            lk.wait(t, "ev")
            for kc in range(NDC):
                lk.nc.tensor.matmul(p_dtr[:, 0:w], xp[:, kc * 96:kc * 96 + 64],
                                    u_t[:, kc * 512:kc * 512 + w],
                                    start=(kc == 0), stop=(kc == NDC - 1))
            for kc in range(NDC):
                lk.nc.tensor.matmul(p_bc[0:16, 0:w], xp[:, kc * 96 + 64:kc * 96 + 80],
                                    u_t[:, kc * 512:kc * 512 + w],
                                    start=(kc == 0), stop=(kc == NDC - 1))
            for kc in range(NDC):
                i = lk.nc.tensor.matmul(p_bc[64:80, 0:w], xp[:, kc * 96 + 80:kc * 96 + 96],
                                        u_t[:, kc * 512:kc * 512 + w],
                                        start=(kc == 0), stop=(kc == NDC - 1))
            lk.inc(i, "mm")

        lk.on("tensor", t_xp)

        def v_xp(v, mo=mo, w=w):
            lk.wait(v, "mm")
            v.tensor_copy(dtr_bf[:, 0:w], p_dtr[:, 0:w])
            v.tensor_copy(bc_sb[:, 0:w], p_bc[0:16, 0:w])
            i = v.tensor_copy(bc_sb[:, 512:512 + w], p_bc[64:80, 0:w])
            lk.mark(v, "ev")

        lk.on("vector", v_xp)

        def st_bc(s, mo=mo, w=w):
            lk.wait(s, "ev")
            i = s.dma_start(out=B_out[:, mo:mo + w], in_=bc_sb[:, 0:w])
            lk.inc(i, "st", 16)
            i = s.dma_start(out=C_out[:, mo:mo + w], in_=bc_sb[:, 512:512 + w])
            lk.inc(i, "st", 16)
            for ci in range(NDC):
                i = s.dma_start(out=u_out[ci * 128:(ci + 1) * 128, mo:mo + w],
                                in_=u_t[:, ci * 512:ci * 512 + w])
                lk.inc(i, "st", 16)

        lk.on("sync", st_bc)

        for ci in range(NDC):
            def t_dt(t, ci=ci, w=w, ec=lk.c["ev"]):
                lk.wait(t, "ev", ec)
                ib, bank = pool.acquire(t)
                i = lk.nc.tensor.matmul(bank[:, 0:w], wdt[:, ci * 128:(ci + 1) * 128],
                                        dtr_bf[:, 0:w], start=True, stop=True)
                lk.inc(i, "mm")
                t_dt.bk = (ib, bank)

            lk.on("tensor", t_dt)
            ib, bank = t_dt.bk

            def v_dt1(v, ci=ci, w=w, ib=ib, bank=bank, scc=lk.c["sc"]):
                lk.wait(v, "mm")
                lk.wait(v, "sc", scc)
                i = v.tensor_scalar(asc[:, 0:w], bank[:, 0:w], bdt[:, ci:ci + 1], 0.0,
                                    ALU.add, ALU.add)
                pool.freed(i, ib)
                lk.mark(v, "ev")

            lk.on("vector", v_dt1)

            def s_dt(s, ci=ci, w=w):
                lk.wait(s, "ev")
                s.activation(csc[:, 0:w], asc[:, 0:w], AF.Abs)
                s.activation(csc[:, 0:w], csc[:, 0:w], AF.Exp, scale=cst[:, 0:1])
                i = s.activation(csc[:, 0:w], csc[:, 0:w], AF.Ln, bias=cst[:, 1:2])
                lk.inc(i, "sc")

            lk.on("scalar", s_dt)

            def v_dt2(v, ci=ci, w=w):
                lk.wait(v, "sc")
                v.tensor_scalar(esc[:, 0:w], asc[:, 0:w], 0.0, 0.0, ALU.max, ALU.add)
                i = v.tensor_tensor(dt_t[:, ci * 512:ci * 512 + w], esc[:, 0:w],
                                    csc[:, 0:w], ALU.add)
                lk.inc(i, "ev")

            lk.on("vector", v_dt2)

        dt_ev = lk.c["ev"]

        def st_dt(s, mo=mo, w=w, de=dt_ev):
            lk.wait(s, "ev", de)
            for ci in range(NDC):
                i = s.dma_start(out=dt_out[ci * 128:(ci + 1) * 128, mo:mo + w],
                                in_=dt_t[:, ci * 512:ci * 512 + w])
                lk.inc(i, "st", 16)

        lk.on("sync", st_dt)

    def fin(s):
        lk.wait(s, "st")

    lk.on("sync", fin)
    lk.end_phase()



def emit_scan(lk, io, pool):
    NCB = 8
    NSEG = 8
    SG = T // NSEG
    SGA = NCB * SG
    dtG, uG, BsG, CsG = io["dtG"], io["uG"], io["BsG"], io["CsG"]
    al_in, m0_in, m1_in = io["s_alog"], io["m0c"], io["m1c"]
    y_out = io["y_s"]
    lk.begin_phase("scan")
    bld = lk.c["ld"]
    bgp = lk.c["gp"]
    dt_s = lk.sb("sn_dt", [128, SGA], BF16)
    u_s = lk.sb("sn_u", [128, SGA], BF16)
    dtB = lk.sb("sn_dtB", [128, SGA], BF16)
    uB = lk.sb("sn_uB", [128, SGA], BF16)
    m0rep = lk.sb("sn_m0rep", [128, SGA], BF16)
    dtu = lk.sb("sn_dtu", [128, SGA], F32)
    brow2 = lk.sb("sn_brow2", [1, 2 * SG], F32)
    crow2 = lk.sb("sn_crow2", [1, 2 * SG], F32)
    brep = lk.sb("sn_brep", [128, 2 * SG], F32)
    crep = lk.sb("sn_crep", [128, 2 * SG], F32)
    asb = lk.sb("sn_asb", [128, NCB * DS], F32)
    m0 = lk.sb("sn_m0", [128, 1], F32)
    m1 = lk.sb("sn_m1", [128, 1], F32)
    abuf = lk.sb("sn_abuf", [128, 2 * SGA], F32)
    bbuf = lk.sb("sn_bbuf", [128, 2 * SGA], F32)
    hbuf = lk.sb("sn_hbuf", [128, 2 * SGA], F32)
    tmy = lk.sb("sn_tmy", [128, SGA], F32)
    yacc = lk.sb("sn_yacc", [128, SGA], F32)
    hst = lk.sb("sn_hst", [128, NCB * DS * 2], F32)
    onesr = lk.sb("sn_onesr", [1, 128], F32)
    p_brep = lk.ps("sn_pbrep", [128, SG], F32)
    p_crep = lk.ps("sn_pcrep", [128, SG], F32)

    def g0(g):
        g.memset(onesr[:], 1.0)
        g.memset(m0rep[:], 1.0)
        i = g.memset(hst[:], 0.0)
        lk.inc(i, "gp")

    lk.on("gpsimd", g0)

    def ld0(s):
        i = s.dma_start(out=asb[:], in_=al_in)
        lk.inc(i, "ld", 16)
        i = s.dma_start(out=m0[:], in_=m0_in)
        lk.inc(i, "ld", 16)
        i = s.dma_start(out=m1[:], in_=m1_in)
        lk.inc(i, "ld", 16)

    lk.on("sync", ld0)

    def s_a(s):
        lk.wait(s, "ld", bld + 16)
        i = s.activation(asb[:], asb[:], AF.Exp)
        lk.inc(i, "sc")

    lk.on("scalar", s_a)

    def v_a(v):
        lk.wait(v, "sc")
        v.tensor_scalar(asb[:], asb[:], -1.0, 0.0, ALU.mult, ALU.add)
        lk.wait(v, "ld", bld + 48)
        lk.wait(v, "gp", bgp + 1)
        v.tensor_scalar(m0rep[:], m0rep[:], m0[:, 0:1], 0.0, ALU.mult, ALU.add)
        lk.mark(v, "ev")

    lk.on("vector", v_a)
    a_ev = lk.c["ev"]

    it = 0
    ev_tail = []
    sc_cp = []
    mm_rep = []
    pl_b = []
    pl_y = []
    dtu3 = dtu.ap().rearrange("p (c t) -> p c t", c=NCB)
    ya3 = yacc.ap().rearrange("p (c t) -> p c t", c=NCB)
    ty3 = tmy.ap().rearrange("p (c t) -> p c t", c=NCB)
    hst4 = hst.ap().rearrange("p (c s u) -> p c s u", c=NCB, u=2)

    for seg in range(NSEG):
        so = seg * SG
        blk_ = so // TLOC
        co = so % TLOC

        def ld_seg(s, blk_=blk_, co=co, evn=lk.c["ev"], bln=lk.c["bl"]):
            lk.wait(s, "ev", evn)
            lk.wait(s, "bl", bln)
            for cb in range(NCB):
                for dst, src, off in ((dt_s, dtG, 0), (u_s, uG, 0),
                                      (dtB, dtG, 1024), (uB, uG, 1024)):
                    i = s.dma_start(out=dst[:, cb * SG:(cb + 1) * SG],
                                    in_=src[blk_ * DN + off + cb * 128:
                                            blk_ * DN + off + (cb + 1) * 128,
                                            co:co + SG])
                    lk.inc(i, "ld", 16)

        lk.on("sync", ld_seg)

        def g_blend(g, evn=a_ev):
            lk.wait(g, "ld")
            lk.wait(g, "ev", evn)
            g.tensor_tensor(dt_s[:], dt_s[:], dtB[:], ALU.subtract)
            g.tensor_tensor(dt_s[:], dt_s[:], m0rep[:], ALU.mult)
            g.tensor_tensor(dt_s[:], dt_s[:], dtB[:], ALU.add)
            g.tensor_tensor(u_s[:], u_s[:], uB[:], ALU.subtract)
            g.tensor_tensor(u_s[:], u_s[:], m0rep[:], ALU.mult)
            g.tensor_tensor(u_s[:], u_s[:], uB[:], ALU.add)
            g.tensor_tensor(dtu[:], dt_s[:], u_s[:], ALU.mult)
            lk.mark(g, "bl")

        lk.on("gpsimd", g_blend)
        bl_done = lk.c["bl"]

        for si in range(DS):
            slot = it % 2

            def ld_bc(s, seg=seg, si=si, slot=slot, it=it, blk_=blk_, co=co):
                if it >= 2:
                    s.wait_ge(lk.sems["mm"], mm_rep[it - 2])
                i = s.dma_start(out=brow2[:1, slot * SG:(slot + 1) * SG],
                                in_=BsG[blk_ * DS + si:blk_ * DS + si + 1, co:co + SG])
                lk.inc(i, "ld", 16)
                i = s.dma_start(out=crow2[:1, slot * SG:(slot + 1) * SG],
                                in_=CsG[blk_ * DS + si:blk_ * DS + si + 1, co:co + SG])
                lk.inc(i, "ld", 16)

            lk.on("sync", ld_bc)

            def t_rep(t, slot=slot, it=it):
                lk.wait(t, "ld")
                lk.wait(t, "gp", bgp + 1)
                if it >= 1:
                    lk.wait(t, "sc", sc_cp[it - 1])
                lk.nc.tensor.matmul(p_brep[:, 0:SG], onesr[:],
                                    brow2[:1, slot * SG:(slot + 1) * SG],
                                    start=True, stop=True)
                i = lk.nc.tensor.matmul(p_crep[:, 0:SG], onesr[:],
                                        crow2[:1, slot * SG:(slot + 1) * SG],
                                        start=True, stop=True)
                lk.inc(i, "mm")

            lk.on("tensor", t_rep)
            mm_rep.append(lk.c["mm"])

            def s_cp(s, slot=slot, si=si, it=it, bld_=bl_done):
                s.wait_ge(lk.sems["mm"], mm_rep[it])
                if it >= 2:
                    lk.wait(s, "pl", pl_y[it - 2])
                s.copy(brep[:, slot * SG:(slot + 1) * SG], p_brep[:, 0:SG])
                s.copy(crep[:, slot * SG:(slot + 1) * SG], p_crep[:, 0:SG])
                if it >= 2:
                    lk.wait(s, "ev", ev_tail[it - 2])
                else:
                    lk.wait(s, "ev", a_ev)
                lk.wait(s, "bl", bld_)
                for cb in range(NCB):
                    i = s.activation(abuf[:, slot * SGA + cb * SG:slot * SGA + (cb + 1) * SG],
                                     dt_s[:, cb * SG:(cb + 1) * SG], AF.Exp,
                                     scale=asb[:, cb * DS + si:cb * DS + si + 1])
                lk.inc(i, "sc")

            lk.on("scalar", s_cp)
            sc_cp.append(lk.c["sc"])

            def g_pool(g, slot=slot, si=si, it=it, seg=seg, stc=lk.c["st"]):
                lk.wait(g, "sc", sc_cp[it])
                if it >= 2:
                    lk.wait(g, "ev", ev_tail[it - 2])
                brep3 = brep.ap()[:, slot * SG:(slot + 1) * SG] \
                    .unsqueeze(1).broadcast_to([128, NCB, SG])
                bb3s = bbuf.ap()[:, slot * SGA:(slot + 1) * SGA] \
                    .rearrange("p (c t) -> p c t", c=NCB)
                g.tensor_tensor(bb3s, dtu3, brep3, ALU.mult)
                lk.mark(g, "pl")
                pl_b.append(lk.c["pl"])
                if si >= 1:
                    pit = it - 1
                    pslot = pit % 2
                    lk.wait(g, "ev", ev_tail[pit])
                    hb3p = hbuf.ap()[:, pslot * SGA:(pslot + 1) * SGA] \
                        .rearrange("p (c t) -> p c t", c=NCB)
                    crep3p = crep.ap()[:, pslot * SG:(pslot + 1) * SG] \
                        .unsqueeze(1).broadcast_to([128, NCB, SG])
                    if si - 1 == 0:
                        lk.wait(g, "st", stc)
                        i = g.tensor_tensor(ya3, hb3p, crep3p, ALU.mult)
                    else:
                        g.tensor_tensor(ty3, hb3p, crep3p, ALU.mult)
                        i = g.tensor_tensor(yacc[:], yacc[:], tmy[:], ALU.add)
                    lk.inc(i, "pl")
                    pl_y.append(lk.c["pl"])

            lk.on("gpsimd", g_pool)

            def v_scan(v, slot=slot, si=si, it=it):
                lk.wait(v, "sc", sc_cp[it])
                lk.wait(v, "pl", pl_b[it])
                lk.wait(v, "gp", bgp + 1)
                if it >= 2:
                    lk.wait(v, "pl", pl_y[it - 2] if len(pl_y) > it - 2 else 0)
                for cb in range(NCB):
                    v.tensor_tensor_scan(hbuf[:, slot * SGA + cb * SG:slot * SGA + (cb + 1) * SG],
                                         abuf[:, slot * SGA + cb * SG:slot * SGA + (cb + 1) * SG],
                                         bbuf[:, slot * SGA + cb * SG:slot * SGA + (cb + 1) * SG],
                                         hst[:, 2 * (cb * DS + si) + 1:2 * (cb * DS + si) + 2],
                                         ALU.mult, ALU.add)
                i = v.tensor_copy(hst4[:, :, si, :],
                                  hbuf.ap()[:, slot * SGA:(slot + 1) * SGA]
                                  .rearrange("p (c t) -> p c t", c=NCB)[:, :, SG - 2:SG])
                lk.inc(i, "ev")

            lk.on("vector", v_scan)
            ev_tail.append(lk.c["ev"])
            it += 1

        def g_fin(g, seg=seg, it=it, so=so):
            pit = it - 1
            pslot = pit % 2
            lk.wait(g, "ev", ev_tail[pit])
            hb3p = hbuf.ap()[:, pslot * SGA:(pslot + 1) * SGA] \
                .rearrange("p (c t) -> p c t", c=NCB)
            crep3p = crep.ap()[:, pslot * SG:(pslot + 1) * SG] \
                .unsqueeze(1).broadcast_to([128, NCB, SG])
            g.tensor_tensor(ty3, hb3p, crep3p, ALU.mult)
            i = g.tensor_tensor(yacc[:], yacc[:], tmy[:], ALU.add)
            lk.inc(i, "pl")
            pl_y.append(lk.c["pl"])
            lk.mark(g, "pl")
            for cb in range(NCB):
                i = g.dma_start(out=y_out[cb * 128:(cb + 1) * 128, so:so + SG],
                                in_=yacc[:, cb * SG:(cb + 1) * SG])
                lk.inc(i, "st", 16)

        lk.on("gpsimd", g_fin)

    def fin(s):
        lk.wait(s, "st")

    lk.on("sync", fin)
    lk.end_phase()



def emit_gate_s1(lk, io, pool):
    yG = io["yG"]
    u_in, z_in, xn1_in = io["u_s"], io["z_s"], io["xn1_s"]
    dsk_in, wout_in = io["g_dsk"], io["g_woutT"]
    gw_in, gb_in, id_in = io["g_gwp"], io["g_gb"], io["ident"]
    m0_in, m1_in = io["m0c"], io["m1c"]
    s1_out = io["s1_s"]
    oh1_out, oh2_out = io["oh1_s"], io["oh2_s"]
    v1_out, v2_out = io["v1_s"], io["v2_s"]
    lk.begin_phase("gate")
    lb = lk.c["ld"]
    wb = lk.c["wt"]
    bgp = lk.c["gp"]
    yA = lk.sb("gs_yA", [128, NDC * 512], BF16)
    yB = lk.sb("gs_yB", [128, NDC * 512], BF16)
    yt = lk.sb("gs_yt", [128, NDC * 512], BF16)
    ut = lk.sb("gs_ut", [128, NDC * 512], BF16)
    zt = lk.sb("gs_zt", [128, NDC * 512], BF16)
    zs = lk.sb("gs_zs", [128, 512], BF16)
    ydt = lk.sb("gs_ydt", [128, 512], F32)
    yg = lk.sb("gs_yg", [128, NDC * 512], BF16)
    wout = lk.sb("gs_wout", [128, NDC * D], BF16)
    dsk = lk.sb("gs_dsk", [128, NDC], F32)
    s1t = lk.sb("gs_s1t", [128, NCH * 512], BF16)
    xn1 = lk.sb("gs_xn1", [128, NCH * TLOC], BF16)
    gwsb = lk.sb("gs_gwp", [128, NCH * E], BF16)
    gbrep = lk.sb("gs_gbrep", [128, E], F32)
    gbrow = lk.sb("gs_gbrow", [1, E], F32)
    ident = lk.sb("gs_ident", [128, 128], F32)
    onesr = lk.sb("gs_onesr", [1, 128], F32)
    m0 = lk.sb("gs_m0", [128, 1], F32)
    m1 = lk.sb("gs_m1", [128, 1], F32)
    gla = lk.sb("gs_gla", [128, 16 * E], F32)
    oh1a = lk.sb("gs_oh1a", [128, 16 * E], F32)
    oh2a = lk.sb("gs_oh2a", [128, 16 * E], F32)
    mska = lk.sb("gs_mska", [128, 16 * E], F32)
    glm16 = lk.sb("gs_glm16", [128, 16], F32)
    va = lk.sb("gs_va", [128, 32], F32)
    ohT1 = lk.sb("gs_ohT1", [8, TLOC], F32)
    ohT2 = lk.sb("gs_ohT2", [8, TLOC], F32)
    vT = lk.sb("gs_vT", [2, TLOC], F32)
    p_g = lk.ps("gs_pg", [128, 8], F32)
    p_t1 = lk.ps("gs_pt1", [8, 128], F32)
    p_t2 = lk.ps("gs_pt2", [8, 128], F32)
    p_t3 = lk.ps("gs_pt3", [2, 128], F32)
    p_gb = lk.ps("gs_pgb", [128, E], F32)

    def g0(g):
        i = g.memset(onesr[:], 1.0)
        lk.inc(i, "gp")
        load_weight_bf16(lk, g, wout, wout_in, NDC, D)
        i = g.dma_start(out=gwsb[:], in_=gw_in)
        lk.inc(i, "wt", 16)

    lk.on("gpsimd", g0)

    def ld0(s):
        for t, srcp in [(dsk, dsk_in), (gbrow, gb_in), (ident, id_in)]:
            i = s.dma_start(out=t[:], in_=srcp)
            lk.inc(i, "ld", 16)
        i = s.dma_start(out=m0[:], in_=m0_in)
        lk.inc(i, "ld", 16)
        i = s.dma_start(out=m1[:], in_=m1_in)
        lk.inc(i, "ld", 16)
        for c in range(NCH):
            i = s.dma_start(out=xn1[:, c * TLOC:(c + 1) * TLOC],
                            in_=xn1_in[c * 128:(c + 1) * 128, :])
            lk.inc(i, "ld", 16)

    lk.on("sync", ld0)

    def t_gb(t):
        lk.wait(t, "ld", lb + 48)
        lk.wait(t, "gp", bgp + 1)
        i = lk.nc.tensor.matmul(p_gb[:, 0:E], onesr[:], gbrow[:1, 0:E],
                                start=True, stop=True)
        lk.inc(i, "mm")

    lk.on("tensor", t_gb)

    def v_gb(v):
        lk.wait(v, "mm")
        i = v.tensor_copy(gbrep[:], p_gb[:, 0:E])
        lk.mark(v, "ev")

    lk.on("vector", v_gb)

    for (mo, w) in MT512:
        def ld_t(s, mo=mo, w=w, evc=lk.c["ev"], bln=lk.c["bl"]):
            lk.wait(s, "ev", evc)
            lk.wait(s, "bl", bln)
            for ci in range(NDC):
                i = s.dma_start(out=yA[:, ci * 512:ci * 512 + w],
                                in_=yG[ci * 128:(ci + 1) * 128, mo:mo + w])
                lk.inc(i, "ld", 16)
                i = s.dma_start(out=yB[:, ci * 512:ci * 512 + w],
                                in_=yG[ci * 128:(ci + 1) * 128, TLOC + mo:TLOC + mo + w])
                lk.inc(i, "ld", 16)
                i = s.dma_start(out=ut[:, ci * 512:ci * 512 + w],
                                in_=u_in[ci * 128:(ci + 1) * 128, mo:mo + w])
                lk.inc(i, "ld", 16)
                i = s.dma_start(out=zt[:, ci * 512:ci * 512 + w],
                                in_=z_in[ci * 128:(ci + 1) * 128, mo:mo + w])
                lk.inc(i, "ld", 16)

        lk.on("sync", ld_t)

        def g_blend(g, w=w, evc=lk.c["ev"]):
            lk.wait(g, "ld")
            lk.wait(g, "ev", evc)
            for ci in range(NDC):
                sl = slice(ci * 512, ci * 512 + w)
                g.tensor_scalar(yt[:, sl], yA[:, sl], m0[:, 0:1], 0.0, ALU.mult, ALU.add)
                g.scalar_tensor_tensor(yt[:, sl], yB[:, sl], m1[:, 0:1], yt[:, sl],
                                       ALU.mult, ALU.add)
            lk.mark(g, "bl")

        lk.on("vector", g_blend)
        bl_done = lk.c["bl"]

        for ci in range(NDC):
            def v_yd(v, ci=ci, w=w, scc=lk.c["sc"], bld_=bl_done):
                lk.wait(v, "ld")
                lk.wait(v, "bl", bld_)
                lk.wait(v, "sc", scc)
                i = v.scalar_tensor_tensor(ydt[:, 0:w], ut[:, ci * 512:ci * 512 + w],
                                           dsk[:, ci:ci + 1], yt[:, ci * 512:ci * 512 + w],
                                           ALU.mult, ALU.add)
                lk.inc(i, "ev")

            lk.on("vector", v_yd)

            def s_z(s, ci=ci, w=w):
                lk.wait(s, "ev")
                i = s.activation(zs[:, 0:w], zt[:, ci * 512:ci * 512 + w], AF.Silu)
                lk.inc(i, "sc")

            lk.on("scalar", s_z)

            def v_yg(v, ci=ci, w=w):
                lk.wait(v, "sc")
                i = v.tensor_tensor(yg[:, ci * 512:ci * 512 + w], ydt[:, 0:w],
                                    zs[:, 0:w], ALU.mult)
                lk.inc(i, "ev")

            lk.on("vector", v_yg)

        def evac_s1(e, bank, n, mo2, w2):
            return e.tensor_copy(s1t[:, n * 512:n * 512 + w2], bank[:, 0:w2])

        mm_block(lk, pool,
                 get_lhsT=lambda kc, n: wout[:, kc * D + n * 128:kc * D + (n + 1) * 128],
                 get_rhs=lambda kc, mo2, w2: yg[:, kc * 512:kc * 512 + w2],
                 nK=NDC, n_list=range(NCH), m_list=[(0, w)],
                 evac=evac_s1, evac_engine="vector",
                 pre_wait=lambda t, c=lk.c["ev"]: (lk.wait(t, "wt", wb + 16 * 16),
                                                   lk.wait(t, "ev", c)))
        emit_barrier(lk, "vector", "ev")

        def st_s1(s, mo=mo, w=w):
            lk.wait(s, "ev")
            for c in range(NCH):
                i = s.dma_start(out=s1_out[c * 128:(c + 1) * 128, mo:mo + w],
                                in_=s1t[:, c * 512:c * 512 + w])
                lk.inc(i, "st", 16)

        lk.on("sync", st_s1)

    for lc in range(16):
        lo = lc * 128

        def t_gl(t, lo=lo, evc=lk.c["ev"]):
            lk.wait(t, "ev", evc)
            lk.wait(t, "wt", wb + 16 * 17)
            for kc in range(NCH):
                i = lk.nc.tensor.matmul(p_g[:, 0:E],
                                        xn1[:, kc * TLOC + lo:kc * TLOC + lo + 128],
                                        gwsb[:, kc * E:(kc + 1) * E],
                                        start=(kc == 0), stop=(kc == NCH - 1))
            lk.inc(i, "mm")

        lk.on("tensor", t_gl)

        def v_ev(v, lc=lc):
            lk.wait(v, "mm")
            i = v.tensor_tensor(gla[:, lc * E:(lc + 1) * E], p_g[:, 0:E], gbrep[:],
                                ALU.add)
            lk.inc(i, "ev")

        lk.on("vector", v_ev)

    gl3 = gla.ap().rearrange("p (l e) -> p l e", e=E)
    oh13 = oh1a.ap().rearrange("p (l e) -> p l e", e=E)
    oh23 = oh2a.ap().rearrange("p (l e) -> p l e", e=E)
    msk3 = mska.ap().rearrange("p (l e) -> p l e", e=E)
    va3 = va.ap().rearrange("p (l k) -> p l k", k=2)
    m1b = va3[:, :, 0:1].broadcast_to([128, 16, E])
    m2b = va3[:, :, 1:2].broadcast_to([128, 16, E])
    g16b = glm16.ap().unsqueeze(2).broadcast_to([128, 16, E])

    def v_b1(v):
        sp = lambda: v.tensor_copy(ydt[:, 0:512], ydt[:, 0:512])
        v.reduce_max(glm16[:], gl3, axis=AX)
        sp()
        i = v.tensor_tensor(gl3, gl3, g16b, ALU.subtract)
        lk.inc(i, "ev")

    lk.on("vector", v_b1)

    def s_b1(s):
        lk.wait(s, "ev")
        i = s.activation(gla[:], gla[:], AF.Exp)
        lk.inc(i, "sc")

    lk.on("scalar", s_b1)

    def v_b2(v):
        lk.wait(v, "sc")
        sp = lambda: v.tensor_copy(ydt[:, 0:512], ydt[:, 0:512])
        v.reduce_sum(glm16[:], gl3, axis=AX)
        sp()
        v.reciprocal(glm16[:], glm16[:])
        sp()
        v.tensor_tensor(gl3, gl3, g16b, ALU.mult)
        v.reduce_max(va3[:, :, 0:1], gl3, axis=AX)
        sp()
        v.tensor_tensor(oh13, gl3, m1b, ALU.is_ge)
        v.tensor_scalar(mska[:], oh1a[:], -1.0, 1.0, ALU.mult, ALU.add)
        v.tensor_tensor(mska[:], gla[:], mska[:], ALU.mult)
        v.reduce_max(va3[:, :, 1:2], msk3, axis=AX)
        sp()
        i = v.tensor_tensor(oh23, msk3, m2b, ALU.is_ge)
        lk.inc(i, "ev")

    lk.on("vector", v_b2)

    for lc in range(16):
        lo = lc * 128

        def t_tr(t, lc=lc, evc=lk.c["ev"]):
            lk.wait(t, "ev", evc)
            lk.nc.tensor.transpose(p_t1[:, 0:128], oh1a[:, lc * E:(lc + 1) * E], ident[:])
            lk.nc.tensor.transpose(p_t2[:, 0:128], oh2a[:, lc * E:(lc + 1) * E], ident[:])
            i = lk.nc.tensor.transpose(p_t3[:, 0:128], va[:, 2 * lc:2 * lc + 2], ident[:])
            lk.inc(i, "mm")

        lk.on("tensor", t_tr)

        def v_tr(v, lo=lo):
            lk.wait(v, "mm")
            v.tensor_copy(ohT1[:, lo:lo + 128], p_t1[:, 0:128])
            v.tensor_copy(ohT2[:, lo:lo + 128], p_t2[:, 0:128])
            i = v.tensor_copy(vT[:, lo:lo + 128], p_t3[:, 0:128])
            lk.mark(v, "ev")

        lk.on("vector", v_tr)

    def st1(s):
        lk.wait(s, "ev")
        i = s.dma_start(out=oh1_out, in_=ohT1[:])
        lk.inc(i, "st", 16)
        i = s.dma_start(out=oh2_out, in_=ohT2[:])
        lk.inc(i, "st", 16)
        i = s.dma_start(out=v1_out, in_=vT[0:1, :])
        lk.inc(i, "st", 16)
        i = s.dma_start(out=v2_out, in_=vT[1:2, :])
        lk.inc(i, "st", 16)
        s.wait_ge(lk.sems["st"], lk.c["st"])

    lk.on("sync", st1)
    lk.end_phase()



def emit_moe(lk, io, pool):
    xn1_in = io["xn1_s"]
    oh1_in, oh2_in = io["oh1_s"], io["oh2_s"]
    v1_in, v2_in = io["v1_s"], io["v2_s"]
    w1_in, w2_in = io["m_w1"], io["m_w2"]
    b1_in, b2_in, sel_in = io["m_b1"], io["m_b2"], io["m_sel"]
    mg_in, mb_in = io["m_lng"], io["m_lnb"]
    g1_out = io["g1_s"]
    lk.begin_phase("moe")
    lb = lk.c["ld"]
    wb = lk.c["wt"]
    bgp = lk.c["gp"]
    xn1 = lk.sb("mo_xn1", [128, NCH * TLOC], BF16)
    boost = lk.sb("mo_boost", [128, NCH * TLOC], BF16)
    g1acc = lk.sb("mo_g1acc", [128, NCH * TLOC], BF16)
    w1e = lk.sb("mo_w1e", [128, NCH * HMOE], BF16)
    w2e = lk.sb("mo_w2e", [128, NDC * D], BF16)
    h_sb = lk.sb("mo_h", [128, NDC * 512], BF16)
    ohf = lk.sb("mo_ohf", [E, 2 * TLOC], BF16)
    vf = lk.sb("mo_vf", [1, 2 * TLOC], BF16)
    selc = lk.sb("mo_sel", [E, E * 128], BF16)
    mrep = lk.sb("mo_mrep", [128, 512], BF16)
    mvrep = lk.sb("mo_mvrep", [128, 512], F32)
    vrep = lk.sb("mo_vrep", [128, 512], F32)
    eot = lk.sb("mo_eot", [128, 512], BF16)
    tmpf = lk.sb("mo_tmpf", [128, 512], F32)
    b1c = lk.sb("mo_b1c", [128, E * 16], F32)
    b2c = lk.sb("mo_b2c", [128, E * 8], F32)
    mg = lk.sb("mo_mg", [128, NCH], F32)
    mb = lk.sb("mo_mb", [128, NCH], F32)
    onesr = lk.sb("mo_onesr", [1, 128], BF16)
    kit = LNScratch(lk, 512, name="mo_ln")
    kit.init_consts()

    def g0(g):
        g.memset(onesr[:], 1.0)
        i = g.dma_start(out=ohf[:, 0:TLOC], in_=oh1_in)
        lk.inc(i, "wt", 16)
        i = g.dma_start(out=ohf[:, TLOC:2 * TLOC], in_=oh2_in)
        lk.inc(i, "wt", 16)
        i = g.dma_start(out=vf[:, 0:TLOC], in_=v1_in)
        lk.inc(i, "wt", 16)
        i = g.dma_start(out=vf[:, TLOC:2 * TLOC], in_=v2_in)
        lk.inc(i, "wt", 16)
        i = g.dma_start(out=selc[:], in_=sel_in)
        lk.inc(i, "wt", 16)
        i = g.nop()
        lk.inc(i, "gp")

    lk.on("gpsimd", g0)

    def ld0(s):
        for c in range(NCH):
            i = s.dma_start(out=xn1[:, c * TLOC:(c + 1) * TLOC],
                            in_=xn1_in[c * 128:(c + 1) * 128, :])
            lk.inc(i, "ld", 16)
        i = s.dma_start(out=b1c[:], in_=b1_in)
        lk.inc(i, "ld", 16)
        i = s.dma_start(out=b2c[:], in_=b2_in)
        lk.inc(i, "ld", 16)
        i = s.dma_start(out=mg[:], in_=mg_in)
        lk.inc(i, "ld", 16)
        i = s.dma_start(out=mb[:], in_=mb_in)
        lk.inc(i, "ld", 16)

    lk.on("sync", ld0)

    def v_init(v):
        lk.wait(v, "ld", lb + 16 * NCH)
        for c in range(NCH):
            v.tensor_copy(boost[:, c * TLOC:(c + 1) * TLOC], xn1[:, c * TLOC:(c + 1) * TLOC])
            i = v.tensor_copy(g1acc[:, c * TLOC:(c + 1) * TLOC], xn1[:, c * TLOC:(c + 1) * TLOC])
        lk.mark(v, "ev")

    lk.on("vector", v_init)

    mm_h_done = [lk.c["mm"]]
    mm_w2_done = [lk.c["mm"]]
    for p2 in range(2):
        rhs_t = xn1 if p2 == 0 else boost
        for e in range(E):
            def g_we(g, e=e, w1w=mm_h_done[0]):
                g.wait_ge(lk.sems["mm"], w1w)
                for kc in range(NCH):
                    i = g.dma_start(out=w1e[:, kc * HMOE:(kc + 1) * HMOE],
                                    in_=w1_in[e * D + kc * 128:e * D + (kc + 1) * 128, :])
                    lk.inc(i, "wt", 16)

            lk.on("gpsimd", g_we)

            def s_we(s, e=e, w2w=mm_w2_done[0]):
                s.wait_ge(lk.sems["mm"], w2w)
                for kc in range(NDC):
                    i = s.dma_start(out=w2e[:, kc * D:(kc + 1) * D],
                                    in_=w2_in[e * HMOE + kc * 128:e * HMOE + (kc + 1) * 128, :])
                    lk.inc(i, "w2t", 16)

            lk.on("sync", s_we)
            wt_w1 = lk.c["wt"]
            wt_w2 = lk.c["w2t"]

            for (mo, w) in MT512:
                def t_rep(t, e=e, p2=p2, mo=mo, w=w):
                    lk.wait(t, "ld")
                    lk.wait(t, "wt", wb + 16 * 5)
                    lk.wait(t, "gp", bgp + 1)
                    ib0, b0 = pool.acquire(t)
                    i = lk.nc.tensor.matmul(b0[:, 0:w], selc[:, e * 128:(e + 1) * 128],
                                            ohf[:, p2 * TLOC + mo:p2 * TLOC + mo + w],
                                            start=True, stop=True)
                    lk.inc(i, "mm")
                    ib1, b1 = pool.acquire(t)
                    i = lk.nc.tensor.matmul(b1[:, 0:w], onesr[:],
                                            vf[:1, p2 * TLOC + mo:p2 * TLOC + mo + w],
                                            start=True, stop=True)
                    lk.inc(i, "mm")
                    t_rep.banks = [(ib0, b0), (ib1, b1)]

                lk.on("tensor", t_rep)
                (ib0, b0), (ib1, b1) = t_rep.banks

                def v_rep(v, ib0=ib0, b0=b0, ib1=ib1, b1=b1, w=w):
                    lk.wait(v, "mm")
                    i = v.tensor_copy(mrep[:, 0:w], b0[:, 0:w])
                    pool.freed(i, ib0)
                    i = v.tensor_copy(vrep[:, 0:w], b1[:, 0:w])
                    pool.freed(i, ib1)
                    v.tensor_tensor(mvrep[:, 0:w], mrep[:, 0:w], vrep[:, 0:w], ALU.mult)
                    lk.mark(v, "ev")

                lk.on("vector", v_rep)

                def evac_h(e_, bank, n, mo2, w2, e=e):
                    return e_.tensor_scalar(h_sb[:, n * 512:n * 512 + w2], bank[:, 0:w2],
                                            b1c[:, e * 16 + n:e * 16 + n + 1], 0.0,
                                            ALU.add, ALU.max)

                mm_block(lk, pool,
                         get_lhsT=lambda kc, n: w1e[:, kc * HMOE + n * 128:kc * HMOE + (n + 1) * 128],
                         get_rhs=lambda kc, mo2, w2, rhs_t=rhs_t, mo=mo: rhs_t[:, kc * TLOC + mo:kc * TLOC + mo + w2],
                         nK=NCH, n_list=range(NDC), m_list=[(0, w)],
                         evac=evac_h, evac_engine="vector",
                         pre_wait=lambda t, wd=wt_w1, c=lk.c["ev"]: (lk.wait(t, "wt", wd),
                                                                     lk.wait(t, "ev", c)))
                emit_barrier(lk, "vector", "ev")
                mm_h_done[0] = lk.c["mm"]

                def evac_o(e_, bank, n, mo2, w2, e=e, p2=p2, mo=mo):
                    e_.tensor_scalar(eot[:, 0:w2], bank[:, 0:w2],
                                     b2c[:, e * 8 + n:e * 8 + n + 1], 0.0, ALU.add, ALU.add)
                    e_.tensor_tensor(tmpf[:, 0:w2], eot[:, 0:w2], mvrep[:, 0:w2], ALU.mult)
                    i = e_.tensor_tensor(g1acc[:, n * TLOC + mo:n * TLOC + mo + w2],
                                         g1acc[:, n * TLOC + mo:n * TLOC + mo + w2],
                                         tmpf[:, 0:w2], ALU.add)
                    if p2 == 0:
                        e_.tensor_tensor(tmpf[:, 0:w2], eot[:, 0:w2], mrep[:, 0:w2],
                                         ALU.mult)
                        i = e_.scalar_tensor_tensor(boost[:, n * TLOC + mo:n * TLOC + mo + w2],
                                                    tmpf[:, 0:w2], 0.5,
                                                    boost[:, n * TLOC + mo:n * TLOC + mo + w2],
                                                    ALU.mult, ALU.add)
                    return i

                mm_block(lk, pool,
                         get_lhsT=lambda kc, n: w2e[:, kc * D + n * 128:kc * D + (n + 1) * 128],
                         get_rhs=lambda kc, mo2, w2: h_sb[:, kc * 512:kc * 512 + w2],
                         nK=NDC, n_list=range(NCH), m_list=[(0, w)],
                         evac=evac_o, evac_engine="vector",
                         pre_wait=lambda t, wd=wt_w2, c=lk.c["ev"]: (lk.wait(t, "w2t", wd),
                                                                     lk.wait(t, "ev", c)))
                emit_barrier(lk, "vector", "ev")
                mm_w2_done[0] = lk.c["mm"]

    sq_full = h_sb.ap().bitcast(F32)
    tsc = tmpf
    g1o = w1e
    for (mo, w) in MT512:
        def get_x(c, mo=mo, w=w):
            return g1acc[:, c * TLOC + mo:c * TLOC + mo + w]

        def put_out(c, v, w=w):
            v.tensor_tensor(tsc[:, 0:w], get_x(c), kit.p_rep[:, 0:w], ALU.subtract)
            v.tensor_tensor(tsc[:, 0:w], tsc[:, 0:w], kit.p_rep[:, 512:512 + w], ALU.mult)
            v.tensor_scalar(g1o[:, c * 512:c * 512 + w], tsc[:, 0:w],
                            mg[:, c:c + 1], mb[:, c:c + 1], ALU.mult, ALU.add)

        emit_ln(lk, kit, get_x, put_out, w, sq_full, out_war_sems=("st",))

        def s_silu(s, mo=mo, w=w):
            lk.wait(s, "ev")
            for c in range(NCH):
                i = s.activation(g1o[:, c * 512:c * 512 + w], g1o[:, c * 512:c * 512 + w],
                                 AF.Silu)
            lk.inc(i, "sc")

        lk.on("scalar", s_silu)

        def st_g(s, mo=mo, w=w):
            lk.wait(s, "sc")
            for c in range(NCH):
                i = s.dma_start(out=g1_out[c * 128:(c + 1) * 128, mo:mo + w],
                                in_=g1o[:, c * 512:c * 512 + w])
                lk.inc(i, "st", 16)

        lk.on("sync", st_g)

    def fin(s):
        lk.wait(s, "st")

    lk.on("sync", fin)
    lk.end_phase()



def build_fused(upto=99, dbg=()):
    lk = L("mega")
    io = {}
    io["xT"] = lk.din("xT", [D, TLOC + 1])
    io["ln1g"] = lk.din("ln1g", [128, NCH])
    io["ln1b"] = lk.din("ln1b", [128, NCH])
    io["p_wq"] = lk.din("p_wq", [D, D], BF16); io["p_wk"] = lk.din("p_wk", [D, D], BF16)
    io["p_wv"] = lk.din("p_wv", [D, D], BF16)
    io["p_bq"] = lk.din("p_bq", [128, NCH]); io["p_bk"] = lk.din("p_bk", [128, NCH])
    io["p_bv"] = lk.din("p_bv", [1, D])
    io["proj"] = lk.din("proj", [DH, DK], BF16)
    io["a_wo"] = lk.din("a_wo", [D, D], BF16); io["a_bo"] = lk.din("a_bo", [128, NCH])
    io["ident"] = lk.din("ident", [128, 128])
    io["ln2g"] = lk.din("ln2g", [128, NCH]); io["ln2b"] = lk.din("ln2b", [128, NCH])
    io["f_w1"] = lk.din("f_w1", [D, DI], BF16); io["f_b1"] = lk.din("f_b1", [128, DI // 128])
    io["f_w2"] = lk.din("f_w2", [DI, D], BF16); io["f_b2"] = lk.din("f_b2", [128, NCH])
    io["r_wrt"] = lk.din("r_wrt", [D, D], BF16); io["r_wkt"] = lk.din("r_wkt", [D, D], BF16)
    io["r_wvt"] = lk.din("r_wvt", [D, D], BF16)
    io["r_mur"] = lk.din("r_mur", [128, NCH]); io["r_muk"] = lk.din("r_muk", [128, NCH])
    io["l_winT"] = lk.din("l_winT", [D, 2 * DN], BF16)
    io["l_convw"] = lk.din("l_convw", [128, NDC * 4])
    io["l_convb"] = lk.din("l_convb", [128, NDC])
    io["l_xprojT"] = lk.din("l_xprojT", [DN, 96], BF16)
    io["l_wdtT"] = lk.din("l_wdtT", [DTR, DN], BF16)
    io["l_bdt"] = lk.din("l_bdt", [128, NDC])
    io["s_alog"] = lk.din("s_alog", [128, 8 * DS])
    io["m0c"] = lk.din("m0c", [128, 1]); io["m1c"] = lk.din("m1c", [128, 1])
    io["g_dsk"] = lk.din("g_dsk", [128, NDC])
    io["g_woutT"] = lk.din("g_woutT", [DN, D], BF16)
    io["g_gwp"] = lk.din("g_gwp", [128, NCH * E], BF16)
    io["g_gb"] = lk.din("g_gb", [1, E])
    io["m_w1"] = lk.din("m_w1", [E * D, HMOE], BF16)
    io["m_w2"] = lk.din("m_w2", [E * HMOE, D], BF16)
    io["m_b1"] = lk.din("m_b1", [128, E * 16])
    io["m_b2"] = lk.din("m_b2", [128, E * 8])
    io["m_sel"] = lk.din("m_sel", [E, E * 128])
    io["m_lng"] = lk.din("m_lng", [128, NCH])
    io["m_lnb"] = lk.din("m_lnb", [128, NCH])
    io["out"] = lk.dout("out", [D, TLOC])

    def mk(name, shape, dt=F32):
        io[name] = (lk.dout(name, shape, dt) if name in dbg
                    else lk.dscratch(name, shape, dt))

    mk("xn_s", [D, TLOC + 1], BF16)
    mk("x2_s", [D, TLOC], BF16)
    mk("qf_s", [D, TLOC], BF16)
    mk("kvp_i", [NH * DH, 2 * DK])
    mk("kvp_o", [NH * DH, 2 * DK])
    mk("xs_s", [D, TLOC], BF16)
    mk("x2b_s", [D, TLOC], BF16)
    mk("s0_s", [D, TLOC], BF16)
    mk("g0_s", [D, TLOC], BF16)
    mk("x1_s", [D, TLOC + 3])
    mk("halo_i", [D, 4])
    mk("halo_o", [2 * D, 4])
    mk("xn1_s", [D, TLOC], BF16)
    mk("u_s", [DN, TLOC], BF16)
    mk("dt_s", [DN, TLOC], BF16)
    mk("z_s", [DN, TLOC], BF16)
    mk("Bs_s", [DS, TLOC])
    mk("Cs_s", [DS, TLOC])
    mk("dtG", [2 * DN, TLOC], BF16)
    mk("uG", [2 * DN, TLOC], BF16)
    mk("BsG", [2 * DS, TLOC])
    mk("CsG", [2 * DS, TLOC])
    mk("y_s", [DN // 2, T], BF16)
    mk("yG", [DN, T], BF16)
    mk("s1_s", [D, TLOC], BF16)
    mk("oh1_s", [E, TLOC])
    mk("oh2_s", [E, TLOC])
    mk("v1_s", [1, TLOC])
    mk("v2_s", [1, TLOC])
    mk("g1_s", [D, TLOC], BF16)

    lk.sem("ld", "wt", "w2t", "nt", "nf", "rf", "mm", "sc", "ev", "st", "gp",
           "ph", "bl", "pl", "pbf0", "pbf1",
           "cc1", "cc2", "cc3", "cc4", "cc5", "cc6", "cc7")
    lk.block()
    pool = PsumPool(lk, 2)

    steps = [
        lambda: emit_xn(lk, io),
        lambda: emit_perf_kv(lk, io, pool),
        lambda: lk.cc("AllReduce", ALU.add, io["kvp_i"], io["kvp_o"], "cc1"),
        lambda: emit_attn(lk, io, pool),
        lambda: emit_ln2(lk, io),
        lambda: emit_ffn(lk, io, pool),
        lambda: emit_rwkv(lk, io, pool),
        lambda: emit_combine(lk, io,
                             x_src=lambda c, mo, w: io["xT"][c * 128:(c + 1) * 128,
                                                            1 + mo:1 + mo + w],
                             s_src=io["s0_s"], g_src=io["g0_s"],
                             out_ap=io["x1_s"], out_off=3, halo_dst=io["halo_i"],
                             tag="cb1"),
        lambda: lk.cc("AllGather", ALU.bypass, io["halo_i"], io["halo_o"], "cc2"),
        lambda: emit_halo_fix(lk, io),
        lambda: emit_l1front(lk, io, pool),
        lambda: lk.cc("AllGather", ALU.bypass, io["dt_s"], io["dtG"], "cc3"),
        lambda: lk.cc("AllGather", ALU.bypass, io["u_s"], io["uG"], "cc4"),
        lambda: lk.cc("AllGather", ALU.bypass, io["Bs_s"], io["BsG"], "cc5"),
        lambda: lk.cc("AllGather", ALU.bypass, io["Cs_s"], io["CsG"], "cc6"),
        lambda: emit_scan(lk, io, pool),
        lambda: lk.cc("AllGather", ALU.bypass, io["y_s"], io["yG"], "cc7"),
        lambda: emit_gate_s1(lk, io, pool),
        lambda: emit_moe(lk, io, pool),
        lambda: emit_combine(lk, io,
                             x_src=lambda c, mo, w: io["x1_s"][c * 128:(c + 1) * 128,
                                                               3 + mo:3 + mo + w],
                             s_src=io["s1_s"], g_src=io["g1_s"],
                             out_ap=io["out"], out_off=0, tag="cb2"),
    ]
    for i, step in enumerate(steps):
        if i >= upto:
            break
        step()

    def fin(s):
        s.wait_ge(lk.sems["st"], lk.c["st"])

    lk.on("sync", fin)
    lk.close()
    return lk



def host_inputs(inputs):
    x = np.asarray(inputs["L0_attn_x"])
    al = np.asarray(inputs["L1_attn_A_log"], np.float32)
    shared = {
        "ln1g": host_prep_feat_cols(inputs["L0_attn_ln1_g"]),
        "ln1b": host_prep_feat_cols(inputs["L0_attn_ln1_b"]),
        "p_wq": _bfh(_f32h(inputs["L0_attn_wq"])),
        "p_wk": _bfh(_f32h(inputs["L0_attn_wk"])),
        "p_wv": _bfh(_f32h(inputs["L0_attn_wv"])),
        "p_bq": host_prep_feat_cols(inputs["L0_attn_bq"]),
        "p_bk": host_prep_feat_cols(inputs["L0_attn_bk"]),
        "p_bv": _f32h(np.asarray(inputs["L0_attn_bv"]).reshape(1, D)),
        "proj": _bfh(_f32h(inputs["L0_attn_proj_mat"])),
        "a_wo": _bfh(_f32h(inputs["L0_attn_wo"])),
        "a_bo": host_prep_feat_cols(inputs["L0_attn_bo"]),
        "ident": np.eye(128, dtype=np.float32),
        "ln2g": host_prep_feat_cols(inputs["L0_attn_ln2_g"]),
        "ln2b": host_prep_feat_cols(inputs["L0_attn_ln2_b"]),
        "f_w1": _bfh(_f32h(inputs["L0_attn_w1"])),
        "f_b1": host_prep_feat_cols(inputs["L0_attn_b1"]),
        "f_w2": _bfh(_f32h(inputs["L0_attn_w2"])),
        "f_b2": host_prep_feat_cols(inputs["L0_attn_b2"]),
        "r_wrt": _bfh(_f32h(np.asarray(inputs["L0_ffn_W_r"]).T)),
        "r_wkt": _bfh(_f32h(np.asarray(inputs["L0_ffn_W_k"]).T)),
        "r_wvt": _bfh(_f32h(np.asarray(inputs["L0_ffn_W_v"]).T)),
        "r_mur": host_prep_feat_cols(inputs["L0_ffn_mu_r"]),
        "r_muk": host_prep_feat_cols(inputs["L0_ffn_mu_k"]),
        "l_winT": _bfh(_f32h(np.asarray(inputs["L1_attn_W_in"]).T)),
        "l_convw": _f32h(np.transpose(
            np.asarray(inputs["L1_attn_conv_w"]).reshape(NDC, 128, 4),
            (1, 0, 2)).reshape(128, NDC * 4)),
        "l_convb": host_prep_feat_cols(inputs["L1_attn_conv_b"]),
        "l_xprojT": _bfh(_f32h(np.asarray(inputs["L1_attn_W_xproj"]).T)),
        "l_wdtT": _bfh(_f32h(np.asarray(inputs["L1_attn_W_dt"]).T)),
        "l_bdt": host_prep_feat_cols(inputs["L1_attn_b_dt"]),
        "g_dsk": host_prep_feat_cols(inputs["L1_attn_D_skip"]),
        "g_woutT": _bfh(_f32h(np.asarray(inputs["L1_attn_W_out"]).T)),
        "g_gwp": _bfh(np.transpose(
            np.asarray(inputs["L1_ffn_gate_W"], np.float32).reshape(NCH, 128, E),
            (1, 0, 2)).reshape(128, NCH * E)),
        "g_gb": _f32h(np.asarray(inputs["L1_ffn_gate_b"]).reshape(1, E)),
        "m_w1": _bfh(_f32h(np.asarray(inputs["L1_ffn_W1"]).reshape(E * D, HMOE))),
        "m_w2": _bfh(_f32h(np.asarray(inputs["L1_ffn_W2"]).reshape(E * HMOE, D))),
        "m_b1": _f32h(np.concatenate(
            [np.asarray(inputs["L1_ffn_b1"])[e].reshape(16, 128).T for e in range(E)],
            axis=1)),
        "m_b2": _f32h(np.concatenate(
            [np.asarray(inputs["L1_ffn_b2"])[e].reshape(8, 128).T for e in range(E)],
            axis=1)),
        "m_sel": _f32h(np.kron(np.eye(E, dtype=np.float32),
                               np.ones((1, 128), np.float32))),
        "m_lng": host_prep_feat_cols(inputs["L1_ffn_ln_gamma"]),
        "m_lnb": host_prep_feat_cols(inputs["L1_ffn_ln_beta"]),
    }
    in_maps = []
    for core in range(NCORES):
        b, j = divmod(core, 2)
        t0 = j * TLOC
        ext = np.zeros((D, TLOC + 1), np.float32)
        ext[:, 1:] = x[b, t0:t0 + TLOC, :].T
        if j == 1:
            ext[:, 0] = x[b, t0 - 1, :]
        r0, r1 = j * 1024, (j + 1) * 1024
        alc = np.ascontiguousarray(np.transpose(
            al[r0:r1].reshape(8, 128, DS), (1, 0, 2)).reshape(128, 8 * DS))
        m = dict(shared)
        m["xT"] = np.ascontiguousarray(ext)
        m["s_alog"] = alc
        m["m0c"] = np.full((128, 1), 1.0 - j, np.float32)
        m["m1c"] = np.full((128, 1), float(j), np.float32)
        in_maps.append(m)
    return in_maps


def kernel(**inputs):
    lk = build_fused()
    in_maps = host_inputs(inputs)
    res = run_bass_kernel_spmd(lk.nc, in_maps, core_ids=list(range(NCORES)))
    out = np.zeros((B, T, D), np.float32)
    for core in range(NCORES):
        b, j = divmod(core, 2)
        out[b, j * TLOC:(j + 1) * TLOC, :] = np.asarray(res.results[core]["out"]).T
    return out


# version 11
# speedup vs baseline: 1.2785x; 1.2785x over previous
import math
from contextlib import ExitStack

import numpy as np
import ml_dtypes

import concourse.bass as bass
import concourse.mybir as mybir
from concourse.bass_utils import run_bass_kernel_spmd

F32 = mybir.dt.float32
BF16 = mybir.dt.bfloat16
AF = mybir.ActivationFunctionType
ALU = mybir.AluOpType
AX = mybir.AxisListType.X

BF = ml_dtypes.bfloat16
NCORES = 8
PAIRS = [[0, 1], [2, 3], [4, 5], [6, 7]]
B, T, D = 4, 4096, 1024
TLOC = T // 2
NH, DH, DK = 16, 64, 128
DI = 4096
DN = 2048
DS = 16
DTR = 64
E, HMOE = 8, 2048
PI = float(np.pi)
EPS = 1e-5
NCH = D // 128
NDC = DN // 128
MT512 = [(o, 512) for o in range(0, TLOC, 512)]



class L:

    def __init__(self, name):
        self.name = name
        self.nc = bass.Bass("TRN2", target_bir_lowering=False)
        self.stack = ExitStack()
        self.ph = None
        self.c = {}
        self.sems = {}

    def din(self, name, shape, dt=F32):
        return self.nc.dram_tensor(name, list(shape), dt, kind="ExternalInput").ap()

    def dout(self, name, shape, dt=F32):
        return self.nc.dram_tensor(name, list(shape), dt, kind="ExternalOutput").ap()

    def dscratch(self, name, shape, dt=F32):
        return self.nc.dram_tensor(name, list(shape), dt, kind="Internal").ap()

    def sb(self, name, shape, dt=F32):
        st = self.ph if self.ph is not None else self.stack
        return st.enter_context(self.nc.sbuf_tensor(name, list(shape), dt))

    def ps(self, name, shape, dt=F32):
        st = self.ph if self.ph is not None else self.stack
        return st.enter_context(self.nc.psum_tensor(name, list(shape), dt))

    def sem(self, *names):
        for n in names:
            self.sems[n] = self.stack.enter_context(self.nc.semaphore(n))
            self.c[n] = 0

    def block(self):
        self.dummy = self.stack.enter_context(self.nc.sbuf_tensor("l_dummy", [1, 16], F32))
        self.blk = self.stack.enter_context(self.nc.Block())

    def mark(self, eng, sname):
        if hasattr(eng, "tensor_copy"):
            i = eng.tensor_copy(self.dummy[:1, 0:8], self.dummy[:1, 8:16])
        else:
            i = eng.copy(self.dummy[:1, 0:8], self.dummy[:1, 8:16])
        self.inc(i, sname)

    def inc(self, inst, sname, n=1):
        inst.then_inc(self.sems[sname], n)
        self.c[sname] += n
        return inst

    def wait(self, eng, sname, cnt=None):
        eng.wait_ge(self.sems[sname], self.c[sname] if cnt is None else cnt)

    def on(self, engine_name, fn):
        getattr(self.blk, engine_name)(fn)

    def begin_phase(self, name):
        assert self.ph is None
        self.ph = ExitStack()
        self._scope = self.nc.named_scope(name)
        self._scope.__enter__()
        waits = [(s, self.c[s]) for s in self.sems if self.c[s] > 0]
        for eng in ("sync", "scalar", "vector", "tensor", "gpsimd"):
            def bw(e, waits=tuple(waits)):
                for s, cnt in waits:
                    e.wait_ge(self.sems[s], cnt)
            self.on(eng, bw)

    def end_phase(self):
        for eng in ("scalar", "vector", "gpsimd"):
            def mk(e):
                self.mark(e, "ph")
            self.on(eng, mk)
        self.ph.close()
        self.ph = None
        self._scope.__exit__(None, None, None)

    def cc(self, kind, op, in_ap, out_ap, sname):
        def g_fn(g):
            g.wait_ge(self.sems["st"], self.c["st"])
            i = g.collective_compute(kind, op, replica_groups=PAIRS,
                                     ins=[in_ap.opt()], outs=[out_ap.opt()])
            self.inc(i, sname)
            g.wait_ge(self.sems[sname], self.c[sname])
        self.on("gpsimd", g_fn)

    def close(self):
        self.stack.close()


def _bfh(a):
    return np.ascontiguousarray(np.asarray(a).astype(BF))


def _f32h(a):
    return np.ascontiguousarray(np.asarray(a, dtype=np.float32))


def _tile_widths(n, w):
    out = []
    while n > 0:
        out.append(min(w, n))
        n -= out[-1]
    return out


def host_prep_feat_cols(vec):
    v = np.asarray(vec, np.float32)
    return np.ascontiguousarray(v.reshape(-1, 128).T)



def emit_ln(lk, kit, get_x, put_out, w, sq_full, out_war_sems=(), nchunks=NCH):
    nc = lk.nc

    def s_fn(s):
        lk.wait(s, "gp", kit.gp_done)
        lk.wait(s, "ld")
        lk.wait(s, "ev")
        for c in range(nchunks):
            i = s.activation(sq_full[:, c * w:(c + 1) * w], get_x(c), AF.Square)
        lk.inc(i, "sc")

    lk.on("scalar", s_fn)

    def t_fn(t):
        lk.wait(t, "gp", kit.gp_done)
        lk.wait(t, "ev")
        lk.wait(t, "ld")
        ones_c = kit.ones_col_bf if get_x(0).dtype == BF16 else kit.ones_col
        for c in range(nchunks):
            nc.tensor.matmul(kit.p_sum[:1, 0:w], ones_c[:], get_x(c),
                             start=(c == 0), stop=(c == nchunks - 1))
        lk.wait(t, "sc")
        for c in range(nchunks):
            i = nc.tensor.matmul(kit.p_sumsq[:1, 0:w], kit.ones_col[:],
                                 sq_full[:, c * w:(c + 1) * w],
                                 start=(c == 0), stop=(c == nchunks - 1))
        lk.inc(i, "mm")

    lk.on("tensor", t_fn)

    def v_fn(v):
        lk.wait(v, "mm")
        mean = kit.row[:1, 0:w]
        var = kit.row[:1, kit.maxw:kit.maxw + w]
        v.tensor_scalar(mean, kit.p_sum[:1, 0:w], 1.0 / D, 0.0, ALU.mult, ALU.add)
        v.tensor_tensor(var, mean, mean, ALU.mult)
        i = v.scalar_tensor_tensor(var, kit.p_sumsq[:1, 0:w], 1.0 / D, var,
                                   ALU.mult, ALU.subtract)
        lk.inc(i, "ev")

    lk.on("vector", v_fn)

    def s2_fn(s):
        lk.wait(s, "gp", kit.gp_done)
        lk.wait(s, "ev")
        var = kit.row[:1, kit.maxw:kit.maxw + w]
        s.activation(var, var, AF.Ln, bias=kit.eps_col[:1])
        i = s.activation(var, var, AF.Exp, scale=kit.half_col[:1])
        lk.inc(i, "sc")

    lk.on("scalar", s2_fn)

    def t2_fn(t):
        lk.wait(t, "sc")
        nc.tensor.matmul(kit.p_rep[:, 0:w], kit.ones_row[:], kit.row[:1, 0:w],
                         start=True, stop=True)
        i = nc.tensor.matmul(kit.p_rep[:, kit.maxw:kit.maxw + w], kit.ones_row[:],
                             kit.row[:1, kit.maxw:kit.maxw + w], start=True, stop=True)
        lk.inc(i, "mm")

    lk.on("tensor", t2_fn)

    def v2_fn(v):
        lk.wait(v, "mm")
        for s_ in out_war_sems:
            lk.wait(v, s_)
        for c in range(nchunks):
            put_out(c, v)
        lk.mark(v, "ev")

    lk.on("vector", v2_fn)


class LNScratch:
    def __init__(self, lk, maxw=512, name="ln"):
        self.lk = lk
        self.maxw = maxw
        self.row = lk.sb(f"{name}_row", [1, 2 * maxw], F32)
        self.p_sum = lk.ps(f"{name}_psum", [1, maxw], F32)
        self.p_sumsq = lk.ps(f"{name}_psumsq", [1, maxw], F32)
        self.p_rep = lk.ps(f"{name}_prep", [128, 2 * maxw], F32)
        self.ones_col = lk.sb(f"{name}_ones_c", [128, 1], F32)
        self.ones_col_bf = lk.sb(f"{name}_ones_cb", [128, 1], BF16)
        self.ones_row = lk.sb(f"{name}_ones_r", [1, 128], F32)
        self.eps_col = lk.sb(f"{name}_eps", [1, 1], F32)
        self.half_col = lk.sb(f"{name}_half", [1, 1], F32)

    def init_consts(self):
        lk = self.lk

        def g_fn(g):
            g.memset(self.ones_col[:], 1.0)
            g.memset(self.ones_col_bf[:], 1.0)
            g.memset(self.ones_row[:], 1.0)
            g.memset(self.eps_col[:], EPS)
            i = g.memset(self.half_col[:], -0.5)
            lk.inc(i, "gp")

        lk.on("gpsimd", g_fn)
        self.gp_done = lk.c["gp"]



class PsumPool:

    def __init__(self, lk, nbanks=2, tag="pb"):
        self.lk = lk
        self.n = nbanks
        self.banks = [lk.ps(f"{tag}{i}", [128, 512], F32) for i in range(nbanks)]
        self.free_sems = [f"{tag}f{i}" for i in range(nbanks)]
        self.uses = [0] * nbanks
        self.rr = 0

    def acquire(self, teng):
        i = self.rr
        self.rr = (self.rr + 1) % self.n
        if self.uses[i] > 0:
            teng.wait_ge(self.lk.sems[self.free_sems[i]], self.uses[i])
        self.uses[i] += 1
        return i, self.banks[i]

    def freed(self, inst, i):
        inst.then_inc(self.lk.sems[self.free_sems[i]], 1)
        self.lk.c[self.free_sems[i]] += 1


def mm_block(lk, pool, get_lhsT, get_rhs, nK, n_list, m_list, evac,
             evac_engine="vector", pre_wait=None):
    nc = lk.nc
    first = True
    for n in n_list:
        for (mo, w) in m_list:
            holder = {}

            def t_fn(t, n=n, mo=mo, w=w, first=first):
                if pre_wait is not None and first:
                    pre_wait(t)
                ib, bank = pool.acquire(t)
                holder["b"] = (ib, bank)
                for kc in range(nK):
                    i = nc.tensor.matmul(bank[:, 0:w], get_lhsT(kc, n),
                                         get_rhs(kc, mo, w),
                                         start=(kc == 0), stop=(kc == nK - 1))
                lk.inc(i, "mm")

            lk.on("tensor", t_fn)
            ib, bank = holder["b"]

            def e_fn(e, n=n, mo=mo, w=w, ib=ib, bank=bank):
                lk.wait(e, "mm")
                last = evac(e, bank, n, mo, w)
                pool.freed(last, ib)

            lk.on(evac_engine, e_fn)
            first = False


def emit_barrier(lk, engine_name, sname="ev"):
    def f(e):
        lk.mark(e, sname)

    lk.on(engine_name, f)


def load_weight_bf16(lk, g, dst_sb, hbm_ap, nK, ncols, sem="wt"):
    for kc in range(nK):
        i = g.dma_start(out=dst_sb[:, kc * ncols:(kc + 1) * ncols],
                        in_=hbm_ap[kc * 128:(kc + 1) * 128, :])
        lk.inc(i, sem, 16)



def emit_xn(lk, io):
    NT = TLOC + 1
    x_in, g_in, b_in = io["xT"], io["ln1g"], io["ln1b"]
    xn_out, x2_out = io["xn_s"], io["x2_s"]
    lk.begin_phase("xn")
    widths = [257] + _tile_widths(TLOC - 256, 512)
    xt = lk.sb("xn_xt", [128, NCH * 512], F32)
    xo = lk.sb("xn_xo", [128, NCH * 512], BF16)
    x2o = lk.sb("xn_x2o", [128, NCH * 512], BF16)
    sq_full = lk.sb("xn_sq", [128, NCH * 512], F32)
    gcol = lk.sb("xn_gcol", [128, NCH], F32)
    bcol = lk.sb("xn_bcol", [128, NCH], F32)
    kit = LNScratch(lk, 512, name="xn_ln")
    kit.init_consts()

    def ld0(s):
        i = s.dma_start(out=gcol[:], in_=g_in)
        lk.inc(i, "ld", 16)
        i = s.dma_start(out=bcol[:], in_=b_in)
        lk.inc(i, "ld", 16)

    lk.on("sync", ld0)

    t0 = 0
    for w in widths:
        o = t0

        def ld_fn(s, o=o, w=w):
            lk.wait(s, "ev")
            for c in range(NCH):
                i = s.dma_start(out=xt[:, c * w:(c + 1) * w],
                                in_=x_in[c * 128:(c + 1) * 128, o:o + w])
                lk.inc(i, "ld", 16)

        lk.on("sync", ld_fn)

        def get_x(c, w=w):
            return xt[:, c * w:(c + 1) * w]

        def put_out(c, v, w=w):
            tmp = sq_full[:, c * w:(c + 1) * w]
            v.tensor_tensor(tmp, get_x(c), kit.p_rep[:, 0:w], ALU.subtract)
            v.tensor_tensor(xo[:, c * w:(c + 1) * w], tmp,
                            kit.p_rep[:, 512:512 + w], ALU.mult)

        emit_ln(lk, kit, get_x, put_out, w, sq_full, out_war_sems=("st",))

        def get_x2(c, w=w):
            return xo[:, c * w:(c + 1) * w]

        def put_out2(c, v, w=w):
            tmp = sq_full[:, c * w:(c + 1) * w]
            d = x2o[:, c * w:(c + 1) * w]
            v.tensor_tensor(tmp, get_x2(c), kit.p_rep[:, 0:w], ALU.subtract)
            v.tensor_tensor(tmp, tmp, kit.p_rep[:, 512:512 + w], ALU.mult)
            v.tensor_scalar(d, tmp, gcol[:, c:c + 1], bcol[:, c:c + 1],
                            ALU.mult, ALU.add)

        emit_ln(lk, kit, get_x2, put_out2, w, sq_full, out_war_sems=("st",))

        def st_fn(s, o=o, w=w):
            lk.wait(s, "ev")
            for c in range(NCH):
                i = s.dma_start(out=xn_out[c * 128:(c + 1) * 128, o:o + w],
                                in_=xo[:, c * w:(c + 1) * w])
                lk.inc(i, "st", 16)
            x2src0 = 1 if o == 0 else 0
            for c in range(NCH):
                i = s.dma_start(
                    out=x2_out[c * 128:(c + 1) * 128, max(o - 1, 0):o + w - 1],
                    in_=x2o[:, c * w + x2src0:(c + 1) * w])
                lk.inc(i, "st", 16)

        lk.on("sync", st_fn)
        t0 += w

    def fin(s):
        lk.wait(s, "st")

    lk.on("sync", fin)
    lk.end_phase()



def emit_perf_kv(lk, io, pool):
    x2_in = io["x2_s"]
    wq_in, wk_in, wv_in = io["p_wq"], io["p_wk"], io["p_wv"]
    bq_in, bk_in, bv_in, proj_in = io["p_bq"], io["p_bk"], io["p_bv"], io["proj"]
    qf_out, kvp_out = io["qf_s"], io["kvp_i"]
    lk.begin_phase("perfkv")
    wb = lk.c["wt"]
    x2 = lk.sb("pk_x2", [128, NCH * TLOC], BF16)
    wq = lk.sb("pk_wq", [128, NCH * D], BF16)
    wk = lk.sb("pk_wk", [128, NCH * D], BF16)
    wv = lk.sb("pk_wv", [128, NCH * D], BF16)
    qf = lk.sb("pk_qf", [128, NCH * TLOC], BF16)
    kf = lk.sb("pk_kf", [128, NCH * TLOC], BF16)
    v_full = lk.sb("pk_v", [128, 16 * D], BF16)
    proj = lk.sb("pk_proj", [128, 2 * DK], BF16)
    bq = lk.sb("pk_bq", [128, NCH], F32)
    bk = lk.sb("pk_bk", [128, NCH], F32)
    bvr = lk.sb("pk_bvr", [1, D], F32)
    bv_rep = lk.sb("pk_bvrep", [128, D], BF16)
    onesr = lk.sb("pk_onesr", [1, 128], F32)
    wsc = lk.sb("pk_wsc", [128, 256], F32)
    csc = lk.sb("pk_csc", [128, 256], F32)
    aw = lk.sb("pk_aw", [128, 256], F32)
    kp = lk.sb("pk_kp", [128, 4 * DK], BF16)
    kvsb = lk.sb("pk_kvsb", [128, 4 * 512], F32)
    cst = lk.sb("pk_cst", [128, 2], F32)
    p_pk = lk.ps("pk_ppk", [128, 2 * DK], F32)
    kv4 = [lk.ps(f"pk_pkv{i}", [128, 512], F32) for i in range(4)]
    bgp = lk.c["gp"]

    def g0(g):
        g.memset(cst[:, 0:1], PI / 2)
        g.memset(cst[:, 1:2], -1.0)
        i = g.memset(onesr[:], 1.0)
        lk.inc(i, "gp")
        load_weight_bf16(lk, g, wq, wq_in, NCH, D)
        load_weight_bf16(lk, g, wk, wk_in, NCH, D)
        load_weight_bf16(lk, g, wv, wv_in, NCH, D)
        g.memset(proj[:], 0.0)
        i = g.dma_start(out=proj[0:64, 0:DK], in_=proj_in)
        lk.inc(i, "wt", 16)
        i = g.dma_start(out=proj[64:128, DK:2 * DK], in_=proj_in)
        lk.inc(i, "wt", 16)

    lk.on("gpsimd", g0)

    def ld0(s):
        for c in range(NCH):
            i = s.dma_start(out=x2[:, c * TLOC:(c + 1) * TLOC],
                            in_=x2_in[c * 128:(c + 1) * 128, :])
            lk.inc(i, "ld", 16)
        for t, srcp in [(bq, bq_in), (bk, bk_in), (bvr, bv_in)]:
            i = s.dma_start(out=t[:], in_=srcp)
            lk.inc(i, "ld", 16)

    lk.on("sync", ld0)

    def t_bv(t):
        lk.wait(t, "ld")
        lk.wait(t, "gp", bgp + 1)
        ib0, b0 = pool.acquire(t)
        i = lk.nc.tensor.matmul(b0[:, 0:512], onesr[:], bvr[:1, 0:512],
                                start=True, stop=True)
        lk.inc(i, "mm")
        ib1, b1 = pool.acquire(t)
        i = lk.nc.tensor.matmul(b1[:, 0:512], onesr[:], bvr[:1, 512:1024],
                                start=True, stop=True)
        lk.inc(i, "mm")
        t_bv.banks = [(ib0, b0), (ib1, b1)]

    lk.on("tensor", t_bv)

    def v_bv(v):
        lk.wait(v, "mm")
        (ib0, b0), (ib1, b1) = t_bv.banks
        i = v.tensor_copy(bv_rep[:, 0:512], b0[:, 0:512])
        pool.freed(i, ib0)
        i = v.tensor_copy(bv_rep[:, 512:1024], b1[:, 0:512])
        pool.freed(i, ib1)
        lk.mark(v, "ev")

    lk.on("vector", v_bv)

    for (wt_sb, b_sb, dst, wt_cnt) in [(wq, bq, qf, 16 * 8), (wk, bk, kf, 16 * 16)]:
        def evac_qk(e, bank, n, mo, w, b_sb=b_sb, dst=dst):
            return e.tensor_scalar(dst[:, n * TLOC + mo:n * TLOC + mo + w],
                                   bank[:, 0:w], b_sb[:, n:n + 1], 0.0,
                                   ALU.add, ALU.add)

        mm_block(lk, pool,
                 get_lhsT=lambda kc, n, wt_sb=wt_sb: wt_sb[:, kc * D + n * 128:kc * D + (n + 1) * 128],
                 get_rhs=lambda kc, mo, w: x2[:, kc * TLOC + mo:kc * TLOC + mo + w],
                 nK=NCH, n_list=range(NCH), m_list=MT512,
                 evac=evac_qk, evac_engine="vector",
                 pre_wait=lambda t, wc=wt_cnt: (lk.wait(t, "wt", wb + wc), lk.wait(t, "ld")))
    emit_barrier(lk, "vector", "ev")

    def evac_v(e, bank, n, mo, w):
        return e.tensor_tensor(v_full[:, n * D + mo:n * D + mo + w], bank[:, 0:w],
                               bv_rep[:, mo:mo + w], ALU.add)

    mm_block(lk, pool,
             get_lhsT=lambda kc, n: x2[:, kc * TLOC + n * 128:kc * TLOC + n * 128 + 128],
             get_rhs=lambda kc, mo, w: wv[:, kc * D + mo:kc * D + mo + w],
             nK=NCH, n_list=range(16), m_list=[(0, 512), (512, 512)],
             evac=evac_v, evac_engine="vector",
             pre_wait=lambda t: (lk.wait(t, "wt", wb + 16 * 24), lk.wait(t, "ev")))
    emit_barrier(lk, "vector", "ev")

    for ch in range(8):
        for lc in range(16):
            lo = lc * 128

            def t_pk(t, ch=ch, lo=lo):
                lk.wait(t, "ev")
                i = lk.nc.tensor.matmul(p_pk[:, 0:2 * DK],
                                        kf[:, ch * TLOC + lo:ch * TLOC + lo + 128],
                                        proj[:, 0:2 * DK],
                                        start=True, stop=True)
                lk.inc(i, "mm")

            lk.on("tensor", t_pk)

            def v_wrap(v):
                lk.wait(v, "mm")
                lk.wait(v, "sc")
                v.tensor_scalar(wsc[:], p_pk[:, 0:2 * DK], 0.125, 0.0, ALU.mult, ALU.add)
                v.tensor_scalar(csc[:], wsc[:], PI, -2 * PI, ALU.is_gt, ALU.mult)
                v.tensor_tensor(wsc[:], wsc[:], csc[:], ALU.add)
                v.tensor_scalar(csc[:], wsc[:], -PI, 2 * PI, ALU.is_lt, ALU.mult)
                i = v.tensor_tensor(wsc[:], wsc[:], csc[:], ALU.add)
                lk.inc(i, "ev")

            lk.on("vector", v_wrap)

            def s_sin(s):
                lk.wait(s, "ev")
                s.activation(aw[:], wsc[:], AF.Abs)
                s.activation(kp[:, 2 * DK:4 * DK], wsc[:], AF.Sin)
                i = s.activation(kp[:, 0:2 * DK], aw[:], AF.Sin,
                                 bias=cst[:, 0:1], scale=cst[:, 1:2])
                lk.inc(i, "sc")

            lk.on("scalar", s_sin)

            def t_kv(t, ch=ch, lc=lc):
                lk.wait(t, "sc")
                ro = (ch % 2) * 64
                co = (ch // 2) * 128
                for hh in range(2):
                    h = 2 * ch + hh
                    lk.nc.tensor.matmul(
                        kv4[2 * hh + 0][ro:ro + 64, co:co + 128],
                        v_full[:, lc * D + h * 64:lc * D + (h + 1) * 64],
                        kp[:, hh * 128:(hh + 1) * 128],
                        start=(lc == 0), stop=(lc == 15))
                    i = lk.nc.tensor.matmul(
                        kv4[2 * hh + 1][ro:ro + 64, co:co + 128],
                        v_full[:, lc * D + h * 64:lc * D + (h + 1) * 64],
                        kp[:, 2 * DK + hh * 128:2 * DK + (hh + 1) * 128],
                        start=(lc == 0), stop=(lc == 15))
                if lc == 15:
                    lk.inc(i, "mm")

            lk.on("tensor", t_kv)

    def v_kv(v):
        lk.wait(v, "mm")
        for q4 in range(4):
            i = v.tensor_copy(kvsb[:, q4 * 512:(q4 + 1) * 512], kv4[q4][:, 0:512])
        lk.mark(v, "ev")

    lk.on("vector", v_kv)

    def st0(s):
        lk.wait(s, "ev")
        for c in range(NCH):
            i = s.dma_start(out=qf_out[c * 128:(c + 1) * 128, :],
                            in_=qf[:, c * TLOC:(c + 1) * TLOC])
            lk.inc(i, "st", 16)
        for h in range(NH):
            ch, hh = h // 2, h % 2
            ro = (ch % 2) * 64
            co = (ch // 2) * 128
            i = s.dma_start(out=kvp_out[h * DH:(h + 1) * DH, 0:128],
                            in_=kvsb[ro:ro + 64, (2 * hh + 0) * 512 + co:
                                     (2 * hh + 0) * 512 + co + 128])
            lk.inc(i, "st", 16)
            i = s.dma_start(out=kvp_out[h * DH:(h + 1) * DH, 128:256],
                            in_=kvsb[ro:ro + 64, (2 * hh + 1) * 512 + co:
                                     (2 * hh + 1) * 512 + co + 128])
            lk.inc(i, "st", 16)
        s.wait_ge(lk.sems["st"], lk.c["st"])

    lk.on("sync", st0)
    lk.end_phase()



def emit_attn(lk, io, pool):
    qf_in, kvp_in, xn_in = io["qf_s"], io["kvp_o"], io["xn_s"]
    wo_in, bo_in, proj_in, id_in = io["a_wo"], io["a_bo"], io["proj"], io["ident"]
    xs_out = io["xs_s"]
    lk.begin_phase("attn")
    wb = lk.c["wt"]
    qf = lk.sb("at_qf", [128, NCH * TLOC], BF16)
    xn = lk.sb("at_xn", [128, NCH * TLOC], BF16)
    kvp_sb = lk.sb("at_kvp", [128, 8 * 256], F32)
    kvt = lk.sb("at_kvt", [128, 2 * D], BF16)
    wo = lk.sb("at_wo", [128, NCH * D], BF16)
    bo = lk.sb("at_bo", [128, NCH], F32)
    proj = lk.sb("at_proj", [128, 2 * DK], BF16)
    ident = lk.sb("at_id", [128, 128], F32)
    wsc = lk.sb("at_wsc", [128, 512], F32)
    csc = lk.sb("at_csc", [128, 512], F32)
    aw = lk.sb("at_aw", [128, 512], F32)
    qp = lk.sb("at_qp", [128, 2 * 512], BF16)
    attn_sb = lk.sb("at_attn", [128, NCH * 512], BF16)
    xs_sb = lk.sb("at_xs", [128, NCH * TLOC], BF16)
    cst = lk.sb("at_cst", [128, 2], F32)
    p_pq = lk.ps("at_ppq", [128, 512], F32)
    p_at = lk.ps("at_pat", [64, 512], F32)
    p_tr = lk.ps("at_ptr", [128, 128], F32)

    def g0(g):
        g.memset(cst[:, 0:1], PI / 2)
        g.memset(cst[:, 1:2], -1.0)
        g.memset(proj[:], 0.0)
        i = g.dma_start(out=proj[0:64, 0:DK], in_=proj_in)
        lk.inc(i, "wt", 16)
        i = g.dma_start(out=proj[64:128, DK:2 * DK], in_=proj_in)
        lk.inc(i, "wt", 16)
        load_weight_bf16(lk, g, wo, wo_in, NCH, D)

    lk.on("gpsimd", g0)

    def ld0(s):
        for c in range(NCH):
            i = s.dma_start(out=qf[:, c * TLOC:(c + 1) * TLOC],
                            in_=qf_in[c * 128:(c + 1) * 128, :])
            lk.inc(i, "ld", 16)
        for c in range(NCH):
            i = s.dma_start(out=xn[:, c * TLOC:(c + 1) * TLOC],
                            in_=xn_in[c * 128:(c + 1) * 128, 1:TLOC + 1])
            lk.inc(i, "ld", 16)
        for c in range(NCH):
            i = s.dma_start(out=kvp_sb[:, c * 256:(c + 1) * 256],
                            in_=kvp_in[c * 128:(c + 1) * 128, :])
            lk.inc(i, "ld", 16)
        i = s.dma_start(out=bo[:], in_=bo_in)
        lk.inc(i, "ld", 16)
        i = s.dma_start(out=ident[:], in_=id_in)
        lk.inc(i, "ld", 16)

    lk.on("sync", ld0)

    for c in range(NCH):
        for mc in range(2):
            def t_tr(t, c=c, mc=mc):
                if c == 0 and mc == 0:
                    lk.wait(t, "ld")
                lk.wait(t, "ev")
                i = lk.nc.tensor.transpose(p_tr[:, 0:128],
                                           kvp_sb[:, c * 256 + mc * 128:c * 256 + (mc + 1) * 128],
                                           ident[:])
                lk.inc(i, "mm")

            lk.on("tensor", t_tr)

            def v_tr(v, c=c, mc=mc):
                lk.wait(v, "mm")
                i = v.tensor_scalar(kvt[:, mc * D + c * 128:mc * D + (c + 1) * 128],
                                    p_tr[:, 0:128], 1.0 / 128.0, 0.0, ALU.mult, ALU.add)
                lk.inc(i, "ev")

            lk.on("vector", v_tr)
    emit_barrier(lk, "vector", "ev")

    for (mo, w) in MT512:
        for h in range(NH):
            ch, po = h // 2, (h % 2) * 64

            def t_pq(t, ch=ch, po=po, mo=mo, w=w):
                lk.wait(t, "ev")
                lk.wait(t, "wt", wb + 16 * 2)
                i = lk.nc.tensor.matmul(p_pq[:, 0:w],
                                        proj[:, (po // 64) * DK:(po // 64) * DK + DK],
                                        qf[:, ch * TLOC + mo:ch * TLOC + mo + w],
                                        start=True, stop=True)
                lk.inc(i, "mm")

            lk.on("tensor", t_pq)

            def v_wrap(v, w=w):
                lk.wait(v, "mm")
                lk.wait(v, "sc")
                v.tensor_scalar(wsc[:, 0:w], p_pq[:, 0:w], 0.125, 0.0, ALU.mult, ALU.add)
                v.tensor_scalar(csc[:, 0:w], wsc[:, 0:w], PI, -2 * PI, ALU.is_gt, ALU.mult)
                v.tensor_tensor(wsc[:, 0:w], wsc[:, 0:w], csc[:, 0:w], ALU.add)
                v.tensor_scalar(csc[:, 0:w], wsc[:, 0:w], -PI, 2 * PI, ALU.is_lt, ALU.mult)
                i = v.tensor_tensor(wsc[:, 0:w], wsc[:, 0:w], csc[:, 0:w], ALU.add)
                lk.inc(i, "ev")

            lk.on("vector", v_wrap)

            def s_sin(s, w=w):
                lk.wait(s, "ev")
                s.activation(aw[:, 0:w], wsc[:, 0:w], AF.Abs)
                s.activation(qp[:, 512:512 + w], wsc[:, 0:w], AF.Sin)
                i = s.activation(qp[:, 0:w], aw[:, 0:w], AF.Sin,
                                 bias=cst[:, 0:1], scale=cst[:, 1:2])
                lk.inc(i, "sc")

            lk.on("scalar", s_sin)

            def t_at(t, h=h, w=w):
                lk.wait(t, "sc")
                lk.nc.tensor.matmul(p_at[:, 0:w], kvt[:, 0 * D + h * 64:0 * D + h * 64 + 64],
                                    qp[:, 0:w], start=True, stop=False)
                i = lk.nc.tensor.matmul(p_at[:, 0:w], kvt[:, 1 * D + h * 64:1 * D + h * 64 + 64],
                                        qp[:, 512:512 + w], start=False, stop=True)
                lk.inc(i, "mm")

            lk.on("tensor", t_at)

            def v_at(v, h=h, ch=ch, po=po, w=w):
                lk.wait(v, "mm")
                i = v.tensor_copy(attn_sb[po:po + 64, ch * 512:ch * 512 + w], p_at[:, 0:w])
                lk.inc(i, "ev")

            lk.on("vector", v_at)

        def evac_o(e, bank, n, mo2, w2, mo=mo):
            return e.scalar_tensor_tensor(xs_sb[:, n * TLOC + mo:n * TLOC + mo + w2],
                                          bank[:, 0:w2], bo[:, n:n + 1],
                                          xn[:, n * TLOC + mo:n * TLOC + mo + w2],
                                          ALU.add, ALU.add)

        mm_block(lk, pool,
                 get_lhsT=lambda kc, n: wo[:, kc * D + n * 128:kc * D + (n + 1) * 128],
                 get_rhs=lambda kc, mo2, w2: attn_sb[:, kc * 512:kc * 512 + w2],
                 nK=NCH, n_list=range(NCH), m_list=[(0, w)],
                 evac=evac_o, evac_engine="vector",
                 pre_wait=lambda t, c=lk.c["ev"]: (lk.wait(t, "wt"), lk.wait(t, "ev", c)))
        emit_barrier(lk, "vector", "ev")

    def st0(s):
        lk.wait(s, "ev")
        for c in range(NCH):
            i = s.dma_start(out=xs_out[c * 128:(c + 1) * 128, :],
                            in_=xs_sb[:, c * TLOC:(c + 1) * TLOC])
            lk.inc(i, "st", 16)
        s.wait_ge(lk.sems["st"], lk.c["st"])

    lk.on("sync", st0)
    lk.end_phase()



def emit_ln2(lk, io):
    xs_in, g_in, b_in, out = io["xs_s"], io["ln2g"], io["ln2b"], io["x2b_s"]
    lk.begin_phase("ln2")
    xt = lk.sb("l2_xt", [128, NCH * 512], BF16)
    xo = lk.sb("l2_xo", [128, NCH * 512], BF16)
    sq_full = lk.sb("l2_sq", [128, NCH * 512], F32)
    tsc = lk.sb("l2_tsc", [128, 512], F32)
    gcol = lk.sb("l2_gcol", [128, NCH], F32)
    bcol = lk.sb("l2_bcol", [128, NCH], F32)
    kit = LNScratch(lk, 512, name="l2_ln")
    kit.init_consts()

    def ld0(s):
        i = s.dma_start(out=gcol[:], in_=g_in)
        lk.inc(i, "ld", 16)
        i = s.dma_start(out=bcol[:], in_=b_in)
        lk.inc(i, "ld", 16)

    lk.on("sync", ld0)

    for (mo, w) in MT512:
        def ld_fn(s, mo=mo, w=w):
            lk.wait(s, "ev")
            for c in range(NCH):
                i = s.dma_start(out=xt[:, c * w:(c + 1) * w],
                                in_=xs_in[c * 128:(c + 1) * 128, mo:mo + w])
                lk.inc(i, "ld", 16)

        lk.on("sync", ld_fn)

        def get_x(c, w=w):
            return xt[:, c * w:(c + 1) * w]

        def put_out(c, v, w=w):
            v.tensor_tensor(tsc[:, 0:w], get_x(c), kit.p_rep[:, 0:w], ALU.subtract)
            v.tensor_tensor(tsc[:, 0:w], tsc[:, 0:w], kit.p_rep[:, 512:512 + w], ALU.mult)
            v.tensor_scalar(xo[:, c * w:(c + 1) * w], tsc[:, 0:w],
                            gcol[:, c:c + 1], bcol[:, c:c + 1], ALU.mult, ALU.add)

        emit_ln(lk, kit, get_x, put_out, w, sq_full, out_war_sems=("st",))

        def st_fn(s, mo=mo, w=w):
            lk.wait(s, "ev")
            for c in range(NCH):
                i = s.dma_start(out=out[c * 128:(c + 1) * 128, mo:mo + w],
                                in_=xo[:, c * w:(c + 1) * w])
                lk.inc(i, "st", 16)

        lk.on("sync", st_fn)

    def fin(s):
        lk.wait(s, "st")

    lk.on("sync", fin)
    lk.end_phase()



def emit_ffn(lk, io, pool):
    NI = DI // 128
    x2b_in, xs_in = io["x2b_s"], io["xs_s"]
    w1_in, b1_in, w2_in, b2_in = io["f_w1"], io["f_b1"], io["f_w2"], io["f_b2"]
    s_out = io["s0_s"]
    lk.begin_phase("ffn")
    wb = lk.c["wt"]
    w1 = lk.sb("ff_w1", [128, NCH * DI], BF16)
    w2 = lk.sb("ff_w2", [128, NI * D], BF16)
    b1c = lk.sb("ff_b1c", [128, NI], F32)
    b2c = lk.sb("ff_b2c", [128, NCH], F32)
    xb = lk.sb("ff_xb", [128, NCH * 512], BF16)
    h = lk.sb("ff_h", [128, NI * 512], BF16)
    xs = lk.sb("ff_xs", [128, NCH * 512], BF16)
    s_t = lk.sb("ff_st", [128, NCH * 512], BF16)

    def g0(g):
        load_weight_bf16(lk, g, w1, w1_in, NCH, DI)
        load_weight_bf16(lk, g, w2, w2_in, NI, D)

    lk.on("gpsimd", g0)

    def ld0(s):
        i = s.dma_start(out=b1c[:], in_=b1_in)
        lk.inc(i, "ld", 16)
        i = s.dma_start(out=b2c[:], in_=b2_in)
        lk.inc(i, "ld", 16)

    lk.on("sync", ld0)

    for (mo, w) in MT512:
        def ld_t(s, mo=mo, w=w, evn=lk.c["ev"]):
            lk.wait(s, "ev", evn)
            for c in range(NCH):
                i = s.dma_start(out=xb[:, c * w:(c + 1) * w],
                                in_=x2b_in[c * 128:(c + 1) * 128, mo:mo + w])
                lk.inc(i, "ld", 16)
                i = s.dma_start(out=xs[:, c * w:(c + 1) * w],
                                in_=xs_in[c * 128:(c + 1) * 128, mo:mo + w])
                lk.inc(i, "ld", 16)

        lk.on("sync", ld_t)

        def evac_h(e, bank, n, mo2, w2_):
            return e.tensor_scalar(h[:, n * 512:n * 512 + w2_], bank[:, 0:w2_],
                                   b1c[:, n:n + 1], 0.0, ALU.add, ALU.max)

        mm_block(lk, pool,
                 get_lhsT=lambda kc, n: w1[:, kc * DI + n * 128:kc * DI + (n + 1) * 128],
                 get_rhs=lambda kc, mo2, w2_, w=w: xb[:, kc * w:kc * w + w2_],
                 nK=NCH, n_list=range(NI), m_list=[(0, w)],
                 evac=evac_h, evac_engine="vector",
                 pre_wait=lambda t, c=lk.c["mm"]: (lk.wait(t, "wt", wb + 16 * NCH),
                                                   lk.wait(t, "ld"),
                                                   lk.wait(t, "mm", c)))
        emit_barrier(lk, "vector", "ev")

        def evac_s(e, bank, n, mo2, w2_, w=w):
            return e.scalar_tensor_tensor(s_t[:, n * 512:n * 512 + w2_],
                                          bank[:, 0:w2_], b2c[:, n:n + 1],
                                          xs[:, n * w:n * w + w2_],
                                          ALU.add, ALU.add)

        mm_block(lk, pool,
                 get_lhsT=lambda kc, n: w2[:, kc * D + n * 128:kc * D + (n + 1) * 128],
                 get_rhs=lambda kc, mo2, w2_: h[:, kc * 512:kc * 512 + w2_],
                 nK=NI, n_list=range(NCH), m_list=[(0, w)],
                 evac=evac_s, evac_engine="vector",
                 pre_wait=lambda t, c=lk.c["ev"], stc=lk.c["st"]: (
                     lk.wait(t, "wt"), lk.wait(t, "ev", c), lk.wait(t, "st", stc)))
        emit_barrier(lk, "vector", "ev")

        def st_t(s, mo=mo, w=w):
            lk.wait(s, "ev")
            for c in range(NCH):
                i = s.dma_start(out=s_out[c * 128:(c + 1) * 128, mo:mo + w],
                                in_=s_t[:, c * 512:c * 512 + w])
                lk.inc(i, "st", 16)

        lk.on("sync", st_t)

    def fin(s):
        lk.wait(s, "st")

    lk.on("sync", fin)
    lk.end_phase()



def emit_rwkv(lk, io, pool):
    xn_in = io["xn_s"]
    wrt, wkt, wvt = io["r_wrt"], io["r_wkt"], io["r_wvt"]
    mur_in, muk_in = io["r_mur"], io["r_muk"]
    g_out = io["g0_s"]
    lk.begin_phase("rwkv")
    wb = lk.c["wt"]
    NT = TLOC + 2
    xn = lk.sb("rw_xn", [128, NCH * NT], BF16)
    wr = lk.sb("rw_wr", [128, NCH * D], BF16)
    wk = lk.sb("rw_wk", [128, NCH * D], BF16)
    wv = lk.sb("rw_wv", [128, NCH * D], BF16)
    mur = lk.sb("rw_mur", [128, NCH], F32)
    muk = lk.sb("rw_muk", [128, NCH], F32)
    dsc = lk.sb("rw_dsc", [128, NCH * 512], F32)
    xr = lk.sb("rw_xr", [128, NCH * 512], BF16)
    xk = lk.sb("rw_xk", [128, NCH * 512], BF16)
    rbf = lk.sb("rw_rbf", [128, NCH * 512], BF16)
    kabf = lk.sb("rw_kabf", [128, NCH * 512], BF16)
    ksc = lk.sb("rw_ksc", [128, 512], BF16)
    rwsc = lk.sb("rw_rwsc", [128, NCH * 512], F32)
    gbf = lk.sb("rw_gbf", [128, NCH * 512], BF16)

    def ld_fn(s):
        for c in range(NCH):
            i = s.dma_start(out=xn[:, c * NT:c * NT + (TLOC + 1)],
                            in_=xn_in[c * 128:(c + 1) * 128, :])
            lk.inc(i, "ld", 16)
        i = s.dma_start(out=mur[:], in_=mur_in)
        lk.inc(i, "ld", 16)
        i = s.dma_start(out=muk[:], in_=muk_in)
        lk.inc(i, "ld", 16)

    lk.on("sync", ld_fn)

    def wt_fn(g):
        load_weight_bf16(lk, g, wr, wrt, NCH, D)
        load_weight_bf16(lk, g, wk, wkt, NCH, D)
        load_weight_bf16(lk, g, wv, wvt, NCH, D)

    lk.on("gpsimd", wt_fn)

    for (mo, w) in MT512:
        mm_before_mix = lk.c["mm"]

        def v_mix(v, mo=mo, w=w, mmw=mm_before_mix):
            lk.wait(v, "ld")
            lk.wait(v, "mm", mmw)
            for c in range(NCH):
                cur = xn[:, c * NT + mo + 1: c * NT + mo + 1 + w]
                prv = xn[:, c * NT + mo: c * NT + mo + w]
                ds = dsc[:, c * 512:c * 512 + w]
                v.tensor_tensor(ds, cur, prv, ALU.subtract)
                v.scalar_tensor_tensor(xr[:, c * w:(c + 1) * w], ds,
                                       mur[:, c:c + 1], prv, ALU.mult, ALU.add)
                i = v.scalar_tensor_tensor(xk[:, c * w:(c + 1) * w], ds,
                                           muk[:, c:c + 1], prv, ALU.mult, ALU.add)
            lk.inc(i, "ev")

        lk.on("vector", v_mix)
        mix_ev = lk.c["ev"]

        def s_war(s, c=lk.c["ev"]):
            lk.wait(s, "ev", c)

        lk.on("scalar", s_war)

        def evac_r(e, bank, n, mo2, w2):
            return e.activation(rbf[:, n * 512:n * 512 + w2], bank[:, 0:w2], AF.Sigmoid)

        mm_block(lk, pool,
                 get_lhsT=lambda kc, n: wr[:, kc * D + n * 128: kc * D + (n + 1) * 128],
                 get_rhs=lambda kc, mo2, w2, w=w: xr[:, kc * w: kc * w + w2],
                 nK=NCH, n_list=range(NCH), m_list=[(0, w)],
                 evac=evac_r, evac_engine="scalar",
                 pre_wait=lambda t, me=mix_ev: (lk.wait(t, "wt", wb + 16 * 8),
                                                lk.wait(t, "ev", me)))

        def evac_k(e, bank, n, mo2, w2):
            e.tensor_scalar(ksc[:, 0:w2], bank[:, 0:w2], 0.0, 0.0, ALU.max, ALU.add)
            return e.tensor_tensor(kabf[:, n * 512:n * 512 + w2], ksc[:, 0:w2],
                                   bank[:, 0:w2], ALU.mult)

        mm_block(lk, pool,
                 get_lhsT=lambda kc, n: wk[:, kc * D + n * 128: kc * D + (n + 1) * 128],
                 get_rhs=lambda kc, mo2, w2, w=w: xk[:, kc * w: kc * w + w2],
                 nK=NCH, n_list=range(NCH), m_list=[(0, w)],
                 evac=evac_k, evac_engine="vector",
                 pre_wait=lambda t: lk.wait(t, "wt", wb + 16 * 16))
        emit_barrier(lk, "vector", "ev")
        kact_ev = lk.c["ev"]

        def evac_vo(e, bank, n, mo2, w2):
            return e.tensor_tensor(rwsc[:, n * 512:n * 512 + w2],
                                   rbf[:, n * 512:n * 512 + w2],
                                   bank[:, 0:w2], ALU.mult)

        mm_block(lk, pool,
                 get_lhsT=lambda kc, n: wv[:, kc * D + n * 128: kc * D + (n + 1) * 128],
                 get_rhs=lambda kc, mo2, w2, w=w: kabf[:, kc * w: kc * w + w2],
                 nK=NCH, n_list=range(NCH), m_list=[(0, w)],
                 evac=evac_vo, evac_engine="vector",
                 pre_wait=lambda t, ke=kact_ev: (lk.wait(t, "wt", wb + 16 * 24),
                                                 lk.wait(t, "ev", ke)))
        emit_barrier(lk, "vector", "ev")
        vo_ev = lk.c["ev"]

        def s_silu(s, mo=mo, w=w, ve=vo_ev):
            lk.wait(s, "ev", ve)
            lk.wait(s, "st")
            for c in range(NCH):
                i = s.activation(gbf[:, c * 512:c * 512 + w],
                                 rwsc[:, c * 512:c * 512 + w], AF.Silu)
            lk.inc(i, "sc")

        lk.on("scalar", s_silu)

        def st_fn(s, mo=mo, w=w, sc=lk.c["sc"]):
            lk.wait(s, "sc", sc)
            for c in range(NCH):
                i = s.dma_start(out=g_out[c * 128:(c + 1) * 128, mo:mo + w],
                                in_=gbf[:, c * 512:c * 512 + w])
                lk.inc(i, "st", 16)

        lk.on("sync", st_fn)

    def fin(s):
        lk.wait(s, "st")

    lk.on("sync", fin)
    lk.end_phase()



def emit_combine(lk, io, x_src, s_src, g_src, out_ap, out_off=0, halo_dst=None,
                 tag="cb1"):
    lk.begin_phase("comb" + tag)
    xt = lk.sb(f"{tag}_xt", [128, NCH * 512], F32)
    st_ = lk.sb(f"{tag}_st", [128, NCH * 512], BF16)
    gt = lk.sb(f"{tag}_gt", [128, NCH * 512], BF16)
    tf = lk.sb(f"{tag}_tf", [128, 512], F32)
    x1 = lk.sb(f"{tag}_x1", [128, NCH * 512], F32)

    for (mo, w) in MT512:
        def ld_fn(s, mo=mo, w=w):
            lk.wait(s, "ev")
            for c in range(NCH):
                i = s.dma_start(out=xt[:, c * w:(c + 1) * w],
                                in_=x_src(c, mo, w))
                lk.inc(i, "ld", 16)
                i = s.dma_start(out=st_[:, c * w:(c + 1) * w],
                                in_=s_src[c * 128:(c + 1) * 128, mo:mo + w])
                lk.inc(i, "ld", 16)
                i = s.dma_start(out=gt[:, c * w:(c + 1) * w],
                                in_=g_src[c * 128:(c + 1) * 128, mo:mo + w])
                lk.inc(i, "ld", 16)

        lk.on("sync", ld_fn)

        def v_fn(v, mo=mo, w=w, stc=lk.c["st"]):
            lk.wait(v, "ld")
            lk.wait(v, "st", stc)
            for c in range(NCH):
                v.tensor_tensor(tf[:, 0:w], st_[:, c * w:(c + 1) * w],
                                gt[:, c * w:(c + 1) * w], ALU.mult)
                v.tensor_tensor(x1[:, c * w:(c + 1) * w], tf[:, 0:w],
                                xt[:, c * w:(c + 1) * w], ALU.add)
            lk.mark(v, "ev")

        lk.on("vector", v_fn)

        def st_fn(s, mo=mo, w=w):
            lk.wait(s, "ev")
            for c in range(NCH):
                i = s.dma_start(out=out_ap[c * 128:(c + 1) * 128,
                                           out_off + mo:out_off + mo + w],
                                in_=x1[:, c * w:(c + 1) * w])
                lk.inc(i, "st", 16)
            if halo_dst is not None and mo + w == TLOC:
                for c in range(NCH):
                    i = s.dma_start(out=halo_dst[c * 128:(c + 1) * 128, 0:4],
                                    in_=x1[:, c * w + w - 4:(c + 1) * w])
                    lk.inc(i, "st", 16)

        lk.on("sync", st_fn)

    def fin(s):
        lk.wait(s, "st")

    lk.on("sync", fin)
    lk.end_phase()



def emit_halo_fix(lk, io):
    halo_o, m1_in, x1_s = io["halo_o"], io["m1c"], io["x1_s"]
    lk.begin_phase("halo")
    hsb = lk.sb("hx_hsb", [128, NCH * 4], F32)
    m1 = lk.sb("hx_m1", [128, 1], F32)

    def ld0(s):
        for c in range(NCH):
            i = s.dma_start(out=hsb[:, c * 4:(c + 1) * 4],
                            in_=halo_o[c * 128:(c + 1) * 128, 0:4])
            lk.inc(i, "ld", 16)
        i = s.dma_start(out=m1[:], in_=m1_in)
        lk.inc(i, "ld", 16)

    lk.on("sync", ld0)

    def v0(v):
        lk.wait(v, "ld")
        v.tensor_scalar(hsb[:], hsb[:], m1[:, 0:1], 0.0, ALU.mult, ALU.add)
        lk.mark(v, "ev")

    lk.on("vector", v0)

    def st0(s):
        lk.wait(s, "ev")
        for c in range(NCH):
            i = s.dma_start(out=x1_s[c * 128:(c + 1) * 128, 0:3],
                            in_=hsb[:, c * 4 + 1:(c + 1) * 4])
            lk.inc(i, "st", 16)
        s.wait_ge(lk.sems["st"], lk.c["st"])

    lk.on("sync", st0)
    lk.end_phase()



def emit_l1front(lk, io, pool):
    EXT = TLOC + 3
    x1_in = io["x1_s"]
    win_in, cw_in, cb_in = io["l_winT"], io["l_convw"], io["l_convb"]
    xp_in, wdt_in, bdt_in = io["l_xprojT"], io["l_wdtT"], io["l_bdt"]
    xn1_out, u_out, dt_out, z_out = io["xn1_s"], io["u_s"], io["dt_s"], io["z_s"]
    B_out, C_out = io["Bs_s"], io["Cs_s"]
    lk.begin_phase("l1f")
    bnt = lk.c["nt"]
    bnf = lk.c["nf"]
    NTE = EXT + 1
    xt = lk.sb("lf_xt", [128, NCH * 512], F32)
    xn1 = lk.sb("lf_xn1", [128, NCH * NTE], BF16)
    sq_full = lk.sb("lf_sq", [128, NCH * 512], F32)
    nring = lk.sb("lf_nring", [128, 2 * NCH * 128], BF16)
    xi = lk.sb("lf_xi", [128, NDC * NTE], BF16)
    u_t = lk.sb("lf_ut", [128, NDC * 512], BF16)
    z_t = lk.sb("lf_zt", [128, NDC * 512], BF16)
    dt_t = lk.sb("lf_dtt", [128, NDC * 512], BF16)
    cw = lk.sb("lf_cw", [128, NDC * 4], F32)
    cb = lk.sb("lf_cb", [128, NDC], F32)
    bdt = lk.sb("lf_bdt", [128, NDC], F32)
    xp = lk.sb("lf_xp", [128, NDC * 96], BF16)
    wdt = lk.sb("lf_wdt", [DTR, DN], BF16)
    dtr_bf = lk.sb("lf_dtrbf", [DTR, 512], BF16)
    bc_sb = lk.sb("lf_bc", [DS, 2 * 512], F32)
    asc = lk.sb("lf_asc", [128, 512], F32)
    csc = lk.sb("lf_csc", [128, 512], F32)
    esc = lk.sb("lf_esc", [128, 512], F32)
    cst = lk.sb("lf_cstf", [128, 2], F32)
    kit = LNScratch(lk, 512, name="lf_ln")
    p_dtr = lk.ps("lf_pdtr", [DTR, 512], F32)
    p_bc = lk.ps("lf_pbc", [128, 512], F32)
    kit.init_consts()

    def g0(g):
        g.memset(cst[:, 0:1], -1.0)
        g.memset(cst[:, 1:2], 1.0)
        load_weight_bf16(lk, g, xp, xp_in, NDC, 96)
        i = g.dma_start(out=wdt[:], in_=wdt_in)
        lk.inc(i, "wt", 16)

    lk.on("gpsimd", g0)

    def ld0(s):
        for t, srcp in [(cw, cw_in), (cb, cb_in), (bdt, bdt_in)]:
            i = s.dma_start(out=t[:], in_=srcp)
            lk.inc(i, "ld", 16)

    lk.on("sync", ld0)

    widths = [259] + _tile_widths(EXT - 259, 512)
    t0 = 0
    for w in widths:
        o = t0

        def ld_fn(s, o=o, w=w):
            lk.wait(s, "ev")
            for c in range(NCH):
                i = s.dma_start(out=xt[:, c * w:(c + 1) * w],
                                in_=x1_in[c * 128:(c + 1) * 128, o:o + w])
                lk.inc(i, "ld", 16)

        lk.on("sync", ld_fn)

        def get_x(c, w=w):
            return xt[:, c * w:(c + 1) * w]

        def put_out(c, v, o=o, w=w):
            tmp = sq_full[:, c * w:(c + 1) * w]
            v.tensor_tensor(tmp, get_x(c), kit.p_rep[:, 0:w], ALU.subtract)
            v.tensor_tensor(xn1[:, c * NTE + o:c * NTE + o + w], tmp,
                            kit.p_rep[:, 512:512 + w], ALU.mult)

        emit_ln(lk, kit, get_x, put_out, w, sq_full, out_war_sems=("st",))

        def st_fn(s, o=o, w=w):
            if o + w > 3:
                lk.wait(s, "ev")
                so = max(o, 3)
                for c in range(NCH):
                    i = s.dma_start(out=xn1_out[c * 128:(c + 1) * 128, so - 3:o + w - 3],
                                    in_=xn1[:, c * NTE + so:c * NTE + o + w])
                    lk.inc(i, "st", 16)

        lk.on("sync", st_fn)
        t0 += w
    xn1_ev = lk.c["ev"]

    n_loads = [0]

    def stream_n(n):
        def g_n(g, n=n):
            uses = n_loads[0]
            if uses >= 2:
                g.wait_ge(lk.sems["nf"], bnf + uses - 1)
            for kc in range(NCH):
                i = g.dma_start(out=nring[:, (n % 2) * (NCH * 128) + kc * 128:
                                          (n % 2) * (NCH * 128) + (kc + 1) * 128],
                                in_=win_in[kc * 128:(kc + 1) * 128, n * 128:(n + 1) * 128])
                lk.inc(i, "nt", 16)
            n_loads[0] += 1

        lk.on("gpsimd", g_n)

    XI_M = [(0, 259), (259, 512), (771, 512), (1283, 512), (1795, 256)]
    Z_M = [(3, 512), (515, 512), (1027, 512), (1539, 512)]

    for n in range(32):
        stream_n(n)
        is_xi = n < 16
        mlist = XI_M if is_xi else Z_M

        def t_xz(t, n=n, mlist=mlist, nl=n_loads[0], ev_need=xn1_ev):
            lk.wait(t, "ev", ev_need)
            t.wait_ge(lk.sems["nt"], bnt + 16 * 8 * nl)
            for gi_, (mo, w) in enumerate(mlist):
                ib, bank = pool.acquire(t)
                for kc in range(NCH):
                    i = lk.nc.tensor.matmul(
                        bank[:, 0:w],
                        nring[:, (n % 2) * (NCH * 128) + kc * 128:(n % 2) * (NCH * 128) + (kc + 1) * 128],
                        xn1[:, kc * NTE + mo:kc * NTE + mo + w],
                        start=(kc == 0), stop=(kc == NCH - 1))
                lk.inc(i, "mm")
                t_xz.banks.append((ib, bank, mo, w, lk.c["mm"]))

        t_xz.banks = []
        lk.on("tensor", t_xz)

        def v_xz(v, n=n, is_xi=is_xi, banks=t_xz.banks, stc=lk.c["st"]):
            if not is_xi:
                lk.wait(v, "st", stc)
            for (ib, bank, mo, w, mmc) in banks:
                lk.wait(v, "mm", mmc)
                if is_xi:
                    i = v.tensor_copy(xi[:, n * NTE + mo:n * NTE + mo + w], bank[:, 0:w])
                else:
                    i = v.tensor_copy(z_t[:, mo - 3:mo - 3 + w], bank[:, 0:w])
                pool.freed(i, ib)
            i = v.tensor_copy(lk.dummy[:1, 0:8], lk.dummy[:1, 8:16])
            i.then_inc(lk.sems["nf"], 1)
            lk.c["nf"] += 1
            lk.mark(v, "ev")

        lk.on("vector", v_xz)

        if not is_xi:
            def st_z(s, n=n):
                lk.wait(s, "ev")
                i = s.dma_start(out=z_out[(n - 16) * 128:(n - 15) * 128, :],
                                in_=z_t[:, 0:TLOC])
                lk.inc(i, "st", 16)

            lk.on("sync", st_z)

    xi_ev = lk.c["ev"]

    for mt, (mo, w) in enumerate(MT512):
        for ci in range(NDC):
            def v_conv(v, ci=ci, mo=mo, w=w, xe=xi_ev, stc=lk.c["st"], scc=lk.c["sc"]):
                lk.wait(v, "ev", xe)
                lk.wait(v, "st", stc)
                lk.wait(v, "sc", scc)
                base = ci * NTE + mo
                v.tensor_scalar(csc[:, 0:w], xi[:, base:base + w],
                                cw[:, ci * 4:ci * 4 + 1], 0.0, ALU.mult, ALU.add)
                v.scalar_tensor_tensor(esc[:, 0:w], xi[:, base + 1:base + 1 + w],
                                       cw[:, ci * 4 + 1:ci * 4 + 2], csc[:, 0:w],
                                       ALU.mult, ALU.add)
                v.scalar_tensor_tensor(csc[:, 0:w], xi[:, base + 2:base + 2 + w],
                                       cw[:, ci * 4 + 2:ci * 4 + 3], esc[:, 0:w],
                                       ALU.mult, ALU.add)
                i = v.scalar_tensor_tensor(esc[:, 0:w], xi[:, base + 3:base + 3 + w],
                                           cw[:, ci * 4 + 3:ci * 4 + 4], csc[:, 0:w],
                                           ALU.mult, ALU.add)
                lk.inc(i, "ev")

            lk.on("vector", v_conv)

            def s_u(s, ci=ci, w=w):
                lk.wait(s, "ev")
                i = s.activation(u_t[:, ci * 512:ci * 512 + w], esc[:, 0:w],
                                 AF.Silu, bias=cb[:, ci:ci + 1])
                lk.inc(i, "sc")

            lk.on("scalar", s_u)

        u_sc = lk.c["sc"]

        def t_xp(t, mo=mo, w=w, us=u_sc):
            lk.wait(t, "sc", us)
            lk.wait(t, "ev")
            for kc in range(NDC):
                lk.nc.tensor.matmul(p_dtr[:, 0:w], xp[:, kc * 96:kc * 96 + 64],
                                    u_t[:, kc * 512:kc * 512 + w],
                                    start=(kc == 0), stop=(kc == NDC - 1))
            for kc in range(NDC):
                lk.nc.tensor.matmul(p_bc[0:16, 0:w], xp[:, kc * 96 + 64:kc * 96 + 80],
                                    u_t[:, kc * 512:kc * 512 + w],
                                    start=(kc == 0), stop=(kc == NDC - 1))
            for kc in range(NDC):
                i = lk.nc.tensor.matmul(p_bc[64:80, 0:w], xp[:, kc * 96 + 80:kc * 96 + 96],
                                        u_t[:, kc * 512:kc * 512 + w],
                                        start=(kc == 0), stop=(kc == NDC - 1))
            lk.inc(i, "mm")

        lk.on("tensor", t_xp)

        def v_xp(v, mo=mo, w=w):
            lk.wait(v, "mm")
            v.tensor_copy(dtr_bf[:, 0:w], p_dtr[:, 0:w])
            v.tensor_copy(bc_sb[:, 0:w], p_bc[0:16, 0:w])
            i = v.tensor_copy(bc_sb[:, 512:512 + w], p_bc[64:80, 0:w])
            lk.mark(v, "ev")

        lk.on("vector", v_xp)

        def st_bc(s, mo=mo, w=w):
            lk.wait(s, "ev")
            i = s.dma_start(out=B_out[:, mo:mo + w], in_=bc_sb[:, 0:w])
            lk.inc(i, "st", 16)
            i = s.dma_start(out=C_out[:, mo:mo + w], in_=bc_sb[:, 512:512 + w])
            lk.inc(i, "st", 16)
            for ci in range(NDC):
                i = s.dma_start(out=u_out[ci * 128:(ci + 1) * 128, mo:mo + w],
                                in_=u_t[:, ci * 512:ci * 512 + w])
                lk.inc(i, "st", 16)

        lk.on("sync", st_bc)

        for ci in range(NDC):
            def t_dt(t, ci=ci, w=w, ec=lk.c["ev"]):
                lk.wait(t, "ev", ec)
                ib, bank = pool.acquire(t)
                i = lk.nc.tensor.matmul(bank[:, 0:w], wdt[:, ci * 128:(ci + 1) * 128],
                                        dtr_bf[:, 0:w], start=True, stop=True)
                lk.inc(i, "mm")
                t_dt.bk = (ib, bank)

            lk.on("tensor", t_dt)
            ib, bank = t_dt.bk

            def v_dt1(v, ci=ci, w=w, ib=ib, bank=bank, scc=lk.c["sc"]):
                lk.wait(v, "mm")
                lk.wait(v, "sc", scc)
                i = v.tensor_scalar(asc[:, 0:w], bank[:, 0:w], bdt[:, ci:ci + 1], 0.0,
                                    ALU.add, ALU.add)
                pool.freed(i, ib)
                lk.mark(v, "ev")

            lk.on("vector", v_dt1)

            def s_dt(s, ci=ci, w=w):
                lk.wait(s, "ev")
                s.activation(csc[:, 0:w], asc[:, 0:w], AF.Abs)
                s.activation(csc[:, 0:w], csc[:, 0:w], AF.Exp, scale=cst[:, 0:1])
                i = s.activation(csc[:, 0:w], csc[:, 0:w], AF.Ln, bias=cst[:, 1:2])
                lk.inc(i, "sc")

            lk.on("scalar", s_dt)

            def v_dt2(v, ci=ci, w=w):
                lk.wait(v, "sc")
                v.tensor_scalar(esc[:, 0:w], asc[:, 0:w], 0.0, 0.0, ALU.max, ALU.add)
                i = v.tensor_tensor(dt_t[:, ci * 512:ci * 512 + w], esc[:, 0:w],
                                    csc[:, 0:w], ALU.add)
                lk.inc(i, "ev")

            lk.on("vector", v_dt2)

        dt_ev = lk.c["ev"]

        def st_dt(s, mo=mo, w=w, de=dt_ev):
            lk.wait(s, "ev", de)
            for ci in range(NDC):
                i = s.dma_start(out=dt_out[ci * 128:(ci + 1) * 128, mo:mo + w],
                                in_=dt_t[:, ci * 512:ci * 512 + w])
                lk.inc(i, "st", 16)

        lk.on("sync", st_dt)

    def fin(s):
        lk.wait(s, "st")

    lk.on("sync", fin)
    lk.end_phase()



def emit_scan(lk, io, pool):
    NCB = 8
    NSEG = 8
    SG = T // NSEG
    dtG, uG, BsG, CsG = io["dtG"], io["uG"], io["BsG"], io["CsG"]
    al_in, m0_in, m1_in = io["s_alog"], io["m0c"], io["m1c"]
    y_out = io["y_s"]
    lk.begin_phase("scan")
    bld = lk.c["ld"]
    bgp = lk.c["gp"]
    dt_s = lk.sb("sn_dt", [128, NCB * SG], BF16)
    u_s = lk.sb("sn_u", [128, NCB * SG], BF16)
    dtA = lk.sb("sn_dtA", [128, 4 * SG], BF16)
    dtB = lk.sb("sn_dtB", [128, 4 * SG], BF16)
    uA = lk.sb("sn_uA", [128, 4 * SG], BF16)
    uB = lk.sb("sn_uB", [128, 4 * SG], BF16)
    dtu = lk.sb("sn_dtu", [128, NCB * SG], F32)
    brow = lk.sb("sn_brow", [1, DS * 512], F32)
    crow = lk.sb("sn_crow", [1, DS * 512], F32)
    asb = lk.sb("sn_asb", [128, NCB * DS], F32)
    m0 = lk.sb("sn_m0", [128, 1], F32)
    m1 = lk.sb("sn_m1", [128, 1], F32)
    abuf = lk.sb("sn_abuf", [128, 2 * NCB * SG], F32)
    bbuf = lk.sb("sn_bbuf", [128, NCB * SG], F32)
    hbuf = lk.sb("sn_hbuf", [128, NCB * SG], F32)
    tmp = bbuf
    brep = lk.sb("sn_brep", [128, SG], F32)
    crep = lk.sb("sn_crep", [128, SG], F32)
    hst = lk.sb("sn_hst", [128, NCB * DS * 2], F32)
    yacc = lk.sb("sn_yacc", [128, NCB * SG], F32)
    ybf = lk.sb("sn_ybf", [128, NCB * SG], BF16)
    onesr = lk.sb("sn_onesr", [1, 128], F32)

    def g0(g):
        g.memset(onesr[:], 1.0)
        i = g.memset(hst[:], 0.0)
        lk.inc(i, "gp")

    lk.on("gpsimd", g0)

    def ld0(s):
        i = s.dma_start(out=asb[:], in_=al_in)
        lk.inc(i, "ld", 16)
        i = s.dma_start(out=m0[:], in_=m0_in)
        lk.inc(i, "ld", 16)
        i = s.dma_start(out=m1[:], in_=m1_in)
        lk.inc(i, "ld", 16)

    lk.on("sync", ld0)

    def s_a(s):
        lk.wait(s, "ld", bld + 16)
        i = s.activation(asb[:], asb[:], AF.Exp)
        lk.inc(i, "sc")

    lk.on("scalar", s_a)

    def v_a(v):
        lk.wait(v, "sc")
        i = v.tensor_scalar(asb[:], asb[:], -1.0, 0.0, ALU.mult, ALU.add)
        lk.mark(v, "ev")

    lk.on("vector", v_a)
    a_ev = lk.c["ev"]

    it = 0
    ev_after_iter = []
    SGA = NCB * SG

    for seg in range(NSEG):
        so = seg * SG
        blk_ = so // TLOC
        co = so % TLOC

        for wv in range(2):
            def ld_wave(s, blk_=blk_, co=co, wv=wv, evn=lk.c["ev"], bln=lk.c["bl"]):
                lk.wait(s, "ev", evn)
                lk.wait(s, "bl", bln)
                for ci in range(4):
                    cb = wv * 4 + ci
                    for dst, src, off in ((dtA, dtG, 0), (uA, uG, 0),
                                          (dtB, dtG, 1024), (uB, uG, 1024)):
                        i = s.dma_start(out=dst[:, ci * SG:(ci + 1) * SG],
                                        in_=src[blk_ * DN + off + cb * 128:
                                                blk_ * DN + off + (cb + 1) * 128,
                                                co:co + SG])
                        lk.inc(i, "ld", 16)
                if wv == 0:
                    for si in range(DS):
                        i = s.dma_start(out=brow[:1, si * SG:(si + 1) * SG],
                                        in_=BsG[blk_ * DS + si:blk_ * DS + si + 1,
                                                co:co + SG])
                        lk.inc(i, "ld", 16)
                        i = s.dma_start(out=crow[:1, si * SG:(si + 1) * SG],
                                        in_=CsG[blk_ * DS + si:blk_ * DS + si + 1,
                                                co:co + SG])
                        lk.inc(i, "ld", 16)

            lk.on("sync", ld_wave)

            def g_blend(g, wv=wv, evn=lk.c["ev"]):
                lk.wait(g, "ld")
                lk.wait(g, "ev", evn)
                o = wv * 4 * SG
                W4 = 4 * SG
                g.tensor_scalar(dt_s[:, o:o + W4], dtA[:], m0[:, 0:1], 0.0,
                                ALU.mult, ALU.add)
                g.scalar_tensor_tensor(dt_s[:, o:o + W4], dtB[:], m1[:, 0:1],
                                       dt_s[:, o:o + W4], ALU.mult, ALU.add)
                g.tensor_scalar(u_s[:, o:o + W4], uA[:], m0[:, 0:1], 0.0,
                                ALU.mult, ALU.add)
                g.scalar_tensor_tensor(u_s[:, o:o + W4], uB[:], m1[:, 0:1],
                                       u_s[:, o:o + W4], ALU.mult, ALU.add)
                lk.mark(g, "bl")

            lk.on("vector", g_blend)
        bl_done = lk.c["bl"]

        def v_dtu(v, stc=lk.c["st"], bld_=bl_done):
            lk.wait(v, "bl", bld_)
            lk.wait(v, "gp", bgp + 1)
            lk.wait(v, "st", stc)
            i = v.tensor_tensor(dtu[:], dt_s[:], u_s[:], ALU.mult)
            lk.mark(v, "ev")

        lk.on("vector", v_dtu)

        dtu3 = dtu.ap().rearrange("p (c t) -> p c t", c=NCB)
        bb3 = bbuf.ap().rearrange("p (c t) -> p c t", c=NCB)
        hb3 = hbuf.ap().rearrange("p (c t) -> p c t", c=NCB)
        ya3 = yacc.ap().rearrange("p (c t) -> p c t", c=NCB)
        tm3 = tmp.ap().rearrange("p (c t) -> p c t", c=NCB)
        hst4 = hst.ap().rearrange("p (c s u) -> p c s u", c=NCB, u=2)

        for si in range(DS):
            def t_rep(t, si=si):
                lk.wait(t, "ld")
                lk.wait(t, "gp", bgp + 1)
                ib0, b0 = pool.acquire(t)
                i = lk.nc.tensor.matmul(b0[:, 0:SG], onesr[:],
                                        brow[:1, si * SG:(si + 1) * SG],
                                        start=True, stop=True)
                lk.inc(i, "mm")
                ib1, b1 = pool.acquire(t)
                i = lk.nc.tensor.matmul(b1[:, 0:SG], onesr[:],
                                        crow[:1, si * SG:(si + 1) * SG],
                                        start=True, stop=True)
                lk.inc(i, "mm")
                t_rep.banks = [(ib0, b0), (ib1, b1)]

            lk.on("tensor", t_rep)
            (ib0, b0), (ib1, b1) = t_rep.banks

            def v_rep(v, ib0=ib0, b0=b0, ib1=ib1, b1=b1):
                lk.wait(v, "mm")
                i = v.tensor_copy(brep[:], b0[:, 0:SG])
                pool.freed(i, ib0)
                i = v.tensor_copy(crep[:], b1[:, 0:SG])
                pool.freed(i, ib1)

            lk.on("vector", v_rep)

            slot = it % 2

            def s_agen(s, si=si, slot=slot, it=it, bld_=bl_done):
                if it >= 2:
                    lk.wait(s, "ev", ev_after_iter[it - 2])
                else:
                    lk.wait(s, "ev", a_ev)
                lk.wait(s, "bl", bld_)
                for cb in range(NCB):
                    i = s.activation(abuf[:, slot * SGA + cb * SG:slot * SGA + (cb + 1) * SG],
                                     dt_s[:, cb * SG:(cb + 1) * SG], AF.Exp,
                                     scale=asb[:, cb * DS + si:cb * DS + si + 1])
                lk.inc(i, "sc")

            lk.on("scalar", s_agen)

            def v_scan(v, si=si, slot=slot, scc=lk.c["sc"]):
                lk.wait(v, "sc", scc)
                lk.wait(v, "gp", bgp + 1)
                brep3 = brep.ap().unsqueeze(1).broadcast_to([128, NCB, SG])
                crep3 = crep.ap().unsqueeze(1).broadcast_to([128, NCB, SG])
                v.tensor_tensor(bb3, dtu3, brep3, ALU.mult)
                for cb in range(NCB):
                    v.tensor_tensor_scan(hbuf[:, cb * SG:(cb + 1) * SG],
                                         abuf[:, slot * SGA + cb * SG:slot * SGA + (cb + 1) * SG],
                                         bbuf[:, cb * SG:(cb + 1) * SG],
                                         hst[:, 2 * (cb * DS + si) + 1:2 * (cb * DS + si) + 2],
                                         ALU.mult, ALU.add)
                if si == 0:
                    v.tensor_tensor(ya3, hb3, crep3, ALU.mult)
                else:
                    v.tensor_tensor(tm3, hb3, crep3, ALU.mult)
                    v.tensor_tensor(yacc[:], yacc[:], tmp[:], ALU.add)
                i = v.tensor_copy(hst4[:, :, si, :], hb3[:, :, SG - 2:SG])
                lk.inc(i, "ev")

            lk.on("vector", v_scan)
            ev_after_iter.append(lk.c["ev"])
            it += 1

        def v_y(v, stc=lk.c["st"]):
            lk.wait(v, "st", stc)
            for cb in range(NCB):
                i = v.tensor_copy(ybf[:, cb * SG:(cb + 1) * SG],
                                  yacc[:, cb * SG:(cb + 1) * SG])
            lk.mark(v, "ev")

        lk.on("vector", v_y)

        def st_y(s, so=so):
            lk.wait(s, "ev")
            for cb in range(NCB):
                i = s.dma_start(out=y_out[cb * 128:(cb + 1) * 128, so:so + SG],
                                in_=ybf[:, cb * SG:(cb + 1) * SG])
                lk.inc(i, "st", 16)

        lk.on("sync", st_y)

    def fin(s):
        lk.wait(s, "st")

    lk.on("sync", fin)
    lk.end_phase()



def emit_gate_s1(lk, io, pool):
    yG = io["yG"]
    u_in, z_in, xn1_in = io["u_s"], io["z_s"], io["xn1_s"]
    dsk_in, wout_in = io["g_dsk"], io["g_woutT"]
    gw_in, gb_in, id_in = io["g_gwp"], io["g_gb"], io["ident"]
    m0_in, m1_in = io["m0c"], io["m1c"]
    s1_out = io["s1_s"]
    oh1_out, oh2_out = io["oh1_s"], io["oh2_s"]
    v1_out, v2_out = io["v1_s"], io["v2_s"]
    lk.begin_phase("gate")
    lb = lk.c["ld"]
    wb = lk.c["wt"]
    bgp = lk.c["gp"]
    yA = lk.sb("gs_yA", [128, NDC * 512], BF16)
    yB = lk.sb("gs_yB", [128, NDC * 512], BF16)
    yt = lk.sb("gs_yt", [128, NDC * 512], BF16)
    ut = lk.sb("gs_ut", [128, NDC * 512], BF16)
    zt = lk.sb("gs_zt", [128, NDC * 512], BF16)
    zs = lk.sb("gs_zs", [128, 512], BF16)
    ydt = lk.sb("gs_ydt", [128, 512], F32)
    yg = lk.sb("gs_yg", [128, NDC * 512], BF16)
    wout = lk.sb("gs_wout", [128, NDC * D], BF16)
    dsk = lk.sb("gs_dsk", [128, NDC], F32)
    s1t = lk.sb("gs_s1t", [128, NCH * 512], BF16)
    xn1 = lk.sb("gs_xn1", [128, NCH * TLOC], BF16)
    gwsb = lk.sb("gs_gwp", [128, NCH * E], BF16)
    gbrep = lk.sb("gs_gbrep", [128, E], F32)
    gbrow = lk.sb("gs_gbrow", [1, E], F32)
    ident = lk.sb("gs_ident", [128, 128], F32)
    onesr = lk.sb("gs_onesr", [1, 128], F32)
    m0 = lk.sb("gs_m0", [128, 1], F32)
    m1 = lk.sb("gs_m1", [128, 1], F32)
    gla = lk.sb("gs_gla", [128, 16 * E], F32)
    oh1a = lk.sb("gs_oh1a", [128, 16 * E], F32)
    oh2a = lk.sb("gs_oh2a", [128, 16 * E], F32)
    mska = lk.sb("gs_mska", [128, 16 * E], F32)
    glm16 = lk.sb("gs_glm16", [128, 16], F32)
    va = lk.sb("gs_va", [128, 32], F32)
    ohT1 = lk.sb("gs_ohT1", [8, TLOC], F32)
    ohT2 = lk.sb("gs_ohT2", [8, TLOC], F32)
    vT = lk.sb("gs_vT", [2, TLOC], F32)
    p_g = lk.ps("gs_pg", [128, 8], F32)
    p_t1 = lk.ps("gs_pt1", [8, 128], F32)
    p_t2 = lk.ps("gs_pt2", [8, 128], F32)
    p_t3 = lk.ps("gs_pt3", [2, 128], F32)
    p_gb = lk.ps("gs_pgb", [128, E], F32)

    def g0(g):
        i = g.memset(onesr[:], 1.0)
        lk.inc(i, "gp")
        load_weight_bf16(lk, g, wout, wout_in, NDC, D)
        i = g.dma_start(out=gwsb[:], in_=gw_in)
        lk.inc(i, "wt", 16)

    lk.on("gpsimd", g0)

    def ld0(s):
        for t, srcp in [(dsk, dsk_in), (gbrow, gb_in), (ident, id_in)]:
            i = s.dma_start(out=t[:], in_=srcp)
            lk.inc(i, "ld", 16)
        i = s.dma_start(out=m0[:], in_=m0_in)
        lk.inc(i, "ld", 16)
        i = s.dma_start(out=m1[:], in_=m1_in)
        lk.inc(i, "ld", 16)
        for c in range(NCH):
            i = s.dma_start(out=xn1[:, c * TLOC:(c + 1) * TLOC],
                            in_=xn1_in[c * 128:(c + 1) * 128, :])
            lk.inc(i, "ld", 16)

    lk.on("sync", ld0)

    def t_gb(t):
        lk.wait(t, "ld", lb + 48)
        lk.wait(t, "gp", bgp + 1)
        i = lk.nc.tensor.matmul(p_gb[:, 0:E], onesr[:], gbrow[:1, 0:E],
                                start=True, stop=True)
        lk.inc(i, "mm")

    lk.on("tensor", t_gb)

    def v_gb(v):
        lk.wait(v, "mm")
        i = v.tensor_copy(gbrep[:], p_gb[:, 0:E])
        lk.mark(v, "ev")

    lk.on("vector", v_gb)

    for (mo, w) in MT512:
        def ld_t(s, mo=mo, w=w, evc=lk.c["ev"], bln=lk.c["bl"]):
            lk.wait(s, "ev", evc)
            lk.wait(s, "bl", bln)
            for ci in range(NDC):
                i = s.dma_start(out=yA[:, ci * 512:ci * 512 + w],
                                in_=yG[ci * 128:(ci + 1) * 128, mo:mo + w])
                lk.inc(i, "ld", 16)
                i = s.dma_start(out=yB[:, ci * 512:ci * 512 + w],
                                in_=yG[ci * 128:(ci + 1) * 128, TLOC + mo:TLOC + mo + w])
                lk.inc(i, "ld", 16)
                i = s.dma_start(out=ut[:, ci * 512:ci * 512 + w],
                                in_=u_in[ci * 128:(ci + 1) * 128, mo:mo + w])
                lk.inc(i, "ld", 16)
                i = s.dma_start(out=zt[:, ci * 512:ci * 512 + w],
                                in_=z_in[ci * 128:(ci + 1) * 128, mo:mo + w])
                lk.inc(i, "ld", 16)

        lk.on("sync", ld_t)

        def g_blend(g, w=w, evc=lk.c["ev"]):
            lk.wait(g, "ld")
            lk.wait(g, "ev", evc)
            for ci in range(NDC):
                sl = slice(ci * 512, ci * 512 + w)
                g.tensor_scalar(yt[:, sl], yA[:, sl], m0[:, 0:1], 0.0, ALU.mult, ALU.add)
                g.scalar_tensor_tensor(yt[:, sl], yB[:, sl], m1[:, 0:1], yt[:, sl],
                                       ALU.mult, ALU.add)
            lk.mark(g, "bl")

        lk.on("vector", g_blend)
        bl_done = lk.c["bl"]

        for ci in range(NDC):
            def v_yd(v, ci=ci, w=w, scc=lk.c["sc"], bld_=bl_done):
                lk.wait(v, "ld")
                lk.wait(v, "bl", bld_)
                lk.wait(v, "sc", scc)
                i = v.scalar_tensor_tensor(ydt[:, 0:w], ut[:, ci * 512:ci * 512 + w],
                                           dsk[:, ci:ci + 1], yt[:, ci * 512:ci * 512 + w],
                                           ALU.mult, ALU.add)
                lk.inc(i, "ev")

            lk.on("vector", v_yd)

            def s_z(s, ci=ci, w=w):
                lk.wait(s, "ev")
                i = s.activation(zs[:, 0:w], zt[:, ci * 512:ci * 512 + w], AF.Silu)
                lk.inc(i, "sc")

            lk.on("scalar", s_z)

            def v_yg(v, ci=ci, w=w):
                lk.wait(v, "sc")
                i = v.tensor_tensor(yg[:, ci * 512:ci * 512 + w], ydt[:, 0:w],
                                    zs[:, 0:w], ALU.mult)
                lk.inc(i, "ev")

            lk.on("vector", v_yg)

        def evac_s1(e, bank, n, mo2, w2):
            return e.tensor_copy(s1t[:, n * 512:n * 512 + w2], bank[:, 0:w2])

        mm_block(lk, pool,
                 get_lhsT=lambda kc, n: wout[:, kc * D + n * 128:kc * D + (n + 1) * 128],
                 get_rhs=lambda kc, mo2, w2: yg[:, kc * 512:kc * 512 + w2],
                 nK=NDC, n_list=range(NCH), m_list=[(0, w)],
                 evac=evac_s1, evac_engine="vector",
                 pre_wait=lambda t, c=lk.c["ev"]: (lk.wait(t, "wt", wb + 16 * 16),
                                                   lk.wait(t, "ev", c)))
        emit_barrier(lk, "vector", "ev")

        def st_s1(s, mo=mo, w=w):
            lk.wait(s, "ev")
            for c in range(NCH):
                i = s.dma_start(out=s1_out[c * 128:(c + 1) * 128, mo:mo + w],
                                in_=s1t[:, c * 512:c * 512 + w])
                lk.inc(i, "st", 16)

        lk.on("sync", st_s1)

    for lc in range(16):
        lo = lc * 128

        def t_gl(t, lo=lo, evc=lk.c["ev"]):
            lk.wait(t, "ev", evc)
            lk.wait(t, "wt", wb + 16 * 17)
            for kc in range(NCH):
                i = lk.nc.tensor.matmul(p_g[:, 0:E],
                                        xn1[:, kc * TLOC + lo:kc * TLOC + lo + 128],
                                        gwsb[:, kc * E:(kc + 1) * E],
                                        start=(kc == 0), stop=(kc == NCH - 1))
            lk.inc(i, "mm")

        lk.on("tensor", t_gl)

        def v_ev(v, lc=lc):
            lk.wait(v, "mm")
            i = v.tensor_tensor(gla[:, lc * E:(lc + 1) * E], p_g[:, 0:E], gbrep[:],
                                ALU.add)
            lk.inc(i, "ev")

        lk.on("vector", v_ev)

    gl3 = gla.ap().rearrange("p (l e) -> p l e", e=E)
    oh13 = oh1a.ap().rearrange("p (l e) -> p l e", e=E)
    oh23 = oh2a.ap().rearrange("p (l e) -> p l e", e=E)
    msk3 = mska.ap().rearrange("p (l e) -> p l e", e=E)
    va3 = va.ap().rearrange("p (l k) -> p l k", k=2)
    m1b = va3[:, :, 0:1].broadcast_to([128, 16, E])
    m2b = va3[:, :, 1:2].broadcast_to([128, 16, E])
    g16b = glm16.ap().unsqueeze(2).broadcast_to([128, 16, E])

    def v_b1(v):
        sp = lambda: v.tensor_copy(ydt[:, 0:512], ydt[:, 0:512])
        v.reduce_max(glm16[:], gl3, axis=AX)
        sp()
        i = v.tensor_tensor(gl3, gl3, g16b, ALU.subtract)
        lk.inc(i, "ev")

    lk.on("vector", v_b1)

    def s_b1(s):
        lk.wait(s, "ev")
        i = s.activation(gla[:], gla[:], AF.Exp)
        lk.inc(i, "sc")

    lk.on("scalar", s_b1)

    def v_b2(v):
        lk.wait(v, "sc")
        sp = lambda: v.tensor_copy(ydt[:, 0:512], ydt[:, 0:512])
        v.reduce_sum(glm16[:], gl3, axis=AX)
        sp()
        v.reciprocal(glm16[:], glm16[:])
        sp()
        v.tensor_tensor(gl3, gl3, g16b, ALU.mult)
        v.reduce_max(va3[:, :, 0:1], gl3, axis=AX)
        sp()
        v.tensor_tensor(oh13, gl3, m1b, ALU.is_ge)
        v.tensor_scalar(mska[:], oh1a[:], -1.0, 1.0, ALU.mult, ALU.add)
        v.tensor_tensor(mska[:], gla[:], mska[:], ALU.mult)
        v.reduce_max(va3[:, :, 1:2], msk3, axis=AX)
        sp()
        i = v.tensor_tensor(oh23, msk3, m2b, ALU.is_ge)
        lk.inc(i, "ev")

    lk.on("vector", v_b2)

    for lc in range(16):
        lo = lc * 128

        def t_tr(t, lc=lc, evc=lk.c["ev"]):
            lk.wait(t, "ev", evc)
            lk.nc.tensor.transpose(p_t1[:, 0:128], oh1a[:, lc * E:(lc + 1) * E], ident[:])
            lk.nc.tensor.transpose(p_t2[:, 0:128], oh2a[:, lc * E:(lc + 1) * E], ident[:])
            i = lk.nc.tensor.transpose(p_t3[:, 0:128], va[:, 2 * lc:2 * lc + 2], ident[:])
            lk.inc(i, "mm")

        lk.on("tensor", t_tr)

        def v_tr(v, lo=lo):
            lk.wait(v, "mm")
            v.tensor_copy(ohT1[:, lo:lo + 128], p_t1[:, 0:128])
            v.tensor_copy(ohT2[:, lo:lo + 128], p_t2[:, 0:128])
            i = v.tensor_copy(vT[:, lo:lo + 128], p_t3[:, 0:128])
            lk.mark(v, "ev")

        lk.on("vector", v_tr)

    def st1(s):
        lk.wait(s, "ev")
        i = s.dma_start(out=oh1_out, in_=ohT1[:])
        lk.inc(i, "st", 16)
        i = s.dma_start(out=oh2_out, in_=ohT2[:])
        lk.inc(i, "st", 16)
        i = s.dma_start(out=v1_out, in_=vT[0:1, :])
        lk.inc(i, "st", 16)
        i = s.dma_start(out=v2_out, in_=vT[1:2, :])
        lk.inc(i, "st", 16)
        s.wait_ge(lk.sems["st"], lk.c["st"])

    lk.on("sync", st1)
    lk.end_phase()



def emit_moe(lk, io, pool):
    xn1_in = io["xn1_s"]
    oh1_in, oh2_in = io["oh1_s"], io["oh2_s"]
    v1_in, v2_in = io["v1_s"], io["v2_s"]
    w1_in, w2_in = io["m_w1"], io["m_w2"]
    b1_in, b2_in, sel_in = io["m_b1"], io["m_b2"], io["m_sel"]
    mg_in, mb_in = io["m_lng"], io["m_lnb"]
    g1_out = io["g1_s"]
    lk.begin_phase("moe")
    lb = lk.c["ld"]
    wb = lk.c["wt"]
    bgp = lk.c["gp"]
    xn1 = lk.sb("mo_xn1", [128, NCH * TLOC], BF16)
    boost = lk.sb("mo_boost", [128, NCH * TLOC], BF16)
    g1acc = lk.sb("mo_g1acc", [128, NCH * TLOC], BF16)
    w1e = lk.sb("mo_w1e", [128, NCH * HMOE], BF16)
    w2e = lk.sb("mo_w2e", [128, NDC * D], BF16)
    h_sb = lk.sb("mo_h", [128, NDC * 512], BF16)
    ohf = lk.sb("mo_ohf", [E, 2 * TLOC], BF16)
    vf = lk.sb("mo_vf", [1, 2 * TLOC], BF16)
    selc = lk.sb("mo_sel", [E, E * 128], BF16)
    mrep = lk.sb("mo_mrep", [128, 512], BF16)
    mvrep = lk.sb("mo_mvrep", [128, 512], F32)
    vrep = lk.sb("mo_vrep", [128, 512], F32)
    eot = lk.sb("mo_eot", [128, 512], BF16)
    tmpf = lk.sb("mo_tmpf", [128, 512], F32)
    b1c = lk.sb("mo_b1c", [128, E * 16], F32)
    b2c = lk.sb("mo_b2c", [128, E * 8], F32)
    mg = lk.sb("mo_mg", [128, NCH], F32)
    mb = lk.sb("mo_mb", [128, NCH], F32)
    onesr = lk.sb("mo_onesr", [1, 128], BF16)
    kit = LNScratch(lk, 512, name="mo_ln")
    kit.init_consts()

    def g0(g):
        g.memset(onesr[:], 1.0)
        i = g.dma_start(out=ohf[:, 0:TLOC], in_=oh1_in)
        lk.inc(i, "wt", 16)
        i = g.dma_start(out=ohf[:, TLOC:2 * TLOC], in_=oh2_in)
        lk.inc(i, "wt", 16)
        i = g.dma_start(out=vf[:, 0:TLOC], in_=v1_in)
        lk.inc(i, "wt", 16)
        i = g.dma_start(out=vf[:, TLOC:2 * TLOC], in_=v2_in)
        lk.inc(i, "wt", 16)
        i = g.dma_start(out=selc[:], in_=sel_in)
        lk.inc(i, "wt", 16)
        i = g.nop()
        lk.inc(i, "gp")

    lk.on("gpsimd", g0)

    def ld0(s):
        for c in range(NCH):
            i = s.dma_start(out=xn1[:, c * TLOC:(c + 1) * TLOC],
                            in_=xn1_in[c * 128:(c + 1) * 128, :])
            lk.inc(i, "ld", 16)
        i = s.dma_start(out=b1c[:], in_=b1_in)
        lk.inc(i, "ld", 16)
        i = s.dma_start(out=b2c[:], in_=b2_in)
        lk.inc(i, "ld", 16)
        i = s.dma_start(out=mg[:], in_=mg_in)
        lk.inc(i, "ld", 16)
        i = s.dma_start(out=mb[:], in_=mb_in)
        lk.inc(i, "ld", 16)

    lk.on("sync", ld0)

    def v_init(v):
        lk.wait(v, "ld", lb + 16 * NCH)
        for c in range(NCH):
            v.tensor_copy(boost[:, c * TLOC:(c + 1) * TLOC], xn1[:, c * TLOC:(c + 1) * TLOC])
            i = v.tensor_copy(g1acc[:, c * TLOC:(c + 1) * TLOC], xn1[:, c * TLOC:(c + 1) * TLOC])
        lk.mark(v, "ev")

    lk.on("vector", v_init)

    mm_h_done = [lk.c["mm"]]
    mm_w2_done = [lk.c["mm"]]
    for p2 in range(2):
        rhs_t = xn1 if p2 == 0 else boost
        for e in range(E):
            def g_we(g, e=e, w1w=mm_h_done[0], w2w=mm_w2_done[0]):
                g.wait_ge(lk.sems["mm"], w1w)
                for kc in range(NCH):
                    i = g.dma_start(out=w1e[:, kc * HMOE:(kc + 1) * HMOE],
                                    in_=w1_in[e * D + kc * 128:e * D + (kc + 1) * 128, :])
                    lk.inc(i, "wt", 16)
                g.wait_ge(lk.sems["mm"], w2w)
                for kc in range(NDC):
                    i = g.dma_start(out=w2e[:, kc * D:(kc + 1) * D],
                                    in_=w2_in[e * HMOE + kc * 128:e * HMOE + (kc + 1) * 128, :])
                    lk.inc(i, "wt", 16)

            lk.on("gpsimd", g_we)
            wt_w1 = lk.c["wt"] - 16 * NDC
            wt_w2 = lk.c["wt"]

            for (mo, w) in MT512:
                def t_rep(t, e=e, p2=p2, mo=mo, w=w, evc=lk.c["ev"]):
                    lk.wait(t, "ev", evc)
                    lk.wait(t, "ld")
                    lk.wait(t, "wt", wb + 16 * 5)
                    lk.wait(t, "gp", bgp + 1)
                    ib0, b0 = pool.acquire(t)
                    i = lk.nc.tensor.matmul(b0[:, 0:w], selc[:, e * 128:(e + 1) * 128],
                                            ohf[:, p2 * TLOC + mo:p2 * TLOC + mo + w],
                                            start=True, stop=True)
                    lk.inc(i, "mm")
                    ib1, b1 = pool.acquire(t)
                    i = lk.nc.tensor.matmul(b1[:, 0:w], onesr[:],
                                            vf[:1, p2 * TLOC + mo:p2 * TLOC + mo + w],
                                            start=True, stop=True)
                    lk.inc(i, "mm")
                    t_rep.banks = [(ib0, b0), (ib1, b1)]

                lk.on("tensor", t_rep)
                (ib0, b0), (ib1, b1) = t_rep.banks

                def v_rep(v, ib0=ib0, b0=b0, ib1=ib1, b1=b1, w=w):
                    lk.wait(v, "mm")
                    i = v.tensor_copy(mrep[:, 0:w], b0[:, 0:w])
                    pool.freed(i, ib0)
                    i = v.tensor_copy(vrep[:, 0:w], b1[:, 0:w])
                    pool.freed(i, ib1)
                    v.tensor_tensor(mvrep[:, 0:w], mrep[:, 0:w], vrep[:, 0:w], ALU.mult)
                    lk.mark(v, "ev")

                lk.on("vector", v_rep)

                def evac_h(e_, bank, n, mo2, w2, e=e):
                    return e_.tensor_scalar(h_sb[:, n * 512:n * 512 + w2], bank[:, 0:w2],
                                            b1c[:, e * 16 + n:e * 16 + n + 1], 0.0,
                                            ALU.add, ALU.max)

                mm_block(lk, pool,
                         get_lhsT=lambda kc, n: w1e[:, kc * HMOE + n * 128:kc * HMOE + (n + 1) * 128],
                         get_rhs=lambda kc, mo2, w2, rhs_t=rhs_t, mo=mo: rhs_t[:, kc * TLOC + mo:kc * TLOC + mo + w2],
                         nK=NCH, n_list=range(NDC), m_list=[(0, w)],
                         evac=evac_h, evac_engine="vector",
                         pre_wait=lambda t, wd=wt_w1, c=lk.c["ev"]: (lk.wait(t, "wt", wd),
                                                                     lk.wait(t, "ev", c)))
                emit_barrier(lk, "vector", "ev")
                mm_h_done[0] = lk.c["mm"]

                def evac_o(e_, bank, n, mo2, w2, e=e, p2=p2, mo=mo):
                    e_.tensor_scalar(eot[:, 0:w2], bank[:, 0:w2],
                                     b2c[:, e * 8 + n:e * 8 + n + 1], 0.0, ALU.add, ALU.add)
                    e_.tensor_tensor(tmpf[:, 0:w2], eot[:, 0:w2], mvrep[:, 0:w2], ALU.mult)
                    i = e_.tensor_tensor(g1acc[:, n * TLOC + mo:n * TLOC + mo + w2],
                                         g1acc[:, n * TLOC + mo:n * TLOC + mo + w2],
                                         tmpf[:, 0:w2], ALU.add)
                    if p2 == 0:
                        e_.tensor_tensor(tmpf[:, 0:w2], eot[:, 0:w2], mrep[:, 0:w2],
                                         ALU.mult)
                        i = e_.scalar_tensor_tensor(boost[:, n * TLOC + mo:n * TLOC + mo + w2],
                                                    tmpf[:, 0:w2], 0.5,
                                                    boost[:, n * TLOC + mo:n * TLOC + mo + w2],
                                                    ALU.mult, ALU.add)
                    return i

                mm_block(lk, pool,
                         get_lhsT=lambda kc, n: w2e[:, kc * D + n * 128:kc * D + (n + 1) * 128],
                         get_rhs=lambda kc, mo2, w2: h_sb[:, kc * 512:kc * 512 + w2],
                         nK=NDC, n_list=range(NCH), m_list=[(0, w)],
                         evac=evac_o, evac_engine="vector",
                         pre_wait=lambda t, wd=wt_w2, c=lk.c["ev"]: (lk.wait(t, "wt", wd),
                                                                     lk.wait(t, "ev", c)))
                emit_barrier(lk, "vector", "ev")
                mm_w2_done[0] = lk.c["mm"]

    sq_full = h_sb.ap().bitcast(F32)
    tsc = tmpf
    g1o = w1e
    for (mo, w) in MT512:
        def get_x(c, mo=mo, w=w):
            return g1acc[:, c * TLOC + mo:c * TLOC + mo + w]

        def put_out(c, v, w=w):
            v.tensor_tensor(tsc[:, 0:w], get_x(c), kit.p_rep[:, 0:w], ALU.subtract)
            v.tensor_tensor(tsc[:, 0:w], tsc[:, 0:w], kit.p_rep[:, 512:512 + w], ALU.mult)
            v.tensor_scalar(g1o[:, c * 512:c * 512 + w], tsc[:, 0:w],
                            mg[:, c:c + 1], mb[:, c:c + 1], ALU.mult, ALU.add)

        emit_ln(lk, kit, get_x, put_out, w, sq_full, out_war_sems=("st",))

        def s_silu(s, mo=mo, w=w):
            lk.wait(s, "ev")
            for c in range(NCH):
                i = s.activation(g1o[:, c * 512:c * 512 + w], g1o[:, c * 512:c * 512 + w],
                                 AF.Silu)
            lk.inc(i, "sc")

        lk.on("scalar", s_silu)

        def st_g(s, mo=mo, w=w):
            lk.wait(s, "sc")
            for c in range(NCH):
                i = s.dma_start(out=g1_out[c * 128:(c + 1) * 128, mo:mo + w],
                                in_=g1o[:, c * 512:c * 512 + w])
                lk.inc(i, "st", 16)

        lk.on("sync", st_g)

    def fin(s):
        lk.wait(s, "st")

    lk.on("sync", fin)
    lk.end_phase()



def build_fused(upto=99, dbg=()):
    lk = L("mega")
    io = {}
    io["xT"] = lk.din("xT", [D, TLOC + 1])
    io["ln1g"] = lk.din("ln1g", [128, NCH])
    io["ln1b"] = lk.din("ln1b", [128, NCH])
    io["p_wq"] = lk.din("p_wq", [D, D]); io["p_wk"] = lk.din("p_wk", [D, D])
    io["p_wv"] = lk.din("p_wv", [D, D])
    io["p_bq"] = lk.din("p_bq", [128, NCH]); io["p_bk"] = lk.din("p_bk", [128, NCH])
    io["p_bv"] = lk.din("p_bv", [1, D])
    io["proj"] = lk.din("proj", [DH, DK])
    io["a_wo"] = lk.din("a_wo", [D, D]); io["a_bo"] = lk.din("a_bo", [128, NCH])
    io["ident"] = lk.din("ident", [128, 128])
    io["ln2g"] = lk.din("ln2g", [128, NCH]); io["ln2b"] = lk.din("ln2b", [128, NCH])
    io["f_w1"] = lk.din("f_w1", [D, DI]); io["f_b1"] = lk.din("f_b1", [128, DI // 128])
    io["f_w2"] = lk.din("f_w2", [DI, D]); io["f_b2"] = lk.din("f_b2", [128, NCH])
    io["r_wrt"] = lk.din("r_wrt", [D, D]); io["r_wkt"] = lk.din("r_wkt", [D, D])
    io["r_wvt"] = lk.din("r_wvt", [D, D])
    io["r_mur"] = lk.din("r_mur", [128, NCH]); io["r_muk"] = lk.din("r_muk", [128, NCH])
    io["l_winT"] = lk.din("l_winT", [D, 2 * DN])
    io["l_convw"] = lk.din("l_convw", [128, NDC * 4])
    io["l_convb"] = lk.din("l_convb", [128, NDC])
    io["l_xprojT"] = lk.din("l_xprojT", [DN, 96])
    io["l_wdtT"] = lk.din("l_wdtT", [DTR, DN])
    io["l_bdt"] = lk.din("l_bdt", [128, NDC])
    io["s_alog"] = lk.din("s_alog", [128, 8 * DS])
    io["m0c"] = lk.din("m0c", [128, 1]); io["m1c"] = lk.din("m1c", [128, 1])
    io["g_dsk"] = lk.din("g_dsk", [128, NDC])
    io["g_woutT"] = lk.din("g_woutT", [DN, D])
    io["g_gwp"] = lk.din("g_gwp", [128, NCH * E])
    io["g_gb"] = lk.din("g_gb", [1, E])
    io["m_w1"] = lk.din("m_w1", [E * D, HMOE])
    io["m_w2"] = lk.din("m_w2", [E * HMOE, D])
    io["m_b1"] = lk.din("m_b1", [128, E * 16])
    io["m_b2"] = lk.din("m_b2", [128, E * 8])
    io["m_sel"] = lk.din("m_sel", [E, E * 128])
    io["m_lng"] = lk.din("m_lng", [128, NCH])
    io["m_lnb"] = lk.din("m_lnb", [128, NCH])
    io["out"] = lk.dout("out", [D, TLOC])

    def mk(name, shape, dt=F32):
        io[name] = (lk.dout(name, shape, dt) if name in dbg
                    else lk.dscratch(name, shape, dt))

    mk("xn_s", [D, TLOC + 1], BF16)
    mk("x2_s", [D, TLOC], BF16)
    mk("qf_s", [D, TLOC], BF16)
    mk("kvp_i", [NH * DH, 2 * DK])
    mk("kvp_o", [NH * DH, 2 * DK])
    mk("xs_s", [D, TLOC], BF16)
    mk("x2b_s", [D, TLOC], BF16)
    mk("s0_s", [D, TLOC], BF16)
    mk("g0_s", [D, TLOC], BF16)
    mk("x1_s", [D, TLOC + 3])
    mk("halo_i", [D, 4])
    mk("halo_o", [2 * D, 4])
    mk("xn1_s", [D, TLOC], BF16)
    mk("u_s", [DN, TLOC], BF16)
    mk("dt_s", [DN, TLOC], BF16)
    mk("z_s", [DN, TLOC], BF16)
    mk("Bs_s", [DS, TLOC])
    mk("Cs_s", [DS, TLOC])
    mk("dtG", [2 * DN, TLOC], BF16)
    mk("uG", [2 * DN, TLOC], BF16)
    mk("BsG", [2 * DS, TLOC])
    mk("CsG", [2 * DS, TLOC])
    mk("y_s", [DN // 2, T], BF16)
    mk("yG", [DN, T], BF16)
    mk("s1_s", [D, TLOC], BF16)
    mk("oh1_s", [E, TLOC])
    mk("oh2_s", [E, TLOC])
    mk("v1_s", [1, TLOC])
    mk("v2_s", [1, TLOC])
    mk("g1_s", [D, TLOC], BF16)

    lk.sem("ld", "wt", "w2t", "nt", "nf", "rf", "mm", "sc", "ev", "st", "gp",
           "ph", "bl", "pbf0", "pbf1",
           "cc1", "cc2", "cc3", "cc4", "cc5", "cc6", "cc7")
    lk.block()
    pool = PsumPool(lk, 2)

    steps = [
        lambda: emit_xn(lk, io),
        lambda: emit_perf_kv(lk, io, pool),
        lambda: lk.cc("AllReduce", ALU.add, io["kvp_i"], io["kvp_o"], "cc1"),
        lambda: emit_attn(lk, io, pool),
        lambda: emit_ln2(lk, io),
        lambda: emit_ffn(lk, io, pool),
        lambda: emit_rwkv(lk, io, pool),
        lambda: emit_combine(lk, io,
                             x_src=lambda c, mo, w: io["xT"][c * 128:(c + 1) * 128,
                                                            1 + mo:1 + mo + w],
                             s_src=io["s0_s"], g_src=io["g0_s"],
                             out_ap=io["x1_s"], out_off=3, halo_dst=io["halo_i"],
                             tag="cb1"),
        lambda: lk.cc("AllGather", ALU.bypass, io["halo_i"], io["halo_o"], "cc2"),
        lambda: emit_halo_fix(lk, io),
        lambda: emit_l1front(lk, io, pool),
        lambda: lk.cc("AllGather", ALU.bypass, io["dt_s"], io["dtG"], "cc3"),
        lambda: lk.cc("AllGather", ALU.bypass, io["u_s"], io["uG"], "cc4"),
        lambda: lk.cc("AllGather", ALU.bypass, io["Bs_s"], io["BsG"], "cc5"),
        lambda: lk.cc("AllGather", ALU.bypass, io["Cs_s"], io["CsG"], "cc6"),
        lambda: emit_scan(lk, io, pool),
        lambda: lk.cc("AllGather", ALU.bypass, io["y_s"], io["yG"], "cc7"),
        lambda: emit_gate_s1(lk, io, pool),
        lambda: emit_moe(lk, io, pool),
        lambda: emit_combine(lk, io,
                             x_src=lambda c, mo, w: io["x1_s"][c * 128:(c + 1) * 128,
                                                               3 + mo:3 + mo + w],
                             s_src=io["s1_s"], g_src=io["g1_s"],
                             out_ap=io["out"], out_off=0, tag="cb2"),
    ]
    for i, step in enumerate(steps):
        if i >= upto:
            break
        step()

    def fin(s):
        s.wait_ge(lk.sems["st"], lk.c["st"])

    lk.on("sync", fin)
    lk.close()
    return lk



def host_inputs(inputs):
    x = np.asarray(inputs["L0_attn_x"])
    al = np.asarray(inputs["L1_attn_A_log"], np.float32)
    shared = {
        "ln1g": host_prep_feat_cols(inputs["L0_attn_ln1_g"]),
        "ln1b": host_prep_feat_cols(inputs["L0_attn_ln1_b"]),
        "p_wq": _f32h(inputs["L0_attn_wq"]),
        "p_wk": _f32h(inputs["L0_attn_wk"]),
        "p_wv": _f32h(inputs["L0_attn_wv"]),
        "p_bq": host_prep_feat_cols(inputs["L0_attn_bq"]),
        "p_bk": host_prep_feat_cols(inputs["L0_attn_bk"]),
        "p_bv": _f32h(np.asarray(inputs["L0_attn_bv"]).reshape(1, D)),
        "proj": _f32h(inputs["L0_attn_proj_mat"]),
        "a_wo": _f32h(inputs["L0_attn_wo"]),
        "a_bo": host_prep_feat_cols(inputs["L0_attn_bo"]),
        "ident": np.eye(128, dtype=np.float32),
        "ln2g": host_prep_feat_cols(inputs["L0_attn_ln2_g"]),
        "ln2b": host_prep_feat_cols(inputs["L0_attn_ln2_b"]),
        "f_w1": _f32h(inputs["L0_attn_w1"]),
        "f_b1": host_prep_feat_cols(inputs["L0_attn_b1"]),
        "f_w2": _f32h(inputs["L0_attn_w2"]),
        "f_b2": host_prep_feat_cols(inputs["L0_attn_b2"]),
        "r_wrt": _f32h(np.asarray(inputs["L0_ffn_W_r"]).T),
        "r_wkt": _f32h(np.asarray(inputs["L0_ffn_W_k"]).T),
        "r_wvt": _f32h(np.asarray(inputs["L0_ffn_W_v"]).T),
        "r_mur": host_prep_feat_cols(inputs["L0_ffn_mu_r"]),
        "r_muk": host_prep_feat_cols(inputs["L0_ffn_mu_k"]),
        "l_winT": _f32h(np.asarray(inputs["L1_attn_W_in"]).T),
        "l_convw": _f32h(np.transpose(
            np.asarray(inputs["L1_attn_conv_w"]).reshape(NDC, 128, 4),
            (1, 0, 2)).reshape(128, NDC * 4)),
        "l_convb": host_prep_feat_cols(inputs["L1_attn_conv_b"]),
        "l_xprojT": _f32h(np.asarray(inputs["L1_attn_W_xproj"]).T),
        "l_wdtT": _f32h(np.asarray(inputs["L1_attn_W_dt"]).T),
        "l_bdt": host_prep_feat_cols(inputs["L1_attn_b_dt"]),
        "g_dsk": host_prep_feat_cols(inputs["L1_attn_D_skip"]),
        "g_woutT": _f32h(np.asarray(inputs["L1_attn_W_out"]).T),
        "g_gwp": _f32h(np.transpose(
            np.asarray(inputs["L1_ffn_gate_W"]).reshape(NCH, 128, E),
            (1, 0, 2)).reshape(128, NCH * E)),
        "g_gb": _f32h(np.asarray(inputs["L1_ffn_gate_b"]).reshape(1, E)),
        "m_w1": _f32h(np.asarray(inputs["L1_ffn_W1"]).reshape(E * D, HMOE)),
        "m_w2": _f32h(np.asarray(inputs["L1_ffn_W2"]).reshape(E * HMOE, D)),
        "m_b1": _f32h(np.concatenate(
            [np.asarray(inputs["L1_ffn_b1"])[e].reshape(16, 128).T for e in range(E)],
            axis=1)),
        "m_b2": _f32h(np.concatenate(
            [np.asarray(inputs["L1_ffn_b2"])[e].reshape(8, 128).T for e in range(E)],
            axis=1)),
        "m_sel": _f32h(np.kron(np.eye(E, dtype=np.float32),
                               np.ones((1, 128), np.float32))),
        "m_lng": host_prep_feat_cols(inputs["L1_ffn_ln_gamma"]),
        "m_lnb": host_prep_feat_cols(inputs["L1_ffn_ln_beta"]),
    }
    in_maps = []
    for core in range(NCORES):
        b, j = divmod(core, 2)
        t0 = j * TLOC
        ext = np.zeros((D, TLOC + 1), np.float32)
        ext[:, 1:] = x[b, t0:t0 + TLOC, :].T
        if j == 1:
            ext[:, 0] = x[b, t0 - 1, :]
        r0, r1 = j * 1024, (j + 1) * 1024
        alc = np.ascontiguousarray(np.transpose(
            al[r0:r1].reshape(8, 128, DS), (1, 0, 2)).reshape(128, 8 * DS))
        m = dict(shared)
        m["xT"] = np.ascontiguousarray(ext)
        m["s_alog"] = alc
        m["m0c"] = np.full((128, 1), 1.0 - j, np.float32)
        m["m1c"] = np.full((128, 1), float(j), np.float32)
        in_maps.append(m)
    return in_maps


def kernel(**inputs):
    lk = build_fused()
    in_maps = host_inputs(inputs)
    res = run_bass_kernel_spmd(lk.nc, in_maps, core_ids=list(range(NCORES)))
    out = np.zeros((B, T, D), np.float32)
    for core in range(NCORES):
        b, j = divmod(core, 2)
        out[b, j * TLOC:(j + 1) * TLOC, :] = np.asarray(res.results[core]["out"]).T
    return out


# version 12
# speedup vs baseline: 1.3075x; 1.0227x over previous
import math
from contextlib import ExitStack

import numpy as np
import ml_dtypes

import concourse.bass as bass
import concourse.mybir as mybir
from concourse.bass_utils import run_bass_kernel_spmd

F32 = mybir.dt.float32
BF16 = mybir.dt.bfloat16
AF = mybir.ActivationFunctionType
ALU = mybir.AluOpType
AX = mybir.AxisListType.X

BF = ml_dtypes.bfloat16
NCORES = 8
PAIRS = [[0, 1], [2, 3], [4, 5], [6, 7]]
B, T, D = 4, 4096, 1024
TLOC = T // 2
NH, DH, DK = 16, 64, 128
DI = 4096
DN = 2048
DS = 16
DTR = 64
E, HMOE = 8, 2048
PI = float(np.pi)
EPS = 1e-5
NCH = D // 128
NDC = DN // 128
MT512 = [(o, 512) for o in range(0, TLOC, 512)]



class L:

    def __init__(self, name):
        self.name = name
        self.nc = bass.Bass("TRN2", target_bir_lowering=False)
        self.stack = ExitStack()
        self.ph = None
        self.c = {}
        self.sems = {}

    def din(self, name, shape, dt=F32):
        return self.nc.dram_tensor(name, list(shape), dt, kind="ExternalInput").ap()

    def dout(self, name, shape, dt=F32):
        return self.nc.dram_tensor(name, list(shape), dt, kind="ExternalOutput").ap()

    def dscratch(self, name, shape, dt=F32):
        return self.nc.dram_tensor(name, list(shape), dt, kind="Internal").ap()

    def sb(self, name, shape, dt=F32):
        st = self.ph if self.ph is not None else self.stack
        return st.enter_context(self.nc.sbuf_tensor(name, list(shape), dt))

    def ps(self, name, shape, dt=F32):
        st = self.ph if self.ph is not None else self.stack
        return st.enter_context(self.nc.psum_tensor(name, list(shape), dt))

    def sem(self, *names):
        for n in names:
            self.sems[n] = self.stack.enter_context(self.nc.semaphore(n))
            self.c[n] = 0

    def block(self):
        self.dummy = self.stack.enter_context(self.nc.sbuf_tensor("l_dummy", [1, 16], F32))
        self.blk = self.stack.enter_context(self.nc.Block())

    def mark(self, eng, sname):
        if hasattr(eng, "tensor_copy"):
            i = eng.tensor_copy(self.dummy[:1, 0:8], self.dummy[:1, 8:16])
        else:
            i = eng.copy(self.dummy[:1, 0:8], self.dummy[:1, 8:16])
        self.inc(i, sname)

    def inc(self, inst, sname, n=1):
        inst.then_inc(self.sems[sname], n)
        self.c[sname] += n
        return inst

    def wait(self, eng, sname, cnt=None):
        eng.wait_ge(self.sems[sname], self.c[sname] if cnt is None else cnt)

    def on(self, engine_name, fn):
        getattr(self.blk, engine_name)(fn)

    def begin_phase(self, name):
        assert self.ph is None
        self.ph = ExitStack()
        self._scope = self.nc.named_scope(name)
        self._scope.__enter__()
        waits = [(s, self.c[s]) for s in self.sems if self.c[s] > 0]
        for eng in ("sync", "scalar", "vector", "tensor", "gpsimd"):
            def bw(e, waits=tuple(waits)):
                for s, cnt in waits:
                    e.wait_ge(self.sems[s], cnt)
            self.on(eng, bw)

    def end_phase(self):
        for eng in ("scalar", "vector", "gpsimd"):
            def mk(e):
                self.mark(e, "ph")
            self.on(eng, mk)
        self.ph.close()
        self.ph = None
        self._scope.__exit__(None, None, None)

    def cc(self, kind, op, in_ap, out_ap, sname):
        def g_fn(g):
            g.wait_ge(self.sems["st"], self.c["st"])
            i = g.collective_compute(kind, op, replica_groups=PAIRS,
                                     ins=[in_ap.opt()], outs=[out_ap.opt()])
            self.inc(i, sname)
            g.wait_ge(self.sems[sname], self.c[sname])
        self.on("gpsimd", g_fn)

    def close(self):
        self.stack.close()


def _bfh(a):
    return np.ascontiguousarray(np.asarray(a).astype(BF))


def _f32h(a):
    return np.ascontiguousarray(np.asarray(a, dtype=np.float32))


def _tile_widths(n, w):
    out = []
    while n > 0:
        out.append(min(w, n))
        n -= out[-1]
    return out


def host_prep_feat_cols(vec):
    v = np.asarray(vec, np.float32)
    return np.ascontiguousarray(v.reshape(-1, 128).T)



def emit_ln(lk, kit, get_x, put_out, w, sq_full, out_war_sems=(), nchunks=NCH):
    nc = lk.nc

    def s_fn(s):
        lk.wait(s, "gp", kit.gp_done)
        lk.wait(s, "ld")
        lk.wait(s, "ev")
        for c in range(nchunks):
            i = s.activation(sq_full[:, c * w:(c + 1) * w], get_x(c), AF.Square)
        lk.inc(i, "sc")

    lk.on("scalar", s_fn)

    def t_fn(t):
        lk.wait(t, "gp", kit.gp_done)
        lk.wait(t, "ev")
        lk.wait(t, "ld")
        ones_c = kit.ones_col_bf if get_x(0).dtype == BF16 else kit.ones_col
        for c in range(nchunks):
            nc.tensor.matmul(kit.p_sum[:1, 0:w], ones_c[:], get_x(c),
                             start=(c == 0), stop=(c == nchunks - 1))
        lk.wait(t, "sc")
        for c in range(nchunks):
            i = nc.tensor.matmul(kit.p_sumsq[:1, 0:w], kit.ones_col[:],
                                 sq_full[:, c * w:(c + 1) * w],
                                 start=(c == 0), stop=(c == nchunks - 1))
        lk.inc(i, "mm")

    lk.on("tensor", t_fn)

    def v_fn(v):
        lk.wait(v, "mm")
        mean = kit.row[:1, 0:w]
        var = kit.row[:1, kit.maxw:kit.maxw + w]
        v.tensor_scalar(mean, kit.p_sum[:1, 0:w], 1.0 / D, 0.0, ALU.mult, ALU.add)
        v.tensor_tensor(var, mean, mean, ALU.mult)
        i = v.scalar_tensor_tensor(var, kit.p_sumsq[:1, 0:w], 1.0 / D, var,
                                   ALU.mult, ALU.subtract)
        lk.inc(i, "ev")

    lk.on("vector", v_fn)

    def s2_fn(s):
        lk.wait(s, "gp", kit.gp_done)
        lk.wait(s, "ev")
        var = kit.row[:1, kit.maxw:kit.maxw + w]
        s.activation(var, var, AF.Ln, bias=kit.eps_col[:1])
        i = s.activation(var, var, AF.Exp, scale=kit.half_col[:1])
        lk.inc(i, "sc")

    lk.on("scalar", s2_fn)

    def t2_fn(t):
        lk.wait(t, "sc")
        nc.tensor.matmul(kit.p_rep[:, 0:w], kit.ones_row[:], kit.row[:1, 0:w],
                         start=True, stop=True)
        i = nc.tensor.matmul(kit.p_rep[:, kit.maxw:kit.maxw + w], kit.ones_row[:],
                             kit.row[:1, kit.maxw:kit.maxw + w], start=True, stop=True)
        lk.inc(i, "mm")

    lk.on("tensor", t2_fn)

    def v2_fn(v):
        lk.wait(v, "mm")
        for s_ in out_war_sems:
            lk.wait(v, s_)
        for c in range(nchunks):
            put_out(c, v)
        lk.mark(v, "ev")

    lk.on("vector", v2_fn)


class LNScratch:
    def __init__(self, lk, maxw=512, name="ln"):
        self.lk = lk
        self.maxw = maxw
        self.row = lk.sb(f"{name}_row", [1, 2 * maxw], F32)
        self.p_sum = lk.ps(f"{name}_psum", [1, maxw], F32)
        self.p_sumsq = lk.ps(f"{name}_psumsq", [1, maxw], F32)
        self.p_rep = lk.ps(f"{name}_prep", [128, 2 * maxw], F32)
        self.ones_col = lk.sb(f"{name}_ones_c", [128, 1], F32)
        self.ones_col_bf = lk.sb(f"{name}_ones_cb", [128, 1], BF16)
        self.ones_row = lk.sb(f"{name}_ones_r", [1, 128], F32)
        self.eps_col = lk.sb(f"{name}_eps", [1, 1], F32)
        self.half_col = lk.sb(f"{name}_half", [1, 1], F32)

    def init_consts(self):
        lk = self.lk

        def g_fn(g):
            g.memset(self.ones_col[:], 1.0)
            g.memset(self.ones_col_bf[:], 1.0)
            g.memset(self.ones_row[:], 1.0)
            g.memset(self.eps_col[:], EPS)
            i = g.memset(self.half_col[:], -0.5)
            lk.inc(i, "gp")

        lk.on("gpsimd", g_fn)
        self.gp_done = lk.c["gp"]



class PsumPool:

    def __init__(self, lk, nbanks=2, tag="pb"):
        self.lk = lk
        self.n = nbanks
        self.banks = [lk.ps(f"{tag}{i}", [128, 512], F32) for i in range(nbanks)]
        self.free_sems = [f"{tag}f{i}" for i in range(nbanks)]
        self.uses = [0] * nbanks
        self.rr = 0

    def acquire(self, teng):
        i = self.rr
        self.rr = (self.rr + 1) % self.n
        if self.uses[i] > 0:
            teng.wait_ge(self.lk.sems[self.free_sems[i]], self.uses[i])
        self.uses[i] += 1
        return i, self.banks[i]

    def freed(self, inst, i):
        inst.then_inc(self.lk.sems[self.free_sems[i]], 1)
        self.lk.c[self.free_sems[i]] += 1


def mm_block(lk, pool, get_lhsT, get_rhs, nK, n_list, m_list, evac,
             evac_engine="vector", pre_wait=None):
    nc = lk.nc
    first = True
    for n in n_list:
        for (mo, w) in m_list:
            holder = {}

            def t_fn(t, n=n, mo=mo, w=w, first=first):
                if pre_wait is not None and first:
                    pre_wait(t)
                ib, bank = pool.acquire(t)
                holder["b"] = (ib, bank)
                for kc in range(nK):
                    i = nc.tensor.matmul(bank[:, 0:w], get_lhsT(kc, n),
                                         get_rhs(kc, mo, w),
                                         start=(kc == 0), stop=(kc == nK - 1))
                lk.inc(i, "mm")

            lk.on("tensor", t_fn)
            ib, bank = holder["b"]

            def e_fn(e, n=n, mo=mo, w=w, ib=ib, bank=bank):
                lk.wait(e, "mm")
                last = evac(e, bank, n, mo, w)
                pool.freed(last, ib)

            lk.on(evac_engine, e_fn)
            first = False


def emit_barrier(lk, engine_name, sname="ev"):
    def f(e):
        lk.mark(e, sname)

    lk.on(engine_name, f)


def load_weight_bf16(lk, g, dst_sb, hbm_ap, nK, ncols, sem="wt"):
    for kc in range(nK):
        i = g.dma_start(out=dst_sb[:, kc * ncols:(kc + 1) * ncols],
                        in_=hbm_ap[kc * 128:(kc + 1) * 128, :])
        lk.inc(i, sem, 16)



def emit_xn(lk, io):
    NT = TLOC + 1
    x_in, g_in, b_in = io["xT"], io["ln1g"], io["ln1b"]
    xn_out, x2_out = io["xn_s"], io["x2_s"]
    lk.begin_phase("xn")
    widths = [257] + _tile_widths(TLOC - 256, 512)
    xt = lk.sb("xn_xt", [128, NCH * 512], F32)
    xo = lk.sb("xn_xo", [128, NCH * 512], BF16)
    x2o = lk.sb("xn_x2o", [128, NCH * 512], BF16)
    sq_full = lk.sb("xn_sq", [128, NCH * 512], F32)
    gcol = lk.sb("xn_gcol", [128, NCH], F32)
    bcol = lk.sb("xn_bcol", [128, NCH], F32)
    kit = LNScratch(lk, 512, name="xn_ln")
    kit.init_consts()

    def ld0(s):
        i = s.dma_start(out=gcol[:], in_=g_in)
        lk.inc(i, "ld", 16)
        i = s.dma_start(out=bcol[:], in_=b_in)
        lk.inc(i, "ld", 16)

    lk.on("sync", ld0)

    t0 = 0
    for w in widths:
        o = t0

        def ld_fn(s, o=o, w=w):
            lk.wait(s, "ev")
            for c in range(NCH):
                i = s.dma_start(out=xt[:, c * w:(c + 1) * w],
                                in_=x_in[c * 128:(c + 1) * 128, o:o + w])
                lk.inc(i, "ld", 16)

        lk.on("sync", ld_fn)

        def get_x(c, w=w):
            return xt[:, c * w:(c + 1) * w]

        def put_out(c, v, w=w):
            tmp = sq_full[:, c * w:(c + 1) * w]
            v.tensor_tensor(tmp, get_x(c), kit.p_rep[:, 0:w], ALU.subtract)
            v.tensor_tensor(xo[:, c * w:(c + 1) * w], tmp,
                            kit.p_rep[:, 512:512 + w], ALU.mult)

        emit_ln(lk, kit, get_x, put_out, w, sq_full, out_war_sems=("st",))

        def get_x2(c, w=w):
            return xo[:, c * w:(c + 1) * w]

        def put_out2(c, v, w=w):
            tmp = sq_full[:, c * w:(c + 1) * w]
            d = x2o[:, c * w:(c + 1) * w]
            v.tensor_tensor(tmp, get_x2(c), kit.p_rep[:, 0:w], ALU.subtract)
            v.tensor_tensor(tmp, tmp, kit.p_rep[:, 512:512 + w], ALU.mult)
            v.tensor_scalar(d, tmp, gcol[:, c:c + 1], bcol[:, c:c + 1],
                            ALU.mult, ALU.add)

        emit_ln(lk, kit, get_x2, put_out2, w, sq_full, out_war_sems=("st",))

        def st_fn(s, o=o, w=w):
            lk.wait(s, "ev")
            for c in range(NCH):
                i = s.dma_start(out=xn_out[c * 128:(c + 1) * 128, o:o + w],
                                in_=xo[:, c * w:(c + 1) * w])
                lk.inc(i, "st", 16)
            x2src0 = 1 if o == 0 else 0
            for c in range(NCH):
                i = s.dma_start(
                    out=x2_out[c * 128:(c + 1) * 128, max(o - 1, 0):o + w - 1],
                    in_=x2o[:, c * w + x2src0:(c + 1) * w])
                lk.inc(i, "st", 16)

        lk.on("sync", st_fn)
        t0 += w

    def fin(s):
        lk.wait(s, "st")

    lk.on("sync", fin)
    lk.end_phase()



def emit_perf_kv(lk, io, pool):
    x2_in = io["x2_s"]
    wq_in, wk_in, wv_in = io["p_wq"], io["p_wk"], io["p_wv"]
    bq_in, bk_in, bv_in, proj_in = io["p_bq"], io["p_bk"], io["p_bv"], io["proj"]
    qf_out, kvp_out = io["qf_s"], io["kvp_i"]
    lk.begin_phase("perfkv")
    wb = lk.c["wt"]
    x2 = lk.sb("pk_x2", [128, NCH * TLOC], BF16)
    wq = lk.sb("pk_wq", [128, NCH * D], BF16)
    wk = lk.sb("pk_wk", [128, NCH * D], BF16)
    wv = lk.sb("pk_wv", [128, NCH * D], BF16)
    qf = lk.sb("pk_qf", [128, NCH * TLOC], BF16)
    kf = lk.sb("pk_kf", [128, NCH * TLOC], BF16)
    v_full = lk.sb("pk_v", [128, 16 * D], BF16)
    proj = lk.sb("pk_proj", [128, 2 * DK], BF16)
    bq = lk.sb("pk_bq", [128, NCH], F32)
    bk = lk.sb("pk_bk", [128, NCH], F32)
    bvr = lk.sb("pk_bvr", [1, D], F32)
    bv_rep = lk.sb("pk_bvrep", [128, D], BF16)
    onesr = lk.sb("pk_onesr", [1, 128], F32)
    wsc = lk.sb("pk_wsc", [128, 256], F32)
    csc = lk.sb("pk_csc", [128, 256], F32)
    aw = lk.sb("pk_aw", [128, 256], F32)
    kp = lk.sb("pk_kp", [128, 4 * DK], BF16)
    kvsb = lk.sb("pk_kvsb", [128, 4 * 512], F32)
    cst = lk.sb("pk_cst", [128, 2], F32)
    p_pk = lk.ps("pk_ppk", [128, 2 * DK], F32)
    kv4 = [lk.ps(f"pk_pkv{i}", [128, 512], F32) for i in range(4)]
    bgp = lk.c["gp"]

    def g0(g):
        g.memset(cst[:, 0:1], PI / 2)
        g.memset(cst[:, 1:2], -1.0)
        i = g.memset(onesr[:], 1.0)
        lk.inc(i, "gp")
        load_weight_bf16(lk, g, wq, wq_in, NCH, D)
        load_weight_bf16(lk, g, wk, wk_in, NCH, D)
        load_weight_bf16(lk, g, wv, wv_in, NCH, D)
        g.memset(proj[:], 0.0)
        i = g.dma_start(out=proj[0:64, 0:DK], in_=proj_in)
        lk.inc(i, "wt", 16)
        i = g.dma_start(out=proj[64:128, DK:2 * DK], in_=proj_in)
        lk.inc(i, "wt", 16)

    lk.on("gpsimd", g0)

    def ld0(s):
        for c in range(NCH):
            i = s.dma_start(out=x2[:, c * TLOC:(c + 1) * TLOC],
                            in_=x2_in[c * 128:(c + 1) * 128, :])
            lk.inc(i, "ld", 16)
        for t, srcp in [(bq, bq_in), (bk, bk_in), (bvr, bv_in)]:
            i = s.dma_start(out=t[:], in_=srcp)
            lk.inc(i, "ld", 16)

    lk.on("sync", ld0)

    def t_bv(t):
        lk.wait(t, "ld")
        lk.wait(t, "gp", bgp + 1)
        ib0, b0 = pool.acquire(t)
        i = lk.nc.tensor.matmul(b0[:, 0:512], onesr[:], bvr[:1, 0:512],
                                start=True, stop=True)
        lk.inc(i, "mm")
        ib1, b1 = pool.acquire(t)
        i = lk.nc.tensor.matmul(b1[:, 0:512], onesr[:], bvr[:1, 512:1024],
                                start=True, stop=True)
        lk.inc(i, "mm")
        t_bv.banks = [(ib0, b0), (ib1, b1)]

    lk.on("tensor", t_bv)

    def v_bv(v):
        lk.wait(v, "mm")
        (ib0, b0), (ib1, b1) = t_bv.banks
        i = v.tensor_copy(bv_rep[:, 0:512], b0[:, 0:512])
        pool.freed(i, ib0)
        i = v.tensor_copy(bv_rep[:, 512:1024], b1[:, 0:512])
        pool.freed(i, ib1)
        lk.mark(v, "ev")

    lk.on("vector", v_bv)

    for (wt_sb, b_sb, dst, wt_cnt) in [(wq, bq, qf, 16 * 8), (wk, bk, kf, 16 * 16)]:
        def evac_qk(e, bank, n, mo, w, b_sb=b_sb, dst=dst):
            return e.tensor_scalar(dst[:, n * TLOC + mo:n * TLOC + mo + w],
                                   bank[:, 0:w], b_sb[:, n:n + 1], 0.0,
                                   ALU.add, ALU.add)

        mm_block(lk, pool,
                 get_lhsT=lambda kc, n, wt_sb=wt_sb: wt_sb[:, kc * D + n * 128:kc * D + (n + 1) * 128],
                 get_rhs=lambda kc, mo, w: x2[:, kc * TLOC + mo:kc * TLOC + mo + w],
                 nK=NCH, n_list=range(NCH), m_list=MT512,
                 evac=evac_qk, evac_engine="vector",
                 pre_wait=lambda t, wc=wt_cnt: (lk.wait(t, "wt", wb + wc), lk.wait(t, "ld")))
    emit_barrier(lk, "vector", "ev")

    def evac_v(e, bank, n, mo, w):
        return e.tensor_tensor(v_full[:, n * D + mo:n * D + mo + w], bank[:, 0:w],
                               bv_rep[:, mo:mo + w], ALU.add)

    mm_block(lk, pool,
             get_lhsT=lambda kc, n: x2[:, kc * TLOC + n * 128:kc * TLOC + n * 128 + 128],
             get_rhs=lambda kc, mo, w: wv[:, kc * D + mo:kc * D + mo + w],
             nK=NCH, n_list=range(16), m_list=[(0, 512), (512, 512)],
             evac=evac_v, evac_engine="vector",
             pre_wait=lambda t: (lk.wait(t, "wt", wb + 16 * 24), lk.wait(t, "ev")))
    emit_barrier(lk, "vector", "ev")

    for ch in range(8):
        for lc in range(16):
            lo = lc * 128

            def t_pk(t, ch=ch, lo=lo):
                lk.wait(t, "ev")
                i = lk.nc.tensor.matmul(p_pk[:, 0:2 * DK],
                                        kf[:, ch * TLOC + lo:ch * TLOC + lo + 128],
                                        proj[:, 0:2 * DK],
                                        start=True, stop=True)
                lk.inc(i, "mm")

            lk.on("tensor", t_pk)

            def v_wrap(v):
                lk.wait(v, "mm")
                lk.wait(v, "sc")
                v.tensor_scalar(wsc[:], p_pk[:, 0:2 * DK], 0.125, 0.0, ALU.mult, ALU.add)
                v.tensor_scalar(csc[:], wsc[:], PI, -2 * PI, ALU.is_gt, ALU.mult)
                v.tensor_tensor(wsc[:], wsc[:], csc[:], ALU.add)
                v.tensor_scalar(csc[:], wsc[:], -PI, 2 * PI, ALU.is_lt, ALU.mult)
                i = v.tensor_tensor(wsc[:], wsc[:], csc[:], ALU.add)
                lk.inc(i, "ev")

            lk.on("vector", v_wrap)

            def s_sin(s):
                lk.wait(s, "ev")
                s.activation(aw[:], wsc[:], AF.Abs)
                s.activation(kp[:, 2 * DK:4 * DK], wsc[:], AF.Sin)
                i = s.activation(kp[:, 0:2 * DK], aw[:], AF.Sin,
                                 bias=cst[:, 0:1], scale=cst[:, 1:2])
                lk.inc(i, "sc")

            lk.on("scalar", s_sin)

            def t_kv(t, ch=ch, lc=lc):
                lk.wait(t, "sc")
                ro = (ch % 2) * 64
                co = (ch // 2) * 128
                for hh in range(2):
                    h = 2 * ch + hh
                    lk.nc.tensor.matmul(
                        kv4[2 * hh + 0][ro:ro + 64, co:co + 128],
                        v_full[:, lc * D + h * 64:lc * D + (h + 1) * 64],
                        kp[:, hh * 128:(hh + 1) * 128],
                        start=(lc == 0), stop=(lc == 15))
                    i = lk.nc.tensor.matmul(
                        kv4[2 * hh + 1][ro:ro + 64, co:co + 128],
                        v_full[:, lc * D + h * 64:lc * D + (h + 1) * 64],
                        kp[:, 2 * DK + hh * 128:2 * DK + (hh + 1) * 128],
                        start=(lc == 0), stop=(lc == 15))
                if lc == 15:
                    lk.inc(i, "mm")

            lk.on("tensor", t_kv)

    def v_kv(v):
        lk.wait(v, "mm")
        for q4 in range(4):
            i = v.tensor_copy(kvsb[:, q4 * 512:(q4 + 1) * 512], kv4[q4][:, 0:512])
        lk.mark(v, "ev")

    lk.on("vector", v_kv)

    def st0(s):
        lk.wait(s, "ev")
        for c in range(NCH):
            i = s.dma_start(out=qf_out[c * 128:(c + 1) * 128, :],
                            in_=qf[:, c * TLOC:(c + 1) * TLOC])
            lk.inc(i, "st", 16)
        for h in range(NH):
            ch, hh = h // 2, h % 2
            ro = (ch % 2) * 64
            co = (ch // 2) * 128
            i = s.dma_start(out=kvp_out[h * DH:(h + 1) * DH, 0:128],
                            in_=kvsb[ro:ro + 64, (2 * hh + 0) * 512 + co:
                                     (2 * hh + 0) * 512 + co + 128])
            lk.inc(i, "st", 16)
            i = s.dma_start(out=kvp_out[h * DH:(h + 1) * DH, 128:256],
                            in_=kvsb[ro:ro + 64, (2 * hh + 1) * 512 + co:
                                     (2 * hh + 1) * 512 + co + 128])
            lk.inc(i, "st", 16)
        s.wait_ge(lk.sems["st"], lk.c["st"])

    lk.on("sync", st0)
    lk.end_phase()



def emit_attn(lk, io, pool):
    qf_in, kvp_in, xn_in = io["qf_s"], io["kvp_o"], io["xn_s"]
    wo_in, bo_in, proj_in, id_in = io["a_wo"], io["a_bo"], io["proj"], io["ident"]
    xs_out = io["xs_s"]
    lk.begin_phase("attn")
    wb = lk.c["wt"]
    qf = lk.sb("at_qf", [128, NCH * TLOC], BF16)
    xn = lk.sb("at_xn", [128, NCH * TLOC], BF16)
    kvp_sb = lk.sb("at_kvp", [128, 8 * 256], F32)
    kvt = lk.sb("at_kvt", [128, 2 * D], BF16)
    wo = lk.sb("at_wo", [128, NCH * D], BF16)
    bo = lk.sb("at_bo", [128, NCH], F32)
    proj = lk.sb("at_proj", [128, 2 * DK], BF16)
    ident = lk.sb("at_id", [128, 128], F32)
    wsc = lk.sb("at_wsc", [128, 512], F32)
    csc = lk.sb("at_csc", [128, 512], F32)
    aw = lk.sb("at_aw", [128, 512], F32)
    qp = lk.sb("at_qp", [128, 2 * 512], BF16)
    attn_sb = lk.sb("at_attn", [128, NCH * 512], BF16)
    xs_sb = lk.sb("at_xs", [128, NCH * TLOC], BF16)
    cst = lk.sb("at_cst", [128, 2], F32)
    p_pq = lk.ps("at_ppq", [128, 512], F32)
    p_at = lk.ps("at_pat", [64, 512], F32)
    p_tr = lk.ps("at_ptr", [128, 128], F32)

    def g0(g):
        g.memset(cst[:, 0:1], PI / 2)
        g.memset(cst[:, 1:2], -1.0)
        g.memset(proj[:], 0.0)
        i = g.dma_start(out=proj[0:64, 0:DK], in_=proj_in)
        lk.inc(i, "wt", 16)
        i = g.dma_start(out=proj[64:128, DK:2 * DK], in_=proj_in)
        lk.inc(i, "wt", 16)
        load_weight_bf16(lk, g, wo, wo_in, NCH, D)

    lk.on("gpsimd", g0)

    def ld0(s):
        for c in range(NCH):
            i = s.dma_start(out=qf[:, c * TLOC:(c + 1) * TLOC],
                            in_=qf_in[c * 128:(c + 1) * 128, :])
            lk.inc(i, "ld", 16)
        for c in range(NCH):
            i = s.dma_start(out=xn[:, c * TLOC:(c + 1) * TLOC],
                            in_=xn_in[c * 128:(c + 1) * 128, 1:TLOC + 1])
            lk.inc(i, "ld", 16)
        for c in range(NCH):
            i = s.dma_start(out=kvp_sb[:, c * 256:(c + 1) * 256],
                            in_=kvp_in[c * 128:(c + 1) * 128, :])
            lk.inc(i, "ld", 16)
        i = s.dma_start(out=bo[:], in_=bo_in)
        lk.inc(i, "ld", 16)
        i = s.dma_start(out=ident[:], in_=id_in)
        lk.inc(i, "ld", 16)

    lk.on("sync", ld0)

    for c in range(NCH):
        for mc in range(2):
            def t_tr(t, c=c, mc=mc):
                if c == 0 and mc == 0:
                    lk.wait(t, "ld")
                lk.wait(t, "ev")
                i = lk.nc.tensor.transpose(p_tr[:, 0:128],
                                           kvp_sb[:, c * 256 + mc * 128:c * 256 + (mc + 1) * 128],
                                           ident[:])
                lk.inc(i, "mm")

            lk.on("tensor", t_tr)

            def v_tr(v, c=c, mc=mc):
                lk.wait(v, "mm")
                i = v.tensor_scalar(kvt[:, mc * D + c * 128:mc * D + (c + 1) * 128],
                                    p_tr[:, 0:128], 1.0 / 128.0, 0.0, ALU.mult, ALU.add)
                lk.inc(i, "ev")

            lk.on("vector", v_tr)
    emit_barrier(lk, "vector", "ev")

    for (mo, w) in MT512:
        for h in range(NH):
            ch, po = h // 2, (h % 2) * 64

            def t_pq(t, ch=ch, po=po, mo=mo, w=w):
                lk.wait(t, "ev")
                lk.wait(t, "wt", wb + 16 * 2)
                i = lk.nc.tensor.matmul(p_pq[:, 0:w],
                                        proj[:, (po // 64) * DK:(po // 64) * DK + DK],
                                        qf[:, ch * TLOC + mo:ch * TLOC + mo + w],
                                        start=True, stop=True)
                lk.inc(i, "mm")

            lk.on("tensor", t_pq)

            def v_wrap(v, w=w):
                lk.wait(v, "mm")
                lk.wait(v, "sc")
                v.tensor_scalar(wsc[:, 0:w], p_pq[:, 0:w], 0.125, 0.0, ALU.mult, ALU.add)
                v.tensor_scalar(csc[:, 0:w], wsc[:, 0:w], PI, -2 * PI, ALU.is_gt, ALU.mult)
                v.tensor_tensor(wsc[:, 0:w], wsc[:, 0:w], csc[:, 0:w], ALU.add)
                v.tensor_scalar(csc[:, 0:w], wsc[:, 0:w], -PI, 2 * PI, ALU.is_lt, ALU.mult)
                i = v.tensor_tensor(wsc[:, 0:w], wsc[:, 0:w], csc[:, 0:w], ALU.add)
                lk.inc(i, "ev")

            lk.on("vector", v_wrap)

            def s_sin(s, w=w):
                lk.wait(s, "ev")
                s.activation(aw[:, 0:w], wsc[:, 0:w], AF.Abs)
                s.activation(qp[:, 512:512 + w], wsc[:, 0:w], AF.Sin)
                i = s.activation(qp[:, 0:w], aw[:, 0:w], AF.Sin,
                                 bias=cst[:, 0:1], scale=cst[:, 1:2])
                lk.inc(i, "sc")

            lk.on("scalar", s_sin)

            def t_at(t, h=h, w=w):
                lk.wait(t, "sc")
                lk.nc.tensor.matmul(p_at[:, 0:w], kvt[:, 0 * D + h * 64:0 * D + h * 64 + 64],
                                    qp[:, 0:w], start=True, stop=False)
                i = lk.nc.tensor.matmul(p_at[:, 0:w], kvt[:, 1 * D + h * 64:1 * D + h * 64 + 64],
                                        qp[:, 512:512 + w], start=False, stop=True)
                lk.inc(i, "mm")

            lk.on("tensor", t_at)

            def v_at(v, h=h, ch=ch, po=po, w=w):
                lk.wait(v, "mm")
                i = v.tensor_copy(attn_sb[po:po + 64, ch * 512:ch * 512 + w], p_at[:, 0:w])
                lk.inc(i, "ev")

            lk.on("vector", v_at)

        def evac_o(e, bank, n, mo2, w2, mo=mo):
            return e.scalar_tensor_tensor(xs_sb[:, n * TLOC + mo:n * TLOC + mo + w2],
                                          bank[:, 0:w2], bo[:, n:n + 1],
                                          xn[:, n * TLOC + mo:n * TLOC + mo + w2],
                                          ALU.add, ALU.add)

        mm_block(lk, pool,
                 get_lhsT=lambda kc, n: wo[:, kc * D + n * 128:kc * D + (n + 1) * 128],
                 get_rhs=lambda kc, mo2, w2: attn_sb[:, kc * 512:kc * 512 + w2],
                 nK=NCH, n_list=range(NCH), m_list=[(0, w)],
                 evac=evac_o, evac_engine="vector",
                 pre_wait=lambda t, c=lk.c["ev"]: (lk.wait(t, "wt"), lk.wait(t, "ev", c)))
        emit_barrier(lk, "vector", "ev")

    def st0(s):
        lk.wait(s, "ev")
        for c in range(NCH):
            i = s.dma_start(out=xs_out[c * 128:(c + 1) * 128, :],
                            in_=xs_sb[:, c * TLOC:(c + 1) * TLOC])
            lk.inc(i, "st", 16)
        s.wait_ge(lk.sems["st"], lk.c["st"])

    lk.on("sync", st0)
    lk.end_phase()



def emit_ln2(lk, io):
    xs_in, g_in, b_in, out = io["xs_s"], io["ln2g"], io["ln2b"], io["x2b_s"]
    lk.begin_phase("ln2")
    xt = lk.sb("l2_xt", [128, NCH * 512], BF16)
    xo = lk.sb("l2_xo", [128, NCH * 512], BF16)
    sq_full = lk.sb("l2_sq", [128, NCH * 512], F32)
    tsc = lk.sb("l2_tsc", [128, 512], F32)
    gcol = lk.sb("l2_gcol", [128, NCH], F32)
    bcol = lk.sb("l2_bcol", [128, NCH], F32)
    kit = LNScratch(lk, 512, name="l2_ln")
    kit.init_consts()

    def ld0(s):
        i = s.dma_start(out=gcol[:], in_=g_in)
        lk.inc(i, "ld", 16)
        i = s.dma_start(out=bcol[:], in_=b_in)
        lk.inc(i, "ld", 16)

    lk.on("sync", ld0)

    for (mo, w) in MT512:
        def ld_fn(s, mo=mo, w=w):
            lk.wait(s, "ev")
            for c in range(NCH):
                i = s.dma_start(out=xt[:, c * w:(c + 1) * w],
                                in_=xs_in[c * 128:(c + 1) * 128, mo:mo + w])
                lk.inc(i, "ld", 16)

        lk.on("sync", ld_fn)

        def get_x(c, w=w):
            return xt[:, c * w:(c + 1) * w]

        def put_out(c, v, w=w):
            v.tensor_tensor(tsc[:, 0:w], get_x(c), kit.p_rep[:, 0:w], ALU.subtract)
            v.tensor_tensor(tsc[:, 0:w], tsc[:, 0:w], kit.p_rep[:, 512:512 + w], ALU.mult)
            v.tensor_scalar(xo[:, c * w:(c + 1) * w], tsc[:, 0:w],
                            gcol[:, c:c + 1], bcol[:, c:c + 1], ALU.mult, ALU.add)

        emit_ln(lk, kit, get_x, put_out, w, sq_full, out_war_sems=("st",))

        def st_fn(s, mo=mo, w=w):
            lk.wait(s, "ev")
            for c in range(NCH):
                i = s.dma_start(out=out[c * 128:(c + 1) * 128, mo:mo + w],
                                in_=xo[:, c * w:(c + 1) * w])
                lk.inc(i, "st", 16)

        lk.on("sync", st_fn)

    def fin(s):
        lk.wait(s, "st")

    lk.on("sync", fin)
    lk.end_phase()



def emit_ffn(lk, io, pool):
    NI = DI // 128
    x2b_in, xs_in = io["x2b_s"], io["xs_s"]
    w1_in, b1_in, w2_in, b2_in = io["f_w1"], io["f_b1"], io["f_w2"], io["f_b2"]
    s_out = io["s0_s"]
    lk.begin_phase("ffn")
    wb = lk.c["wt"]
    w1 = lk.sb("ff_w1", [128, NCH * DI], BF16)
    w2 = lk.sb("ff_w2", [128, NI * D], BF16)
    b1c = lk.sb("ff_b1c", [128, NI], F32)
    b2c = lk.sb("ff_b2c", [128, NCH], F32)
    xb = lk.sb("ff_xb", [128, NCH * 512], BF16)
    h = lk.sb("ff_h", [128, NI * 512], BF16)
    xs = lk.sb("ff_xs", [128, NCH * 512], BF16)
    s_t = lk.sb("ff_st", [128, NCH * 512], BF16)

    def g0(g):
        load_weight_bf16(lk, g, w1, w1_in, NCH, DI)
        load_weight_bf16(lk, g, w2, w2_in, NI, D)

    lk.on("gpsimd", g0)

    def ld0(s):
        i = s.dma_start(out=b1c[:], in_=b1_in)
        lk.inc(i, "ld", 16)
        i = s.dma_start(out=b2c[:], in_=b2_in)
        lk.inc(i, "ld", 16)

    lk.on("sync", ld0)

    for (mo, w) in MT512:
        def ld_t(s, mo=mo, w=w, evn=lk.c["ev"]):
            lk.wait(s, "ev", evn)
            for c in range(NCH):
                i = s.dma_start(out=xb[:, c * w:(c + 1) * w],
                                in_=x2b_in[c * 128:(c + 1) * 128, mo:mo + w])
                lk.inc(i, "ld", 16)
                i = s.dma_start(out=xs[:, c * w:(c + 1) * w],
                                in_=xs_in[c * 128:(c + 1) * 128, mo:mo + w])
                lk.inc(i, "ld", 16)

        lk.on("sync", ld_t)

        def evac_h(e, bank, n, mo2, w2_):
            return e.tensor_scalar(h[:, n * 512:n * 512 + w2_], bank[:, 0:w2_],
                                   b1c[:, n:n + 1], 0.0, ALU.add, ALU.max)

        mm_block(lk, pool,
                 get_lhsT=lambda kc, n: w1[:, kc * DI + n * 128:kc * DI + (n + 1) * 128],
                 get_rhs=lambda kc, mo2, w2_, w=w: xb[:, kc * w:kc * w + w2_],
                 nK=NCH, n_list=range(NI), m_list=[(0, w)],
                 evac=evac_h, evac_engine="vector",
                 pre_wait=lambda t, c=lk.c["mm"]: (lk.wait(t, "wt", wb + 16 * NCH),
                                                   lk.wait(t, "ld"),
                                                   lk.wait(t, "mm", c)))
        emit_barrier(lk, "vector", "ev")

        def evac_s(e, bank, n, mo2, w2_, w=w):
            return e.scalar_tensor_tensor(s_t[:, n * 512:n * 512 + w2_],
                                          bank[:, 0:w2_], b2c[:, n:n + 1],
                                          xs[:, n * w:n * w + w2_],
                                          ALU.add, ALU.add)

        mm_block(lk, pool,
                 get_lhsT=lambda kc, n: w2[:, kc * D + n * 128:kc * D + (n + 1) * 128],
                 get_rhs=lambda kc, mo2, w2_: h[:, kc * 512:kc * 512 + w2_],
                 nK=NI, n_list=range(NCH), m_list=[(0, w)],
                 evac=evac_s, evac_engine="vector",
                 pre_wait=lambda t, c=lk.c["ev"], stc=lk.c["st"]: (
                     lk.wait(t, "wt"), lk.wait(t, "ev", c), lk.wait(t, "st", stc)))
        emit_barrier(lk, "vector", "ev")

        def st_t(s, mo=mo, w=w):
            lk.wait(s, "ev")
            for c in range(NCH):
                i = s.dma_start(out=s_out[c * 128:(c + 1) * 128, mo:mo + w],
                                in_=s_t[:, c * 512:c * 512 + w])
                lk.inc(i, "st", 16)

        lk.on("sync", st_t)

    def fin(s):
        lk.wait(s, "st")

    lk.on("sync", fin)
    lk.end_phase()



def emit_rwkv(lk, io, pool):
    xn_in = io["xn_s"]
    wrt, wkt, wvt = io["r_wrt"], io["r_wkt"], io["r_wvt"]
    mur_in, muk_in = io["r_mur"], io["r_muk"]
    g_out = io["g0_s"]
    lk.begin_phase("rwkv")
    wb = lk.c["wt"]
    NT = TLOC + 2
    xn = lk.sb("rw_xn", [128, NCH * NT], BF16)
    wr = lk.sb("rw_wr", [128, NCH * D], BF16)
    wk = lk.sb("rw_wk", [128, NCH * D], BF16)
    wv = lk.sb("rw_wv", [128, NCH * D], BF16)
    mur = lk.sb("rw_mur", [128, NCH], F32)
    muk = lk.sb("rw_muk", [128, NCH], F32)
    dsc = lk.sb("rw_dsc", [128, NCH * 512], F32)
    xr = lk.sb("rw_xr", [128, NCH * 512], BF16)
    xk = lk.sb("rw_xk", [128, NCH * 512], BF16)
    rbf = lk.sb("rw_rbf", [128, NCH * 512], BF16)
    kabf = lk.sb("rw_kabf", [128, NCH * 512], BF16)
    ksc = lk.sb("rw_ksc", [128, 512], BF16)
    rwsc = lk.sb("rw_rwsc", [128, NCH * 512], F32)
    gbf = lk.sb("rw_gbf", [128, NCH * 512], BF16)

    def ld_fn(s):
        for c in range(NCH):
            i = s.dma_start(out=xn[:, c * NT:c * NT + (TLOC + 1)],
                            in_=xn_in[c * 128:(c + 1) * 128, :])
            lk.inc(i, "ld", 16)
        i = s.dma_start(out=mur[:], in_=mur_in)
        lk.inc(i, "ld", 16)
        i = s.dma_start(out=muk[:], in_=muk_in)
        lk.inc(i, "ld", 16)

    lk.on("sync", ld_fn)

    def wt_fn(g):
        load_weight_bf16(lk, g, wr, wrt, NCH, D)
        load_weight_bf16(lk, g, wk, wkt, NCH, D)
        load_weight_bf16(lk, g, wv, wvt, NCH, D)

    lk.on("gpsimd", wt_fn)

    for (mo, w) in MT512:
        mm_before_mix = lk.c["mm"]

        def v_mix(v, mo=mo, w=w, mmw=mm_before_mix):
            lk.wait(v, "ld")
            lk.wait(v, "mm", mmw)
            for c in range(NCH):
                cur = xn[:, c * NT + mo + 1: c * NT + mo + 1 + w]
                prv = xn[:, c * NT + mo: c * NT + mo + w]
                ds = dsc[:, c * 512:c * 512 + w]
                v.tensor_tensor(ds, cur, prv, ALU.subtract)
                v.scalar_tensor_tensor(xr[:, c * w:(c + 1) * w], ds,
                                       mur[:, c:c + 1], prv, ALU.mult, ALU.add)
                i = v.scalar_tensor_tensor(xk[:, c * w:(c + 1) * w], ds,
                                           muk[:, c:c + 1], prv, ALU.mult, ALU.add)
            lk.inc(i, "ev")

        lk.on("vector", v_mix)
        mix_ev = lk.c["ev"]

        def s_war(s, c=lk.c["ev"]):
            lk.wait(s, "ev", c)

        lk.on("scalar", s_war)

        def evac_r(e, bank, n, mo2, w2):
            return e.activation(rbf[:, n * 512:n * 512 + w2], bank[:, 0:w2], AF.Sigmoid)

        mm_block(lk, pool,
                 get_lhsT=lambda kc, n: wr[:, kc * D + n * 128: kc * D + (n + 1) * 128],
                 get_rhs=lambda kc, mo2, w2, w=w: xr[:, kc * w: kc * w + w2],
                 nK=NCH, n_list=range(NCH), m_list=[(0, w)],
                 evac=evac_r, evac_engine="scalar",
                 pre_wait=lambda t, me=mix_ev: (lk.wait(t, "wt", wb + 16 * 8),
                                                lk.wait(t, "ev", me)))

        def evac_k(e, bank, n, mo2, w2):
            e.tensor_scalar(ksc[:, 0:w2], bank[:, 0:w2], 0.0, 0.0, ALU.max, ALU.add)
            return e.tensor_tensor(kabf[:, n * 512:n * 512 + w2], ksc[:, 0:w2],
                                   bank[:, 0:w2], ALU.mult)

        mm_block(lk, pool,
                 get_lhsT=lambda kc, n: wk[:, kc * D + n * 128: kc * D + (n + 1) * 128],
                 get_rhs=lambda kc, mo2, w2, w=w: xk[:, kc * w: kc * w + w2],
                 nK=NCH, n_list=range(NCH), m_list=[(0, w)],
                 evac=evac_k, evac_engine="vector",
                 pre_wait=lambda t: lk.wait(t, "wt", wb + 16 * 16))
        emit_barrier(lk, "vector", "ev")
        kact_ev = lk.c["ev"]

        def evac_vo(e, bank, n, mo2, w2):
            return e.tensor_tensor(rwsc[:, n * 512:n * 512 + w2],
                                   rbf[:, n * 512:n * 512 + w2],
                                   bank[:, 0:w2], ALU.mult)

        mm_block(lk, pool,
                 get_lhsT=lambda kc, n: wv[:, kc * D + n * 128: kc * D + (n + 1) * 128],
                 get_rhs=lambda kc, mo2, w2, w=w: kabf[:, kc * w: kc * w + w2],
                 nK=NCH, n_list=range(NCH), m_list=[(0, w)],
                 evac=evac_vo, evac_engine="vector",
                 pre_wait=lambda t, ke=kact_ev: (lk.wait(t, "wt", wb + 16 * 24),
                                                 lk.wait(t, "ev", ke)))
        emit_barrier(lk, "vector", "ev")
        vo_ev = lk.c["ev"]

        def s_silu(s, mo=mo, w=w, ve=vo_ev):
            lk.wait(s, "ev", ve)
            lk.wait(s, "st")
            for c in range(NCH):
                i = s.activation(gbf[:, c * 512:c * 512 + w],
                                 rwsc[:, c * 512:c * 512 + w], AF.Silu)
            lk.inc(i, "sc")

        lk.on("scalar", s_silu)

        def st_fn(s, mo=mo, w=w, sc=lk.c["sc"]):
            lk.wait(s, "sc", sc)
            for c in range(NCH):
                i = s.dma_start(out=g_out[c * 128:(c + 1) * 128, mo:mo + w],
                                in_=gbf[:, c * 512:c * 512 + w])
                lk.inc(i, "st", 16)

        lk.on("sync", st_fn)

    def fin(s):
        lk.wait(s, "st")

    lk.on("sync", fin)
    lk.end_phase()



def emit_combine(lk, io, x_src, s_src, g_src, out_ap, out_off=0, halo_dst=None,
                 tag="cb1"):
    lk.begin_phase("comb" + tag)
    xt = [lk.sb(f"{tag}_xt{i}", [128, NCH * 512], F32) for i in range(2)]
    st_ = [lk.sb(f"{tag}_st{i}", [128, NCH * 512], BF16) for i in range(2)]
    gt = [lk.sb(f"{tag}_gt{i}", [128, NCH * 512], BF16) for i in range(2)]
    x1 = [lk.sb(f"{tag}_x1{i}", [128, NCH * 512], F32) for i in range(2)]
    tf = lk.sb(f"{tag}_tf", [128, 512], F32)
    ev_hist = []
    st_hist = []

    for k, (mo, w) in enumerate(MT512):
        sl = k % 2

        def ld_fn(s, mo=mo, w=w, sl=sl, k=k):
            if k >= 2:
                lk.wait(s, "ev", ev_hist[k - 2])
            for c in range(NCH):
                i = s.dma_start(out=xt[sl][:, c * w:(c + 1) * w],
                                in_=x_src(c, mo, w))
                lk.inc(i, "ld", 16)
                i = s.dma_start(out=st_[sl][:, c * w:(c + 1) * w],
                                in_=s_src[c * 128:(c + 1) * 128, mo:mo + w])
                lk.inc(i, "ld", 16)
                i = s.dma_start(out=gt[sl][:, c * w:(c + 1) * w],
                                in_=g_src[c * 128:(c + 1) * 128, mo:mo + w])
                lk.inc(i, "ld", 16)

        lk.on("sync", ld_fn)

        def v_fn(v, mo=mo, w=w, sl=sl, k=k):
            lk.wait(v, "ld")
            if k >= 2:
                lk.wait(v, "st", st_hist[k - 2])
            for c in range(NCH):
                v.tensor_tensor(tf[:, 0:w], st_[sl][:, c * w:(c + 1) * w],
                                gt[sl][:, c * w:(c + 1) * w], ALU.mult)
                v.tensor_tensor(x1[sl][:, c * w:(c + 1) * w], tf[:, 0:w],
                                xt[sl][:, c * w:(c + 1) * w], ALU.add)
            lk.mark(v, "ev")

        lk.on("vector", v_fn)
        ev_hist.append(lk.c["ev"])

        def st_fn(s, mo=mo, w=w, sl=sl):
            lk.wait(s, "ev")
            for c in range(NCH):
                i = s.dma_start(out=out_ap[c * 128:(c + 1) * 128,
                                           out_off + mo:out_off + mo + w],
                                in_=x1[sl][:, c * w:(c + 1) * w])
                lk.inc(i, "st", 16)
            if halo_dst is not None and mo + w == TLOC:
                for c in range(NCH):
                    i = s.dma_start(out=halo_dst[c * 128:(c + 1) * 128, 0:4],
                                    in_=x1[sl][:, c * w + w - 4:(c + 1) * w])
                    lk.inc(i, "st", 16)

        lk.on("sync", st_fn)
        st_hist.append(lk.c["st"])

    def fin(s):
        lk.wait(s, "st")

    lk.on("sync", fin)
    lk.end_phase()



def emit_halo_fix(lk, io):
    halo_o, m1_in, x1_s = io["halo_o"], io["m1c"], io["x1_s"]
    lk.begin_phase("halo")
    hsb = lk.sb("hx_hsb", [128, NCH * 4], F32)
    m1 = lk.sb("hx_m1", [128, 1], F32)

    def ld0(s):
        for c in range(NCH):
            i = s.dma_start(out=hsb[:, c * 4:(c + 1) * 4],
                            in_=halo_o[c * 128:(c + 1) * 128, 0:4])
            lk.inc(i, "ld", 16)
        i = s.dma_start(out=m1[:], in_=m1_in)
        lk.inc(i, "ld", 16)

    lk.on("sync", ld0)

    def v0(v):
        lk.wait(v, "ld")
        v.tensor_scalar(hsb[:], hsb[:], m1[:, 0:1], 0.0, ALU.mult, ALU.add)
        lk.mark(v, "ev")

    lk.on("vector", v0)

    def st0(s):
        lk.wait(s, "ev")
        for c in range(NCH):
            i = s.dma_start(out=x1_s[c * 128:(c + 1) * 128, 0:3],
                            in_=hsb[:, c * 4 + 1:(c + 1) * 4])
            lk.inc(i, "st", 16)
        s.wait_ge(lk.sems["st"], lk.c["st"])

    lk.on("sync", st0)
    lk.end_phase()



def emit_l1front(lk, io, pool):
    EXT = TLOC + 3
    x1_in = io["x1_s"]
    win_in, cw_in, cb_in = io["l_winT"], io["l_convw"], io["l_convb"]
    xp_in, wdt_in, bdt_in = io["l_xprojT"], io["l_wdtT"], io["l_bdt"]
    xn1_out, u_out, dt_out, z_out = io["xn1_s"], io["u_s"], io["dt_s"], io["z_s"]
    B_out, C_out = io["Bs_s"], io["Cs_s"]
    lk.begin_phase("l1f")
    bnt = lk.c["nt"]
    bnf = lk.c["nf"]
    NTE = EXT + 1
    xt = lk.sb("lf_xt", [128, NCH * 512], F32)
    xn1 = lk.sb("lf_xn1", [128, NCH * NTE], BF16)
    sq_full = lk.sb("lf_sq", [128, NCH * 512], F32)
    nring = lk.sb("lf_nring", [128, 2 * NCH * 128], BF16)
    xi = lk.sb("lf_xi", [128, NDC * NTE], BF16)
    u_t = lk.sb("lf_ut", [128, NDC * 512], BF16)
    z_t = lk.sb("lf_zt", [128, NDC * 512], BF16)
    dt_t = lk.sb("lf_dtt", [128, NDC * 512], BF16)
    cw = lk.sb("lf_cw", [128, NDC * 4], F32)
    cb = lk.sb("lf_cb", [128, NDC], F32)
    bdt = lk.sb("lf_bdt", [128, NDC], F32)
    xp = lk.sb("lf_xp", [128, NDC * 96], BF16)
    wdt = lk.sb("lf_wdt", [DTR, DN], BF16)
    dtr_bf = lk.sb("lf_dtrbf", [DTR, 512], BF16)
    bc_sb = lk.sb("lf_bc", [DS, 2 * 512], F32)
    asc = lk.sb("lf_asc", [128, 512], F32)
    csc = lk.sb("lf_csc", [128, 512], F32)
    esc = lk.sb("lf_esc", [128, 512], F32)
    cst = lk.sb("lf_cstf", [128, 2], F32)
    kit = LNScratch(lk, 512, name="lf_ln")
    p_dtr = lk.ps("lf_pdtr", [DTR, 512], F32)
    p_bc = lk.ps("lf_pbc", [128, 512], F32)
    kit.init_consts()

    def g0(g):
        g.memset(cst[:, 0:1], -1.0)
        g.memset(cst[:, 1:2], 1.0)
        load_weight_bf16(lk, g, xp, xp_in, NDC, 96)
        i = g.dma_start(out=wdt[:], in_=wdt_in)
        lk.inc(i, "wt", 16)

    lk.on("gpsimd", g0)

    def ld0(s):
        for t, srcp in [(cw, cw_in), (cb, cb_in), (bdt, bdt_in)]:
            i = s.dma_start(out=t[:], in_=srcp)
            lk.inc(i, "ld", 16)

    lk.on("sync", ld0)

    widths = [259] + _tile_widths(EXT - 259, 512)
    t0 = 0
    for w in widths:
        o = t0

        def ld_fn(s, o=o, w=w):
            lk.wait(s, "ev")
            for c in range(NCH):
                i = s.dma_start(out=xt[:, c * w:(c + 1) * w],
                                in_=x1_in[c * 128:(c + 1) * 128, o:o + w])
                lk.inc(i, "ld", 16)

        lk.on("sync", ld_fn)

        def get_x(c, w=w):
            return xt[:, c * w:(c + 1) * w]

        def put_out(c, v, o=o, w=w):
            tmp = sq_full[:, c * w:(c + 1) * w]
            v.tensor_tensor(tmp, get_x(c), kit.p_rep[:, 0:w], ALU.subtract)
            v.tensor_tensor(xn1[:, c * NTE + o:c * NTE + o + w], tmp,
                            kit.p_rep[:, 512:512 + w], ALU.mult)

        emit_ln(lk, kit, get_x, put_out, w, sq_full, out_war_sems=("st",))

        def st_fn(s, o=o, w=w):
            if o + w > 3:
                lk.wait(s, "ev")
                so = max(o, 3)
                for c in range(NCH):
                    i = s.dma_start(out=xn1_out[c * 128:(c + 1) * 128, so - 3:o + w - 3],
                                    in_=xn1[:, c * NTE + so:c * NTE + o + w])
                    lk.inc(i, "st", 16)

        lk.on("sync", st_fn)
        t0 += w
    xn1_ev = lk.c["ev"]

    n_loads = [0]

    def stream_n(n):
        def g_n(g, n=n):
            uses = n_loads[0]
            if uses >= 2:
                g.wait_ge(lk.sems["nf"], bnf + uses - 1)
            for kc in range(NCH):
                i = g.dma_start(out=nring[:, (n % 2) * (NCH * 128) + kc * 128:
                                          (n % 2) * (NCH * 128) + (kc + 1) * 128],
                                in_=win_in[kc * 128:(kc + 1) * 128, n * 128:(n + 1) * 128])
                lk.inc(i, "nt", 16)
            n_loads[0] += 1

        lk.on("gpsimd", g_n)

    XI_M = [(0, 259), (259, 512), (771, 512), (1283, 512), (1795, 256)]
    Z_M = [(3, 512), (515, 512), (1027, 512), (1539, 512)]

    for n in range(32):
        stream_n(n)
        is_xi = n < 16
        mlist = XI_M if is_xi else Z_M

        def t_xz(t, n=n, mlist=mlist, nl=n_loads[0], ev_need=xn1_ev):
            lk.wait(t, "ev", ev_need)
            t.wait_ge(lk.sems["nt"], bnt + 16 * 8 * nl)
            for gi_, (mo, w) in enumerate(mlist):
                ib, bank = pool.acquire(t)
                for kc in range(NCH):
                    i = lk.nc.tensor.matmul(
                        bank[:, 0:w],
                        nring[:, (n % 2) * (NCH * 128) + kc * 128:(n % 2) * (NCH * 128) + (kc + 1) * 128],
                        xn1[:, kc * NTE + mo:kc * NTE + mo + w],
                        start=(kc == 0), stop=(kc == NCH - 1))
                lk.inc(i, "mm")
                t_xz.banks.append((ib, bank, mo, w, lk.c["mm"]))

        t_xz.banks = []
        lk.on("tensor", t_xz)

        def v_xz(v, n=n, is_xi=is_xi, banks=t_xz.banks, stc=lk.c["st"]):
            if not is_xi:
                lk.wait(v, "st", stc)
            for (ib, bank, mo, w, mmc) in banks:
                lk.wait(v, "mm", mmc)
                if is_xi:
                    i = v.tensor_copy(xi[:, n * NTE + mo:n * NTE + mo + w], bank[:, 0:w])
                else:
                    i = v.tensor_copy(z_t[:, mo - 3:mo - 3 + w], bank[:, 0:w])
                pool.freed(i, ib)
            i = v.tensor_copy(lk.dummy[:1, 0:8], lk.dummy[:1, 8:16])
            i.then_inc(lk.sems["nf"], 1)
            lk.c["nf"] += 1
            lk.mark(v, "ev")

        lk.on("vector", v_xz)

        if not is_xi:
            def st_z(s, n=n):
                lk.wait(s, "ev")
                i = s.dma_start(out=z_out[(n - 16) * 128:(n - 15) * 128, :],
                                in_=z_t[:, 0:TLOC])
                lk.inc(i, "st", 16)

            lk.on("sync", st_z)

    xi_ev = lk.c["ev"]

    for mt, (mo, w) in enumerate(MT512):
        for ci in range(NDC):
            def v_conv(v, ci=ci, mo=mo, w=w, xe=xi_ev, stc=lk.c["st"], scc=lk.c["sc"]):
                lk.wait(v, "ev", xe)
                lk.wait(v, "st", stc)
                lk.wait(v, "sc", scc)
                base = ci * NTE + mo
                v.tensor_scalar(csc[:, 0:w], xi[:, base:base + w],
                                cw[:, ci * 4:ci * 4 + 1], 0.0, ALU.mult, ALU.add)
                v.scalar_tensor_tensor(esc[:, 0:w], xi[:, base + 1:base + 1 + w],
                                       cw[:, ci * 4 + 1:ci * 4 + 2], csc[:, 0:w],
                                       ALU.mult, ALU.add)
                v.scalar_tensor_tensor(csc[:, 0:w], xi[:, base + 2:base + 2 + w],
                                       cw[:, ci * 4 + 2:ci * 4 + 3], esc[:, 0:w],
                                       ALU.mult, ALU.add)
                i = v.scalar_tensor_tensor(esc[:, 0:w], xi[:, base + 3:base + 3 + w],
                                           cw[:, ci * 4 + 3:ci * 4 + 4], csc[:, 0:w],
                                           ALU.mult, ALU.add)
                lk.inc(i, "ev")

            lk.on("vector", v_conv)

            def s_u(s, ci=ci, w=w):
                lk.wait(s, "ev")
                i = s.activation(u_t[:, ci * 512:ci * 512 + w], esc[:, 0:w],
                                 AF.Silu, bias=cb[:, ci:ci + 1])
                lk.inc(i, "sc")

            lk.on("scalar", s_u)

        u_sc = lk.c["sc"]

        def t_xp(t, mo=mo, w=w, us=u_sc):
            lk.wait(t, "sc", us)
            lk.wait(t, "ev")
            for kc in range(NDC):
                lk.nc.tensor.matmul(p_dtr[:, 0:w], xp[:, kc * 96:kc * 96 + 64],
                                    u_t[:, kc * 512:kc * 512 + w],
                                    start=(kc == 0), stop=(kc == NDC - 1))
            for kc in range(NDC):
                lk.nc.tensor.matmul(p_bc[0:16, 0:w], xp[:, kc * 96 + 64:kc * 96 + 80],
                                    u_t[:, kc * 512:kc * 512 + w],
                                    start=(kc == 0), stop=(kc == NDC - 1))
            for kc in range(NDC):
                i = lk.nc.tensor.matmul(p_bc[64:80, 0:w], xp[:, kc * 96 + 80:kc * 96 + 96],
                                        u_t[:, kc * 512:kc * 512 + w],
                                        start=(kc == 0), stop=(kc == NDC - 1))
            lk.inc(i, "mm")

        lk.on("tensor", t_xp)

        def v_xp(v, mo=mo, w=w):
            lk.wait(v, "mm")
            v.tensor_copy(dtr_bf[:, 0:w], p_dtr[:, 0:w])
            v.tensor_copy(bc_sb[:, 0:w], p_bc[0:16, 0:w])
            i = v.tensor_copy(bc_sb[:, 512:512 + w], p_bc[64:80, 0:w])
            lk.mark(v, "ev")

        lk.on("vector", v_xp)

        def st_bc(s, mo=mo, w=w):
            lk.wait(s, "ev")
            i = s.dma_start(out=B_out[:, mo:mo + w], in_=bc_sb[:, 0:w])
            lk.inc(i, "st", 16)
            i = s.dma_start(out=C_out[:, mo:mo + w], in_=bc_sb[:, 512:512 + w])
            lk.inc(i, "st", 16)
            for ci in range(NDC):
                i = s.dma_start(out=u_out[ci * 128:(ci + 1) * 128, mo:mo + w],
                                in_=u_t[:, ci * 512:ci * 512 + w])
                lk.inc(i, "st", 16)

        lk.on("sync", st_bc)

        for ci in range(NDC):
            def t_dt(t, ci=ci, w=w, ec=lk.c["ev"]):
                lk.wait(t, "ev", ec)
                ib, bank = pool.acquire(t)
                i = lk.nc.tensor.matmul(bank[:, 0:w], wdt[:, ci * 128:(ci + 1) * 128],
                                        dtr_bf[:, 0:w], start=True, stop=True)
                lk.inc(i, "mm")
                t_dt.bk = (ib, bank)

            lk.on("tensor", t_dt)
            ib, bank = t_dt.bk

            def v_dt1(v, ci=ci, w=w, ib=ib, bank=bank, scc=lk.c["sc"]):
                lk.wait(v, "mm")
                lk.wait(v, "sc", scc)
                i = v.tensor_scalar(asc[:, 0:w], bank[:, 0:w], bdt[:, ci:ci + 1], 0.0,
                                    ALU.add, ALU.add)
                pool.freed(i, ib)
                lk.mark(v, "ev")

            lk.on("vector", v_dt1)

            def s_dt(s, ci=ci, w=w):
                lk.wait(s, "ev")
                s.activation(csc[:, 0:w], asc[:, 0:w], AF.Abs)
                s.activation(csc[:, 0:w], csc[:, 0:w], AF.Exp, scale=cst[:, 0:1])
                i = s.activation(csc[:, 0:w], csc[:, 0:w], AF.Ln, bias=cst[:, 1:2])
                lk.inc(i, "sc")

            lk.on("scalar", s_dt)

            def v_dt2(v, ci=ci, w=w):
                lk.wait(v, "sc")
                v.tensor_scalar(esc[:, 0:w], asc[:, 0:w], 0.0, 0.0, ALU.max, ALU.add)
                i = v.tensor_tensor(dt_t[:, ci * 512:ci * 512 + w], esc[:, 0:w],
                                    csc[:, 0:w], ALU.add)
                lk.inc(i, "ev")

            lk.on("vector", v_dt2)

        dt_ev = lk.c["ev"]

        def st_dt(s, mo=mo, w=w, de=dt_ev):
            lk.wait(s, "ev", de)
            for ci in range(NDC):
                i = s.dma_start(out=dt_out[ci * 128:(ci + 1) * 128, mo:mo + w],
                                in_=dt_t[:, ci * 512:ci * 512 + w])
                lk.inc(i, "st", 16)

        lk.on("sync", st_dt)

    def fin(s):
        lk.wait(s, "st")

    lk.on("sync", fin)
    lk.end_phase()



def emit_scan(lk, io, pool):
    NCB = 8
    NSEG = 8
    SG = T // NSEG
    dtG, uG, BsG, CsG = io["dtG"], io["uG"], io["BsG"], io["CsG"]
    al_in, m0_in, m1_in = io["s_alog"], io["m0c"], io["m1c"]
    y_out = io["y_s"]
    lk.begin_phase("scan")
    bld = lk.c["ld"]
    bgp = lk.c["gp"]
    dt_s = lk.sb("sn_dt", [128, NCB * SG], BF16)
    u_s = lk.sb("sn_u", [128, NCB * SG], BF16)
    dtA = lk.sb("sn_dtA", [128, 4 * SG], BF16)
    dtB = lk.sb("sn_dtB", [128, 4 * SG], BF16)
    uA = lk.sb("sn_uA", [128, 4 * SG], BF16)
    uB = lk.sb("sn_uB", [128, 4 * SG], BF16)
    dtu = lk.sb("sn_dtu", [128, NCB * SG], F32)
    brow = lk.sb("sn_brow", [1, DS * 512], F32)
    crow = lk.sb("sn_crow", [1, DS * 512], F32)
    asb = lk.sb("sn_asb", [128, NCB * DS], F32)
    m0 = lk.sb("sn_m0", [128, 1], F32)
    m1 = lk.sb("sn_m1", [128, 1], F32)
    abuf = lk.sb("sn_abuf", [128, 2 * NCB * SG], F32)
    bbuf = lk.sb("sn_bbuf", [128, NCB * SG], F32)
    hbuf = lk.sb("sn_hbuf", [128, NCB * SG], F32)
    tmp = bbuf
    brep = lk.sb("sn_brep", [128, SG], F32)
    crep = lk.sb("sn_crep", [128, SG], F32)
    hst = lk.sb("sn_hst", [128, NCB * DS * 2], F32)
    yacc = lk.sb("sn_yacc", [128, NCB * SG], F32)
    ybf = lk.sb("sn_ybf", [128, NCB * SG], BF16)
    onesr = lk.sb("sn_onesr", [1, 128], F32)

    def g0(g):
        g.memset(onesr[:], 1.0)
        i = g.memset(hst[:], 0.0)
        lk.inc(i, "gp")

    lk.on("gpsimd", g0)

    def ld0(s):
        i = s.dma_start(out=asb[:], in_=al_in)
        lk.inc(i, "ld", 16)
        i = s.dma_start(out=m0[:], in_=m0_in)
        lk.inc(i, "ld", 16)
        i = s.dma_start(out=m1[:], in_=m1_in)
        lk.inc(i, "ld", 16)

    lk.on("sync", ld0)

    def s_a(s):
        lk.wait(s, "ld", bld + 16)
        i = s.activation(asb[:], asb[:], AF.Exp)
        lk.inc(i, "sc")

    lk.on("scalar", s_a)

    def v_a(v):
        lk.wait(v, "sc")
        i = v.tensor_scalar(asb[:], asb[:], -1.0, 0.0, ALU.mult, ALU.add)
        lk.mark(v, "ev")

    lk.on("vector", v_a)
    a_ev = lk.c["ev"]

    it = 0
    ev_after_iter = []
    SGA = NCB * SG

    for seg in range(NSEG):
        so = seg * SG
        blk_ = so // TLOC
        co = so % TLOC

        for wv in range(2):
            def ld_wave(s, blk_=blk_, co=co, wv=wv, evn=lk.c["ev"], bln=lk.c["bl"]):
                lk.wait(s, "ev", evn)
                lk.wait(s, "bl", bln)
                for ci in range(4):
                    cb = wv * 4 + ci
                    for dst, src, off in ((dtA, dtG, 0), (uA, uG, 0),
                                          (dtB, dtG, 1024), (uB, uG, 1024)):
                        i = s.dma_start(out=dst[:, ci * SG:(ci + 1) * SG],
                                        in_=src[blk_ * DN + off + cb * 128:
                                                blk_ * DN + off + (cb + 1) * 128,
                                                co:co + SG])
                        lk.inc(i, "ld", 16)
                if wv == 0:
                    for si in range(DS):
                        i = s.dma_start(out=brow[:1, si * SG:(si + 1) * SG],
                                        in_=BsG[blk_ * DS + si:blk_ * DS + si + 1,
                                                co:co + SG])
                        lk.inc(i, "ld", 16)
                        i = s.dma_start(out=crow[:1, si * SG:(si + 1) * SG],
                                        in_=CsG[blk_ * DS + si:blk_ * DS + si + 1,
                                                co:co + SG])
                        lk.inc(i, "ld", 16)

            lk.on("sync", ld_wave)

            def g_blend(g, wv=wv, evn=lk.c["ev"]):
                lk.wait(g, "ld")
                lk.wait(g, "ev", evn)
                o = wv * 4 * SG
                W4 = 4 * SG
                g.tensor_scalar(dt_s[:, o:o + W4], dtA[:], m0[:, 0:1], 0.0,
                                ALU.mult, ALU.add)
                g.scalar_tensor_tensor(dt_s[:, o:o + W4], dtB[:], m1[:, 0:1],
                                       dt_s[:, o:o + W4], ALU.mult, ALU.add)
                g.tensor_scalar(u_s[:, o:o + W4], uA[:], m0[:, 0:1], 0.0,
                                ALU.mult, ALU.add)
                g.scalar_tensor_tensor(u_s[:, o:o + W4], uB[:], m1[:, 0:1],
                                       u_s[:, o:o + W4], ALU.mult, ALU.add)
                lk.mark(g, "bl")

            lk.on("vector", g_blend)
        bl_done = lk.c["bl"]

        def v_dtu(v, stc=lk.c["st"], bld_=bl_done):
            lk.wait(v, "bl", bld_)
            lk.wait(v, "gp", bgp + 1)
            lk.wait(v, "st", stc)
            i = v.tensor_tensor(dtu[:], dt_s[:], u_s[:], ALU.mult)
            lk.mark(v, "ev")

        lk.on("vector", v_dtu)

        dtu3 = dtu.ap().rearrange("p (c t) -> p c t", c=NCB)
        bb3 = bbuf.ap().rearrange("p (c t) -> p c t", c=NCB)
        hb3 = hbuf.ap().rearrange("p (c t) -> p c t", c=NCB)
        ya3 = yacc.ap().rearrange("p (c t) -> p c t", c=NCB)
        tm3 = tmp.ap().rearrange("p (c t) -> p c t", c=NCB)
        hst4 = hst.ap().rearrange("p (c s u) -> p c s u", c=NCB, u=2)

        for si in range(DS):
            def t_rep(t, si=si):
                lk.wait(t, "ld")
                lk.wait(t, "gp", bgp + 1)
                ib0, b0 = pool.acquire(t)
                i = lk.nc.tensor.matmul(b0[:, 0:SG], onesr[:],
                                        brow[:1, si * SG:(si + 1) * SG],
                                        start=True, stop=True)
                lk.inc(i, "mm")
                ib1, b1 = pool.acquire(t)
                i = lk.nc.tensor.matmul(b1[:, 0:SG], onesr[:],
                                        crow[:1, si * SG:(si + 1) * SG],
                                        start=True, stop=True)
                lk.inc(i, "mm")
                t_rep.banks = [(ib0, b0), (ib1, b1)]

            lk.on("tensor", t_rep)
            (ib0, b0), (ib1, b1) = t_rep.banks

            def v_rep(v, ib0=ib0, b0=b0, ib1=ib1, b1=b1):
                lk.wait(v, "mm")
                i = v.tensor_copy(brep[:], b0[:, 0:SG])
                pool.freed(i, ib0)
                i = v.tensor_copy(crep[:], b1[:, 0:SG])
                pool.freed(i, ib1)

            lk.on("vector", v_rep)

            slot = it % 2

            def s_agen(s, si=si, slot=slot, it=it, bld_=bl_done):
                if it >= 2:
                    lk.wait(s, "ev", ev_after_iter[it - 2])
                else:
                    lk.wait(s, "ev", a_ev)
                lk.wait(s, "bl", bld_)
                for cb in range(NCB):
                    i = s.activation(abuf[:, slot * SGA + cb * SG:slot * SGA + (cb + 1) * SG],
                                     dt_s[:, cb * SG:(cb + 1) * SG], AF.Exp,
                                     scale=asb[:, cb * DS + si:cb * DS + si + 1])
                lk.inc(i, "sc")

            lk.on("scalar", s_agen)

            def v_scan(v, si=si, slot=slot, scc=lk.c["sc"]):
                lk.wait(v, "sc", scc)
                lk.wait(v, "gp", bgp + 1)
                brep3 = brep.ap().unsqueeze(1).broadcast_to([128, NCB, SG])
                crep3 = crep.ap().unsqueeze(1).broadcast_to([128, NCB, SG])
                v.tensor_tensor(bb3, dtu3, brep3, ALU.mult)
                for cb in range(NCB):
                    v.tensor_tensor_scan(hbuf[:, cb * SG:(cb + 1) * SG],
                                         abuf[:, slot * SGA + cb * SG:slot * SGA + (cb + 1) * SG],
                                         bbuf[:, cb * SG:(cb + 1) * SG],
                                         hst[:, 2 * (cb * DS + si) + 1:2 * (cb * DS + si) + 2],
                                         ALU.mult, ALU.add)
                if si == 0:
                    v.tensor_tensor(ya3, hb3, crep3, ALU.mult)
                else:
                    v.tensor_tensor(tm3, hb3, crep3, ALU.mult)
                    v.tensor_tensor(yacc[:], yacc[:], tmp[:], ALU.add)
                i = v.tensor_copy(hst4[:, :, si, :], hb3[:, :, SG - 2:SG])
                lk.inc(i, "ev")

            lk.on("vector", v_scan)
            ev_after_iter.append(lk.c["ev"])
            it += 1

        def v_y(v, stc=lk.c["st"]):
            lk.wait(v, "st", stc)
            for cb in range(NCB):
                i = v.tensor_copy(ybf[:, cb * SG:(cb + 1) * SG],
                                  yacc[:, cb * SG:(cb + 1) * SG])
            lk.mark(v, "ev")

        lk.on("vector", v_y)

        def st_y(s, so=so):
            lk.wait(s, "ev")
            for cb in range(NCB):
                i = s.dma_start(out=y_out[cb * 128:(cb + 1) * 128, so:so + SG],
                                in_=ybf[:, cb * SG:(cb + 1) * SG])
                lk.inc(i, "st", 16)

        lk.on("sync", st_y)

    def fin(s):
        lk.wait(s, "st")

    lk.on("sync", fin)
    lk.end_phase()



def emit_gate_s1(lk, io, pool):
    yG = io["yG"]
    u_in, z_in, xn1_in = io["u_s"], io["z_s"], io["xn1_s"]
    dsk_in, wout_in = io["g_dsk"], io["g_woutT"]
    gw_in, gb_in, id_in = io["g_gwp"], io["g_gb"], io["ident"]
    m0_in, m1_in = io["m0c"], io["m1c"]
    s1_out = io["s1_s"]
    oh1_out, oh2_out = io["oh1_s"], io["oh2_s"]
    v1_out, v2_out = io["v1_s"], io["v2_s"]
    lk.begin_phase("gate")
    lb = lk.c["ld"]
    wb = lk.c["wt"]
    bgp = lk.c["gp"]
    yA = lk.sb("gs_yA", [128, NDC * 512], BF16)
    yB = lk.sb("gs_yB", [128, NDC * 512], BF16)
    yt = lk.sb("gs_yt", [128, NDC * 512], BF16)
    ut = lk.sb("gs_ut", [128, NDC * 512], BF16)
    zt = lk.sb("gs_zt", [128, NDC * 512], BF16)
    zs = lk.sb("gs_zs", [128, 512], BF16)
    ydt = lk.sb("gs_ydt", [128, 512], F32)
    yg = lk.sb("gs_yg", [128, NDC * 512], BF16)
    wout = lk.sb("gs_wout", [128, NDC * D], BF16)
    dsk = lk.sb("gs_dsk", [128, NDC], F32)
    s1t = lk.sb("gs_s1t", [128, NCH * 512], BF16)
    xn1 = lk.sb("gs_xn1", [128, NCH * TLOC], BF16)
    gwsb = lk.sb("gs_gwp", [128, NCH * E], BF16)
    gbrep = lk.sb("gs_gbrep", [128, E], F32)
    gbrow = lk.sb("gs_gbrow", [1, E], F32)
    ident = lk.sb("gs_ident", [128, 128], F32)
    onesr = lk.sb("gs_onesr", [1, 128], F32)
    m0 = lk.sb("gs_m0", [128, 1], F32)
    m1 = lk.sb("gs_m1", [128, 1], F32)
    gla = lk.sb("gs_gla", [128, 16 * E], F32)
    oh1a = lk.sb("gs_oh1a", [128, 16 * E], F32)
    oh2a = lk.sb("gs_oh2a", [128, 16 * E], F32)
    mska = lk.sb("gs_mska", [128, 16 * E], F32)
    glm16 = lk.sb("gs_glm16", [128, 16], F32)
    va = lk.sb("gs_va", [128, 32], F32)
    ohT1 = lk.sb("gs_ohT1", [8, TLOC], F32)
    ohT2 = lk.sb("gs_ohT2", [8, TLOC], F32)
    vT = lk.sb("gs_vT", [2, TLOC], F32)
    p_g = lk.ps("gs_pg", [128, 8], F32)
    p_t1 = lk.ps("gs_pt1", [8, 128], F32)
    p_t2 = lk.ps("gs_pt2", [8, 128], F32)
    p_t3 = lk.ps("gs_pt3", [2, 128], F32)
    p_gb = lk.ps("gs_pgb", [128, E], F32)

    def g0(g):
        i = g.memset(onesr[:], 1.0)
        lk.inc(i, "gp")
        load_weight_bf16(lk, g, wout, wout_in, NDC, D)
        i = g.dma_start(out=gwsb[:], in_=gw_in)
        lk.inc(i, "wt", 16)

    lk.on("gpsimd", g0)

    def ld0(s):
        for t, srcp in [(dsk, dsk_in), (gbrow, gb_in), (ident, id_in)]:
            i = s.dma_start(out=t[:], in_=srcp)
            lk.inc(i, "ld", 16)
        i = s.dma_start(out=m0[:], in_=m0_in)
        lk.inc(i, "ld", 16)
        i = s.dma_start(out=m1[:], in_=m1_in)
        lk.inc(i, "ld", 16)
        for c in range(NCH):
            i = s.dma_start(out=xn1[:, c * TLOC:(c + 1) * TLOC],
                            in_=xn1_in[c * 128:(c + 1) * 128, :])
            lk.inc(i, "ld", 16)

    lk.on("sync", ld0)

    def t_gb(t):
        lk.wait(t, "ld", lb + 48)
        lk.wait(t, "gp", bgp + 1)
        i = lk.nc.tensor.matmul(p_gb[:, 0:E], onesr[:], gbrow[:1, 0:E],
                                start=True, stop=True)
        lk.inc(i, "mm")

    lk.on("tensor", t_gb)

    def v_gb(v):
        lk.wait(v, "mm")
        i = v.tensor_copy(gbrep[:], p_gb[:, 0:E])
        lk.mark(v, "ev")

    lk.on("vector", v_gb)

    for (mo, w) in MT512:
        def ld_t(s, mo=mo, w=w, evc=lk.c["ev"], bln=lk.c["bl"]):
            lk.wait(s, "ev", evc)
            lk.wait(s, "bl", bln)
            for ci in range(NDC):
                i = s.dma_start(out=yA[:, ci * 512:ci * 512 + w],
                                in_=yG[ci * 128:(ci + 1) * 128, mo:mo + w])
                lk.inc(i, "ld", 16)
                i = s.dma_start(out=yB[:, ci * 512:ci * 512 + w],
                                in_=yG[ci * 128:(ci + 1) * 128, TLOC + mo:TLOC + mo + w])
                lk.inc(i, "ld", 16)
                i = s.dma_start(out=ut[:, ci * 512:ci * 512 + w],
                                in_=u_in[ci * 128:(ci + 1) * 128, mo:mo + w])
                lk.inc(i, "ld", 16)
                i = s.dma_start(out=zt[:, ci * 512:ci * 512 + w],
                                in_=z_in[ci * 128:(ci + 1) * 128, mo:mo + w])
                lk.inc(i, "ld", 16)

        lk.on("sync", ld_t)

        def g_blend(g, w=w, evc=lk.c["ev"]):
            lk.wait(g, "ld")
            lk.wait(g, "ev", evc)
            for ci in range(NDC):
                sl = slice(ci * 512, ci * 512 + w)
                g.tensor_scalar(yt[:, sl], yA[:, sl], m0[:, 0:1], 0.0, ALU.mult, ALU.add)
                g.scalar_tensor_tensor(yt[:, sl], yB[:, sl], m1[:, 0:1], yt[:, sl],
                                       ALU.mult, ALU.add)
            lk.mark(g, "bl")

        lk.on("vector", g_blend)
        bl_done = lk.c["bl"]

        for ci in range(NDC):
            def v_yd(v, ci=ci, w=w, scc=lk.c["sc"], bld_=bl_done):
                lk.wait(v, "ld")
                lk.wait(v, "bl", bld_)
                lk.wait(v, "sc", scc)
                i = v.scalar_tensor_tensor(ydt[:, 0:w], ut[:, ci * 512:ci * 512 + w],
                                           dsk[:, ci:ci + 1], yt[:, ci * 512:ci * 512 + w],
                                           ALU.mult, ALU.add)
                lk.inc(i, "ev")

            lk.on("vector", v_yd)

            def s_z(s, ci=ci, w=w):
                lk.wait(s, "ev")
                i = s.activation(zs[:, 0:w], zt[:, ci * 512:ci * 512 + w], AF.Silu)
                lk.inc(i, "sc")

            lk.on("scalar", s_z)

            def v_yg(v, ci=ci, w=w):
                lk.wait(v, "sc")
                i = v.tensor_tensor(yg[:, ci * 512:ci * 512 + w], ydt[:, 0:w],
                                    zs[:, 0:w], ALU.mult)
                lk.inc(i, "ev")

            lk.on("vector", v_yg)

        def evac_s1(e, bank, n, mo2, w2):
            return e.tensor_copy(s1t[:, n * 512:n * 512 + w2], bank[:, 0:w2])

        mm_block(lk, pool,
                 get_lhsT=lambda kc, n: wout[:, kc * D + n * 128:kc * D + (n + 1) * 128],
                 get_rhs=lambda kc, mo2, w2: yg[:, kc * 512:kc * 512 + w2],
                 nK=NDC, n_list=range(NCH), m_list=[(0, w)],
                 evac=evac_s1, evac_engine="vector",
                 pre_wait=lambda t, c=lk.c["ev"]: (lk.wait(t, "wt", wb + 16 * 16),
                                                   lk.wait(t, "ev", c)))
        emit_barrier(lk, "vector", "ev")

        def st_s1(s, mo=mo, w=w):
            lk.wait(s, "ev")
            for c in range(NCH):
                i = s.dma_start(out=s1_out[c * 128:(c + 1) * 128, mo:mo + w],
                                in_=s1t[:, c * 512:c * 512 + w])
                lk.inc(i, "st", 16)

        lk.on("sync", st_s1)

    for lc in range(16):
        lo = lc * 128

        def t_gl(t, lo=lo, evc=lk.c["ev"]):
            lk.wait(t, "ev", evc)
            lk.wait(t, "wt", wb + 16 * 17)
            for kc in range(NCH):
                i = lk.nc.tensor.matmul(p_g[:, 0:E],
                                        xn1[:, kc * TLOC + lo:kc * TLOC + lo + 128],
                                        gwsb[:, kc * E:(kc + 1) * E],
                                        start=(kc == 0), stop=(kc == NCH - 1))
            lk.inc(i, "mm")

        lk.on("tensor", t_gl)

        def v_ev(v, lc=lc):
            lk.wait(v, "mm")
            i = v.tensor_tensor(gla[:, lc * E:(lc + 1) * E], p_g[:, 0:E], gbrep[:],
                                ALU.add)
            lk.inc(i, "ev")

        lk.on("vector", v_ev)

    gl3 = gla.ap().rearrange("p (l e) -> p l e", e=E)
    oh13 = oh1a.ap().rearrange("p (l e) -> p l e", e=E)
    oh23 = oh2a.ap().rearrange("p (l e) -> p l e", e=E)
    msk3 = mska.ap().rearrange("p (l e) -> p l e", e=E)
    va3 = va.ap().rearrange("p (l k) -> p l k", k=2)
    m1b = va3[:, :, 0:1].broadcast_to([128, 16, E])
    m2b = va3[:, :, 1:2].broadcast_to([128, 16, E])
    g16b = glm16.ap().unsqueeze(2).broadcast_to([128, 16, E])

    def v_b1(v):
        sp = lambda: v.tensor_copy(ydt[:, 0:512], ydt[:, 0:512])
        v.reduce_max(glm16[:], gl3, axis=AX)
        sp()
        i = v.tensor_tensor(gl3, gl3, g16b, ALU.subtract)
        lk.inc(i, "ev")

    lk.on("vector", v_b1)

    def s_b1(s):
        lk.wait(s, "ev")
        i = s.activation(gla[:], gla[:], AF.Exp)
        lk.inc(i, "sc")

    lk.on("scalar", s_b1)

    def v_b2(v):
        lk.wait(v, "sc")
        sp = lambda: v.tensor_copy(ydt[:, 0:512], ydt[:, 0:512])
        v.reduce_sum(glm16[:], gl3, axis=AX)
        sp()
        v.reciprocal(glm16[:], glm16[:])
        sp()
        v.tensor_tensor(gl3, gl3, g16b, ALU.mult)
        v.reduce_max(va3[:, :, 0:1], gl3, axis=AX)
        sp()
        v.tensor_tensor(oh13, gl3, m1b, ALU.is_ge)
        v.tensor_scalar(mska[:], oh1a[:], -1.0, 1.0, ALU.mult, ALU.add)
        v.tensor_tensor(mska[:], gla[:], mska[:], ALU.mult)
        v.reduce_max(va3[:, :, 1:2], msk3, axis=AX)
        sp()
        i = v.tensor_tensor(oh23, msk3, m2b, ALU.is_ge)
        lk.inc(i, "ev")

    lk.on("vector", v_b2)

    for lc in range(16):
        lo = lc * 128

        def t_tr(t, lc=lc, evc=lk.c["ev"]):
            lk.wait(t, "ev", evc)
            lk.nc.tensor.transpose(p_t1[:, 0:128], oh1a[:, lc * E:(lc + 1) * E], ident[:])
            lk.nc.tensor.transpose(p_t2[:, 0:128], oh2a[:, lc * E:(lc + 1) * E], ident[:])
            i = lk.nc.tensor.transpose(p_t3[:, 0:128], va[:, 2 * lc:2 * lc + 2], ident[:])
            lk.inc(i, "mm")

        lk.on("tensor", t_tr)

        def v_tr(v, lo=lo):
            lk.wait(v, "mm")
            v.tensor_copy(ohT1[:, lo:lo + 128], p_t1[:, 0:128])
            v.tensor_copy(ohT2[:, lo:lo + 128], p_t2[:, 0:128])
            i = v.tensor_copy(vT[:, lo:lo + 128], p_t3[:, 0:128])
            lk.mark(v, "ev")

        lk.on("vector", v_tr)

    def st1(s):
        lk.wait(s, "ev")
        i = s.dma_start(out=oh1_out, in_=ohT1[:])
        lk.inc(i, "st", 16)
        i = s.dma_start(out=oh2_out, in_=ohT2[:])
        lk.inc(i, "st", 16)
        i = s.dma_start(out=v1_out, in_=vT[0:1, :])
        lk.inc(i, "st", 16)
        i = s.dma_start(out=v2_out, in_=vT[1:2, :])
        lk.inc(i, "st", 16)
        s.wait_ge(lk.sems["st"], lk.c["st"])

    lk.on("sync", st1)
    lk.end_phase()



def emit_moe(lk, io, pool):
    xn1_in = io["xn1_s"]
    oh1_in, oh2_in = io["oh1_s"], io["oh2_s"]
    v1_in, v2_in = io["v1_s"], io["v2_s"]
    w1_in, w2_in = io["m_w1"], io["m_w2"]
    b1_in, b2_in, sel_in = io["m_b1"], io["m_b2"], io["m_sel"]
    mg_in, mb_in = io["m_lng"], io["m_lnb"]
    g1_out = io["g1_s"]
    lk.begin_phase("moe")
    lb = lk.c["ld"]
    wb = lk.c["wt"]
    bgp = lk.c["gp"]
    xn1 = lk.sb("mo_xn1", [128, NCH * TLOC], BF16)
    boost = lk.sb("mo_boost", [128, NCH * TLOC], BF16)
    g1acc = lk.sb("mo_g1acc", [128, NCH * TLOC], BF16)
    w1e = lk.sb("mo_w1e", [128, NCH * HMOE], BF16)
    w2e = lk.sb("mo_w2e", [128, NDC * D], BF16)
    h_sb = lk.sb("mo_h", [128, NDC * 512], BF16)
    ohf = lk.sb("mo_ohf", [E, 2 * TLOC], BF16)
    vf = lk.sb("mo_vf", [1, 2 * TLOC], BF16)
    selc = lk.sb("mo_sel", [E, E * 128], BF16)
    mrep = lk.sb("mo_mrep", [128, 512], BF16)
    mvrep = lk.sb("mo_mvrep", [128, 512], F32)
    vrep = lk.sb("mo_vrep", [128, 512], F32)
    eot = lk.sb("mo_eot", [128, 512], BF16)
    tmpf = lk.sb("mo_tmpf", [128, 512], F32)
    b1c = lk.sb("mo_b1c", [128, E * 16], F32)
    b2c = lk.sb("mo_b2c", [128, E * 8], F32)
    mg = lk.sb("mo_mg", [128, NCH], F32)
    mb = lk.sb("mo_mb", [128, NCH], F32)
    onesr = lk.sb("mo_onesr", [1, 128], BF16)
    kit = LNScratch(lk, 512, name="mo_ln")
    kit.init_consts()

    def g0(g):
        g.memset(onesr[:], 1.0)
        i = g.dma_start(out=ohf[:, 0:TLOC], in_=oh1_in)
        lk.inc(i, "wt", 16)
        i = g.dma_start(out=ohf[:, TLOC:2 * TLOC], in_=oh2_in)
        lk.inc(i, "wt", 16)
        i = g.dma_start(out=vf[:, 0:TLOC], in_=v1_in)
        lk.inc(i, "wt", 16)
        i = g.dma_start(out=vf[:, TLOC:2 * TLOC], in_=v2_in)
        lk.inc(i, "wt", 16)
        i = g.dma_start(out=selc[:], in_=sel_in)
        lk.inc(i, "wt", 16)
        i = g.nop()
        lk.inc(i, "gp")

    lk.on("gpsimd", g0)

    def ld0(s):
        for c in range(NCH):
            i = s.dma_start(out=xn1[:, c * TLOC:(c + 1) * TLOC],
                            in_=xn1_in[c * 128:(c + 1) * 128, :])
            lk.inc(i, "ld", 16)
        i = s.dma_start(out=b1c[:], in_=b1_in)
        lk.inc(i, "ld", 16)
        i = s.dma_start(out=b2c[:], in_=b2_in)
        lk.inc(i, "ld", 16)
        i = s.dma_start(out=mg[:], in_=mg_in)
        lk.inc(i, "ld", 16)
        i = s.dma_start(out=mb[:], in_=mb_in)
        lk.inc(i, "ld", 16)

    lk.on("sync", ld0)

    def v_init(v):
        lk.wait(v, "ld", lb + 16 * NCH)
        for c in range(NCH):
            v.tensor_copy(boost[:, c * TLOC:(c + 1) * TLOC], xn1[:, c * TLOC:(c + 1) * TLOC])
            i = v.tensor_copy(g1acc[:, c * TLOC:(c + 1) * TLOC], xn1[:, c * TLOC:(c + 1) * TLOC])
        lk.mark(v, "ev")

    lk.on("vector", v_init)

    mm_h_done = [lk.c["mm"]]
    mm_w2_done = [lk.c["mm"]]
    for p2 in range(2):
        rhs_t = xn1 if p2 == 0 else boost
        for e in range(E):
            def g_we(g, e=e, w1w=mm_h_done[0], w2w=mm_w2_done[0]):
                g.wait_ge(lk.sems["mm"], w1w)
                for kc in range(NCH):
                    i = g.dma_start(out=w1e[:, kc * HMOE:(kc + 1) * HMOE],
                                    in_=w1_in[e * D + kc * 128:e * D + (kc + 1) * 128, :])
                    lk.inc(i, "wt", 16)
                g.wait_ge(lk.sems["mm"], w2w)
                for kc in range(NDC):
                    i = g.dma_start(out=w2e[:, kc * D:(kc + 1) * D],
                                    in_=w2_in[e * HMOE + kc * 128:e * HMOE + (kc + 1) * 128, :])
                    lk.inc(i, "wt", 16)

            lk.on("gpsimd", g_we)
            wt_w1 = lk.c["wt"] - 16 * NDC
            wt_w2 = lk.c["wt"]

            for (mo, w) in MT512:
                def t_rep(t, e=e, p2=p2, mo=mo, w=w, evc=lk.c["ev"]):
                    lk.wait(t, "ev", evc)
                    lk.wait(t, "ld")
                    lk.wait(t, "wt", wb + 16 * 5)
                    lk.wait(t, "gp", bgp + 1)
                    ib0, b0 = pool.acquire(t)
                    i = lk.nc.tensor.matmul(b0[:, 0:w], selc[:, e * 128:(e + 1) * 128],
                                            ohf[:, p2 * TLOC + mo:p2 * TLOC + mo + w],
                                            start=True, stop=True)
                    lk.inc(i, "mm")
                    ib1, b1 = pool.acquire(t)
                    i = lk.nc.tensor.matmul(b1[:, 0:w], onesr[:],
                                            vf[:1, p2 * TLOC + mo:p2 * TLOC + mo + w],
                                            start=True, stop=True)
                    lk.inc(i, "mm")
                    t_rep.banks = [(ib0, b0), (ib1, b1)]

                lk.on("tensor", t_rep)
                (ib0, b0), (ib1, b1) = t_rep.banks

                def v_rep(v, ib0=ib0, b0=b0, ib1=ib1, b1=b1, w=w):
                    lk.wait(v, "mm")
                    i = v.tensor_copy(mrep[:, 0:w], b0[:, 0:w])
                    pool.freed(i, ib0)
                    i = v.tensor_copy(vrep[:, 0:w], b1[:, 0:w])
                    pool.freed(i, ib1)
                    v.tensor_tensor(mvrep[:, 0:w], mrep[:, 0:w], vrep[:, 0:w], ALU.mult)
                    lk.mark(v, "ev")

                lk.on("vector", v_rep)

                def evac_h(e_, bank, n, mo2, w2, e=e):
                    return e_.tensor_scalar(h_sb[:, n * 512:n * 512 + w2], bank[:, 0:w2],
                                            b1c[:, e * 16 + n:e * 16 + n + 1], 0.0,
                                            ALU.add, ALU.max)

                mm_block(lk, pool,
                         get_lhsT=lambda kc, n: w1e[:, kc * HMOE + n * 128:kc * HMOE + (n + 1) * 128],
                         get_rhs=lambda kc, mo2, w2, rhs_t=rhs_t, mo=mo: rhs_t[:, kc * TLOC + mo:kc * TLOC + mo + w2],
                         nK=NCH, n_list=range(NDC), m_list=[(0, w)],
                         evac=evac_h, evac_engine="vector",
                         pre_wait=lambda t, wd=wt_w1, c=lk.c["ev"]: (lk.wait(t, "wt", wd),
                                                                     lk.wait(t, "ev", c)))
                emit_barrier(lk, "vector", "ev")
                mm_h_done[0] = lk.c["mm"]

                def evac_o(e_, bank, n, mo2, w2, e=e, p2=p2, mo=mo):
                    e_.tensor_scalar(eot[:, 0:w2], bank[:, 0:w2],
                                     b2c[:, e * 8 + n:e * 8 + n + 1], 0.0, ALU.add, ALU.add)
                    e_.tensor_tensor(tmpf[:, 0:w2], eot[:, 0:w2], mvrep[:, 0:w2], ALU.mult)
                    i = e_.tensor_tensor(g1acc[:, n * TLOC + mo:n * TLOC + mo + w2],
                                         g1acc[:, n * TLOC + mo:n * TLOC + mo + w2],
                                         tmpf[:, 0:w2], ALU.add)
                    if p2 == 0:
                        e_.tensor_tensor(tmpf[:, 0:w2], eot[:, 0:w2], mrep[:, 0:w2],
                                         ALU.mult)
                        i = e_.scalar_tensor_tensor(boost[:, n * TLOC + mo:n * TLOC + mo + w2],
                                                    tmpf[:, 0:w2], 0.5,
                                                    boost[:, n * TLOC + mo:n * TLOC + mo + w2],
                                                    ALU.mult, ALU.add)
                    return i

                mm_block(lk, pool,
                         get_lhsT=lambda kc, n: w2e[:, kc * D + n * 128:kc * D + (n + 1) * 128],
                         get_rhs=lambda kc, mo2, w2: h_sb[:, kc * 512:kc * 512 + w2],
                         nK=NDC, n_list=range(NCH), m_list=[(0, w)],
                         evac=evac_o, evac_engine="vector",
                         pre_wait=lambda t, wd=wt_w2, c=lk.c["ev"]: (lk.wait(t, "wt", wd),
                                                                     lk.wait(t, "ev", c)))
                emit_barrier(lk, "vector", "ev")
                mm_w2_done[0] = lk.c["mm"]

    sq_full = h_sb.ap().bitcast(F32)
    tsc = tmpf
    g1o = w1e
    for (mo, w) in MT512:
        def get_x(c, mo=mo, w=w):
            return g1acc[:, c * TLOC + mo:c * TLOC + mo + w]

        def put_out(c, v, w=w):
            v.tensor_tensor(tsc[:, 0:w], get_x(c), kit.p_rep[:, 0:w], ALU.subtract)
            v.tensor_tensor(tsc[:, 0:w], tsc[:, 0:w], kit.p_rep[:, 512:512 + w], ALU.mult)
            v.tensor_scalar(g1o[:, c * 512:c * 512 + w], tsc[:, 0:w],
                            mg[:, c:c + 1], mb[:, c:c + 1], ALU.mult, ALU.add)

        emit_ln(lk, kit, get_x, put_out, w, sq_full, out_war_sems=("st",))

        def s_silu(s, mo=mo, w=w):
            lk.wait(s, "ev")
            for c in range(NCH):
                i = s.activation(g1o[:, c * 512:c * 512 + w], g1o[:, c * 512:c * 512 + w],
                                 AF.Silu)
            lk.inc(i, "sc")

        lk.on("scalar", s_silu)

        def st_g(s, mo=mo, w=w):
            lk.wait(s, "sc")
            for c in range(NCH):
                i = s.dma_start(out=g1_out[c * 128:(c + 1) * 128, mo:mo + w],
                                in_=g1o[:, c * 512:c * 512 + w])
                lk.inc(i, "st", 16)

        lk.on("sync", st_g)

    def fin(s):
        lk.wait(s, "st")

    lk.on("sync", fin)
    lk.end_phase()



def build_fused(upto=99, dbg=()):
    lk = L("mega")
    io = {}
    io["xT"] = lk.din("xT", [D, TLOC + 1])
    io["ln1g"] = lk.din("ln1g", [128, NCH])
    io["ln1b"] = lk.din("ln1b", [128, NCH])
    io["p_wq"] = lk.din("p_wq", [D, D]); io["p_wk"] = lk.din("p_wk", [D, D])
    io["p_wv"] = lk.din("p_wv", [D, D])
    io["p_bq"] = lk.din("p_bq", [128, NCH]); io["p_bk"] = lk.din("p_bk", [128, NCH])
    io["p_bv"] = lk.din("p_bv", [1, D])
    io["proj"] = lk.din("proj", [DH, DK])
    io["a_wo"] = lk.din("a_wo", [D, D]); io["a_bo"] = lk.din("a_bo", [128, NCH])
    io["ident"] = lk.din("ident", [128, 128])
    io["ln2g"] = lk.din("ln2g", [128, NCH]); io["ln2b"] = lk.din("ln2b", [128, NCH])
    io["f_w1"] = lk.din("f_w1", [D, DI]); io["f_b1"] = lk.din("f_b1", [128, DI // 128])
    io["f_w2"] = lk.din("f_w2", [DI, D]); io["f_b2"] = lk.din("f_b2", [128, NCH])
    io["r_wrt"] = lk.din("r_wrt", [D, D]); io["r_wkt"] = lk.din("r_wkt", [D, D])
    io["r_wvt"] = lk.din("r_wvt", [D, D])
    io["r_mur"] = lk.din("r_mur", [128, NCH]); io["r_muk"] = lk.din("r_muk", [128, NCH])
    io["l_winT"] = lk.din("l_winT", [D, 2 * DN])
    io["l_convw"] = lk.din("l_convw", [128, NDC * 4])
    io["l_convb"] = lk.din("l_convb", [128, NDC])
    io["l_xprojT"] = lk.din("l_xprojT", [DN, 96])
    io["l_wdtT"] = lk.din("l_wdtT", [DTR, DN])
    io["l_bdt"] = lk.din("l_bdt", [128, NDC])
    io["s_alog"] = lk.din("s_alog", [128, 8 * DS])
    io["m0c"] = lk.din("m0c", [128, 1]); io["m1c"] = lk.din("m1c", [128, 1])
    io["g_dsk"] = lk.din("g_dsk", [128, NDC])
    io["g_woutT"] = lk.din("g_woutT", [DN, D])
    io["g_gwp"] = lk.din("g_gwp", [128, NCH * E])
    io["g_gb"] = lk.din("g_gb", [1, E])
    io["m_w1"] = lk.din("m_w1", [E * D, HMOE])
    io["m_w2"] = lk.din("m_w2", [E * HMOE, D])
    io["m_b1"] = lk.din("m_b1", [128, E * 16])
    io["m_b2"] = lk.din("m_b2", [128, E * 8])
    io["m_sel"] = lk.din("m_sel", [E, E * 128])
    io["m_lng"] = lk.din("m_lng", [128, NCH])
    io["m_lnb"] = lk.din("m_lnb", [128, NCH])
    io["out"] = lk.dout("out", [D, TLOC])

    def mk(name, shape, dt=F32):
        io[name] = (lk.dout(name, shape, dt) if name in dbg
                    else lk.dscratch(name, shape, dt))

    mk("xn_s", [D, TLOC + 1], BF16)
    mk("x2_s", [D, TLOC], BF16)
    mk("qf_s", [D, TLOC], BF16)
    mk("kvp_i", [NH * DH, 2 * DK])
    mk("kvp_o", [NH * DH, 2 * DK])
    mk("xs_s", [D, TLOC], BF16)
    mk("x2b_s", [D, TLOC], BF16)
    mk("s0_s", [D, TLOC], BF16)
    mk("g0_s", [D, TLOC], BF16)
    mk("x1_s", [D, TLOC + 3])
    mk("halo_i", [D, 4])
    mk("halo_o", [2 * D, 4])
    mk("xn1_s", [D, TLOC], BF16)
    mk("u_s", [DN, TLOC], BF16)
    mk("dt_s", [DN, TLOC], BF16)
    mk("z_s", [DN, TLOC], BF16)
    mk("Bs_s", [DS, TLOC])
    mk("Cs_s", [DS, TLOC])
    mk("dtG", [2 * DN, TLOC], BF16)
    mk("uG", [2 * DN, TLOC], BF16)
    mk("BsG", [2 * DS, TLOC])
    mk("CsG", [2 * DS, TLOC])
    mk("y_s", [DN // 2, T], BF16)
    mk("yG", [DN, T], BF16)
    mk("s1_s", [D, TLOC], BF16)
    mk("oh1_s", [E, TLOC])
    mk("oh2_s", [E, TLOC])
    mk("v1_s", [1, TLOC])
    mk("v2_s", [1, TLOC])
    mk("g1_s", [D, TLOC], BF16)

    lk.sem("ld", "wt", "w2t", "nt", "nf", "rf", "mm", "sc", "ev", "st", "gp",
           "ph", "bl", "pbf0", "pbf1",
           "cc1", "cc2", "cc3", "cc4", "cc5", "cc6", "cc7")
    lk.block()
    pool = PsumPool(lk, 2)

    steps = [
        lambda: emit_xn(lk, io),
        lambda: emit_perf_kv(lk, io, pool),
        lambda: lk.cc("AllReduce", ALU.add, io["kvp_i"], io["kvp_o"], "cc1"),
        lambda: emit_attn(lk, io, pool),
        lambda: emit_ln2(lk, io),
        lambda: emit_ffn(lk, io, pool),
        lambda: emit_rwkv(lk, io, pool),
        lambda: emit_combine(lk, io,
                             x_src=lambda c, mo, w: io["xT"][c * 128:(c + 1) * 128,
                                                            1 + mo:1 + mo + w],
                             s_src=io["s0_s"], g_src=io["g0_s"],
                             out_ap=io["x1_s"], out_off=3, halo_dst=io["halo_i"],
                             tag="cb1"),
        lambda: lk.cc("AllGather", ALU.bypass, io["halo_i"], io["halo_o"], "cc2"),
        lambda: emit_halo_fix(lk, io),
        lambda: emit_l1front(lk, io, pool),
        lambda: lk.cc("AllGather", ALU.bypass, io["dt_s"], io["dtG"], "cc3"),
        lambda: lk.cc("AllGather", ALU.bypass, io["u_s"], io["uG"], "cc4"),
        lambda: lk.cc("AllGather", ALU.bypass, io["Bs_s"], io["BsG"], "cc5"),
        lambda: lk.cc("AllGather", ALU.bypass, io["Cs_s"], io["CsG"], "cc6"),
        lambda: emit_scan(lk, io, pool),
        lambda: lk.cc("AllGather", ALU.bypass, io["y_s"], io["yG"], "cc7"),
        lambda: emit_gate_s1(lk, io, pool),
        lambda: emit_moe(lk, io, pool),
        lambda: emit_combine(lk, io,
                             x_src=lambda c, mo, w: io["x1_s"][c * 128:(c + 1) * 128,
                                                               3 + mo:3 + mo + w],
                             s_src=io["s1_s"], g_src=io["g1_s"],
                             out_ap=io["out"], out_off=0, tag="cb2"),
    ]
    for i, step in enumerate(steps):
        if i >= upto:
            break
        step()

    def fin(s):
        s.wait_ge(lk.sems["st"], lk.c["st"])

    lk.on("sync", fin)
    lk.close()
    return lk



def host_inputs(inputs):
    x = np.asarray(inputs["L0_attn_x"])
    al = np.asarray(inputs["L1_attn_A_log"], np.float32)
    shared = {
        "ln1g": host_prep_feat_cols(inputs["L0_attn_ln1_g"]),
        "ln1b": host_prep_feat_cols(inputs["L0_attn_ln1_b"]),
        "p_wq": _f32h(inputs["L0_attn_wq"]),
        "p_wk": _f32h(inputs["L0_attn_wk"]),
        "p_wv": _f32h(inputs["L0_attn_wv"]),
        "p_bq": host_prep_feat_cols(inputs["L0_attn_bq"]),
        "p_bk": host_prep_feat_cols(inputs["L0_attn_bk"]),
        "p_bv": _f32h(np.asarray(inputs["L0_attn_bv"]).reshape(1, D)),
        "proj": _f32h(inputs["L0_attn_proj_mat"]),
        "a_wo": _f32h(inputs["L0_attn_wo"]),
        "a_bo": host_prep_feat_cols(inputs["L0_attn_bo"]),
        "ident": np.eye(128, dtype=np.float32),
        "ln2g": host_prep_feat_cols(inputs["L0_attn_ln2_g"]),
        "ln2b": host_prep_feat_cols(inputs["L0_attn_ln2_b"]),
        "f_w1": _f32h(inputs["L0_attn_w1"]),
        "f_b1": host_prep_feat_cols(inputs["L0_attn_b1"]),
        "f_w2": _f32h(inputs["L0_attn_w2"]),
        "f_b2": host_prep_feat_cols(inputs["L0_attn_b2"]),
        "r_wrt": _f32h(np.asarray(inputs["L0_ffn_W_r"]).T),
        "r_wkt": _f32h(np.asarray(inputs["L0_ffn_W_k"]).T),
        "r_wvt": _f32h(np.asarray(inputs["L0_ffn_W_v"]).T),
        "r_mur": host_prep_feat_cols(inputs["L0_ffn_mu_r"]),
        "r_muk": host_prep_feat_cols(inputs["L0_ffn_mu_k"]),
        "l_winT": _f32h(np.asarray(inputs["L1_attn_W_in"]).T),
        "l_convw": _f32h(np.transpose(
            np.asarray(inputs["L1_attn_conv_w"]).reshape(NDC, 128, 4),
            (1, 0, 2)).reshape(128, NDC * 4)),
        "l_convb": host_prep_feat_cols(inputs["L1_attn_conv_b"]),
        "l_xprojT": _f32h(np.asarray(inputs["L1_attn_W_xproj"]).T),
        "l_wdtT": _f32h(np.asarray(inputs["L1_attn_W_dt"]).T),
        "l_bdt": host_prep_feat_cols(inputs["L1_attn_b_dt"]),
        "g_dsk": host_prep_feat_cols(inputs["L1_attn_D_skip"]),
        "g_woutT": _f32h(np.asarray(inputs["L1_attn_W_out"]).T),
        "g_gwp": _f32h(np.transpose(
            np.asarray(inputs["L1_ffn_gate_W"]).reshape(NCH, 128, E),
            (1, 0, 2)).reshape(128, NCH * E)),
        "g_gb": _f32h(np.asarray(inputs["L1_ffn_gate_b"]).reshape(1, E)),
        "m_w1": _f32h(np.asarray(inputs["L1_ffn_W1"]).reshape(E * D, HMOE)),
        "m_w2": _f32h(np.asarray(inputs["L1_ffn_W2"]).reshape(E * HMOE, D)),
        "m_b1": _f32h(np.concatenate(
            [np.asarray(inputs["L1_ffn_b1"])[e].reshape(16, 128).T for e in range(E)],
            axis=1)),
        "m_b2": _f32h(np.concatenate(
            [np.asarray(inputs["L1_ffn_b2"])[e].reshape(8, 128).T for e in range(E)],
            axis=1)),
        "m_sel": _f32h(np.kron(np.eye(E, dtype=np.float32),
                               np.ones((1, 128), np.float32))),
        "m_lng": host_prep_feat_cols(inputs["L1_ffn_ln_gamma"]),
        "m_lnb": host_prep_feat_cols(inputs["L1_ffn_ln_beta"]),
    }
    in_maps = []
    for core in range(NCORES):
        b, j = divmod(core, 2)
        t0 = j * TLOC
        ext = np.zeros((D, TLOC + 1), np.float32)
        ext[:, 1:] = x[b, t0:t0 + TLOC, :].T
        if j == 1:
            ext[:, 0] = x[b, t0 - 1, :]
        r0, r1 = j * 1024, (j + 1) * 1024
        alc = np.ascontiguousarray(np.transpose(
            al[r0:r1].reshape(8, 128, DS), (1, 0, 2)).reshape(128, 8 * DS))
        m = dict(shared)
        m["xT"] = np.ascontiguousarray(ext)
        m["s_alog"] = alc
        m["m0c"] = np.full((128, 1), 1.0 - j, np.float32)
        m["m1c"] = np.full((128, 1), float(j), np.float32)
        in_maps.append(m)
    return in_maps


def kernel(**inputs):
    lk = build_fused()
    in_maps = host_inputs(inputs)
    res = run_bass_kernel_spmd(lk.nc, in_maps, core_ids=list(range(NCORES)))
    out = np.zeros((B, T, D), np.float32)
    for core in range(NCORES):
        b, j = divmod(core, 2)
        out[b, j * TLOC:(j + 1) * TLOC, :] = np.asarray(res.results[core]["out"]).T
    return out


# version 13
# speedup vs baseline: 1.3210x; 1.0104x over previous
import math
from contextlib import ExitStack

import numpy as np
import ml_dtypes

import concourse.bass as bass
import concourse.mybir as mybir
from concourse.bass_utils import run_bass_kernel_spmd

F32 = mybir.dt.float32
BF16 = mybir.dt.bfloat16
AF = mybir.ActivationFunctionType
ALU = mybir.AluOpType
AX = mybir.AxisListType.X

BF = ml_dtypes.bfloat16
NCORES = 8
PAIRS = [[0, 1], [2, 3], [4, 5], [6, 7]]
B, T, D = 4, 4096, 1024
TLOC = T // 2
NH, DH, DK = 16, 64, 128
DI = 4096
DN = 2048
DS = 16
DTR = 64
E, HMOE = 8, 2048
PI = float(np.pi)
EPS = 1e-5
NCH = D // 128
NDC = DN // 128
MT512 = [(o, 512) for o in range(0, TLOC, 512)]



class L:

    def __init__(self, name):
        self.name = name
        self.nc = bass.Bass("TRN2", target_bir_lowering=False)
        self.stack = ExitStack()
        self.ph = None
        self.c = {}
        self.sems = {}

    def din(self, name, shape, dt=F32):
        return self.nc.dram_tensor(name, list(shape), dt, kind="ExternalInput").ap()

    def dout(self, name, shape, dt=F32):
        return self.nc.dram_tensor(name, list(shape), dt, kind="ExternalOutput").ap()

    def dscratch(self, name, shape, dt=F32):
        return self.nc.dram_tensor(name, list(shape), dt, kind="Internal").ap()

    def sb(self, name, shape, dt=F32):
        st = self.ph if self.ph is not None else self.stack
        return st.enter_context(self.nc.sbuf_tensor(name, list(shape), dt))

    def ps(self, name, shape, dt=F32):
        st = self.ph if self.ph is not None else self.stack
        return st.enter_context(self.nc.psum_tensor(name, list(shape), dt))

    def sem(self, *names):
        for n in names:
            self.sems[n] = self.stack.enter_context(self.nc.semaphore(n))
            self.c[n] = 0

    def block(self):
        self.dummy = self.stack.enter_context(self.nc.sbuf_tensor("l_dummy", [1, 16], F32))
        self.blk = self.stack.enter_context(self.nc.Block())

    def mark(self, eng, sname):
        if hasattr(eng, "tensor_copy"):
            i = eng.tensor_copy(self.dummy[:1, 0:8], self.dummy[:1, 8:16])
        else:
            i = eng.copy(self.dummy[:1, 0:8], self.dummy[:1, 8:16])
        self.inc(i, sname)

    def inc(self, inst, sname, n=1):
        inst.then_inc(self.sems[sname], n)
        self.c[sname] += n
        return inst

    def wait(self, eng, sname, cnt=None):
        eng.wait_ge(self.sems[sname], self.c[sname] if cnt is None else cnt)

    def on(self, engine_name, fn):
        getattr(self.blk, engine_name)(fn)

    def begin_phase(self, name):
        assert self.ph is None
        self.ph = ExitStack()
        self._scope = self.nc.named_scope(name)
        self._scope.__enter__()
        waits = [(s, self.c[s]) for s in self.sems if self.c[s] > 0]
        for eng in ("sync", "scalar", "vector", "tensor", "gpsimd"):
            def bw(e, waits=tuple(waits)):
                for s, cnt in waits:
                    e.wait_ge(self.sems[s], cnt)
            self.on(eng, bw)

    def end_phase(self):
        for eng in ("scalar", "vector", "gpsimd"):
            def mk(e):
                self.mark(e, "ph")
            self.on(eng, mk)
        self.ph.close()
        self.ph = None
        self._scope.__exit__(None, None, None)

    def cc(self, kind, op, in_ap, out_ap, sname):
        def g_fn(g):
            g.wait_ge(self.sems["st"], self.c["st"])
            i = g.collective_compute(kind, op, replica_groups=PAIRS,
                                     ins=[in_ap.opt()], outs=[out_ap.opt()])
            self.inc(i, sname)
            g.wait_ge(self.sems[sname], self.c[sname])
        self.on("gpsimd", g_fn)

    def close(self):
        self.stack.close()


def _bfh(a):
    return np.ascontiguousarray(np.asarray(a).astype(BF))


def _f32h(a):
    return np.ascontiguousarray(np.asarray(a, dtype=np.float32))


def _tile_widths(n, w):
    out = []
    while n > 0:
        out.append(min(w, n))
        n -= out[-1]
    return out


def host_prep_feat_cols(vec):
    v = np.asarray(vec, np.float32)
    return np.ascontiguousarray(v.reshape(-1, 128).T)



def emit_ln(lk, kit, get_x, put_out, w, sq_full, out_war_sems=(), nchunks=NCH):
    nc = lk.nc

    def s_fn(s):
        lk.wait(s, "gp", kit.gp_done)
        lk.wait(s, "ld")
        lk.wait(s, "ev")
        for c in range(nchunks):
            i = s.activation(sq_full[:, c * w:(c + 1) * w], get_x(c), AF.Square)
        lk.inc(i, "sc")

    lk.on("scalar", s_fn)

    def t_fn(t):
        lk.wait(t, "gp", kit.gp_done)
        lk.wait(t, "ev")
        lk.wait(t, "ld")
        ones_c = kit.ones_col_bf if get_x(0).dtype == BF16 else kit.ones_col
        for c in range(nchunks):
            nc.tensor.matmul(kit.p_sum[:1, 0:w], ones_c[:], get_x(c),
                             start=(c == 0), stop=(c == nchunks - 1))
        lk.wait(t, "sc")
        for c in range(nchunks):
            i = nc.tensor.matmul(kit.p_sumsq[:1, 0:w], kit.ones_col[:],
                                 sq_full[:, c * w:(c + 1) * w],
                                 start=(c == 0), stop=(c == nchunks - 1))
        lk.inc(i, "mm")

    lk.on("tensor", t_fn)

    def v_fn(v):
        lk.wait(v, "mm")
        mean = kit.row[:1, 0:w]
        var = kit.row[:1, kit.maxw:kit.maxw + w]
        v.tensor_scalar(mean, kit.p_sum[:1, 0:w], 1.0 / D, 0.0, ALU.mult, ALU.add)
        v.tensor_tensor(var, mean, mean, ALU.mult)
        i = v.scalar_tensor_tensor(var, kit.p_sumsq[:1, 0:w], 1.0 / D, var,
                                   ALU.mult, ALU.subtract)
        lk.inc(i, "ev")

    lk.on("vector", v_fn)

    def s2_fn(s):
        lk.wait(s, "gp", kit.gp_done)
        lk.wait(s, "ev")
        var = kit.row[:1, kit.maxw:kit.maxw + w]
        s.activation(var, var, AF.Ln, bias=kit.eps_col[:1])
        i = s.activation(var, var, AF.Exp, scale=kit.half_col[:1])
        lk.inc(i, "sc")

    lk.on("scalar", s2_fn)

    def t2_fn(t):
        lk.wait(t, "sc")
        nc.tensor.matmul(kit.p_rep[:, 0:w], kit.ones_row[:], kit.row[:1, 0:w],
                         start=True, stop=True)
        i = nc.tensor.matmul(kit.p_rep[:, kit.maxw:kit.maxw + w], kit.ones_row[:],
                             kit.row[:1, kit.maxw:kit.maxw + w], start=True, stop=True)
        lk.inc(i, "mm")

    lk.on("tensor", t2_fn)

    def v2_fn(v):
        lk.wait(v, "mm")
        for s_ in out_war_sems:
            lk.wait(v, s_)
        for c in range(nchunks):
            put_out(c, v)
        lk.mark(v, "ev")

    lk.on("vector", v2_fn)


class LNScratch:
    def __init__(self, lk, maxw=512, name="ln"):
        self.lk = lk
        self.maxw = maxw
        self.row = lk.sb(f"{name}_row", [1, 2 * maxw], F32)
        self.p_sum = lk.ps(f"{name}_psum", [1, maxw], F32)
        self.p_sumsq = lk.ps(f"{name}_psumsq", [1, maxw], F32)
        self.p_rep = lk.ps(f"{name}_prep", [128, 2 * maxw], F32)
        self.ones_col = lk.sb(f"{name}_ones_c", [128, 1], F32)
        self.ones_col_bf = lk.sb(f"{name}_ones_cb", [128, 1], BF16)
        self.ones_row = lk.sb(f"{name}_ones_r", [1, 128], F32)
        self.eps_col = lk.sb(f"{name}_eps", [1, 1], F32)
        self.half_col = lk.sb(f"{name}_half", [1, 1], F32)

    def init_consts(self):
        lk = self.lk

        def g_fn(g):
            g.memset(self.ones_col[:], 1.0)
            g.memset(self.ones_col_bf[:], 1.0)
            g.memset(self.ones_row[:], 1.0)
            g.memset(self.eps_col[:], EPS)
            i = g.memset(self.half_col[:], -0.5)
            lk.inc(i, "gp")

        lk.on("gpsimd", g_fn)
        self.gp_done = lk.c["gp"]



class PsumPool:

    def __init__(self, lk, nbanks=2, tag="pb"):
        self.lk = lk
        self.n = nbanks
        self.banks = [lk.ps(f"{tag}{i}", [128, 512], F32) for i in range(nbanks)]
        self.free_sems = [f"{tag}f{i}" for i in range(nbanks)]
        self.uses = [0] * nbanks
        self.rr = 0

    def acquire(self, teng):
        i = self.rr
        self.rr = (self.rr + 1) % self.n
        if self.uses[i] > 0:
            teng.wait_ge(self.lk.sems[self.free_sems[i]], self.uses[i])
        self.uses[i] += 1
        return i, self.banks[i]

    def freed(self, inst, i):
        inst.then_inc(self.lk.sems[self.free_sems[i]], 1)
        self.lk.c[self.free_sems[i]] += 1


def mm_block(lk, pool, get_lhsT, get_rhs, nK, n_list, m_list, evac,
             evac_engine="vector", pre_wait=None):
    nc = lk.nc
    first = True
    for n in n_list:
        for (mo, w) in m_list:
            holder = {}

            def t_fn(t, n=n, mo=mo, w=w, first=first):
                if pre_wait is not None and first:
                    pre_wait(t)
                ib, bank = pool.acquire(t)
                holder["b"] = (ib, bank)
                for kc in range(nK):
                    i = nc.tensor.matmul(bank[:, 0:w], get_lhsT(kc, n),
                                         get_rhs(kc, mo, w),
                                         start=(kc == 0), stop=(kc == nK - 1))
                lk.inc(i, "mm")

            lk.on("tensor", t_fn)
            ib, bank = holder["b"]

            def e_fn(e, n=n, mo=mo, w=w, ib=ib, bank=bank):
                lk.wait(e, "mm")
                last = evac(e, bank, n, mo, w)
                pool.freed(last, ib)

            lk.on(evac_engine, e_fn)
            first = False


def emit_barrier(lk, engine_name, sname="ev"):
    def f(e):
        lk.mark(e, sname)

    lk.on(engine_name, f)


def load_weight_bf16(lk, g, dst_sb, hbm_ap, nK, ncols, sem="wt"):
    for kc in range(nK):
        i = g.dma_start(out=dst_sb[:, kc * ncols:(kc + 1) * ncols],
                        in_=hbm_ap[kc * 128:(kc + 1) * 128, :])
        lk.inc(i, sem, 16)



def emit_xn(lk, io):
    NT = TLOC + 1
    x_in, g_in, b_in = io["xT"], io["ln1g"], io["ln1b"]
    xn_out, x2_out = io["xn_s"], io["x2_s"]
    lk.begin_phase("xn")
    widths = [257] + _tile_widths(TLOC - 256, 512)
    xt = lk.sb("xn_xt", [128, NCH * 512], F32)
    xo = lk.sb("xn_xo", [128, NCH * 512], BF16)
    x2o = lk.sb("xn_x2o", [128, NCH * 512], BF16)
    sq_full = lk.sb("xn_sq", [128, NCH * 512], F32)
    gcol = lk.sb("xn_gcol", [128, NCH], F32)
    bcol = lk.sb("xn_bcol", [128, NCH], F32)
    kit = LNScratch(lk, 512, name="xn_ln")
    kit.init_consts()

    def ld0(s):
        i = s.dma_start(out=gcol[:], in_=g_in)
        lk.inc(i, "ld", 16)
        i = s.dma_start(out=bcol[:], in_=b_in)
        lk.inc(i, "ld", 16)

    lk.on("sync", ld0)

    t0 = 0
    for w in widths:
        o = t0

        def ld_fn(s, o=o, w=w):
            lk.wait(s, "ev")
            for c in range(NCH):
                i = s.dma_start(out=xt[:, c * w:(c + 1) * w],
                                in_=x_in[c * 128:(c + 1) * 128, o:o + w])
                lk.inc(i, "ld", 16)

        lk.on("sync", ld_fn)

        def get_x(c, w=w):
            return xt[:, c * w:(c + 1) * w]

        def put_out(c, v, w=w):
            tmp = sq_full[:, c * w:(c + 1) * w]
            v.tensor_tensor(tmp, get_x(c), kit.p_rep[:, 0:w], ALU.subtract)
            v.tensor_tensor(xo[:, c * w:(c + 1) * w], tmp,
                            kit.p_rep[:, 512:512 + w], ALU.mult)

        emit_ln(lk, kit, get_x, put_out, w, sq_full, out_war_sems=("st",))

        def get_x2(c, w=w):
            return xo[:, c * w:(c + 1) * w]

        def put_out2(c, v, w=w):
            tmp = sq_full[:, c * w:(c + 1) * w]
            d = x2o[:, c * w:(c + 1) * w]
            v.tensor_tensor(tmp, get_x2(c), kit.p_rep[:, 0:w], ALU.subtract)
            v.tensor_tensor(tmp, tmp, kit.p_rep[:, 512:512 + w], ALU.mult)
            v.tensor_scalar(d, tmp, gcol[:, c:c + 1], bcol[:, c:c + 1],
                            ALU.mult, ALU.add)

        emit_ln(lk, kit, get_x2, put_out2, w, sq_full, out_war_sems=("st",))

        def st_fn(s, o=o, w=w):
            lk.wait(s, "ev")
            for c in range(NCH):
                i = s.dma_start(out=xn_out[c * 128:(c + 1) * 128, o:o + w],
                                in_=xo[:, c * w:(c + 1) * w])
                lk.inc(i, "st", 16)
            x2src0 = 1 if o == 0 else 0
            for c in range(NCH):
                i = s.dma_start(
                    out=x2_out[c * 128:(c + 1) * 128, max(o - 1, 0):o + w - 1],
                    in_=x2o[:, c * w + x2src0:(c + 1) * w])
                lk.inc(i, "st", 16)

        lk.on("sync", st_fn)
        t0 += w

    def fin(s):
        lk.wait(s, "st")

    lk.on("sync", fin)
    lk.end_phase()



def emit_perf_kv(lk, io, pool):
    x2_in = io["x2_s"]
    wq_in, wk_in, wv_in = io["p_wq"], io["p_wk"], io["p_wv"]
    bq_in, bk_in, bv_in, proj_in = io["p_bq"], io["p_bk"], io["p_bv"], io["proj"]
    qf_out, kvp_out = io["qf_s"], io["kvp_i"]
    lk.begin_phase("perfkv")
    wb = lk.c["wt"]
    x2 = lk.sb("pk_x2", [128, NCH * TLOC], BF16)
    wq = lk.sb("pk_wq", [128, NCH * D], BF16)
    wk = lk.sb("pk_wk", [128, NCH * D], BF16)
    wv = lk.sb("pk_wv", [128, NCH * D], BF16)
    qf = lk.sb("pk_qf", [128, NCH * TLOC], BF16)
    kf = lk.sb("pk_kf", [128, NCH * TLOC], BF16)
    v_full = lk.sb("pk_v", [128, 16 * D], BF16)
    proj = lk.sb("pk_proj", [128, 2 * DK], BF16)
    bq = lk.sb("pk_bq", [128, NCH], F32)
    bk = lk.sb("pk_bk", [128, NCH], F32)
    bvr = lk.sb("pk_bvr", [1, D], F32)
    bv_rep = lk.sb("pk_bvrep", [128, D], BF16)
    onesr = lk.sb("pk_onesr", [1, 128], F32)
    wsc = lk.sb("pk_wsc", [128, 256], F32)
    csc = lk.sb("pk_csc", [128, 256], F32)
    aw = lk.sb("pk_aw", [128, 256], F32)
    kp = lk.sb("pk_kp", [128, 4 * DK], BF16)
    kvsb = lk.sb("pk_kvsb", [128, 4 * 512], F32)
    cst = lk.sb("pk_cst", [128, 2], F32)
    p_pk = lk.ps("pk_ppk", [128, 2 * DK], F32)
    kv4 = [lk.ps(f"pk_pkv{i}", [128, 512], F32) for i in range(4)]
    bgp = lk.c["gp"]

    def g0(g):
        g.memset(cst[:, 0:1], PI / 2)
        g.memset(cst[:, 1:2], -1.0)
        i = g.memset(onesr[:], 1.0)
        lk.inc(i, "gp")
        load_weight_bf16(lk, g, wq, wq_in, NCH, D)
        load_weight_bf16(lk, g, wk, wk_in, NCH, D)
        load_weight_bf16(lk, g, wv, wv_in, NCH, D)
        g.memset(proj[:], 0.0)
        i = g.dma_start(out=proj[0:64, 0:DK], in_=proj_in)
        lk.inc(i, "wt", 16)
        i = g.dma_start(out=proj[64:128, DK:2 * DK], in_=proj_in)
        lk.inc(i, "wt", 16)

    lk.on("gpsimd", g0)

    def ld0(s):
        for c in range(NCH):
            i = s.dma_start(out=x2[:, c * TLOC:(c + 1) * TLOC],
                            in_=x2_in[c * 128:(c + 1) * 128, :])
            lk.inc(i, "ld", 16)
        for t, srcp in [(bq, bq_in), (bk, bk_in), (bvr, bv_in)]:
            i = s.dma_start(out=t[:], in_=srcp)
            lk.inc(i, "ld", 16)

    lk.on("sync", ld0)

    def t_bv(t):
        lk.wait(t, "ld")
        lk.wait(t, "gp", bgp + 1)
        ib0, b0 = pool.acquire(t)
        i = lk.nc.tensor.matmul(b0[:, 0:512], onesr[:], bvr[:1, 0:512],
                                start=True, stop=True)
        lk.inc(i, "mm")
        ib1, b1 = pool.acquire(t)
        i = lk.nc.tensor.matmul(b1[:, 0:512], onesr[:], bvr[:1, 512:1024],
                                start=True, stop=True)
        lk.inc(i, "mm")
        t_bv.banks = [(ib0, b0), (ib1, b1)]

    lk.on("tensor", t_bv)

    def v_bv(v):
        lk.wait(v, "mm")
        (ib0, b0), (ib1, b1) = t_bv.banks
        i = v.tensor_copy(bv_rep[:, 0:512], b0[:, 0:512])
        pool.freed(i, ib0)
        i = v.tensor_copy(bv_rep[:, 512:1024], b1[:, 0:512])
        pool.freed(i, ib1)
        lk.mark(v, "ev")

    lk.on("vector", v_bv)

    for (wt_sb, b_sb, dst, wt_cnt) in [(wq, bq, qf, 16 * 8), (wk, bk, kf, 16 * 16)]:
        def evac_qk(e, bank, n, mo, w, b_sb=b_sb, dst=dst):
            return e.tensor_scalar(dst[:, n * TLOC + mo:n * TLOC + mo + w],
                                   bank[:, 0:w], b_sb[:, n:n + 1], 0.0,
                                   ALU.add, ALU.add)

        mm_block(lk, pool,
                 get_lhsT=lambda kc, n, wt_sb=wt_sb: wt_sb[:, kc * D + n * 128:kc * D + (n + 1) * 128],
                 get_rhs=lambda kc, mo, w: x2[:, kc * TLOC + mo:kc * TLOC + mo + w],
                 nK=NCH, n_list=range(NCH), m_list=MT512,
                 evac=evac_qk, evac_engine="vector",
                 pre_wait=lambda t, wc=wt_cnt: (lk.wait(t, "wt", wb + wc), lk.wait(t, "ld")))
    emit_barrier(lk, "vector", "ev")

    def evac_v(e, bank, n, mo, w):
        return e.tensor_tensor(v_full[:, n * D + mo:n * D + mo + w], bank[:, 0:w],
                               bv_rep[:, mo:mo + w], ALU.add)

    mm_block(lk, pool,
             get_lhsT=lambda kc, n: x2[:, kc * TLOC + n * 128:kc * TLOC + n * 128 + 128],
             get_rhs=lambda kc, mo, w: wv[:, kc * D + mo:kc * D + mo + w],
             nK=NCH, n_list=range(16), m_list=[(0, 512), (512, 512)],
             evac=evac_v, evac_engine="vector",
             pre_wait=lambda t: (lk.wait(t, "wt", wb + 16 * 24), lk.wait(t, "ev")))
    emit_barrier(lk, "vector", "ev")

    for ch in range(8):
        for lc in range(16):
            lo = lc * 128

            def t_pk(t, ch=ch, lo=lo):
                lk.wait(t, "ev")
                i = lk.nc.tensor.matmul(p_pk[:, 0:2 * DK],
                                        kf[:, ch * TLOC + lo:ch * TLOC + lo + 128],
                                        proj[:, 0:2 * DK],
                                        start=True, stop=True)
                lk.inc(i, "mm")

            lk.on("tensor", t_pk)

            def v_wrap(v):
                lk.wait(v, "mm")
                lk.wait(v, "sc")
                v.tensor_scalar(wsc[:], p_pk[:, 0:2 * DK], 0.125, 0.0, ALU.mult, ALU.add)
                v.tensor_scalar(csc[:], wsc[:], PI, -2 * PI, ALU.is_gt, ALU.mult)
                v.tensor_tensor(wsc[:], wsc[:], csc[:], ALU.add)
                v.tensor_scalar(csc[:], wsc[:], -PI, 2 * PI, ALU.is_lt, ALU.mult)
                i = v.tensor_tensor(wsc[:], wsc[:], csc[:], ALU.add)
                lk.inc(i, "ev")

            lk.on("vector", v_wrap)

            def s_sin(s):
                lk.wait(s, "ev")
                s.activation(aw[:], wsc[:], AF.Abs)
                s.activation(kp[:, 2 * DK:4 * DK], wsc[:], AF.Sin)
                i = s.activation(kp[:, 0:2 * DK], aw[:], AF.Sin,
                                 bias=cst[:, 0:1], scale=cst[:, 1:2])
                lk.inc(i, "sc")

            lk.on("scalar", s_sin)

            def t_kv(t, ch=ch, lc=lc):
                lk.wait(t, "sc")
                ro = (ch % 2) * 64
                co = (ch // 2) * 128
                for hh in range(2):
                    h = 2 * ch + hh
                    lk.nc.tensor.matmul(
                        kv4[2 * hh + 0][ro:ro + 64, co:co + 128],
                        v_full[:, lc * D + h * 64:lc * D + (h + 1) * 64],
                        kp[:, hh * 128:(hh + 1) * 128],
                        start=(lc == 0), stop=(lc == 15))
                    i = lk.nc.tensor.matmul(
                        kv4[2 * hh + 1][ro:ro + 64, co:co + 128],
                        v_full[:, lc * D + h * 64:lc * D + (h + 1) * 64],
                        kp[:, 2 * DK + hh * 128:2 * DK + (hh + 1) * 128],
                        start=(lc == 0), stop=(lc == 15))
                if lc == 15:
                    lk.inc(i, "mm")

            lk.on("tensor", t_kv)

    def v_kv(v):
        lk.wait(v, "mm")
        for q4 in range(4):
            i = v.tensor_copy(kvsb[:, q4 * 512:(q4 + 1) * 512], kv4[q4][:, 0:512])
        lk.mark(v, "ev")

    lk.on("vector", v_kv)

    def st0(s):
        lk.wait(s, "ev")
        for c in range(NCH):
            i = s.dma_start(out=qf_out[c * 128:(c + 1) * 128, :],
                            in_=qf[:, c * TLOC:(c + 1) * TLOC])
            lk.inc(i, "st", 16)
        for h in range(NH):
            ch, hh = h // 2, h % 2
            ro = (ch % 2) * 64
            co = (ch // 2) * 128
            i = s.dma_start(out=kvp_out[h * DH:(h + 1) * DH, 0:128],
                            in_=kvsb[ro:ro + 64, (2 * hh + 0) * 512 + co:
                                     (2 * hh + 0) * 512 + co + 128])
            lk.inc(i, "st", 16)
            i = s.dma_start(out=kvp_out[h * DH:(h + 1) * DH, 128:256],
                            in_=kvsb[ro:ro + 64, (2 * hh + 1) * 512 + co:
                                     (2 * hh + 1) * 512 + co + 128])
            lk.inc(i, "st", 16)
        s.wait_ge(lk.sems["st"], lk.c["st"])

    lk.on("sync", st0)
    lk.end_phase()



def emit_attn(lk, io, pool):
    qf_in, kvp_in, xn_in = io["qf_s"], io["kvp_o"], io["xn_s"]
    wo_in, bo_in, proj_in, id_in = io["a_wo"], io["a_bo"], io["proj"], io["ident"]
    xs_out = io["xs_s"]
    lk.begin_phase("attn")
    wb = lk.c["wt"]
    qf = lk.sb("at_qf", [128, NCH * TLOC], BF16)
    xn = lk.sb("at_xn", [128, NCH * TLOC], BF16)
    kvp_sb = lk.sb("at_kvp", [128, 8 * 256], F32)
    kvt = lk.sb("at_kvt", [128, 2 * D], BF16)
    wo = lk.sb("at_wo", [128, NCH * D], BF16)
    bo = lk.sb("at_bo", [128, NCH], F32)
    proj = lk.sb("at_proj", [128, 2 * DK], BF16)
    ident = lk.sb("at_id", [128, 128], F32)
    wsc = lk.sb("at_wsc", [128, 512], F32)
    csc = lk.sb("at_csc", [128, 512], F32)
    aw = lk.sb("at_aw", [128, 512], F32)
    qp = lk.sb("at_qp", [128, 2 * 512], BF16)
    attn_sb = lk.sb("at_attn", [128, NCH * 512], BF16)
    xs_sb = lk.sb("at_xs", [128, NCH * TLOC], BF16)
    cst = lk.sb("at_cst", [128, 2], F32)
    p_pq = lk.ps("at_ppq", [128, 512], F32)
    p_at = lk.ps("at_pat", [64, 512], F32)
    p_tr = lk.ps("at_ptr", [128, 128], F32)

    def g0(g):
        g.memset(cst[:, 0:1], PI / 2)
        g.memset(cst[:, 1:2], -1.0)
        g.memset(proj[:], 0.0)
        i = g.dma_start(out=proj[0:64, 0:DK], in_=proj_in)
        lk.inc(i, "wt", 16)
        i = g.dma_start(out=proj[64:128, DK:2 * DK], in_=proj_in)
        lk.inc(i, "wt", 16)
        load_weight_bf16(lk, g, wo, wo_in, NCH, D)

    lk.on("gpsimd", g0)

    def ld0(s):
        for c in range(NCH):
            i = s.dma_start(out=qf[:, c * TLOC:(c + 1) * TLOC],
                            in_=qf_in[c * 128:(c + 1) * 128, :])
            lk.inc(i, "ld", 16)
        for c in range(NCH):
            i = s.dma_start(out=xn[:, c * TLOC:(c + 1) * TLOC],
                            in_=xn_in[c * 128:(c + 1) * 128, 1:TLOC + 1])
            lk.inc(i, "ld", 16)
        for c in range(NCH):
            i = s.dma_start(out=kvp_sb[:, c * 256:(c + 1) * 256],
                            in_=kvp_in[c * 128:(c + 1) * 128, :])
            lk.inc(i, "ld", 16)
        i = s.dma_start(out=bo[:], in_=bo_in)
        lk.inc(i, "ld", 16)
        i = s.dma_start(out=ident[:], in_=id_in)
        lk.inc(i, "ld", 16)

    lk.on("sync", ld0)

    for c in range(NCH):
        for mc in range(2):
            def t_tr(t, c=c, mc=mc):
                if c == 0 and mc == 0:
                    lk.wait(t, "ld")
                lk.wait(t, "ev")
                i = lk.nc.tensor.transpose(p_tr[:, 0:128],
                                           kvp_sb[:, c * 256 + mc * 128:c * 256 + (mc + 1) * 128],
                                           ident[:])
                lk.inc(i, "mm")

            lk.on("tensor", t_tr)

            def v_tr(v, c=c, mc=mc):
                lk.wait(v, "mm")
                i = v.tensor_scalar(kvt[:, mc * D + c * 128:mc * D + (c + 1) * 128],
                                    p_tr[:, 0:128], 1.0 / 128.0, 0.0, ALU.mult, ALU.add)
                lk.inc(i, "ev")

            lk.on("vector", v_tr)
    emit_barrier(lk, "vector", "ev")

    for (mo, w) in MT512:
        for h in range(NH):
            ch, po = h // 2, (h % 2) * 64

            def t_pq(t, ch=ch, po=po, mo=mo, w=w):
                lk.wait(t, "ev")
                lk.wait(t, "wt", wb + 16 * 2)
                i = lk.nc.tensor.matmul(p_pq[:, 0:w],
                                        proj[:, (po // 64) * DK:(po // 64) * DK + DK],
                                        qf[:, ch * TLOC + mo:ch * TLOC + mo + w],
                                        start=True, stop=True)
                lk.inc(i, "mm")

            lk.on("tensor", t_pq)

            def v_wrap(v, w=w):
                lk.wait(v, "mm")
                lk.wait(v, "sc")
                v.tensor_scalar(wsc[:, 0:w], p_pq[:, 0:w], 0.125, 0.0, ALU.mult, ALU.add)
                v.tensor_scalar(csc[:, 0:w], wsc[:, 0:w], PI, -2 * PI, ALU.is_gt, ALU.mult)
                v.tensor_tensor(wsc[:, 0:w], wsc[:, 0:w], csc[:, 0:w], ALU.add)
                v.tensor_scalar(csc[:, 0:w], wsc[:, 0:w], -PI, 2 * PI, ALU.is_lt, ALU.mult)
                i = v.tensor_tensor(wsc[:, 0:w], wsc[:, 0:w], csc[:, 0:w], ALU.add)
                lk.inc(i, "ev")

            lk.on("vector", v_wrap)

            def s_sin(s, w=w):
                lk.wait(s, "ev")
                s.activation(aw[:, 0:w], wsc[:, 0:w], AF.Abs)
                s.activation(qp[:, 512:512 + w], wsc[:, 0:w], AF.Sin)
                i = s.activation(qp[:, 0:w], aw[:, 0:w], AF.Sin,
                                 bias=cst[:, 0:1], scale=cst[:, 1:2])
                lk.inc(i, "sc")

            lk.on("scalar", s_sin)

            def t_at(t, h=h, w=w):
                lk.wait(t, "sc")
                lk.nc.tensor.matmul(p_at[:, 0:w], kvt[:, 0 * D + h * 64:0 * D + h * 64 + 64],
                                    qp[:, 0:w], start=True, stop=False)
                i = lk.nc.tensor.matmul(p_at[:, 0:w], kvt[:, 1 * D + h * 64:1 * D + h * 64 + 64],
                                        qp[:, 512:512 + w], start=False, stop=True)
                lk.inc(i, "mm")

            lk.on("tensor", t_at)

            def v_at(v, h=h, ch=ch, po=po, w=w):
                lk.wait(v, "mm")
                i = v.tensor_copy(attn_sb[po:po + 64, ch * 512:ch * 512 + w], p_at[:, 0:w])
                lk.inc(i, "ev")

            lk.on("vector", v_at)

        def evac_o(e, bank, n, mo2, w2, mo=mo):
            return e.scalar_tensor_tensor(xs_sb[:, n * TLOC + mo:n * TLOC + mo + w2],
                                          bank[:, 0:w2], bo[:, n:n + 1],
                                          xn[:, n * TLOC + mo:n * TLOC + mo + w2],
                                          ALU.add, ALU.add)

        mm_block(lk, pool,
                 get_lhsT=lambda kc, n: wo[:, kc * D + n * 128:kc * D + (n + 1) * 128],
                 get_rhs=lambda kc, mo2, w2: attn_sb[:, kc * 512:kc * 512 + w2],
                 nK=NCH, n_list=range(NCH), m_list=[(0, w)],
                 evac=evac_o, evac_engine="vector",
                 pre_wait=lambda t, c=lk.c["ev"]: (lk.wait(t, "wt"), lk.wait(t, "ev", c)))
        emit_barrier(lk, "vector", "ev")

    def st0(s):
        lk.wait(s, "ev")
        for c in range(NCH):
            i = s.dma_start(out=xs_out[c * 128:(c + 1) * 128, :],
                            in_=xs_sb[:, c * TLOC:(c + 1) * TLOC])
            lk.inc(i, "st", 16)
        s.wait_ge(lk.sems["st"], lk.c["st"])

    lk.on("sync", st0)
    lk.end_phase()



def emit_ln2(lk, io):
    xs_in, g_in, b_in, out = io["xs_s"], io["ln2g"], io["ln2b"], io["x2b_s"]
    lk.begin_phase("ln2")
    xt = lk.sb("l2_xt", [128, NCH * 512], BF16)
    xo = lk.sb("l2_xo", [128, NCH * 512], BF16)
    sq_full = lk.sb("l2_sq", [128, NCH * 512], F32)
    tsc = lk.sb("l2_tsc", [128, 512], F32)
    gcol = lk.sb("l2_gcol", [128, NCH], F32)
    bcol = lk.sb("l2_bcol", [128, NCH], F32)
    kit = LNScratch(lk, 512, name="l2_ln")
    kit.init_consts()

    def ld0(s):
        i = s.dma_start(out=gcol[:], in_=g_in)
        lk.inc(i, "ld", 16)
        i = s.dma_start(out=bcol[:], in_=b_in)
        lk.inc(i, "ld", 16)

    lk.on("sync", ld0)

    for (mo, w) in MT512:
        def ld_fn(s, mo=mo, w=w):
            lk.wait(s, "ev")
            for c in range(NCH):
                i = s.dma_start(out=xt[:, c * w:(c + 1) * w],
                                in_=xs_in[c * 128:(c + 1) * 128, mo:mo + w])
                lk.inc(i, "ld", 16)

        lk.on("sync", ld_fn)

        def get_x(c, w=w):
            return xt[:, c * w:(c + 1) * w]

        def put_out(c, v, w=w):
            v.tensor_tensor(tsc[:, 0:w], get_x(c), kit.p_rep[:, 0:w], ALU.subtract)
            v.tensor_tensor(tsc[:, 0:w], tsc[:, 0:w], kit.p_rep[:, 512:512 + w], ALU.mult)
            v.tensor_scalar(xo[:, c * w:(c + 1) * w], tsc[:, 0:w],
                            gcol[:, c:c + 1], bcol[:, c:c + 1], ALU.mult, ALU.add)

        emit_ln(lk, kit, get_x, put_out, w, sq_full, out_war_sems=("st",))

        def st_fn(s, mo=mo, w=w):
            lk.wait(s, "ev")
            for c in range(NCH):
                i = s.dma_start(out=out[c * 128:(c + 1) * 128, mo:mo + w],
                                in_=xo[:, c * w:(c + 1) * w])
                lk.inc(i, "st", 16)

        lk.on("sync", st_fn)

    def fin(s):
        lk.wait(s, "st")

    lk.on("sync", fin)
    lk.end_phase()



def emit_ffn(lk, io, pool):
    NI = DI // 128
    x2b_in, xs_in = io["x2b_s"], io["xs_s"]
    w1_in, b1_in, w2_in, b2_in = io["f_w1"], io["f_b1"], io["f_w2"], io["f_b2"]
    s_out = io["s0_s"]
    lk.begin_phase("ffn")
    wb = lk.c["wt"]
    w1 = lk.sb("ff_w1", [128, NCH * DI], BF16)
    w2 = lk.sb("ff_w2", [128, NI * D], BF16)
    b1c = lk.sb("ff_b1c", [128, NI], F32)
    b2c = lk.sb("ff_b2c", [128, NCH], F32)
    xb = lk.sb("ff_xb", [128, NCH * 512], BF16)
    h = lk.sb("ff_h", [128, NI * 512], BF16)
    xs = lk.sb("ff_xs", [128, NCH * 512], BF16)
    s_t = lk.sb("ff_st", [128, NCH * 512], BF16)

    def g0(g):
        load_weight_bf16(lk, g, w1, w1_in, NCH, DI)
        load_weight_bf16(lk, g, w2, w2_in, NI, D)

    lk.on("gpsimd", g0)

    def ld0(s):
        i = s.dma_start(out=b1c[:], in_=b1_in)
        lk.inc(i, "ld", 16)
        i = s.dma_start(out=b2c[:], in_=b2_in)
        lk.inc(i, "ld", 16)

    lk.on("sync", ld0)

    for (mo, w) in MT512:
        def ld_t(s, mo=mo, w=w, evn=lk.c["ev"]):
            lk.wait(s, "ev", evn)
            for c in range(NCH):
                i = s.dma_start(out=xb[:, c * w:(c + 1) * w],
                                in_=x2b_in[c * 128:(c + 1) * 128, mo:mo + w])
                lk.inc(i, "ld", 16)
                i = s.dma_start(out=xs[:, c * w:(c + 1) * w],
                                in_=xs_in[c * 128:(c + 1) * 128, mo:mo + w])
                lk.inc(i, "ld", 16)

        lk.on("sync", ld_t)

        def evac_h(e, bank, n, mo2, w2_):
            return e.tensor_scalar(h[:, n * 512:n * 512 + w2_], bank[:, 0:w2_],
                                   b1c[:, n:n + 1], 0.0, ALU.add, ALU.max)

        mm_block(lk, pool,
                 get_lhsT=lambda kc, n: w1[:, kc * DI + n * 128:kc * DI + (n + 1) * 128],
                 get_rhs=lambda kc, mo2, w2_, w=w: xb[:, kc * w:kc * w + w2_],
                 nK=NCH, n_list=range(NI), m_list=[(0, w)],
                 evac=evac_h, evac_engine="vector",
                 pre_wait=lambda t, c=lk.c["mm"]: (lk.wait(t, "wt", wb + 16 * NCH),
                                                   lk.wait(t, "ld"),
                                                   lk.wait(t, "mm", c)))
        emit_barrier(lk, "vector", "ev")

        def evac_s(e, bank, n, mo2, w2_, w=w):
            return e.scalar_tensor_tensor(s_t[:, n * 512:n * 512 + w2_],
                                          bank[:, 0:w2_], b2c[:, n:n + 1],
                                          xs[:, n * w:n * w + w2_],
                                          ALU.add, ALU.add)

        mm_block(lk, pool,
                 get_lhsT=lambda kc, n: w2[:, kc * D + n * 128:kc * D + (n + 1) * 128],
                 get_rhs=lambda kc, mo2, w2_: h[:, kc * 512:kc * 512 + w2_],
                 nK=NI, n_list=range(NCH), m_list=[(0, w)],
                 evac=evac_s, evac_engine="vector",
                 pre_wait=lambda t, c=lk.c["ev"], stc=lk.c["st"]: (
                     lk.wait(t, "wt"), lk.wait(t, "ev", c), lk.wait(t, "st", stc)))
        emit_barrier(lk, "vector", "ev")

        def st_t(s, mo=mo, w=w):
            lk.wait(s, "ev")
            for c in range(NCH):
                i = s.dma_start(out=s_out[c * 128:(c + 1) * 128, mo:mo + w],
                                in_=s_t[:, c * 512:c * 512 + w])
                lk.inc(i, "st", 16)

        lk.on("sync", st_t)

    def fin(s):
        lk.wait(s, "st")

    lk.on("sync", fin)
    lk.end_phase()



def emit_rwkv(lk, io, pool):
    xn_in = io["xn_s"]
    wrt, wkt, wvt = io["r_wrt"], io["r_wkt"], io["r_wvt"]
    mur_in, muk_in = io["r_mur"], io["r_muk"]
    g_out = io["g0_s"]
    lk.begin_phase("rwkv")
    wb = lk.c["wt"]
    NT = TLOC + 2
    xn = lk.sb("rw_xn", [128, NCH * NT], BF16)
    wr = lk.sb("rw_wr", [128, NCH * D], BF16)
    wk = lk.sb("rw_wk", [128, NCH * D], BF16)
    wv = lk.sb("rw_wv", [128, NCH * D], BF16)
    mur = lk.sb("rw_mur", [128, NCH], F32)
    muk = lk.sb("rw_muk", [128, NCH], F32)
    dsc = lk.sb("rw_dsc", [128, NCH * 512], F32)
    xr = lk.sb("rw_xr", [128, NCH * 512], BF16)
    xk = lk.sb("rw_xk", [128, NCH * 512], BF16)
    rbf = lk.sb("rw_rbf", [128, NCH * 512], BF16)
    kabf = lk.sb("rw_kabf", [128, NCH * 512], BF16)
    ksc = lk.sb("rw_ksc", [128, 512], BF16)
    rwsc = lk.sb("rw_rwsc", [128, NCH * 512], F32)
    gbf = lk.sb("rw_gbf", [128, NCH * 512], BF16)

    def ld_fn(s):
        for c in range(NCH):
            i = s.dma_start(out=xn[:, c * NT:c * NT + (TLOC + 1)],
                            in_=xn_in[c * 128:(c + 1) * 128, :])
            lk.inc(i, "ld", 16)
        i = s.dma_start(out=mur[:], in_=mur_in)
        lk.inc(i, "ld", 16)
        i = s.dma_start(out=muk[:], in_=muk_in)
        lk.inc(i, "ld", 16)

    lk.on("sync", ld_fn)

    def wt_fn(g):
        load_weight_bf16(lk, g, wr, wrt, NCH, D)
        load_weight_bf16(lk, g, wk, wkt, NCH, D)
        load_weight_bf16(lk, g, wv, wvt, NCH, D)

    lk.on("gpsimd", wt_fn)

    for (mo, w) in MT512:
        mm_before_mix = lk.c["mm"]

        def v_mix(v, mo=mo, w=w, mmw=mm_before_mix):
            lk.wait(v, "ld")
            lk.wait(v, "mm", mmw)
            for c in range(NCH):
                cur = xn[:, c * NT + mo + 1: c * NT + mo + 1 + w]
                prv = xn[:, c * NT + mo: c * NT + mo + w]
                ds = dsc[:, c * 512:c * 512 + w]
                v.tensor_tensor(ds, cur, prv, ALU.subtract)
                v.scalar_tensor_tensor(xr[:, c * w:(c + 1) * w], ds,
                                       mur[:, c:c + 1], prv, ALU.mult, ALU.add)
                i = v.scalar_tensor_tensor(xk[:, c * w:(c + 1) * w], ds,
                                           muk[:, c:c + 1], prv, ALU.mult, ALU.add)
            lk.inc(i, "ev")

        lk.on("vector", v_mix)
        mix_ev = lk.c["ev"]

        def s_war(s, c=lk.c["ev"]):
            lk.wait(s, "ev", c)

        lk.on("scalar", s_war)

        def evac_r(e, bank, n, mo2, w2):
            return e.activation(rbf[:, n * 512:n * 512 + w2], bank[:, 0:w2], AF.Sigmoid)

        mm_block(lk, pool,
                 get_lhsT=lambda kc, n: wr[:, kc * D + n * 128: kc * D + (n + 1) * 128],
                 get_rhs=lambda kc, mo2, w2, w=w: xr[:, kc * w: kc * w + w2],
                 nK=NCH, n_list=range(NCH), m_list=[(0, w)],
                 evac=evac_r, evac_engine="scalar",
                 pre_wait=lambda t, me=mix_ev: (lk.wait(t, "wt", wb + 16 * 8),
                                                lk.wait(t, "ev", me)))

        def evac_k(e, bank, n, mo2, w2):
            e.tensor_scalar(ksc[:, 0:w2], bank[:, 0:w2], 0.0, 0.0, ALU.max, ALU.add)
            return e.tensor_tensor(kabf[:, n * 512:n * 512 + w2], ksc[:, 0:w2],
                                   bank[:, 0:w2], ALU.mult)

        mm_block(lk, pool,
                 get_lhsT=lambda kc, n: wk[:, kc * D + n * 128: kc * D + (n + 1) * 128],
                 get_rhs=lambda kc, mo2, w2, w=w: xk[:, kc * w: kc * w + w2],
                 nK=NCH, n_list=range(NCH), m_list=[(0, w)],
                 evac=evac_k, evac_engine="vector",
                 pre_wait=lambda t: lk.wait(t, "wt", wb + 16 * 16))
        emit_barrier(lk, "vector", "ev")
        kact_ev = lk.c["ev"]

        def evac_vo(e, bank, n, mo2, w2):
            return e.tensor_tensor(rwsc[:, n * 512:n * 512 + w2],
                                   rbf[:, n * 512:n * 512 + w2],
                                   bank[:, 0:w2], ALU.mult)

        mm_block(lk, pool,
                 get_lhsT=lambda kc, n: wv[:, kc * D + n * 128: kc * D + (n + 1) * 128],
                 get_rhs=lambda kc, mo2, w2, w=w: kabf[:, kc * w: kc * w + w2],
                 nK=NCH, n_list=range(NCH), m_list=[(0, w)],
                 evac=evac_vo, evac_engine="vector",
                 pre_wait=lambda t, ke=kact_ev: (lk.wait(t, "wt", wb + 16 * 24),
                                                 lk.wait(t, "ev", ke)))
        emit_barrier(lk, "vector", "ev")
        vo_ev = lk.c["ev"]

        def s_silu(s, mo=mo, w=w, ve=vo_ev):
            lk.wait(s, "ev", ve)
            lk.wait(s, "st")
            for c in range(NCH):
                i = s.activation(gbf[:, c * 512:c * 512 + w],
                                 rwsc[:, c * 512:c * 512 + w], AF.Silu)
            lk.inc(i, "sc")

        lk.on("scalar", s_silu)

        def st_fn(s, mo=mo, w=w, sc=lk.c["sc"]):
            lk.wait(s, "sc", sc)
            for c in range(NCH):
                i = s.dma_start(out=g_out[c * 128:(c + 1) * 128, mo:mo + w],
                                in_=gbf[:, c * 512:c * 512 + w])
                lk.inc(i, "st", 16)

        lk.on("sync", st_fn)

    def fin(s):
        lk.wait(s, "st")

    lk.on("sync", fin)
    lk.end_phase()



def emit_combine(lk, io, x_src, s_src, g_src, out_ap, out_off=0, halo_dst=None,
                 tag="cb1"):
    lk.begin_phase("comb" + tag)
    xt = lk.sb(f"{tag}_xt", [128, NCH * 512], F32)
    st_ = lk.sb(f"{tag}_st", [128, NCH * 512], BF16)
    gt = lk.sb(f"{tag}_gt", [128, NCH * 512], BF16)
    tf = lk.sb(f"{tag}_tf", [128, 512], F32)
    x1 = lk.sb(f"{tag}_x1", [128, NCH * 512], F32)

    for (mo, w) in MT512:
        def ld_fn(s, mo=mo, w=w):
            lk.wait(s, "ev")
            for c in range(NCH):
                i = s.dma_start(out=xt[:, c * w:(c + 1) * w],
                                in_=x_src(c, mo, w))
                lk.inc(i, "ld", 16)
                i = s.dma_start(out=st_[:, c * w:(c + 1) * w],
                                in_=s_src[c * 128:(c + 1) * 128, mo:mo + w])
                lk.inc(i, "ld", 16)
                i = s.dma_start(out=gt[:, c * w:(c + 1) * w],
                                in_=g_src[c * 128:(c + 1) * 128, mo:mo + w])
                lk.inc(i, "ld", 16)

        lk.on("sync", ld_fn)

        def v_fn(v, mo=mo, w=w, stc=lk.c["st"]):
            lk.wait(v, "ld")
            lk.wait(v, "st", stc)
            for c in range(NCH):
                v.tensor_tensor(tf[:, 0:w], st_[:, c * w:(c + 1) * w],
                                gt[:, c * w:(c + 1) * w], ALU.mult)
                v.tensor_tensor(x1[:, c * w:(c + 1) * w], tf[:, 0:w],
                                xt[:, c * w:(c + 1) * w], ALU.add)
            lk.mark(v, "ev")

        lk.on("vector", v_fn)

        def st_fn(s, mo=mo, w=w):
            lk.wait(s, "ev")
            for c in range(NCH):
                i = s.dma_start(out=out_ap[c * 128:(c + 1) * 128,
                                           out_off + mo:out_off + mo + w],
                                in_=x1[:, c * w:(c + 1) * w])
                lk.inc(i, "st", 16)
            if halo_dst is not None and mo + w == TLOC:
                for c in range(NCH):
                    i = s.dma_start(out=halo_dst[c * 128:(c + 1) * 128, 0:4],
                                    in_=x1[:, c * w + w - 4:(c + 1) * w])
                    lk.inc(i, "st", 16)

        lk.on("sync", st_fn)

    def fin(s):
        lk.wait(s, "st")

    lk.on("sync", fin)
    lk.end_phase()



def emit_halo_fix(lk, io):
    halo_o, m1_in, x1_s = io["halo_o"], io["m1c"], io["x1_s"]
    lk.begin_phase("halo")
    hsb = lk.sb("hx_hsb", [128, NCH * 4], F32)
    m1 = lk.sb("hx_m1", [128, 1], F32)

    def ld0(s):
        for c in range(NCH):
            i = s.dma_start(out=hsb[:, c * 4:(c + 1) * 4],
                            in_=halo_o[c * 128:(c + 1) * 128, 0:4])
            lk.inc(i, "ld", 16)
        i = s.dma_start(out=m1[:], in_=m1_in)
        lk.inc(i, "ld", 16)

    lk.on("sync", ld0)

    def v0(v):
        lk.wait(v, "ld")
        v.tensor_scalar(hsb[:], hsb[:], m1[:, 0:1], 0.0, ALU.mult, ALU.add)
        lk.mark(v, "ev")

    lk.on("vector", v0)

    def st0(s):
        lk.wait(s, "ev")
        for c in range(NCH):
            i = s.dma_start(out=x1_s[c * 128:(c + 1) * 128, 0:3],
                            in_=hsb[:, c * 4 + 1:(c + 1) * 4])
            lk.inc(i, "st", 16)
        s.wait_ge(lk.sems["st"], lk.c["st"])

    lk.on("sync", st0)
    lk.end_phase()



def emit_l1front(lk, io, pool):
    EXT = TLOC + 3
    x1_in = io["x1_s"]
    win_in, cw_in, cb_in = io["l_winT"], io["l_convw"], io["l_convb"]
    xp_in, wdt_in, bdt_in = io["l_xprojT"], io["l_wdtT"], io["l_bdt"]
    xn1_out, u_out, dt_out, z_out = io["xn1_s"], io["u_s"], io["dt_s"], io["z_s"]
    B_out, C_out = io["Bs_s"], io["Cs_s"]
    lk.begin_phase("l1f")
    bnt = lk.c["nt"]
    bnf = lk.c["nf"]
    NTE = EXT + 1
    xt = lk.sb("lf_xt", [128, NCH * 512], F32)
    xn1 = lk.sb("lf_xn1", [128, NCH * NTE], BF16)
    sq_full = lk.sb("lf_sq", [128, NCH * 512], F32)
    nring = lk.sb("lf_nring", [128, 2 * NCH * 128], BF16)
    xi = lk.sb("lf_xi", [128, NDC * NTE], BF16)
    u_t = lk.sb("lf_ut", [128, NDC * 512], BF16)
    z_t = lk.sb("lf_zt", [128, NDC * 512], BF16)
    dt_t = lk.sb("lf_dtt", [128, NDC * 512], BF16)
    cw = lk.sb("lf_cw", [128, NDC * 4], F32)
    cb = lk.sb("lf_cb", [128, NDC], F32)
    bdt = lk.sb("lf_bdt", [128, NDC], F32)
    xp = lk.sb("lf_xp", [128, NDC * 96], BF16)
    wdt = lk.sb("lf_wdt", [DTR, DN], BF16)
    dtr_bf = lk.sb("lf_dtrbf", [DTR, 512], BF16)
    bc_sb = lk.sb("lf_bc", [DS, 2 * 512], F32)
    asc = lk.sb("lf_asc", [128, 512], F32)
    csc = lk.sb("lf_csc", [128, 512], F32)
    esc = lk.sb("lf_esc", [128, 512], F32)
    cst = lk.sb("lf_cstf", [128, 2], F32)
    kit = LNScratch(lk, 512, name="lf_ln")
    p_dtr = lk.ps("lf_pdtr", [DTR, 512], F32)
    p_bc = lk.ps("lf_pbc", [128, 512], F32)
    kit.init_consts()

    def g0(g):
        g.memset(cst[:, 0:1], -1.0)
        g.memset(cst[:, 1:2], 1.0)
        load_weight_bf16(lk, g, xp, xp_in, NDC, 96)
        i = g.dma_start(out=wdt[:], in_=wdt_in)
        lk.inc(i, "wt", 16)

    lk.on("gpsimd", g0)

    def ld0(s):
        for t, srcp in [(cw, cw_in), (cb, cb_in), (bdt, bdt_in)]:
            i = s.dma_start(out=t[:], in_=srcp)
            lk.inc(i, "ld", 16)

    lk.on("sync", ld0)

    widths = [259] + _tile_widths(EXT - 259, 512)
    t0 = 0
    for w in widths:
        o = t0

        def ld_fn(s, o=o, w=w):
            lk.wait(s, "ev")
            for c in range(NCH):
                i = s.dma_start(out=xt[:, c * w:(c + 1) * w],
                                in_=x1_in[c * 128:(c + 1) * 128, o:o + w])
                lk.inc(i, "ld", 16)

        lk.on("sync", ld_fn)

        def get_x(c, w=w):
            return xt[:, c * w:(c + 1) * w]

        def put_out(c, v, o=o, w=w):
            tmp = sq_full[:, c * w:(c + 1) * w]
            v.tensor_tensor(tmp, get_x(c), kit.p_rep[:, 0:w], ALU.subtract)
            v.tensor_tensor(xn1[:, c * NTE + o:c * NTE + o + w], tmp,
                            kit.p_rep[:, 512:512 + w], ALU.mult)

        emit_ln(lk, kit, get_x, put_out, w, sq_full, out_war_sems=("st",))

        def st_fn(s, o=o, w=w):
            if o + w > 3:
                lk.wait(s, "ev")
                so = max(o, 3)
                for c in range(NCH):
                    i = s.dma_start(out=xn1_out[c * 128:(c + 1) * 128, so - 3:o + w - 3],
                                    in_=xn1[:, c * NTE + so:c * NTE + o + w])
                    lk.inc(i, "st", 16)

        lk.on("sync", st_fn)
        t0 += w
    xn1_ev = lk.c["ev"]

    n_loads = [0]

    def stream_n(n):
        def g_n(g, n=n):
            uses = n_loads[0]
            if uses >= 2:
                g.wait_ge(lk.sems["nf"], bnf + uses - 1)
            for kc in range(NCH):
                i = g.dma_start(out=nring[:, (n % 2) * (NCH * 128) + kc * 128:
                                          (n % 2) * (NCH * 128) + (kc + 1) * 128],
                                in_=win_in[kc * 128:(kc + 1) * 128, n * 128:(n + 1) * 128])
                lk.inc(i, "nt", 16)
            n_loads[0] += 1

        lk.on("gpsimd", g_n)

    XI_M = [(0, 259), (259, 512), (771, 512), (1283, 512), (1795, 256)]
    Z_M = [(3, 512), (515, 512), (1027, 512), (1539, 512)]

    for n in range(32):
        stream_n(n)
        is_xi = n < 16
        mlist = XI_M if is_xi else Z_M

        def t_xz(t, n=n, mlist=mlist, nl=n_loads[0], ev_need=xn1_ev):
            lk.wait(t, "ev", ev_need)
            t.wait_ge(lk.sems["nt"], bnt + 16 * 8 * nl)
            for gi_, (mo, w) in enumerate(mlist):
                ib, bank = pool.acquire(t)
                for kc in range(NCH):
                    i = lk.nc.tensor.matmul(
                        bank[:, 0:w],
                        nring[:, (n % 2) * (NCH * 128) + kc * 128:(n % 2) * (NCH * 128) + (kc + 1) * 128],
                        xn1[:, kc * NTE + mo:kc * NTE + mo + w],
                        start=(kc == 0), stop=(kc == NCH - 1))
                lk.inc(i, "mm")
                t_xz.banks.append((ib, bank, mo, w, lk.c["mm"]))

        t_xz.banks = []
        lk.on("tensor", t_xz)

        def v_xz(v, n=n, is_xi=is_xi, banks=t_xz.banks, stc=lk.c["st"]):
            if not is_xi:
                lk.wait(v, "st", stc)
            for (ib, bank, mo, w, mmc) in banks:
                lk.wait(v, "mm", mmc)
                if is_xi:
                    i = v.tensor_copy(xi[:, n * NTE + mo:n * NTE + mo + w], bank[:, 0:w])
                else:
                    i = v.tensor_copy(z_t[:, mo - 3:mo - 3 + w], bank[:, 0:w])
                pool.freed(i, ib)
            i = v.tensor_copy(lk.dummy[:1, 0:8], lk.dummy[:1, 8:16])
            i.then_inc(lk.sems["nf"], 1)
            lk.c["nf"] += 1
            lk.mark(v, "ev")

        lk.on("vector", v_xz)

        if not is_xi:
            def st_z(s, n=n):
                lk.wait(s, "ev")
                i = s.dma_start(out=z_out[(n - 16) * 128:(n - 15) * 128, :],
                                in_=z_t[:, 0:TLOC])
                lk.inc(i, "st", 16)

            lk.on("sync", st_z)

    xi_ev = lk.c["ev"]

    for mt, (mo, w) in enumerate(MT512):
        for ci in range(NDC):
            def v_conv(v, ci=ci, mo=mo, w=w, xe=xi_ev, stc=lk.c["st"], scc=lk.c["sc"]):
                lk.wait(v, "ev", xe)
                lk.wait(v, "st", stc)
                lk.wait(v, "sc", scc)
                base = ci * NTE + mo
                v.tensor_scalar(csc[:, 0:w], xi[:, base:base + w],
                                cw[:, ci * 4:ci * 4 + 1], 0.0, ALU.mult, ALU.add)
                v.scalar_tensor_tensor(esc[:, 0:w], xi[:, base + 1:base + 1 + w],
                                       cw[:, ci * 4 + 1:ci * 4 + 2], csc[:, 0:w],
                                       ALU.mult, ALU.add)
                v.scalar_tensor_tensor(csc[:, 0:w], xi[:, base + 2:base + 2 + w],
                                       cw[:, ci * 4 + 2:ci * 4 + 3], esc[:, 0:w],
                                       ALU.mult, ALU.add)
                i = v.scalar_tensor_tensor(esc[:, 0:w], xi[:, base + 3:base + 3 + w],
                                           cw[:, ci * 4 + 3:ci * 4 + 4], csc[:, 0:w],
                                           ALU.mult, ALU.add)
                lk.inc(i, "ev")

            lk.on("vector", v_conv)

            def s_u(s, ci=ci, w=w):
                lk.wait(s, "ev")
                i = s.activation(u_t[:, ci * 512:ci * 512 + w], esc[:, 0:w],
                                 AF.Silu, bias=cb[:, ci:ci + 1])
                lk.inc(i, "sc")

            lk.on("scalar", s_u)

        u_sc = lk.c["sc"]

        def t_xp(t, mo=mo, w=w, us=u_sc):
            lk.wait(t, "sc", us)
            lk.wait(t, "ev")
            for kc in range(NDC):
                lk.nc.tensor.matmul(p_dtr[:, 0:w], xp[:, kc * 96:kc * 96 + 64],
                                    u_t[:, kc * 512:kc * 512 + w],
                                    start=(kc == 0), stop=(kc == NDC - 1))
            for kc in range(NDC):
                lk.nc.tensor.matmul(p_bc[0:16, 0:w], xp[:, kc * 96 + 64:kc * 96 + 80],
                                    u_t[:, kc * 512:kc * 512 + w],
                                    start=(kc == 0), stop=(kc == NDC - 1))
            for kc in range(NDC):
                i = lk.nc.tensor.matmul(p_bc[64:80, 0:w], xp[:, kc * 96 + 80:kc * 96 + 96],
                                        u_t[:, kc * 512:kc * 512 + w],
                                        start=(kc == 0), stop=(kc == NDC - 1))
            lk.inc(i, "mm")

        lk.on("tensor", t_xp)

        def v_xp(v, mo=mo, w=w):
            lk.wait(v, "mm")
            v.tensor_copy(dtr_bf[:, 0:w], p_dtr[:, 0:w])
            v.tensor_copy(bc_sb[:, 0:w], p_bc[0:16, 0:w])
            i = v.tensor_copy(bc_sb[:, 512:512 + w], p_bc[64:80, 0:w])
            lk.mark(v, "ev")

        lk.on("vector", v_xp)

        def st_bc(s, mo=mo, w=w):
            lk.wait(s, "ev")
            i = s.dma_start(out=B_out[:, mo:mo + w], in_=bc_sb[:, 0:w])
            lk.inc(i, "st", 16)
            i = s.dma_start(out=C_out[:, mo:mo + w], in_=bc_sb[:, 512:512 + w])
            lk.inc(i, "st", 16)
            for ci in range(NDC):
                i = s.dma_start(out=u_out[ci * 128:(ci + 1) * 128, mo:mo + w],
                                in_=u_t[:, ci * 512:ci * 512 + w])
                lk.inc(i, "st", 16)

        lk.on("sync", st_bc)

        for ci in range(NDC):
            def t_dt(t, ci=ci, w=w, ec=lk.c["ev"]):
                lk.wait(t, "ev", ec)
                ib, bank = pool.acquire(t)
                i = lk.nc.tensor.matmul(bank[:, 0:w], wdt[:, ci * 128:(ci + 1) * 128],
                                        dtr_bf[:, 0:w], start=True, stop=True)
                lk.inc(i, "mm")
                t_dt.bk = (ib, bank)

            lk.on("tensor", t_dt)
            ib, bank = t_dt.bk

            def v_dt1(v, ci=ci, w=w, ib=ib, bank=bank, scc=lk.c["sc"]):
                lk.wait(v, "mm")
                lk.wait(v, "sc", scc)
                i = v.tensor_scalar(asc[:, 0:w], bank[:, 0:w], bdt[:, ci:ci + 1], 0.0,
                                    ALU.add, ALU.add)
                pool.freed(i, ib)
                lk.mark(v, "ev")

            lk.on("vector", v_dt1)

            def s_dt(s, ci=ci, w=w):
                lk.wait(s, "ev")
                s.activation(csc[:, 0:w], asc[:, 0:w], AF.Abs)
                s.activation(csc[:, 0:w], csc[:, 0:w], AF.Exp, scale=cst[:, 0:1])
                i = s.activation(csc[:, 0:w], csc[:, 0:w], AF.Ln, bias=cst[:, 1:2])
                lk.inc(i, "sc")

            lk.on("scalar", s_dt)

            def v_dt2(v, ci=ci, w=w):
                lk.wait(v, "sc")
                v.tensor_scalar(esc[:, 0:w], asc[:, 0:w], 0.0, 0.0, ALU.max, ALU.add)
                i = v.tensor_tensor(dt_t[:, ci * 512:ci * 512 + w], esc[:, 0:w],
                                    csc[:, 0:w], ALU.add)
                lk.inc(i, "ev")

            lk.on("vector", v_dt2)

        dt_ev = lk.c["ev"]

        def st_dt(s, mo=mo, w=w, de=dt_ev):
            lk.wait(s, "ev", de)
            for ci in range(NDC):
                i = s.dma_start(out=dt_out[ci * 128:(ci + 1) * 128, mo:mo + w],
                                in_=dt_t[:, ci * 512:ci * 512 + w])
                lk.inc(i, "st", 16)

        lk.on("sync", st_dt)

    def fin(s):
        lk.wait(s, "st")

    lk.on("sync", fin)
    lk.end_phase()



def emit_scan(lk, io, pool):
    NCB = 8
    NSEG = 8
    SG = T // NSEG
    dtG, uG, BsG, CsG = io["dtG"], io["uG"], io["BsG"], io["CsG"]
    al_in, m0_in, m1_in = io["s_alog"], io["m0c"], io["m1c"]
    y_out = io["y_s"]
    lk.begin_phase("scan")
    bld = lk.c["ld"]
    bgp = lk.c["gp"]
    dt_s = lk.sb("sn_dt", [128, NCB * SG], BF16)
    u_s = lk.sb("sn_u", [128, NCB * SG], BF16)
    dtA = lk.sb("sn_dtA", [128, 4 * SG], BF16)
    dtB = lk.sb("sn_dtB", [128, 4 * SG], BF16)
    uA = lk.sb("sn_uA", [128, 4 * SG], BF16)
    uB = lk.sb("sn_uB", [128, 4 * SG], BF16)
    dtu = lk.sb("sn_dtu", [128, NCB * SG], F32)
    brow = lk.sb("sn_brow", [1, DS * 512], F32)
    crow = lk.sb("sn_crow", [1, DS * 512], F32)
    asb = lk.sb("sn_asb", [128, NCB * DS], F32)
    m0 = lk.sb("sn_m0", [128, 1], F32)
    m1 = lk.sb("sn_m1", [128, 1], F32)
    abuf = lk.sb("sn_abuf", [128, 2 * NCB * SG], F32)
    bbuf = lk.sb("sn_bbuf", [128, NCB * SG], F32)
    hbuf = lk.sb("sn_hbuf", [128, NCB * SG], F32)
    tmp = bbuf
    brep = lk.sb("sn_brep", [128, SG], F32)
    crep = lk.sb("sn_crep", [128, SG], F32)
    hst = lk.sb("sn_hst", [128, NCB * DS * 2], F32)
    yacc = lk.sb("sn_yacc", [128, NCB * SG], F32)
    ybf = lk.sb("sn_ybf", [128, NCB * SG], BF16)
    onesr = lk.sb("sn_onesr", [1, 128], F32)

    def g0(g):
        g.memset(onesr[:], 1.0)
        i = g.memset(hst[:], 0.0)
        lk.inc(i, "gp")

    lk.on("gpsimd", g0)

    def ld0(s):
        i = s.dma_start(out=asb[:], in_=al_in)
        lk.inc(i, "ld", 16)
        i = s.dma_start(out=m0[:], in_=m0_in)
        lk.inc(i, "ld", 16)
        i = s.dma_start(out=m1[:], in_=m1_in)
        lk.inc(i, "ld", 16)

    lk.on("sync", ld0)

    def s_a(s):
        lk.wait(s, "ld", bld + 16)
        i = s.activation(asb[:], asb[:], AF.Exp)
        lk.inc(i, "sc")

    lk.on("scalar", s_a)

    def v_a(v):
        lk.wait(v, "sc")
        i = v.tensor_scalar(asb[:], asb[:], -1.0, 0.0, ALU.mult, ALU.add)
        lk.mark(v, "ev")

    lk.on("vector", v_a)
    a_ev = lk.c["ev"]

    it = 0
    ev_after_iter = []
    SGA = NCB * SG

    for seg in range(NSEG):
        so = seg * SG
        blk_ = so // TLOC
        co = so % TLOC

        for wv in range(2):
            def ld_wave(s, blk_=blk_, co=co, wv=wv, evn=lk.c["ev"], bln=lk.c["bl"]):
                lk.wait(s, "ev", evn)
                lk.wait(s, "bl", bln)
                for ci in range(4):
                    cb = wv * 4 + ci
                    for dst, src, off in ((dtA, dtG, 0), (uA, uG, 0),
                                          (dtB, dtG, 1024), (uB, uG, 1024)):
                        i = s.dma_start(out=dst[:, ci * SG:(ci + 1) * SG],
                                        in_=src[blk_ * DN + off + cb * 128:
                                                blk_ * DN + off + (cb + 1) * 128,
                                                co:co + SG])
                        lk.inc(i, "ld", 16)
                if wv == 0:
                    for si in range(DS):
                        i = s.dma_start(out=brow[:1, si * SG:(si + 1) * SG],
                                        in_=BsG[blk_ * DS + si:blk_ * DS + si + 1,
                                                co:co + SG])
                        lk.inc(i, "ld", 16)
                        i = s.dma_start(out=crow[:1, si * SG:(si + 1) * SG],
                                        in_=CsG[blk_ * DS + si:blk_ * DS + si + 1,
                                                co:co + SG])
                        lk.inc(i, "ld", 16)

            lk.on("sync", ld_wave)

            def g_blend(g, wv=wv, evn=lk.c["ev"]):
                lk.wait(g, "ld")
                lk.wait(g, "ev", evn)
                o = wv * 4 * SG
                W4 = 4 * SG
                g.tensor_scalar(dt_s[:, o:o + W4], dtA[:], m0[:, 0:1], 0.0,
                                ALU.mult, ALU.add)
                g.scalar_tensor_tensor(dt_s[:, o:o + W4], dtB[:], m1[:, 0:1],
                                       dt_s[:, o:o + W4], ALU.mult, ALU.add)
                g.tensor_scalar(u_s[:, o:o + W4], uA[:], m0[:, 0:1], 0.0,
                                ALU.mult, ALU.add)
                g.scalar_tensor_tensor(u_s[:, o:o + W4], uB[:], m1[:, 0:1],
                                       u_s[:, o:o + W4], ALU.mult, ALU.add)
                lk.mark(g, "bl")

            lk.on("vector", g_blend)
        bl_done = lk.c["bl"]

        def v_dtu(v, stc=lk.c["st"], bld_=bl_done):
            lk.wait(v, "bl", bld_)
            lk.wait(v, "gp", bgp + 1)
            lk.wait(v, "st", stc)
            i = v.tensor_tensor(dtu[:], dt_s[:], u_s[:], ALU.mult)
            lk.mark(v, "ev")

        lk.on("vector", v_dtu)

        dtu3 = dtu.ap().rearrange("p (c t) -> p c t", c=NCB)
        bb3 = bbuf.ap().rearrange("p (c t) -> p c t", c=NCB)
        hb3 = hbuf.ap().rearrange("p (c t) -> p c t", c=NCB)
        ya3 = yacc.ap().rearrange("p (c t) -> p c t", c=NCB)
        tm3 = tmp.ap().rearrange("p (c t) -> p c t", c=NCB)
        hst4 = hst.ap().rearrange("p (c s u) -> p c s u", c=NCB, u=2)

        for si in range(DS):
            def t_rep(t, si=si):
                lk.wait(t, "ld")
                lk.wait(t, "gp", bgp + 1)
                ib0, b0 = pool.acquire(t)
                i = lk.nc.tensor.matmul(b0[:, 0:SG], onesr[:],
                                        brow[:1, si * SG:(si + 1) * SG],
                                        start=True, stop=True)
                lk.inc(i, "mm")
                ib1, b1 = pool.acquire(t)
                i = lk.nc.tensor.matmul(b1[:, 0:SG], onesr[:],
                                        crow[:1, si * SG:(si + 1) * SG],
                                        start=True, stop=True)
                lk.inc(i, "mm")
                t_rep.banks = [(ib0, b0), (ib1, b1)]

            lk.on("tensor", t_rep)
            (ib0, b0), (ib1, b1) = t_rep.banks

            def v_rep(v, ib0=ib0, b0=b0, ib1=ib1, b1=b1):
                lk.wait(v, "mm")
                i = v.tensor_copy(brep[:], b0[:, 0:SG])
                pool.freed(i, ib0)
                i = v.tensor_copy(crep[:], b1[:, 0:SG])
                pool.freed(i, ib1)

            lk.on("vector", v_rep)

            slot = it % 2

            def s_agen(s, si=si, slot=slot, it=it, bld_=bl_done):
                if it >= 2:
                    lk.wait(s, "ev", ev_after_iter[it - 2])
                else:
                    lk.wait(s, "ev", a_ev)
                lk.wait(s, "bl", bld_)
                for cb in range(NCB):
                    i = s.activation(abuf[:, slot * SGA + cb * SG:slot * SGA + (cb + 1) * SG],
                                     dt_s[:, cb * SG:(cb + 1) * SG], AF.Exp,
                                     scale=asb[:, cb * DS + si:cb * DS + si + 1])
                lk.inc(i, "sc")

            lk.on("scalar", s_agen)

            def v_scan(v, si=si, slot=slot, scc=lk.c["sc"]):
                lk.wait(v, "sc", scc)
                lk.wait(v, "gp", bgp + 1)
                brep3 = brep.ap().unsqueeze(1).broadcast_to([128, NCB, SG])
                crep3 = crep.ap().unsqueeze(1).broadcast_to([128, NCB, SG])
                v.tensor_tensor(bb3, dtu3, brep3, ALU.mult)
                for cb in range(NCB):
                    v.tensor_tensor_scan(hbuf[:, cb * SG:(cb + 1) * SG],
                                         abuf[:, slot * SGA + cb * SG:slot * SGA + (cb + 1) * SG],
                                         bbuf[:, cb * SG:(cb + 1) * SG],
                                         hst[:, 2 * (cb * DS + si) + 1:2 * (cb * DS + si) + 2],
                                         ALU.mult, ALU.add)
                if si == 0:
                    v.tensor_tensor(ya3, hb3, crep3, ALU.mult)
                else:
                    v.tensor_tensor(tm3, hb3, crep3, ALU.mult)
                    v.tensor_tensor(yacc[:], yacc[:], tmp[:], ALU.add)
                i = v.tensor_copy(hst4[:, :, si, :], hb3[:, :, SG - 2:SG])
                lk.inc(i, "ev")

            lk.on("vector", v_scan)
            ev_after_iter.append(lk.c["ev"])
            it += 1

        def v_y(v, stc=lk.c["st"]):
            lk.wait(v, "st", stc)
            for cb in range(NCB):
                i = v.tensor_copy(ybf[:, cb * SG:(cb + 1) * SG],
                                  yacc[:, cb * SG:(cb + 1) * SG])
            lk.mark(v, "ev")

        lk.on("vector", v_y)

        def st_y(s, so=so):
            lk.wait(s, "ev")
            for cb in range(NCB):
                i = s.dma_start(out=y_out[cb * 128:(cb + 1) * 128, so:so + SG],
                                in_=ybf[:, cb * SG:(cb + 1) * SG])
                lk.inc(i, "st", 16)

        lk.on("sync", st_y)

    def fin(s):
        lk.wait(s, "st")

    lk.on("sync", fin)
    lk.end_phase()



def emit_gate_s1(lk, io, pool):
    yG = io["yG"]
    u_in, z_in, xn1_in = io["u_s"], io["z_s"], io["xn1_s"]
    dsk_in, wout_in = io["g_dsk"], io["g_woutT"]
    gw_in, gb_in, id_in = io["g_gwp"], io["g_gb"], io["ident"]
    m0_in, m1_in = io["m0c"], io["m1c"]
    s1_out = io["s1_s"]
    oh1_out, oh2_out = io["oh1_s"], io["oh2_s"]
    v1_out, v2_out = io["v1_s"], io["v2_s"]
    lk.begin_phase("gate")
    lb = lk.c["ld"]
    wb = lk.c["wt"]
    bgp = lk.c["gp"]
    yA = lk.sb("gs_yA", [128, NDC * 512], BF16)
    yB = lk.sb("gs_yB", [128, NDC * 512], BF16)
    yt = lk.sb("gs_yt", [128, NDC * 512], BF16)
    ut = lk.sb("gs_ut", [128, NDC * 512], BF16)
    zt = lk.sb("gs_zt", [128, NDC * 512], BF16)
    zs = lk.sb("gs_zs", [128, 512], BF16)
    ydt = lk.sb("gs_ydt", [128, 512], F32)
    yg = lk.sb("gs_yg", [128, NDC * 512], BF16)
    wout = lk.sb("gs_wout", [128, NDC * D], BF16)
    dsk = lk.sb("gs_dsk", [128, NDC], F32)
    s1t = lk.sb("gs_s1t", [128, NCH * 512], BF16)
    xn1 = lk.sb("gs_xn1", [128, NCH * TLOC], BF16)
    gwsb = lk.sb("gs_gwp", [128, NCH * E], BF16)
    gbrep = lk.sb("gs_gbrep", [128, E], F32)
    gbrow = lk.sb("gs_gbrow", [1, E], F32)
    ident = lk.sb("gs_ident", [128, 128], F32)
    onesr = lk.sb("gs_onesr", [1, 128], F32)
    m0 = lk.sb("gs_m0", [128, 1], F32)
    m1 = lk.sb("gs_m1", [128, 1], F32)
    gla = lk.sb("gs_gla", [128, 16 * E], F32)
    oh1a = lk.sb("gs_oh1a", [128, 16 * E], F32)
    oh2a = lk.sb("gs_oh2a", [128, 16 * E], F32)
    mska = lk.sb("gs_mska", [128, 16 * E], F32)
    glm16 = lk.sb("gs_glm16", [128, 16], F32)
    va = lk.sb("gs_va", [128, 32], F32)
    ohT1 = lk.sb("gs_ohT1", [8, TLOC], F32)
    ohT2 = lk.sb("gs_ohT2", [8, TLOC], F32)
    vT = lk.sb("gs_vT", [2, TLOC], F32)
    p_g = lk.ps("gs_pg", [128, 8], F32)
    p_t1 = lk.ps("gs_pt1", [8, 128], F32)
    p_t2 = lk.ps("gs_pt2", [8, 128], F32)
    p_t3 = lk.ps("gs_pt3", [2, 128], F32)
    p_gb = lk.ps("gs_pgb", [128, E], F32)

    def g0(g):
        i = g.memset(onesr[:], 1.0)
        lk.inc(i, "gp")
        load_weight_bf16(lk, g, wout, wout_in, NDC, D)
        i = g.dma_start(out=gwsb[:], in_=gw_in)
        lk.inc(i, "wt", 16)

    lk.on("gpsimd", g0)

    def ld0(s):
        for t, srcp in [(dsk, dsk_in), (gbrow, gb_in), (ident, id_in)]:
            i = s.dma_start(out=t[:], in_=srcp)
            lk.inc(i, "ld", 16)
        i = s.dma_start(out=m0[:], in_=m0_in)
        lk.inc(i, "ld", 16)
        i = s.dma_start(out=m1[:], in_=m1_in)
        lk.inc(i, "ld", 16)
        for c in range(NCH):
            i = s.dma_start(out=xn1[:, c * TLOC:(c + 1) * TLOC],
                            in_=xn1_in[c * 128:(c + 1) * 128, :])
            lk.inc(i, "ld", 16)

    lk.on("sync", ld0)

    def t_gb(t):
        lk.wait(t, "ld", lb + 48)
        lk.wait(t, "gp", bgp + 1)
        i = lk.nc.tensor.matmul(p_gb[:, 0:E], onesr[:], gbrow[:1, 0:E],
                                start=True, stop=True)
        lk.inc(i, "mm")

    lk.on("tensor", t_gb)

    def v_gb(v):
        lk.wait(v, "mm")
        i = v.tensor_copy(gbrep[:], p_gb[:, 0:E])
        lk.mark(v, "ev")

    lk.on("vector", v_gb)

    for (mo, w) in MT512:
        def ld_t(s, mo=mo, w=w, evc=lk.c["ev"], bln=lk.c["bl"]):
            lk.wait(s, "ev", evc)
            lk.wait(s, "bl", bln)
            for ci in range(NDC):
                i = s.dma_start(out=yA[:, ci * 512:ci * 512 + w],
                                in_=yG[ci * 128:(ci + 1) * 128, mo:mo + w])
                lk.inc(i, "ld", 16)
                i = s.dma_start(out=yB[:, ci * 512:ci * 512 + w],
                                in_=yG[ci * 128:(ci + 1) * 128, TLOC + mo:TLOC + mo + w])
                lk.inc(i, "ld", 16)
                i = s.dma_start(out=ut[:, ci * 512:ci * 512 + w],
                                in_=u_in[ci * 128:(ci + 1) * 128, mo:mo + w])
                lk.inc(i, "ld", 16)
                i = s.dma_start(out=zt[:, ci * 512:ci * 512 + w],
                                in_=z_in[ci * 128:(ci + 1) * 128, mo:mo + w])
                lk.inc(i, "ld", 16)

        lk.on("sync", ld_t)

        def g_blend(g, w=w, evc=lk.c["ev"]):
            lk.wait(g, "ld")
            lk.wait(g, "ev", evc)
            for ci in range(NDC):
                sl = slice(ci * 512, ci * 512 + w)
                g.tensor_scalar(yt[:, sl], yA[:, sl], m0[:, 0:1], 0.0, ALU.mult, ALU.add)
                g.scalar_tensor_tensor(yt[:, sl], yB[:, sl], m1[:, 0:1], yt[:, sl],
                                       ALU.mult, ALU.add)
            lk.mark(g, "bl")

        lk.on("vector", g_blend)
        bl_done = lk.c["bl"]

        for ci in range(NDC):
            def v_yd(v, ci=ci, w=w, scc=lk.c["sc"], bld_=bl_done):
                lk.wait(v, "ld")
                lk.wait(v, "bl", bld_)
                lk.wait(v, "sc", scc)
                i = v.scalar_tensor_tensor(ydt[:, 0:w], ut[:, ci * 512:ci * 512 + w],
                                           dsk[:, ci:ci + 1], yt[:, ci * 512:ci * 512 + w],
                                           ALU.mult, ALU.add)
                lk.inc(i, "ev")

            lk.on("vector", v_yd)

            def s_z(s, ci=ci, w=w):
                lk.wait(s, "ev")
                i = s.activation(zs[:, 0:w], zt[:, ci * 512:ci * 512 + w], AF.Silu)
                lk.inc(i, "sc")

            lk.on("scalar", s_z)

            def v_yg(v, ci=ci, w=w):
                lk.wait(v, "sc")
                i = v.tensor_tensor(yg[:, ci * 512:ci * 512 + w], ydt[:, 0:w],
                                    zs[:, 0:w], ALU.mult)
                lk.inc(i, "ev")

            lk.on("vector", v_yg)

        def evac_s1(e, bank, n, mo2, w2):
            return e.tensor_copy(s1t[:, n * 512:n * 512 + w2], bank[:, 0:w2])

        mm_block(lk, pool,
                 get_lhsT=lambda kc, n: wout[:, kc * D + n * 128:kc * D + (n + 1) * 128],
                 get_rhs=lambda kc, mo2, w2: yg[:, kc * 512:kc * 512 + w2],
                 nK=NDC, n_list=range(NCH), m_list=[(0, w)],
                 evac=evac_s1, evac_engine="vector",
                 pre_wait=lambda t, c=lk.c["ev"]: (lk.wait(t, "wt", wb + 16 * 16),
                                                   lk.wait(t, "ev", c)))
        emit_barrier(lk, "vector", "ev")

        def st_s1(s, mo=mo, w=w):
            lk.wait(s, "ev")
            for c in range(NCH):
                i = s.dma_start(out=s1_out[c * 128:(c + 1) * 128, mo:mo + w],
                                in_=s1t[:, c * 512:c * 512 + w])
                lk.inc(i, "st", 16)

        lk.on("sync", st_s1)

    for lc in range(16):
        lo = lc * 128

        def t_gl(t, lo=lo, evc=lk.c["ev"]):
            lk.wait(t, "ev", evc)
            lk.wait(t, "wt", wb + 16 * 17)
            for kc in range(NCH):
                i = lk.nc.tensor.matmul(p_g[:, 0:E],
                                        xn1[:, kc * TLOC + lo:kc * TLOC + lo + 128],
                                        gwsb[:, kc * E:(kc + 1) * E],
                                        start=(kc == 0), stop=(kc == NCH - 1))
            lk.inc(i, "mm")

        lk.on("tensor", t_gl)

        def v_ev(v, lc=lc):
            lk.wait(v, "mm")
            i = v.tensor_tensor(gla[:, lc * E:(lc + 1) * E], p_g[:, 0:E], gbrep[:],
                                ALU.add)
            lk.inc(i, "ev")

        lk.on("vector", v_ev)

    gl3 = gla.ap().rearrange("p (l e) -> p l e", e=E)
    oh13 = oh1a.ap().rearrange("p (l e) -> p l e", e=E)
    oh23 = oh2a.ap().rearrange("p (l e) -> p l e", e=E)
    msk3 = mska.ap().rearrange("p (l e) -> p l e", e=E)
    va3 = va.ap().rearrange("p (l k) -> p l k", k=2)
    m1b = va3[:, :, 0:1].broadcast_to([128, 16, E])
    m2b = va3[:, :, 1:2].broadcast_to([128, 16, E])
    g16b = glm16.ap().unsqueeze(2).broadcast_to([128, 16, E])

    def v_b1(v):
        sp = lambda: v.tensor_copy(ydt[:, 0:512], ydt[:, 0:512])
        v.reduce_max(glm16[:], gl3, axis=AX)
        sp()
        i = v.tensor_tensor(gl3, gl3, g16b, ALU.subtract)
        lk.inc(i, "ev")

    lk.on("vector", v_b1)

    def s_b1(s):
        lk.wait(s, "ev")
        i = s.activation(gla[:], gla[:], AF.Exp)
        lk.inc(i, "sc")

    lk.on("scalar", s_b1)

    def v_b2(v):
        lk.wait(v, "sc")
        sp = lambda: v.tensor_copy(ydt[:, 0:512], ydt[:, 0:512])
        v.reduce_sum(glm16[:], gl3, axis=AX)
        sp()
        v.reciprocal(glm16[:], glm16[:])
        sp()
        v.tensor_tensor(gl3, gl3, g16b, ALU.mult)
        v.reduce_max(va3[:, :, 0:1], gl3, axis=AX)
        sp()
        v.tensor_tensor(oh13, gl3, m1b, ALU.is_ge)
        v.tensor_scalar(mska[:], oh1a[:], -1.0, 1.0, ALU.mult, ALU.add)
        v.tensor_tensor(mska[:], gla[:], mska[:], ALU.mult)
        v.reduce_max(va3[:, :, 1:2], msk3, axis=AX)
        sp()
        i = v.tensor_tensor(oh23, msk3, m2b, ALU.is_ge)
        lk.inc(i, "ev")

    lk.on("vector", v_b2)

    for lc in range(16):
        lo = lc * 128

        def t_tr(t, lc=lc, evc=lk.c["ev"]):
            lk.wait(t, "ev", evc)
            lk.nc.tensor.transpose(p_t1[:, 0:128], oh1a[:, lc * E:(lc + 1) * E], ident[:])
            lk.nc.tensor.transpose(p_t2[:, 0:128], oh2a[:, lc * E:(lc + 1) * E], ident[:])
            i = lk.nc.tensor.transpose(p_t3[:, 0:128], va[:, 2 * lc:2 * lc + 2], ident[:])
            lk.inc(i, "mm")

        lk.on("tensor", t_tr)

        def v_tr(v, lo=lo):
            lk.wait(v, "mm")
            v.tensor_copy(ohT1[:, lo:lo + 128], p_t1[:, 0:128])
            v.tensor_copy(ohT2[:, lo:lo + 128], p_t2[:, 0:128])
            i = v.tensor_copy(vT[:, lo:lo + 128], p_t3[:, 0:128])
            lk.mark(v, "ev")

        lk.on("vector", v_tr)

    def st1(s):
        lk.wait(s, "ev")
        i = s.dma_start(out=oh1_out, in_=ohT1[:])
        lk.inc(i, "st", 16)
        i = s.dma_start(out=oh2_out, in_=ohT2[:])
        lk.inc(i, "st", 16)
        i = s.dma_start(out=v1_out, in_=vT[0:1, :])
        lk.inc(i, "st", 16)
        i = s.dma_start(out=v2_out, in_=vT[1:2, :])
        lk.inc(i, "st", 16)
        s.wait_ge(lk.sems["st"], lk.c["st"])

    lk.on("sync", st1)
    lk.end_phase()



def emit_moe(lk, io, pool):
    xn1_in = io["xn1_s"]
    oh1_in, oh2_in = io["oh1_s"], io["oh2_s"]
    v1_in, v2_in = io["v1_s"], io["v2_s"]
    w1_in, w2_in = io["m_w1"], io["m_w2"]
    b1_in, b2_in, sel_in = io["m_b1"], io["m_b2"], io["m_sel"]
    mg_in, mb_in = io["m_lng"], io["m_lnb"]
    g1_out = io["g1_s"]
    lk.begin_phase("moe")
    lb = lk.c["ld"]
    wb = lk.c["wt"]
    bgp = lk.c["gp"]
    xn1 = lk.sb("mo_xn1", [128, NCH * TLOC], BF16)
    boost = lk.sb("mo_boost", [128, NCH * TLOC], BF16)
    g1acc = lk.sb("mo_g1acc", [128, NCH * TLOC], BF16)
    w1e = lk.sb("mo_w1e", [128, NCH * HMOE], BF16)
    w2e = lk.sb("mo_w2e", [128, NDC * D], BF16)
    h_sb = lk.sb("mo_h", [128, NDC * 512], BF16)
    ohf = lk.sb("mo_ohf", [E, 2 * TLOC], BF16)
    vf = lk.sb("mo_vf", [1, 2 * TLOC], BF16)
    selc = lk.sb("mo_sel", [E, E * 128], BF16)
    mrep = lk.sb("mo_mrep", [128, 512], BF16)
    mvrep = lk.sb("mo_mvrep", [128, 512], F32)
    vrep = lk.sb("mo_vrep", [128, 512], F32)
    eot = lk.sb("mo_eot", [128, 512], BF16)
    tmpf = lk.sb("mo_tmpf", [128, 512], F32)
    b1c = lk.sb("mo_b1c", [128, E * 16], F32)
    b2c = lk.sb("mo_b2c", [128, E * 8], F32)
    mg = lk.sb("mo_mg", [128, NCH], F32)
    mb = lk.sb("mo_mb", [128, NCH], F32)
    onesr = lk.sb("mo_onesr", [1, 128], BF16)
    kit = LNScratch(lk, 512, name="mo_ln")
    kit.init_consts()

    def g0(g):
        g.memset(onesr[:], 1.0)
        i = g.dma_start(out=ohf[:, 0:TLOC], in_=oh1_in)
        lk.inc(i, "wt", 16)
        i = g.dma_start(out=ohf[:, TLOC:2 * TLOC], in_=oh2_in)
        lk.inc(i, "wt", 16)
        i = g.dma_start(out=vf[:, 0:TLOC], in_=v1_in)
        lk.inc(i, "wt", 16)
        i = g.dma_start(out=vf[:, TLOC:2 * TLOC], in_=v2_in)
        lk.inc(i, "wt", 16)
        i = g.dma_start(out=selc[:], in_=sel_in)
        lk.inc(i, "wt", 16)
        i = g.nop()
        lk.inc(i, "gp")

    lk.on("gpsimd", g0)

    def ld0(s):
        for c in range(NCH):
            i = s.dma_start(out=xn1[:, c * TLOC:(c + 1) * TLOC],
                            in_=xn1_in[c * 128:(c + 1) * 128, :])
            lk.inc(i, "ld", 16)
        i = s.dma_start(out=b1c[:], in_=b1_in)
        lk.inc(i, "ld", 16)
        i = s.dma_start(out=b2c[:], in_=b2_in)
        lk.inc(i, "ld", 16)
        i = s.dma_start(out=mg[:], in_=mg_in)
        lk.inc(i, "ld", 16)
        i = s.dma_start(out=mb[:], in_=mb_in)
        lk.inc(i, "ld", 16)

    lk.on("sync", ld0)

    def v_init(v):
        lk.wait(v, "ld", lb + 16 * NCH)
        for c in range(NCH):
            v.tensor_copy(boost[:, c * TLOC:(c + 1) * TLOC], xn1[:, c * TLOC:(c + 1) * TLOC])
            i = v.tensor_copy(g1acc[:, c * TLOC:(c + 1) * TLOC], xn1[:, c * TLOC:(c + 1) * TLOC])
        lk.mark(v, "ev")

    lk.on("vector", v_init)

    mm_h_done = [lk.c["mm"]]
    mm_w2_done = [lk.c["mm"]]
    for p2 in range(2):
        rhs_t = xn1 if p2 == 0 else boost
        for e in range(E):
            def g_we(g, e=e, w1w=mm_h_done[0], w2w=mm_w2_done[0]):
                g.wait_ge(lk.sems["mm"], w1w)
                for kc in range(NCH):
                    i = g.dma_start(out=w1e[:, kc * HMOE:(kc + 1) * HMOE],
                                    in_=w1_in[e * D + kc * 128:e * D + (kc + 1) * 128, :])
                    lk.inc(i, "wt", 16)
                g.wait_ge(lk.sems["mm"], w2w)
                for kc in range(NDC):
                    i = g.dma_start(out=w2e[:, kc * D:(kc + 1) * D],
                                    in_=w2_in[e * HMOE + kc * 128:e * HMOE + (kc + 1) * 128, :])
                    lk.inc(i, "wt", 16)

            lk.on("gpsimd", g_we)
            wt_w1 = lk.c["wt"] - 16 * NDC
            wt_w2 = lk.c["wt"]

            for (mo, w) in MT512:
                def t_rep(t, e=e, p2=p2, mo=mo, w=w, evc=lk.c["ev"]):
                    lk.wait(t, "ev", evc)
                    lk.wait(t, "ld")
                    lk.wait(t, "wt", wb + 16 * 5)
                    lk.wait(t, "gp", bgp + 1)
                    ib0, b0 = pool.acquire(t)
                    i = lk.nc.tensor.matmul(b0[:, 0:w], selc[:, e * 128:(e + 1) * 128],
                                            ohf[:, p2 * TLOC + mo:p2 * TLOC + mo + w],
                                            start=True, stop=True)
                    lk.inc(i, "mm")
                    ib1, b1 = pool.acquire(t)
                    i = lk.nc.tensor.matmul(b1[:, 0:w], onesr[:],
                                            vf[:1, p2 * TLOC + mo:p2 * TLOC + mo + w],
                                            start=True, stop=True)
                    lk.inc(i, "mm")
                    t_rep.banks = [(ib0, b0), (ib1, b1)]

                lk.on("tensor", t_rep)
                (ib0, b0), (ib1, b1) = t_rep.banks

                def v_rep(v, ib0=ib0, b0=b0, ib1=ib1, b1=b1, w=w):
                    lk.wait(v, "mm")
                    i = v.tensor_copy(mrep[:, 0:w], b0[:, 0:w])
                    pool.freed(i, ib0)
                    i = v.tensor_copy(vrep[:, 0:w], b1[:, 0:w])
                    pool.freed(i, ib1)
                    v.tensor_tensor(mvrep[:, 0:w], mrep[:, 0:w], vrep[:, 0:w], ALU.mult)
                    lk.mark(v, "ev")

                lk.on("vector", v_rep)

                def evac_h(e_, bank, n, mo2, w2, e=e):
                    return e_.tensor_scalar(h_sb[:, n * 512:n * 512 + w2], bank[:, 0:w2],
                                            b1c[:, e * 16 + n:e * 16 + n + 1], 0.0,
                                            ALU.add, ALU.max)

                mm_block(lk, pool,
                         get_lhsT=lambda kc, n: w1e[:, kc * HMOE + n * 128:kc * HMOE + (n + 1) * 128],
                         get_rhs=lambda kc, mo2, w2, rhs_t=rhs_t, mo=mo: rhs_t[:, kc * TLOC + mo:kc * TLOC + mo + w2],
                         nK=NCH, n_list=range(NDC), m_list=[(0, w)],
                         evac=evac_h, evac_engine="vector",
                         pre_wait=lambda t, wd=wt_w1, c=lk.c["ev"]: (lk.wait(t, "wt", wd),
                                                                     lk.wait(t, "ev", c)))
                emit_barrier(lk, "vector", "ev")
                mm_h_done[0] = lk.c["mm"]

                def evac_o(e_, bank, n, mo2, w2, e=e, p2=p2, mo=mo):
                    e_.tensor_scalar(eot[:, 0:w2], bank[:, 0:w2],
                                     b2c[:, e * 8 + n:e * 8 + n + 1], 0.0, ALU.add, ALU.add)
                    e_.tensor_tensor(tmpf[:, 0:w2], eot[:, 0:w2], mvrep[:, 0:w2], ALU.mult)
                    i = e_.tensor_tensor(g1acc[:, n * TLOC + mo:n * TLOC + mo + w2],
                                         g1acc[:, n * TLOC + mo:n * TLOC + mo + w2],
                                         tmpf[:, 0:w2], ALU.add)
                    if p2 == 0:
                        e_.tensor_tensor(tmpf[:, 0:w2], eot[:, 0:w2], mrep[:, 0:w2],
                                         ALU.mult)
                        i = e_.scalar_tensor_tensor(boost[:, n * TLOC + mo:n * TLOC + mo + w2],
                                                    tmpf[:, 0:w2], 0.5,
                                                    boost[:, n * TLOC + mo:n * TLOC + mo + w2],
                                                    ALU.mult, ALU.add)
                    return i

                mm_block(lk, pool,
                         get_lhsT=lambda kc, n: w2e[:, kc * D + n * 128:kc * D + (n + 1) * 128],
                         get_rhs=lambda kc, mo2, w2: h_sb[:, kc * 512:kc * 512 + w2],
                         nK=NDC, n_list=range(NCH), m_list=[(0, w)],
                         evac=evac_o, evac_engine="vector",
                         pre_wait=lambda t, wd=wt_w2, c=lk.c["ev"]: (lk.wait(t, "wt", wd),
                                                                     lk.wait(t, "ev", c)))
                emit_barrier(lk, "vector", "ev")
                mm_w2_done[0] = lk.c["mm"]

    sq_full = h_sb.ap().bitcast(F32)
    tsc = tmpf
    g1o = w1e
    for (mo, w) in MT512:
        def get_x(c, mo=mo, w=w):
            return g1acc[:, c * TLOC + mo:c * TLOC + mo + w]

        def put_out(c, v, w=w):
            v.tensor_tensor(tsc[:, 0:w], get_x(c), kit.p_rep[:, 0:w], ALU.subtract)
            v.tensor_tensor(tsc[:, 0:w], tsc[:, 0:w], kit.p_rep[:, 512:512 + w], ALU.mult)
            v.tensor_scalar(g1o[:, c * 512:c * 512 + w], tsc[:, 0:w],
                            mg[:, c:c + 1], mb[:, c:c + 1], ALU.mult, ALU.add)

        emit_ln(lk, kit, get_x, put_out, w, sq_full, out_war_sems=("st",))

        def s_silu(s, mo=mo, w=w):
            lk.wait(s, "ev")
            for c in range(NCH):
                i = s.activation(g1o[:, c * 512:c * 512 + w], g1o[:, c * 512:c * 512 + w],
                                 AF.Silu)
            lk.inc(i, "sc")

        lk.on("scalar", s_silu)

        def st_g(s, mo=mo, w=w):
            lk.wait(s, "sc")
            for c in range(NCH):
                i = s.dma_start(out=g1_out[c * 128:(c + 1) * 128, mo:mo + w],
                                in_=g1o[:, c * 512:c * 512 + w])
                lk.inc(i, "st", 16)

        lk.on("sync", st_g)

    def fin(s):
        lk.wait(s, "st")

    lk.on("sync", fin)
    lk.end_phase()



def build_fused(upto=99, dbg=()):
    lk = L("mega")
    io = {}
    io["xT"] = lk.din("xT", [D, TLOC + 1])
    io["ln1g"] = lk.din("ln1g", [128, NCH])
    io["ln1b"] = lk.din("ln1b", [128, NCH])
    io["p_wq"] = lk.din("p_wq", [D, D]); io["p_wk"] = lk.din("p_wk", [D, D])
    io["p_wv"] = lk.din("p_wv", [D, D])
    io["p_bq"] = lk.din("p_bq", [128, NCH]); io["p_bk"] = lk.din("p_bk", [128, NCH])
    io["p_bv"] = lk.din("p_bv", [1, D])
    io["proj"] = lk.din("proj", [DH, DK])
    io["a_wo"] = lk.din("a_wo", [D, D]); io["a_bo"] = lk.din("a_bo", [128, NCH])
    io["ident"] = lk.din("ident", [128, 128])
    io["ln2g"] = lk.din("ln2g", [128, NCH]); io["ln2b"] = lk.din("ln2b", [128, NCH])
    io["f_w1"] = lk.din("f_w1", [D, DI]); io["f_b1"] = lk.din("f_b1", [128, DI // 128])
    io["f_w2"] = lk.din("f_w2", [DI, D]); io["f_b2"] = lk.din("f_b2", [128, NCH])
    io["r_wrt"] = lk.din("r_wrt", [D, D]); io["r_wkt"] = lk.din("r_wkt", [D, D])
    io["r_wvt"] = lk.din("r_wvt", [D, D])
    io["r_mur"] = lk.din("r_mur", [128, NCH]); io["r_muk"] = lk.din("r_muk", [128, NCH])
    io["l_winT"] = lk.din("l_winT", [D, 2 * DN])
    io["l_convw"] = lk.din("l_convw", [128, NDC * 4])
    io["l_convb"] = lk.din("l_convb", [128, NDC])
    io["l_xprojT"] = lk.din("l_xprojT", [DN, 96])
    io["l_wdtT"] = lk.din("l_wdtT", [DTR, DN])
    io["l_bdt"] = lk.din("l_bdt", [128, NDC])
    io["s_alog"] = lk.din("s_alog", [128, 8 * DS])
    io["m0c"] = lk.din("m0c", [128, 1]); io["m1c"] = lk.din("m1c", [128, 1])
    io["g_dsk"] = lk.din("g_dsk", [128, NDC])
    io["g_woutT"] = lk.din("g_woutT", [DN, D])
    io["g_gwp"] = lk.din("g_gwp", [128, NCH * E])
    io["g_gb"] = lk.din("g_gb", [1, E])
    io["m_w1"] = lk.din("m_w1", [E * D, HMOE])
    io["m_w2"] = lk.din("m_w2", [E * HMOE, D])
    io["m_b1"] = lk.din("m_b1", [128, E * 16])
    io["m_b2"] = lk.din("m_b2", [128, E * 8])
    io["m_sel"] = lk.din("m_sel", [E, E * 128])
    io["m_lng"] = lk.din("m_lng", [128, NCH])
    io["m_lnb"] = lk.din("m_lnb", [128, NCH])
    io["out"] = lk.dout("out", [D, TLOC])

    def mk(name, shape, dt=F32):
        io[name] = (lk.dout(name, shape, dt) if name in dbg
                    else lk.dscratch(name, shape, dt))

    mk("xn_s", [D, TLOC + 1], BF16)
    mk("x2_s", [D, TLOC], BF16)
    mk("qf_s", [D, TLOC], BF16)
    mk("kvp_i", [NH * DH, 2 * DK])
    mk("kvp_o", [NH * DH, 2 * DK])
    mk("xs_s", [D, TLOC], BF16)
    mk("x2b_s", [D, TLOC], BF16)
    mk("s0_s", [D, TLOC], BF16)
    mk("g0_s", [D, TLOC], BF16)
    mk("x1_s", [D, TLOC + 3])
    mk("halo_i", [D, 4])
    mk("halo_o", [2 * D, 4])
    mk("xn1_s", [D, TLOC], BF16)
    mk("u_s", [DN, TLOC], BF16)
    mk("dt_s", [DN, TLOC], BF16)
    mk("z_s", [DN, TLOC], BF16)
    mk("Bs_s", [DS, TLOC])
    mk("Cs_s", [DS, TLOC])
    mk("dtG", [2 * DN, TLOC], BF16)
    mk("uG", [2 * DN, TLOC], BF16)
    mk("BsG", [2 * DS, TLOC])
    mk("CsG", [2 * DS, TLOC])
    mk("y_s", [DN // 2, T], BF16)
    mk("yG", [DN, T], BF16)
    mk("s1_s", [D, TLOC], BF16)
    mk("oh1_s", [E, TLOC])
    mk("oh2_s", [E, TLOC])
    mk("v1_s", [1, TLOC])
    mk("v2_s", [1, TLOC])
    mk("g1_s", [D, TLOC], BF16)

    lk.sem("ld", "wt", "w2t", "nt", "nf", "rf", "mm", "sc", "ev", "st", "gp",
           "ph", "bl", "pbf0", "pbf1",
           "cc1", "cc2", "cc3", "cc4", "cc5", "cc6", "cc7")
    lk.block()
    pool = PsumPool(lk, 2)

    def g_init(g):
        i = g.memset(lk.dummy[:], 0.0)
        lk.inc(i, "gp")

    lk.on("gpsimd", g_init)

    steps = [
        lambda: emit_xn(lk, io),
        lambda: emit_perf_kv(lk, io, pool),
        lambda: lk.cc("AllReduce", ALU.add, io["kvp_i"], io["kvp_o"], "cc1"),
        lambda: emit_attn(lk, io, pool),
        lambda: emit_ln2(lk, io),
        lambda: emit_ffn(lk, io, pool),
        lambda: emit_rwkv(lk, io, pool),
        lambda: emit_combine(lk, io,
                             x_src=lambda c, mo, w: io["xT"][c * 128:(c + 1) * 128,
                                                            1 + mo:1 + mo + w],
                             s_src=io["s0_s"], g_src=io["g0_s"],
                             out_ap=io["x1_s"], out_off=3, halo_dst=io["halo_i"],
                             tag="cb1"),
        lambda: lk.cc("AllGather", ALU.bypass, io["halo_i"], io["halo_o"], "cc2"),
        lambda: emit_halo_fix(lk, io),
        lambda: emit_l1front(lk, io, pool),
        lambda: lk.cc("AllGather", ALU.bypass, io["dt_s"], io["dtG"], "cc3"),
        lambda: lk.cc("AllGather", ALU.bypass, io["u_s"], io["uG"], "cc4"),
        lambda: lk.cc("AllGather", ALU.bypass, io["Bs_s"], io["BsG"], "cc5"),
        lambda: lk.cc("AllGather", ALU.bypass, io["Cs_s"], io["CsG"], "cc6"),
        lambda: emit_scan(lk, io, pool),
        lambda: lk.cc("AllGather", ALU.bypass, io["y_s"], io["yG"], "cc7"),
        lambda: emit_gate_s1(lk, io, pool),
        lambda: emit_moe(lk, io, pool),
        lambda: emit_combine(lk, io,
                             x_src=lambda c, mo, w: io["x1_s"][c * 128:(c + 1) * 128,
                                                               3 + mo:3 + mo + w],
                             s_src=io["s1_s"], g_src=io["g1_s"],
                             out_ap=io["out"], out_off=0, tag="cb2"),
    ]
    for i, step in enumerate(steps):
        if i >= upto:
            break
        step()

    def fin(s):
        s.wait_ge(lk.sems["st"], lk.c["st"])

    lk.on("sync", fin)
    lk.close()
    return lk



def host_inputs(inputs):
    x = np.asarray(inputs["L0_attn_x"])
    al = np.asarray(inputs["L1_attn_A_log"], np.float32)
    shared = {
        "ln1g": host_prep_feat_cols(inputs["L0_attn_ln1_g"]),
        "ln1b": host_prep_feat_cols(inputs["L0_attn_ln1_b"]),
        "p_wq": _f32h(inputs["L0_attn_wq"]),
        "p_wk": _f32h(inputs["L0_attn_wk"]),
        "p_wv": _f32h(inputs["L0_attn_wv"]),
        "p_bq": host_prep_feat_cols(inputs["L0_attn_bq"]),
        "p_bk": host_prep_feat_cols(inputs["L0_attn_bk"]),
        "p_bv": _f32h(np.asarray(inputs["L0_attn_bv"]).reshape(1, D)),
        "proj": _f32h(inputs["L0_attn_proj_mat"]),
        "a_wo": _f32h(inputs["L0_attn_wo"]),
        "a_bo": host_prep_feat_cols(inputs["L0_attn_bo"]),
        "ident": np.eye(128, dtype=np.float32),
        "ln2g": host_prep_feat_cols(inputs["L0_attn_ln2_g"]),
        "ln2b": host_prep_feat_cols(inputs["L0_attn_ln2_b"]),
        "f_w1": _f32h(inputs["L0_attn_w1"]),
        "f_b1": host_prep_feat_cols(inputs["L0_attn_b1"]),
        "f_w2": _f32h(inputs["L0_attn_w2"]),
        "f_b2": host_prep_feat_cols(inputs["L0_attn_b2"]),
        "r_wrt": _f32h(np.asarray(inputs["L0_ffn_W_r"]).T),
        "r_wkt": _f32h(np.asarray(inputs["L0_ffn_W_k"]).T),
        "r_wvt": _f32h(np.asarray(inputs["L0_ffn_W_v"]).T),
        "r_mur": host_prep_feat_cols(inputs["L0_ffn_mu_r"]),
        "r_muk": host_prep_feat_cols(inputs["L0_ffn_mu_k"]),
        "l_winT": _f32h(np.asarray(inputs["L1_attn_W_in"]).T),
        "l_convw": _f32h(np.transpose(
            np.asarray(inputs["L1_attn_conv_w"]).reshape(NDC, 128, 4),
            (1, 0, 2)).reshape(128, NDC * 4)),
        "l_convb": host_prep_feat_cols(inputs["L1_attn_conv_b"]),
        "l_xprojT": _f32h(np.asarray(inputs["L1_attn_W_xproj"]).T),
        "l_wdtT": _f32h(np.asarray(inputs["L1_attn_W_dt"]).T),
        "l_bdt": host_prep_feat_cols(inputs["L1_attn_b_dt"]),
        "g_dsk": host_prep_feat_cols(inputs["L1_attn_D_skip"]),
        "g_woutT": _f32h(np.asarray(inputs["L1_attn_W_out"]).T),
        "g_gwp": _f32h(np.transpose(
            np.asarray(inputs["L1_ffn_gate_W"]).reshape(NCH, 128, E),
            (1, 0, 2)).reshape(128, NCH * E)),
        "g_gb": _f32h(np.asarray(inputs["L1_ffn_gate_b"]).reshape(1, E)),
        "m_w1": _f32h(np.asarray(inputs["L1_ffn_W1"]).reshape(E * D, HMOE)),
        "m_w2": _f32h(np.asarray(inputs["L1_ffn_W2"]).reshape(E * HMOE, D)),
        "m_b1": _f32h(np.concatenate(
            [np.asarray(inputs["L1_ffn_b1"])[e].reshape(16, 128).T for e in range(E)],
            axis=1)),
        "m_b2": _f32h(np.concatenate(
            [np.asarray(inputs["L1_ffn_b2"])[e].reshape(8, 128).T for e in range(E)],
            axis=1)),
        "m_sel": _f32h(np.kron(np.eye(E, dtype=np.float32),
                               np.ones((1, 128), np.float32))),
        "m_lng": host_prep_feat_cols(inputs["L1_ffn_ln_gamma"]),
        "m_lnb": host_prep_feat_cols(inputs["L1_ffn_ln_beta"]),
    }
    in_maps = []
    for core in range(NCORES):
        b, j = divmod(core, 2)
        t0 = j * TLOC
        ext = np.zeros((D, TLOC + 1), np.float32)
        ext[:, 1:] = x[b, t0:t0 + TLOC, :].T
        if j == 1:
            ext[:, 0] = x[b, t0 - 1, :]
        r0, r1 = j * 1024, (j + 1) * 1024
        alc = np.ascontiguousarray(np.transpose(
            al[r0:r1].reshape(8, 128, DS), (1, 0, 2)).reshape(128, 8 * DS))
        m = dict(shared)
        m["xT"] = np.ascontiguousarray(ext)
        m["s_alog"] = alc
        m["m0c"] = np.full((128, 1), 1.0 - j, np.float32)
        m["m1c"] = np.full((128, 1), float(j), np.float32)
        in_maps.append(m)
    return in_maps


def kernel(**inputs):
    lk = build_fused()
    in_maps = host_inputs(inputs)
    res = run_bass_kernel_spmd(lk.nc, in_maps, core_ids=list(range(NCORES)))
    out = np.zeros((B, T, D), np.float32)
    for core in range(NCORES):
        b, j = divmod(core, 2)
        out[b, j * TLOC:(j + 1) * TLOC, :] = np.asarray(res.results[core]["out"]).T
    return out
